# version 03
# baseline (speedup 1.0000x reference)
import contextlib
import numpy as np
import ml_dtypes
import concourse.bass as bass
import concourse.mybir as mybir
from concourse.bass_utils import run_bass_kernel_spmd

F32 = mybir.dt.float32
BF16 = mybir.dt.bfloat16
AF = mybir.ActivationFunctionType
ALU = mybir.AluOpType
bfloat16 = ml_dtypes.bfloat16

NCORES = 8
B = 2
S = 8192
D = 1024
NQ = 16
NKV = 4
HD = 64
TPC = 2048
NT = TPC // 128
ROPE_BASE = 10000.0
MAGIC1 = float(3 * 2 ** 22)
MAGICH = float(3 * 2 ** 21)
LN_EPS = 1e-5



def _strips(core):
    cc = core % 4
    return (cc, 7 - cc)


def _local_positions(core):
    s0, s1 = _strips(core)
    return np.concatenate([np.arange(s0 * 1024, (s0 + 1) * 1024),
                           np.arange(s1 * 1024, (s1 + 1) * 1024)])


def _rope_tables(core):
    inv_freq = (1.0 / ROPE_BASE ** (np.arange(0, HD, 2, dtype=np.float64) / HD))
    pos = _local_positions(core).astype(np.float64)
    ang = pos[:, None] * inv_freq[None, :]
    return np.cos(ang).astype(np.float32), np.sin(ang).astype(np.float32)


def _shard_x_fm(x, core):
    b = core // 4
    rows = x[b][_local_positions(core)]
    return np.ascontiguousarray(rows.T)


class Ctr:
    def __init__(self):
        self.v = 0

    def inc(self, n=1):
        self.v += n
        return self.v


def _bcast(ap, reps):
    p, w = ap.shape
    return ap.unsqueeze(1).broadcast_to([p, reps, w])


DTSIZE = {F32: 4, BF16: 2, mybir.dt.int32: 4}


class Alloc:

    def __init__(self, nc, es, arena=None):
        self.nc, self.es, self.arena = nc, es, arena
        self.off = 0
        self.hwm = 0

    def reset(self):
        self.phase_hwms = getattr(self, "phase_hwms", [])
        self.phase_hwms.append(self.off)
        self.off = 0

    def sbuf(self, name, shape, dtype):
        if self.arena is None:
            return self.es.enter_context(self.nc.sbuf_tensor(name, shape, dtype)).ap()
        p, w = shape
        nel = (w * DTSIZE[dtype] + 3) // 4
        assert self.off + nel <= self.arena.shape[1], \
            f"arena overflow at {name}: {self.off + nel}"
        view = self.arena[:, self.off:self.off + nel]
        self.off += nel
        self.hwm = max(self.hwm, self.off)
        if dtype != F32:
            view = view.bitcast(dtype)[:, 0:w]
        if p != 128:
            view = view[0:p, :]
        return view



def emit_qkv(S, AL, io, has_bias):
    nc = S.nc
    op, wait, inc_last, C = S.op, S.wait, S.inc_last, S.C
    xb_sb = AL.sbuf("q_xb", [128, 8 * TPC], BF16)
    wq_sb = AL.sbuf("q_wq", [128, 8 * 1024], BF16)
    wk_sb = AL.sbuf("q_wk", [128, 8 * 256], BF16)
    wv_sb = AL.sbuf("q_wv", [128, 8 * 256], BF16)
    bias_sb = AL.sbuf("q_bias", [1, 1536], BF16)
    ones1 = AL.sbuf("q_ones1", [1, 128], BF16)
    cos_sb = AL.sbuf("q_cos", [128, NT * 32], F32)
    sin_sb = AL.sbuf("q_sin", [128, NT * 32], F32)
    qo_sb = AL.sbuf("q_qo", [128, 2 * 1024], BF16)
    ko_sb = AL.sbuf("q_ko", [128, 2 * 256], BF16)
    vo_sb = AL.sbuf("q_vo", [128, 2 * 256], BF16)
    rsc = AL.sbuf("q_rsc", [128, 512], F32)
    P = S.P
    pq = [P[:, 512 * i:512 * (i + 1)] for i in range(4)]
    pk = [P[:, 2048 + 512 * i:2048 + 512 * i + 256] for i in range(2)]
    pv = [P[:, 3072 + 512 * i:3072 + 512 * i + 256] for i in range(2)]

    op("sync", lambda: nc.sync.dma_start(
        out=cos_sb.rearrange("p (t i) -> p t i", i=32),
        in_=io["cosT"].rearrange("(t p) i -> p t i", p=128)))
    inc_last("sync", "ld", 16)
    op("sync", lambda: nc.sync.dma_start(
        out=sin_sb.rearrange("p (t i) -> p t i", i=32),
        in_=io["sinT"].rearrange("(t p) i -> p t i", p=128)))
    inc_last("sync", "ld", 16)
    op("gpsimd", lambda: nc.gpsimd.dma_start(
        out=xb_sb.rearrange("p (k t) -> p k t", k=8),
        in_=io["xfm"].rearrange("(k p) t -> p k t", p=128)))
    inc_last("gpsimd", "ld", 16)
    op("gpsimd", lambda: nc.gpsimd.dma_start(
        out=wq_sb.rearrange("p (k c) -> p k c", k=8),
        in_=io["wq"].rearrange("(k p) c -> p k c", p=128)))
    inc_last("gpsimd", "ld", 16)
    op("gpsimd", lambda: nc.gpsimd.dma_start(
        out=wk_sb.rearrange("p (k c) -> p k c", k=8),
        in_=io["wk"].rearrange("(k p) c -> p k c", p=128)))
    inc_last("gpsimd", "ld", 16)
    op("gpsimd", lambda: nc.gpsimd.dma_start(
        out=wv_sb.rearrange("p (k c) -> p k c", k=8),
        in_=io["wv"].rearrange("(k p) c -> p k c", p=128)))
    inc_last("gpsimd", "ld", 16)
    if has_bias:
        op("gpsimd", lambda: nc.gpsimd.dma_start(out=bias_sb, in_=io["bqkv"]))
        inc_last("gpsimd", "ld", 16)
    lds = C["ld"].v
    op("vector", lambda: nc.vector.memset(ones1, 1.0))
    inc_last("vector", "cv", 1)
    cvs = C["cv"].v

    rp_hist = []
    st_hist = []
    for t in range(NT):
        sl = t % 2
        if t == 0:
            wait("tensor", "ld", lds)
            wait("tensor", "cv", cvs)
        if t >= 2:
            wait("tensor", "rp", rp_hist[t - 2])
        lhs = lambda k, t=t: xb_sb[:, 2048 * k + 128 * t:2048 * k + 128 * (t + 1)]
        for b in range(2):
            for k in range(8):
                def qf(b=b, k=k, t=t, sl=sl):
                    return nc.tensor.matmul(
                        pq[2 * sl + b], lhs(k, t),
                        wq_sb[:, 1024 * k + 512 * b:1024 * k + 512 * (b + 1)],
                        start=(k == 0), stop=(k == 7 and not has_bias))
                op("tensor", qf)
            if has_bias:
                def qbf(b=b, sl=sl):
                    return nc.tensor.matmul(
                        pq[2 * sl + b], ones1, io_bias_slice(bias_sb, 512 * b, 512),
                        start=False, stop=True)
                op("tensor", qbf)
        for k in range(8):
            def kf(k=k, t=t, sl=sl):
                return nc.tensor.matmul(pk[sl], lhs(k, t),
                                        wk_sb[:, 256 * k:256 * (k + 1)],
                                        start=(k == 0),
                                        stop=(k == 7 and not has_bias))
            op("tensor", kf)
        if has_bias:
            op("tensor", lambda sl=sl: nc.tensor.matmul(
                pk[sl], ones1, io_bias_slice(bias_sb, 1024, 256),
                start=False, stop=True))
        for k in range(8):
            def vf(k=k, t=t, sl=sl):
                return nc.tensor.matmul(pv[sl], lhs(k, t),
                                        wv_sb[:, 256 * k:256 * (k + 1)],
                                        start=(k == 0),
                                        stop=(k == 7 and not has_bias))
            op("tensor", vf)
        if has_bias:
            op("tensor", lambda sl=sl: nc.tensor.matmul(
                pv[sl], ones1, io_bias_slice(bias_sb, 1280, 256),
                start=False, stop=True))
        inc_last("tensor", "mm", 1)
        mmv = C["mm"].v

        wait("vector", "mm", mmv)
        if t >= 2:
            wait("vector", "st", st_hist[t - 2])
        cosv = _bcast(cos_sb[:, 32 * t:32 * (t + 1)], 8)
        sinv = _bcast(sin_sb[:, 32 * t:32 * (t + 1)], 8)
        cosv4 = _bcast(cos_sb[:, 32 * t:32 * (t + 1)], 4)
        sinv4 = _bcast(sin_sb[:, 32 * t:32 * (t + 1)], 4)
        for b in range(2):
            Pq = pq[2 * sl + b].rearrange("p (h d) -> p h d", d=64)
            Plo, Phi = Pq[:, :, 0:32], Pq[:, :, 32:64]
            qov = qo_sb[:, 1024 * sl + 512 * b:1024 * sl + 512 * (b + 1)] \
                .rearrange("p (h d) -> p h d", d=64)
            Av = rsc[:, 0:256].rearrange("p (h d) -> p h d", d=32)
            Bv = rsc[:, 256:512].rearrange("p (h d) -> p h d", d=32)
            op("vector", lambda Phi=Phi, Av=Av, sinv=sinv:
               nc.vector.tensor_tensor(Av, Phi, sinv, ALU.mult))
            op("vector", lambda Plo=Plo, Bv=Bv, cosv=cosv:
               nc.vector.tensor_tensor(Bv, Plo, cosv, ALU.mult))
            op("vector", lambda qov=qov, Bv=Bv, Av=Av:
               nc.vector.tensor_tensor(qov[:, :, 0:32], Bv, Av, ALU.subtract))
            op("vector", lambda Plo=Plo, Av=Av, sinv=sinv:
               nc.vector.tensor_tensor(Av, Plo, sinv, ALU.mult))
            op("vector", lambda Phi=Phi, Bv=Bv, cosv=cosv:
               nc.vector.tensor_tensor(Bv, Phi, cosv, ALU.mult))
            op("vector", lambda qov=qov, Bv=Bv, Av=Av:
               nc.vector.tensor_tensor(qov[:, :, 32:64], Bv, Av, ALU.add))
        Pk = pk[sl].rearrange("p (h d) -> p h d", d=64)
        Plo, Phi = Pk[:, :, 0:32], Pk[:, :, 32:64]
        kov = ko_sb[:, 256 * sl:256 * (sl + 1)].rearrange("p (h d) -> p h d", d=64)
        Av = rsc[:, 0:128].rearrange("p (h d) -> p h d", d=32)
        Bv = rsc[:, 256:384].rearrange("p (h d) -> p h d", d=32)
        op("vector", lambda Phi=Phi, Av=Av, sinv4=sinv4:
           nc.vector.tensor_tensor(Av, Phi, sinv4, ALU.mult))
        op("vector", lambda Plo=Plo, Bv=Bv, cosv4=cosv4:
           nc.vector.tensor_tensor(Bv, Plo, cosv4, ALU.mult))
        op("vector", lambda kov=kov, Bv=Bv, Av=Av:
           nc.vector.tensor_tensor(kov[:, :, 0:32], Bv, Av, ALU.subtract))
        op("vector", lambda Plo=Plo, Av=Av, sinv4=sinv4:
           nc.vector.tensor_tensor(Av, Plo, sinv4, ALU.mult))
        op("vector", lambda Phi=Phi, Bv=Bv, cosv4=cosv4:
           nc.vector.tensor_tensor(Bv, Phi, cosv4, ALU.mult))
        op("vector", lambda kov=kov, Bv=Bv, Av=Av:
           nc.vector.tensor_tensor(kov[:, :, 32:64], Bv, Av, ALU.add))
        op("vector", lambda sl=sl: nc.vector.tensor_copy(
            vo_sb[:, 256 * sl:256 * (sl + 1)], pv[sl]))
        rp_hist.append(inc_last("vector", "rp", 1))

        op("sync", lambda v=rp_hist[-1]: nc.sync.wait_ge(S.sems["rp"], v))
        op("sync", lambda t=t, sl=sl: nc.sync.dma_start(
            out=io["qtm"][128 * t:128 * (t + 1), :],
            in_=qo_sb[:, 1024 * sl:1024 * (sl + 1)]))
        inc_last("sync", "st", 16)
        op("sync", lambda t=t, sl=sl: nc.sync.dma_start(
            out=io["ktm"][128 * t:128 * (t + 1), :],
            in_=ko_sb[:, 256 * sl:256 * (sl + 1)]))
        inc_last("sync", "st", 16)
        op("sync", lambda t=t, sl=sl: nc.sync.dma_start(
            out=io["vtm"][128 * t:128 * (t + 1), :],
            in_=vo_sb[:, 256 * sl:256 * (sl + 1)]))
        st_hist.append(inc_last("sync", "st", 16))
    op("sync", lambda v=C["st"].v: nc.sync.wait_ge(S.sems["st"], v))
    return C["st"].v


def io_bias_slice(bias_sb, off, n):
    return bias_sb[0:1, off:off + n]


def build_qkv(has_bias):
    nc = bass.Bass("TRN2", target_bir_lowering=False)
    io = {}
    io["xfm"] = nc.dram_tensor("xfm", [D, TPC], F32, kind="ExternalInput").ap()
    io["wq"] = nc.dram_tensor("wq", [D, NQ * HD], F32, kind="ExternalInput").ap()
    io["wk"] = nc.dram_tensor("wk", [D, NKV * HD], F32, kind="ExternalInput").ap()
    io["wv"] = nc.dram_tensor("wv", [D, NKV * HD], F32, kind="ExternalInput").ap()
    if has_bias:
        io["bqkv"] = nc.dram_tensor("bqkv", [1, 1536], F32, kind="ExternalInput").ap()
    io["cosT"] = nc.dram_tensor("cosT", [TPC, 32], F32, kind="ExternalInput").ap()
    io["sinT"] = nc.dram_tensor("sinT", [TPC, 32], F32, kind="ExternalInput").ap()
    io["qtm"] = nc.dram_tensor("qtm", [TPC, NQ * HD], BF16, kind="ExternalOutput").ap()
    io["ktm"] = nc.dram_tensor("ktm", [TPC, NKV * HD], BF16, kind="ExternalOutput").ap()
    io["vtm"] = nc.dram_tensor("vtm", [TPC, NKV * HD], BF16, kind="ExternalOutput").ap()
    with contextlib.ExitStack() as es:
        AL = Alloc(nc, es)
        S = Sched(nc, es, ["ld", "cv", "mm", "rp", "st"])
        S.P = es.enter_context(nc.psum_tensor("P", [128, 4096], F32)).ap()
        block = es.enter_context(nc.Block())
        emit_qkv(S, AL, io, has_bias)
        S.emit(block)
    return nc


def run_qkv(layer, x_shards, weights, res):
    has_bias = (layer == 0)
    nc = build_qkv(has_bias)
    in_maps = []
    for core in range(NCORES):
        cos, sin = _rope_tables(core)
        m = dict(xfm=x_shards[core], wq=weights["wq"], wk=weights["wk"],
                 wv=weights["wv"], cosT=cos, sinT=sin)
        if has_bias:
            m["bqkv"] = np.concatenate([weights["bq"], weights["bk"],
                                        weights["bv"]])[None, :].astype(np.float32)
        in_maps.append(m)
    out = run_bass_kernel_spmd(nc, in_maps, core_ids=list(range(NCORES)))
    res.extend([out])
    return [r["qtm"] for r in out.results], [r["ktm"] for r in out.results], \
           [r["vtm"] for r in out.results]





def _alibi_slopes16():
    start = 2.0 ** (-2.0 ** (-(np.log2(16) - 3)))
    return np.array([start * start ** i for i in range(16)], dtype=np.float64)


def _l1_window_tiles():
    sl = _alibi_slopes16()
    w = np.ceil((115.0 / sl) / 128.0).astype(int)
    return np.minimum(w, 64)


def _pair_heads(p):
    base = 8 if p >= 4 else 0
    a = base + (p % 4)
    return a, a + 4


QPERM = []
for _p in range(8):
    _a, _b = _pair_heads(_p)
    QPERM += [_a, _b]


def attn_tensors(AL, layer):
    causal = (layer == 0)
    T = {}
    T["K_sb"] = [AL.sbuf(f"a_K{i}", [128, S_], BF16) for i in range(2)]
    T["V_sb"] = AL.sbuf("a_V", [128, 64 * 4 * 65], BF16)
    T["Q_sb"] = AL.sbuf("a_Q", [128, 2 * TPC], BF16)
    T["eX"] = AL.sbuf("a_eX", [128, 2 * 1024], BF16)
    T["Ofm"] = AL.sbuf("a_Ofm", [128, 8 * TPC], BF16)
    T["dsc64"] = AL.sbuf("a_dsc64", [128, 2048], F32)
    T["denp"] = AL.sbuf("a_denp", [128, 256], F32)
    T["rp"] = AL.sbuf("a_rp", [128, 256], BF16)
    T["rrow"] = AL.sbuf("a_rrow", [1, 16 * 512], BF16)
    T["wo_sb"] = AL.sbuf("a_wo", [128, 8 * 1024], BF16)
    T["bo_sb"] = AL.sbuf("a_bo", [128, 8], F32)
    T["ones1"] = AL.sbuf("a_ones1", [1, 128], BF16)
    T["oneK"] = AL.sbuf("a_oneK", [128, 1], BF16)
    T["msk"] = AL.sbuf("a_msk", [128, 512], BF16)
    if causal:
        T["dcol"] = AL.sbuf("a_dcol", [128, 16], F32)
        T["trow"] = AL.sbuf("a_trow", [128, 512], F32)
    else:
        T["alb"] = AL.sbuf("a_alb", [128, 64 * 16], F32)
    T["rb_sb"] = AL.sbuf("a_rb", [128, 512], BF16)
    T["On_sb"] = AL.sbuf("a_On", [128, 8 * 512], BF16)
    T["xstg"] = AL.sbuf("a_xstg", [128, 2 * 512], F32)
    T["s_c"] = AL.sbuf("a_sc", [128, 8 * 512], F32)
    T["sbf"] = AL.sbuf("a_sbf", [128, 2 * 512], BF16)
    T["ssq"] = AL.sbuf("a_ssq", [128, 2 * 512], BF16)
    T["stat"] = AL.sbuf("a_stat", [1, 2560], F32)
    T["x1stg"] = AL.sbuf("a_x1stg", [128, 2 * 512], F32)
    return T


S_ = S


def attn_psum(S):
    P = S.P
    return dict(Sp=[P[:, 0:1024], P[:, 1024:2048]],
                OA=P[0:65, 2048:2560], OB=P[0:65, 2560:3072],
                pbcF=P[:, 0:512], py=[P[:, 1024:1536], P[:, 1536:2048]],
                pmean=P[0:1, 2048:2560], pmsq=P[0:1, 2560:3072],
                prb=P[:, 3072:3584], pmr=P[:, 3584:4096])


def emit_attn_pre(S, T, io, layer, rank_major=False):
    nc = S.nc
    causal = (layer == 0)
    op, wait, inc_last, C = S.op, S.wait, S.inc_last, S.C
    if not rank_major:
        srcrows = [(1024 * s, 1024) for s in range(8)]
    else:
        srcrows = []
        for s in range(8):
            r = min(s, 7 - s)
            part = 0 if s < 4 else 1
            srcrows.append((2048 * r + 1024 * part, 1024))
    for t in range(2):
        for s in range(8):
            sr = srcrows[s][0]
            def ktf(t=t, s=s, sr=sr):
                return nc.sync.dma_start_transpose(
                    T["K_sb"][t][:, 1024 * s:1024 * (s + 1)],
                    io["kall"][sr:sr + 1024, 128 * t:128 * (t + 1)])
            op("sync", ktf)
            inc_last("sync", "ld", 16)
    op("vector", lambda: nc.vector.memset(T["V_sb"], 1.0))
    op("vector", lambda: nc.vector.memset(T["ones1"], 1.0))
    op("vector", lambda: nc.vector.memset(T["oneK"], 1.0 / 1024.0))
    inc_last("vector", "cv", 1)
    op("sync", lambda v=C["cv"].v: nc.sync.wait_ge(S.sems["cv"], v))
    for jv in range(4):
        for s in range(8):
            sr = srcrows[s][0]
            vdst = T["V_sb"].rearrange("p (c j d) -> p c j d", c=64, j=4)[
                :, 8 * s:8 * (s + 1), jv, 0:64]
            vsrc = io["vall"][sr:sr + 1024, :].rearrange(
                "(c p) jd -> p c jd", p=128)[:, :, 64 * jv:64 * (jv + 1)]
            op("sync", lambda vdst=vdst, vsrc=vsrc:
               nc.sync.dma_start(out=vdst, in_=vsrc))
            inc_last("sync", "ld", 16)
    if causal:
        op("sync", lambda: nc.sync.dma_start(out=T["dcol"], in_=io["dcolT"]))
        inc_last("sync", "ld", 16)
        op("sync", lambda: nc.sync.dma_start(out=T["trow"], in_=io["trowT"]))
        inc_last("sync", "ld", 16)
    else:
        op("sync", lambda: nc.sync.dma_start(out=T["alb"], in_=io["albT"]))
        inc_last("sync", "ld", 16)
    op("sync", lambda: nc.sync.dma_start(out=T["bo_sb"], in_=io["bo"]))
    inc_last("sync", "ld", 16)
    op("gpsimd", lambda: nc.gpsimd.dma_start(
        out=T["wo_sb"].rearrange("p (k c) -> p k c", k=8),
        in_=io["wo"].rearrange("(k p) c -> p k c", p=128)))
    inc_last("gpsimd", "ld", 16)
    return C["ld"].v


def emit_attn_core(S, Sb, T, io, layer, cc, emit_sync=True):
    nc = S.nc
    causal = (layer == 0)
    strips = (cc, 7 - cc)
    wtiles = _l1_window_tiles() if not causal else None
    PV = attn_psum(S)
    Sp, OA, OB = PV["Sp"], PV["OA"], PV["OB"]
    C = S.C
    lds = C["ld"].v

    def jrange(p, hh, c_):
        if causal:
            return list(range(8 * strips[hh] + 4 * (c_ + 1)))
        a, b = _pair_heads(p)
        return list(range(64 - max(wtiles[a], wtiles[b]), 64))

    def active(p, j, ab):
        if causal:
            return True
        return j >= 64 - wtiles[_pair_heads(p)[ab]]

    def q_load(p):
        S.op("sync", lambda p=p, sl=p % 2: nc.sync.dma_start_transpose(
            T["Q_sb"][:, TPC * sl:TPC * (sl + 1)], io["qtm"][:, 128 * p:128 * (p + 1)]))
        S.inc_last("sync", "q", 16)

    qb0 = C["q"].v
    if emit_sync:
        q_load(0)
        q_load(1)
    else:
        C["q"].inc(32)

    ex_hist = []
    pv_hist = []
    ucount = 0
    for p in range(8):
        hA, hB = _pair_heads(p)
        kv = [hA // 4, hB // 4]
        kt = T["K_sb"][0] if p < 4 else T["K_sb"][1]
        ktb = [64 * (kv[0] % 2), 64 * (kv[1] % 2)]
        qsl = p % 2
        for hh in range(2):
            for c_ in (0, 1):
                js = jrange(p, hh, c_)
                tqo = TPC * qsl + 1024 * hh + 512 * c_
                for ji, j in enumerate(js):
                    sl = ucount % 2
                    acts = [ab for ab in range(2) if active(p, j, ab)]
                    if ucount == 0:
                        Sb.wait("tensor", "ld", lds)
                    Sb.wait("tensor", "q", qb0 + 16 * (p + 1))
                    if ucount >= 2:
                        Sb.wait("tensor", "ex", ex_hist[ucount - 2])
                    for ab in acts:
                        def mmf(ab=ab, j=j, sl=sl, tqo=tqo, kt=kt, kb=ktb[ab]):
                            return nc.tensor.matmul(
                                Sp[sl][:, 512 * ab:512 * (ab + 1)],
                                kt[kb:kb + 64, 128 * j:128 * (j + 1)],
                                T["Q_sb"][64 * ab:64 * ab + 64, tqo:tqo + 512],
                                start=True, stop=True,
                                tile_position=(64 * ab, 0))
                        Sb.op("tensor", mmf)
                    Sb.inc_last("tensor", "qk", 1)

                    Sb.wait("scalar", "qk", C["qk"].v)
                    if ucount >= 2:
                        Sb.wait("scalar", "pv", pv_hist[ucount - 2])
                    if causal:
                        def ef(sl=sl):
                            return nc.scalar.activation(
                                T["eX"][:, 1024 * sl:1024 * (sl + 1)], Sp[sl],
                                AF.Exp, scale=0.125)
                        Sb.op("scalar", ef)
                    else:
                        for ab in acts:
                            h = _pair_heads(p)[ab]
                            def ef(sl=sl, ab=ab, j=j, h=h):
                                return nc.scalar.activation(
                                    T["eX"][:, 1024 * sl + 512 * ab:
                                            1024 * sl + 512 * (ab + 1)],
                                    Sp[sl][:, 512 * ab:512 * (ab + 1)],
                                    AF.Exp, scale=0.125,
                                    bias=T["alb"][:, 16 * j + h:16 * j + h + 1])
                            Sb.op("scalar", ef)
                    Sb.inc_last("scalar", "ex", 1)
                    ex_hist.append(C["ex"].v)

                    nmask = 0
                    if causal:
                        mi = None
                        m0 = 8 * strips[hh] + 4 * c_
                        if m0 <= j < m0 + 4:
                            mi = hh * 8 + c_ * 4 + (j - m0)
                        if mi is not None:
                            Sb.wait("vector", "ex", C["ex"].v)
                            def mkf(mi=mi):
                                return nc.vector.tensor_scalar(
                                    T["msk"], T["trow"], T["dcol"][:, mi:mi + 1], 0.0,
                                    ALU.subtract, ALU.is_ge)
                            Sb.op("vector", mkf)
                            for ab in range(2):
                                def mtf(sl=sl, ab=ab):
                                    v = T["eX"][:, 1024 * sl + 512 * ab:
                                                1024 * sl + 512 * (ab + 1)]
                                    return nc.vector.tensor_tensor(v, v, T["msk"], ALU.mult)
                                Sb.op("vector", mtf)
                            Sb.inc_last("vector", "mk", 1)
                            nmask = 1

                    Sb.wait("tensor", "ex", C["ex"].v)
                    if nmask:
                        Sb.wait("tensor", "mk", C["mk"].v)
                    for ab in acts:
                        Ox = OA if ab == 0 else OB
                        h = _pair_heads(p)[ab]
                        if causal:
                            st_ = (j == 0)
                            sp_ = (j == js[-1])
                        else:
                            st_ = (j == 64 - wtiles[h])
                            sp_ = (j == 63)
                        def pvf(Ox=Ox, sl=sl, ab=ab, j=j,
                                kvi=kv[ab], st_=st_, sp_=sp_):
                            return nc.tensor.matmul(
                                Ox,
                                T["V_sb"][:, (4 * j + kvi) * 65:(4 * j + kvi) * 65 + 65],
                                T["eX"][:, 1024 * sl + 512 * ab:
                                        1024 * sl + 512 * (ab + 1)],
                                start=st_, stop=sp_)
                        Sb.op("tensor", pvf)
                    Sb.inc_last("tensor", "pv", 1)
                    pv_hist.append(C["pv"].v)
                    ucount += 1

                Sb.wait("vector", "pv", C["pv"].v)
                for ab in range(2):
                    h = _pair_heads(p)[ab]
                    Ox = OA if ab == 0 else OB
                    gc = 2 * hh + c_
                    def evf(h=h, Ox=Ox, gc=gc):
                        return nc.vector.tensor_copy(
                            T["Ofm"][(h % 2) * 64:(h % 2) * 64 + 64,
                                     TPC * (h // 2) + 512 * gc:
                                     TPC * (h // 2) + 512 * (gc + 1)],
                            Ox[0:64, :])
                    Sb.op("vector", evf)
                    def dnf(Ox=Ox, c_=c_, ab=ab):
                        return nc.vector.tensor_copy(
                            T["dsc64"][64:65, 512 * (2 * ab + c_):512 * (2 * ab + c_ + 1)],
                            Ox[64:65, :])
                    Sb.op("vector", dnf)
                Sb.inc_last("vector", "oev", 1)
                Sb.wait("tensor", "oev", C["oev"].v)
                if c_ == 1:
                    phi = p * 2 + hh
                    if emit_sync:
                        S.op("sync", lambda v=C["oev"].v: nc.sync.wait_ge(S.sems["oev"], v))

                        def ddf(phi=phi):
                            return nc.sync.dma_start(
                                out=io["dscr"].rearrange("(a w) -> a w", w=2048)[phi:phi + 1, :],
                                in_=T["dsc64"][64:65, :])
                        S.op("sync", ddf)
                        S.inc_last("sync", "dd", 16)
                    else:
                        C["dd"].inc(16)
                    Sb.wait("vector", "dd", C["dd"].v)
        if p + 2 < 8:
            if emit_sync:
                S.op("sync", lambda v=C["qk"].v: nc.sync.wait_ge(S.sems["qk"], v))
                q_load(p + 2)
            else:
                C["q"].inc(16)


def emit_attn_post(S, T, io, layer):
    nc = S.nc
    op, wait, inc_last, C = S.op, S.wait, S.inc_last, S.C
    PV = attn_psum(S)
    pbcF, py = PV["pbcF"], PV["py"]
    pmean, pmsq, prb, pmr = PV["pmean"], PV["pmsq"], PV["prb"], PV["pmr"]

    def unit_of(h, gc):
        if h < 4:
            p_, ab = h, 0
        elif h < 8:
            p_, ab = h - 4, 1
        elif h < 12:
            p_, ab = h - 4, 0
        else:
            p_, ab = h - 8, 1
        hh, c_ = gc // 2, gc % 2
        return ((p_ * 2 + hh) * 2 + ab) * 2 + c_

    op("sync", lambda v=C["dd"].v: nc.sync.wait_ge(S.sems["dd"], v))
    op("sync", lambda: nc.sync.dma_start(
        out=T["denp"], in_=io["dscr"].rearrange("(p w) -> p w", w=256)))
    inc_last("sync", "db", 16)
    wait("vector", "db", C["db"].v)

    def _recipf():
        with nc.allow_low_precision(reason="bf16 softmax normalizer"):
            return nc.vector.reciprocal(T["rp"], T["denp"])
    op("vector", _recipf)
    inc_last("vector", "db", 1)
    op("sync", lambda v=C["db"].v: nc.sync.wait_ge(S.sems["db"], v))
    op("sync", lambda: nc.sync.dma_start(
        out=io["rscr"].rearrange("(p w) -> p w", w=256), in_=T["rp"]))
    inc_last("sync", "db", 16)

    stt_hist = []
    sm_hist = []
    x1s_hist = []
    xsb0 = C["xs"].v
    for c in range(4):
        op("sync", lambda v=C["db"].v: nc.sync.wait_ge(S.sems["db"], v))
        if c > 0:
            op("sync", lambda v=C["bc"].v: nc.sync.wait_ge(S.sems["bc"], v))
        hh_, c__ = c // 2, c % 2
        for quart in range(4):
            ab_ = 1 if quart in (1, 3) else 0
            p0 = [0, 0, 4, 4][quart]
            u0 = ((p0 * 2 + hh_) * 2 + ab_) * 2 + c__

            def rldf(quart=quart, u0=u0):
                return nc.sync.dma_start(
                    out=T["rrow"][0:1, 2048 * quart:2048 * (quart + 1)]
                    .rearrange("a (h w) -> a h w", w=512),
                    in_=io["rscr"].rearrange("(u w) -> u w", w=512)[u0:u0 + 25:8, :]
                    .unsqueeze(0))
            op("sync", rldf)
            inc_last("sync", "x1s", 16)
        def xpf_emit(o, c):
            ko_ = 8 * c + o
            if ko_ >= 2:
                op("sync", lambda v=stt_hist[ko_ - 2]:
                   nc.sync.wait_ge(S.sems["stt"], v))

            def xpf(o=o, c=c):
                return nc.sync.dma_start(
                    out=T["xstg"][:, 512 * (o % 2):512 * (o % 2 + 1)],
                    in_=io["xfm"][128 * o:128 * (o + 1), 512 * c:512 * (c + 1)])
            op("sync", xpf)
            inc_last("sync", "xs", 16)
        xpf_emit(0, c)
        xpf_emit(1, c)
        for k in range(8):
            if k == 0:
                wait("tensor", "x1s", C["x1s"].v)
                if c == 0:
                    wait("tensor", "oev", C["oev"].v)
            for sub in range(2):
                h = 2 * k + sub
                quart = h // 4
                hl = quart * 4 + (h % 4)

                def bcf(sub=sub, hl=hl):
                    return nc.tensor.matmul(
                        pbcF[64 * sub:64 * (sub + 1), :], T["ones1"][0:1, 0:64],
                        T["rrow"][0:1, 512 * hl:512 * (hl + 1)],
                        start=True, stop=True,
                        tile_position=(0, 64 * sub))
                op("tensor", bcf)
            inc_last("tensor", "bc", 1)
            wait("vector", "bc", C["bc"].v)
            op("vector", lambda: nc.vector.tensor_copy(T["rb_sb"], pbcF))

            def nrf(k=k, c=c):
                return nc.vector.tensor_tensor(
                    T["On_sb"][:, 512 * k:512 * (k + 1)],
                    T["Ofm"][:, TPC * k + 512 * c:TPC * k + 512 * (c + 1)],
                    T["rb_sb"], ALU.mult)
            op("vector", nrf)
            inc_last("vector", "nr", 1)
            wait("tensor", "nr", C["nr"].v)
        for o in range(8):
            ysl = o % 2
            ko = 8 * c + o
            if ko >= 2:
                wait("tensor", "stt", stt_hist[ko - 2])
            for k in range(8):
                def yf(o=o, k=k, ysl=ysl):
                    return nc.tensor.matmul(
                        py[ysl],
                        T["wo_sb"][:, 1024 * k + 128 * o:1024 * k + 128 * (o + 1)],
                        T["On_sb"][:, 512 * k:512 * (k + 1)],
                        start=(k == 0), stop=(k == 7))
                op("tensor", yf)
            inc_last("tensor", "y", 1)
            wait("vector", "y", C["y"].v)
            wait("vector", "xs", xsb0 + 16 * (8 * c + o + 1))

            def sf(o=o, ysl=ysl):
                return nc.vector.scalar_tensor_tensor(
                    T["s_c"][:, 512 * o:512 * (o + 1)], py[ysl],
                    T["bo_sb"][:, o:o + 1], T["xstg"][:, 512 * (o % 2):512 * (o % 2 + 1)],
                    ALU.add, ALU.add)
            op("vector", sf)
            stt_hist.append(inc_last("vector", "stt", 1))
            if o + 2 < 8:
                xpf_emit(o + 2, c)
            if ko >= 2:
                wait("vector", "sm", sm_hist[ko - 2])

            def sbff(o=o):
                return nc.vector.tensor_copy(
                    T["sbf"][:, 512 * (o % 2):512 * (o % 2 + 1)],
                    T["s_c"][:, 512 * o:512 * (o + 1)])
            op("vector", sbff)

            def ssqf(o=o):
                sl_ = slice(512 * (o % 2), 512 * (o % 2 + 1))
                return nc.vector.tensor_tensor(
                    T["ssq"][:, sl_], T["sbf"][:, sl_], T["sbf"][:, sl_], ALU.mult)
            op("vector", ssqf)
            inc_last("vector", "sq", 1)
            wait("tensor", "sq", C["sq"].v)

            def smf(o=o):
                return nc.tensor.matmul(
                    pmean, T["oneK"], T["sbf"][:, 512 * (o % 2):512 * (o % 2 + 1)],
                    start=(o == 0), stop=(o == 7))
            op("tensor", smf)

            def sqmf(o=o):
                return nc.tensor.matmul(
                    pmsq, T["oneK"], T["ssq"][:, 512 * (o % 2):512 * (o % 2 + 1)],
                    start=(o == 0), stop=(o == 7))
            op("tensor", sqmf)
            sm_hist.append(inc_last("tensor", "sm", 1))
        _emit_ln_tail(S, T, io, prb, pmr, pmean, pmsq, c, stt_hist, x1s_hist,
                      io["x1"])
        wait("tensor", "ap", C["ap"].v)
    op("sync", lambda v=C["x1s"].v: nc.sync.wait_ge(S.sems["x1s"], v))


def _emit_ln_tail(S, T, io, prb, pmr, pmean, pmsq, c, stt_hist, x1s_hist, xout):
    nc = S.nc
    op, wait, inc_last, C = S.op, S.wait, S.inc_last, S.C
    stat = T["stat"]
    wait("vector", "sm", C["sm"].v)
    op("vector", lambda: nc.vector.tensor_copy(stat[0:1, 0:512], pmean))
    op("vector", lambda: nc.vector.tensor_copy(stat[0:1, 512:1024], pmsq))
    op("vector", lambda: nc.vector.tensor_tensor(
        stat[0:1, 1024:1536], stat[0:1, 0:512], stat[0:1, 0:512], ALU.mult))
    op("vector", lambda: nc.vector.tensor_tensor(
        stat[0:1, 1536:2048], stat[0:1, 512:1024], stat[0:1, 1024:1536],
        ALU.subtract))
    op("vector", lambda: nc.vector.tensor_scalar(
        stat[0:1, 1536:2048], stat[0:1, 1536:2048], LN_EPS, None,
        ALU.add, ALU.bypass))
    inc_last("vector", "nr", 1)
    wait("scalar", "nr", C["nr"].v)
    op("scalar", lambda: nc.scalar.activation(
        stat[0:1, 1024:1536], stat[0:1, 1536:2048], AF.Ln))
    op("scalar", lambda: nc.scalar.activation(
        stat[0:1, 512:1024], stat[0:1, 1024:1536], AF.Exp, scale=-0.5))
    inc_last("scalar", "ex", 1)
    wait("vector", "ex", C["ex"].v)
    op("vector", lambda: nc.vector.scalar_tensor_tensor(
        stat[0:1, 1536:2048], stat[0:1, 0:512], -1.0,
        stat[0:1, 512:1024], ALU.mult, ALU.mult))
    rstd_bf = stat[0:1, 2048:2304].bitcast(BF16)
    mr_bf = stat[0:1, 2304:2560].bitcast(BF16)
    op("vector", lambda rb=rstd_bf: nc.vector.tensor_copy(rb, stat[0:1, 512:1024]))
    op("vector", lambda mb=mr_bf: nc.vector.tensor_copy(mb, stat[0:1, 1536:2048]))
    inc_last("vector", "nr", 1)
    wait("tensor", "nr", C["nr"].v)
    op("tensor", lambda rb=rstd_bf: nc.tensor.matmul(
        prb, T["ones1"], rb, start=True, stop=True))
    op("tensor", lambda mb=mr_bf: nc.tensor.matmul(
        pmr, T["ones1"], mb, start=True, stop=True))
    inc_last("tensor", "bc", 1)
    wait("vector", "bc", C["bc"].v)
    for o in range(8):
        ko = len(x1s_hist)
        osl = ko % 2
        if ko >= 2:
            wait("vector", "x1s", x1s_hist[ko - 2])

        def a1f(o=o, osl=osl):
            return nc.vector.tensor_tensor(
                T["x1stg"][:, 512 * osl:512 * (osl + 1)],
                T["s_c"][:, 512 * o:512 * (o + 1)], prb, ALU.mult)
        op("vector", a1f)

        def a2f(osl=osl):
            return nc.vector.tensor_tensor(
                T["x1stg"][:, 512 * osl:512 * (osl + 1)],
                T["x1stg"][:, 512 * osl:512 * (osl + 1)], pmr, ALU.add)
        op("vector", a2f)
        inc_last("vector", "ap", 1)
        op("sync", lambda v=C["ap"].v: nc.sync.wait_ge(S.sems["ap"], v))

        def x1f(o=o, c=c, osl=osl):
            return nc.sync.dma_start(
                out=xout[128 * o:128 * (o + 1), 512 * c:512 * (c + 1)],
                in_=T["x1stg"][:, 512 * osl:512 * (osl + 1)])
        op("sync", x1f)
        x1s_hist.append(inc_last("sync", "x1s", 16))


ATTN_SEMS = ["ld", "cv", "q", "qk", "ex", "mk", "pv", "oev", "dd", "db",
             "bc", "nr", "y", "stt", "sq", "sm", "ap", "x1s", "xs"]


def build_attn(layer, cc):
    causal = (layer == 0)
    nc = bass.Bass("TRN2", target_bir_lowering=False)
    io = {}
    io["qtm"] = nc.dram_tensor("qtm", [TPC, 1024], BF16, kind="ExternalInput").ap()
    io["kall"] = nc.dram_tensor("kall", [S_, 256], BF16, kind="ExternalInput").ap()
    io["vall"] = nc.dram_tensor("vall", [S_, 256], BF16, kind="ExternalInput").ap()
    io["xfm"] = nc.dram_tensor("xfm", [D, TPC], F32, kind="ExternalInput").ap()
    io["wo"] = nc.dram_tensor("wo", [1024, D], F32, kind="ExternalInput").ap()
    io["bo"] = nc.dram_tensor("bo", [128, 8], F32, kind="ExternalInput").ap()
    if causal:
        io["dcolT"] = nc.dram_tensor("dcolT", [128, 16], F32, kind="ExternalInput").ap()
        io["trowT"] = nc.dram_tensor("trowT", [128, 512], F32, kind="ExternalInput").ap()
    else:
        io["albT"] = nc.dram_tensor("albT", [128, 64 * 16], F32, kind="ExternalInput").ap()
    io["x1"] = nc.dram_tensor("x1", [D, TPC], F32, kind="ExternalOutput").ap()
    io["dscr"] = nc.dram_tensor("dscr", [64 * 512], F32, kind="Internal").ap()
    io["rscr"] = nc.dram_tensor("rscr", [64 * 512], BF16, kind="Internal").ap()
    with contextlib.ExitStack() as es:
        AL = Alloc(nc, es)
        S = Sched(nc, es, ATTN_SEMS)
        S.P = es.enter_context(nc.psum_tensor("P", [128, 4096], F32)).ap()
        block = es.enter_context(nc.Block())
        T = attn_tensors(AL, layer)
        emit_attn_pre(S, T, io, layer)
        emit_attn_core(S, S, T, io, layer, cc, emit_sync=True)
        emit_attn_post(S, T, io, layer)
        S.emit(block)
    return nc


def _host_dcol(cc):
    out = np.zeros((128, 16), np.float32)
    p_ = np.arange(128, dtype=np.float32)
    for hh, st in enumerate((cc, 7 - cc)):
        for c_ in (0, 1):
            q0 = 1024 * st + 512 * c_
            for m in range(4):
                j = 8 * st + 4 * c_ + m
                out[:, hh * 8 + c_ * 4 + m] = 128 * j + p_ - q0
    return out


def _host_alb():
    sl = _alibi_slopes16()
    out = np.zeros((128, 64 * 16), np.float32)
    p_ = np.arange(128, dtype=np.float64)
    for j in range(64):
        for h in range(16):
            out[:, 16 * j + h] = (sl[h] * (128 * j + p_ - (S - 1))).astype(np.float32)
    return out


def run_attn(layer, q_shards, k_batches, v_batches, x_shards, wo, bo, res):
    outs = [None] * NCORES
    trow = np.tile(np.arange(512, dtype=np.float32)[None, :], (128, 1))
    for cc in range(4):
        nc = build_attn(layer, cc)
        cores = [cc, cc + 4]
        in_maps = []
        for core in cores:
            b = core // 4
            m = dict(qtm=q_shards[core], kall=k_batches[b], vall=v_batches[b],
                     xfm=x_shards[core], wo=wo,
                     bo=np.ascontiguousarray(bo.astype(np.float32).reshape(8, 128).T))
            if layer == 0:
                m["dcolT"] = _host_dcol(cc)
                m["trowT"] = trow
            else:
                m["albT"] = _host_alb()
            in_maps.append(m)
        out = run_bass_kernel_spmd(nc, in_maps, core_ids=cores)
        res.append(out)
        for i, core in enumerate(cores):
            outs[core] = out.results[i]
    return outs



def emit_amat(S, AL, io):
    nc = S.nc
    op, wait, inc_last, C = S.op, S.wait, S.inc_last, S.C
    wr_sb = AL.sbuf("m_wr", [128, 8 * 1024], F32)
    wi_sb = AL.sbuf("m_wi", [128, 8 * 1024], F32)
    win_sb = AL.sbuf("m_win", [128, 8 * 1024], F32)
    Cm_sb = AL.sbuf("m_Cm", [128, 8 * 1024], F32)
    Sm_sb = AL.sbuf("m_Sm", [128, 8 * 1024], F32)
    Cs_sb = AL.sbuf("m_Cs", [128, 8 * 128], F32)
    Ss_sb = AL.sbuf("m_Ss", [128, 8 * 128], F32)
    PT_sb = AL.sbuf("m_PT", [128, 1024], F32)
    QT_sb = AL.sbuf("m_QT", [128, 1024], F32)
    Ar_sb = AL.sbuf("m_Ar", [128, 1024], F32)
    P = S.P
    for name, dst, w in [("wfr", wr_sb, 1024), ("wfi", wi_sb, 1024),
                         ("Cm", Cm_sb, 1024), ("Sm", Sm_sb, 1024),
                         ("Csl", Cs_sb, 128), ("Ssl", Ss_sb, 128)]:
        op("sync", lambda name=name, dst=dst: nc.sync.dma_start(
            out=dst.rearrange("p (k c) -> p k c", k=8),
            in_=io[name].rearrange("(k p) c -> p k c", p=128)))
        inc_last("sync", "ld", 16)
    lds = C["ld"].v
    wait("vector", "ld", lds)
    op("vector", lambda: nc.vector.tensor_scalar(
        win_sb, wi_sb, -1.0, None, ALU.mult, ALU.bypass))
    inc_last("vector", "cv", 1)
    cvs = C["cv"].v
    wait("tensor", "ld", lds)
    wait("tensor", "cv", cvs)
    ev_hist = []
    for kc in range(8):
        sl = kc % 2
        if kc >= 2:
            wait("tensor", "ev", ev_hist[kc - 2])
        for jc in range(8):
            for w_sb, c_sb in [(wr_sb, Cs_sb), (wi_sb, Ss_sb)]:
                def m1(kc=kc, jc=jc, w_sb=w_sb, c_sb=c_sb, sl=sl):
                    return nc.tensor.matmul(
                        P[:, 512 * sl:512 * sl + 128],
                        w_sb[:, 1024 * jc + 128 * kc:1024 * jc + 128 * (kc + 1)],
                        c_sb[:, 128 * jc:128 * (jc + 1)],
                        start=(jc == 0 and w_sb is wr_sb),
                        stop=(jc == 7 and w_sb is wi_sb))
                op("tensor", m1)
        for jc in range(8):
            for w_sb, c_sb in [(wr_sb, Ss_sb), (win_sb, Cs_sb)]:
                def m2(kc=kc, jc=jc, w_sb=w_sb, c_sb=c_sb, sl=sl):
                    return nc.tensor.matmul(
                        P[:, 1024 + 512 * sl:1024 + 512 * sl + 128],
                        w_sb[:, 1024 * jc + 128 * kc:1024 * jc + 128 * (kc + 1)],
                        c_sb[:, 128 * jc:128 * (jc + 1)],
                        start=(jc == 0 and w_sb is wr_sb),
                        stop=(jc == 7 and w_sb is win_sb))
                op("tensor", m2)
        inc_last("tensor", "mm", 1)
        wait("vector", "mm", C["mm"].v)
        op("vector", lambda kc=kc, sl=sl: nc.vector.tensor_copy(
            PT_sb[:, 128 * kc:128 * (kc + 1)],
            P[:, 512 * sl:512 * sl + 128]))
        op("vector", lambda kc=kc, sl=sl: nc.vector.tensor_copy(
            QT_sb[:, 128 * kc:128 * (kc + 1)],
            P[:, 1024 + 512 * sl:1024 + 512 * sl + 128]))
        ev_hist.append(inc_last("vector", "ev", 1))
    wait("tensor", "ev", C["ev"].v)
    for ob in range(2):
        for kc in range(8):
            for t_sb, c_sb in [(PT_sb, Cm_sb), (QT_sb, Sm_sb)]:
                def m3(ob=ob, kc=kc, t_sb=t_sb, c_sb=c_sb):
                    return nc.tensor.matmul(
                        P[:, 512 * ob:512 * (ob + 1)],
                        t_sb[:, 128 * kc:128 * (kc + 1)],
                        c_sb[:, 1024 * kc + 512 * ob:1024 * kc + 512 * (ob + 1)],
                        start=(kc == 0 and t_sb is PT_sb),
                        stop=(kc == 7 and t_sb is QT_sb))
                op("tensor", m3)
        inc_last("tensor", "mm", 1)
        wait("vector", "mm", C["mm"].v)
        op("vector", lambda ob=ob: nc.vector.tensor_copy(
            Ar_sb[:, 512 * ob:512 * (ob + 1)], P[:, 512 * ob:512 * (ob + 1)]))
        inc_last("vector", "ev", 1)
    op("gpsimd", lambda v=C["ev"].v: nc.gpsimd.wait_ge(S.sems["ev"], v))
    op("gpsimd", lambda: nc.gpsimd.dma_start(out=io["Ar"], in_=Ar_sb))
    inc_last("gpsimd", "st", 16)
    op("gpsimd", lambda v=C["st"].v: nc.gpsimd.wait_ge(S.sems["st"], v))


def build_amat():
    nc = bass.Bass("TRN2", target_bir_lowering=False)
    io = {}
    for n, sh in [("wfr", [D, D]), ("wfi", [D, D]), ("Cm", [D, D]),
                  ("Sm", [D, D]), ("Csl", [D, 128]), ("Ssl", [D, 128])]:
        io[n] = nc.dram_tensor(n, sh, F32, kind="ExternalInput").ap()
    io["Ar"] = nc.dram_tensor("Ar", [128, D], F32, kind="ExternalOutput").ap()
    with contextlib.ExitStack() as es:
        AL = Alloc(nc, es)
        S = Sched(nc, es, ["ld", "cv", "mm", "ev", "st"])
        S.P = es.enter_context(nc.psum_tensor("P", [128, 2048], F32)).ap()
        block = es.enter_context(nc.Block())
        emit_amat(S, AL, io)
        S.emit(block)
    return nc


def _dft_tables():
    kk = np.arange(D, dtype=np.float64)
    ang = 2.0 * np.pi * np.outer(kk, kk) / D
    ang = np.mod(np.outer(kk, kk), D) * (2.0 * np.pi / D)
    return np.cos(ang).astype(np.float32), np.sin(ang).astype(np.float32)


def run_amat(wfr, wfi, res):
    nc = build_amat()
    Cm, Sm = _dft_tables()
    in_maps = []
    for core in range(NCORES):
        sl = slice(128 * core, 128 * (core + 1))
        in_maps.append(dict(
            wfr=wfr, wfi=wfi, Cm=Cm, Sm=Sm,
            Csl=np.ascontiguousarray(Cm[:, sl] / D),
            Ssl=np.ascontiguousarray(Sm[:, sl] / D)))
    out = run_bass_kernel_spmd(nc, in_maps, core_ids=list(range(NCORES)))
    res.append(out)
    return np.concatenate([r["Ar"] for r in out.results], 0)


class Sched:

    def __init__(self, nc, es, semnames):
        self.nc = nc
        self.sems = {n: es.enter_context(nc.semaphore("s_" + n)) for n in semnames}
        self.C = {n: Ctr() for n in semnames}
        self.ops = {k: [] for k in ["sync", "gpsimd", "tensor", "vector", "scalar"]}

    def op(self, eng, f):
        self.ops[eng].append(f)

    def wait(self, eng, sem, val):
        if val > 0:
            nc = self.nc
            self.op(eng, lambda s=self.sems[sem], v=val:
                    getattr(nc, eng).wait_ge(s, v))

    def inc_last(self, eng, sem, n=1):
        prev = self.ops[eng][-1]

        def f(prev=prev, s=self.sems[sem], n=n):
            inst = prev()
            inst.then_inc(s, n)
            return inst
        self.ops[eng][-1] = f
        return self.C[sem].inc(n)

    def emit(self, block):
        ops = self.ops

        @block.sync
        def _(sync):
            for f in ops["sync"]:
                f()

        @block.gpsimd
        def _(gpsimd):
            for f in ops["gpsimd"]:
                f()

        @block.tensor
        def _(tensor):
            for f in ops["tensor"]:
                f()

        @block.scalar
        def _(scalar):
            for f in ops["scalar"]:
                f()

        @block.vector
        def _(vector):
            for f in ops["vector"]:
                f()


def _emit_resid_ln(S, nc, P, views, y_emit, xap, bias_sb, xout, c_list=(0, 1, 2, 3),
                   state=None, xpf_hook=None):
    py, pmean, pmsq, prb, pmr = views["py"], views["pmean"], views["pmsq"], \
        views["prb"], views["pmr"]
    s_cT, sbf, ssq, stat, x1stg = views["s_c"], views["sbf"], views["ssq"], \
        views["stat"], views["x1stg"]
    oneK, ones1 = views["oneK"], views["ones1"]
    op, wait, inc_last, C = S.op, S.wait, S.inc_last, S.C
    if state is None:
        state = {}
    stt_hist = state.setdefault("stt_hist", [])
    sm_hist = state.setdefault("sm_hist", [])
    x1s_hist = state.setdefault("x1s_hist", [])
    kos = state.setdefault("kos", Ctr())
    for c in c_list:
        for o in range(8):
            ko = kos.v
            kos.inc(1)
            ysl = ko % 2
            if ko >= 2:
                wait("tensor", "stt", stt_hist[ko - 2])
            y_emit(c, o, py[ysl])
            inc_last("tensor", "y", 1)
            wait("vector", "y", C["y"].v)

            def sf(o=o, c=c, ysl=ysl):
                return nc.vector.scalar_tensor_tensor(
                    s_cT[:, 512 * o:512 * (o + 1)], py[ysl][:, :],
                    bias_sb[:, o:o + 1], xap(c, o),
                    ALU.add, ALU.add)
            op("vector", sf)
            stt_hist.append(inc_last("vector", "stt", 1))
            if xpf_hook is not None and o + 2 < 8:
                xpf_hook(o + 2, c)
            if ko >= 2:
                wait("vector", "sm", sm_hist[ko - 2])

            def sbff(o=o, ysl=ysl):
                return nc.vector.tensor_copy(
                    sbf[:, 512 * ysl:512 * (ysl + 1)],
                    s_cT[:, 512 * o:512 * (o + 1)])
            op("vector", sbff)

            def ssqf(ysl=ysl):
                sl_ = slice(512 * ysl, 512 * (ysl + 1))
                return nc.vector.tensor_tensor(
                    ssq[:, sl_], sbf[:, sl_], sbf[:, sl_], ALU.mult)
            op("vector", ssqf)
            inc_last("vector", "sq", 1)
            wait("tensor", "sq", C["sq"].v)

            def smf(o=o, ysl=ysl):
                return nc.tensor.matmul(
                    pmean, oneK[:], sbf[:, 512 * ysl:512 * (ysl + 1)],
                    start=(o == 0), stop=(o == 7))
            op("tensor", smf)

            def sqmf(o=o, ysl=ysl):
                return nc.tensor.matmul(
                    pmsq, oneK[:], ssq[:, 512 * ysl:512 * (ysl + 1)],
                    start=(o == 0), stop=(o == 7))
            op("tensor", sqmf)
            sm_hist.append(inc_last("tensor", "sm", 1))
        wait("vector", "sm", C["sm"].v)
        op("vector", lambda: nc.vector.tensor_copy(stat[0:1, 0:512], pmean))
        op("vector", lambda: nc.vector.tensor_copy(stat[0:1, 512:1024], pmsq))
        op("vector", lambda: nc.vector.tensor_tensor(
            stat[0:1, 1024:1536], stat[0:1, 0:512], stat[0:1, 0:512], ALU.mult))
        op("vector", lambda: nc.vector.tensor_tensor(
            stat[0:1, 1536:2048], stat[0:1, 512:1024], stat[0:1, 1024:1536],
            ALU.subtract))
        op("vector", lambda: nc.vector.tensor_scalar(
            stat[0:1, 1536:2048], stat[0:1, 1536:2048], LN_EPS, None,
            ALU.add, ALU.bypass))
        inc_last("vector", "nr", 1)
        wait("scalar", "nr", C["nr"].v)
        op("scalar", lambda: nc.scalar.activation(
            stat[0:1, 1024:1536], stat[0:1, 1536:2048], AF.Ln))
        op("scalar", lambda: nc.scalar.activation(
            stat[0:1, 512:1024], stat[0:1, 1024:1536], AF.Exp, scale=-0.5))
        inc_last("scalar", "ex", 1)
        wait("vector", "ex", C["ex"].v)
        op("vector", lambda: nc.vector.scalar_tensor_tensor(
            stat[0:1, 1536:2048], stat[0:1, 0:512], -1.0,
            stat[0:1, 512:1024], ALU.mult, ALU.mult))
        rstd_bf = stat[0:1, 2048:2304].bitcast(BF16)
        mr_bf = stat[0:1, 2304:2560].bitcast(BF16)
        op("vector", lambda rb=rstd_bf: nc.vector.tensor_copy(
            rb, stat[0:1, 512:1024]))
        op("vector", lambda mb=mr_bf: nc.vector.tensor_copy(
            mb, stat[0:1, 1536:2048]))
        inc_last("vector", "nr", 1)
        wait("tensor", "nr", C["nr"].v)
        op("tensor", lambda rb=rstd_bf: nc.tensor.matmul(
            prb, ones1[:], rb, start=True, stop=True))
        op("tensor", lambda mb=mr_bf: nc.tensor.matmul(
            pmr, ones1[:], mb, start=True, stop=True))
        inc_last("tensor", "bc", 1)
        wait("vector", "bc", C["bc"].v)
        for o in range(8):
            ko = len(x1s_hist)
            osl = ko % 2
            if ko >= 2:
                wait("vector", "x1s", x1s_hist[ko - 2])

            def a1f(o=o, osl=osl):
                return nc.vector.tensor_tensor(
                    x1stg[:, 512 * osl:512 * (osl + 1)],
                    s_cT[:, 512 * o:512 * (o + 1)], prb[:, :], ALU.mult)
            op("vector", a1f)

            def a2f(osl=osl):
                return nc.vector.tensor_tensor(
                    x1stg[:, 512 * osl:512 * (osl + 1)],
                    x1stg[:, 512 * osl:512 * (osl + 1)], pmr[:, :], ALU.add)
            op("vector", a2f)
            inc_last("vector", "ap", 1)
            op("sync", lambda v=C["ap"].v: nc.sync.wait_ge(S.sems["ap"], v))

            def x1f(o=o, c=c, osl=osl):
                return nc.sync.dma_start(
                    out=xout[128 * o:128 * (o + 1), 512 * c:512 * (c + 1)],
                    in_=x1stg[:, 512 * osl:512 * (osl + 1)])
            op("sync", x1f)
            x1s_hist.append(inc_last("sync", "x1s", 16))
        wait("tensor", "ap", C["ap"].v)


def emit_ffn0(S, AL, io):
    nc = S.nc
    op, wait, inc_last, C = S.op, S.wait, S.inc_last, S.C
    x_sb = AL.sbuf("f0_x", [128, 8 * 2048], F32)
    xb_sb = AL.sbuf("f0_xb", [128, 8 * 2048], BF16)
    A_sb = AL.sbuf("f0_A", [128, 8 * 1024], BF16)
    fb_sb = AL.sbuf("f0_fb", [128, 8], F32)
    T = dict(ones1=AL.sbuf("f0_ones1", [1, 128], BF16),
             oneK=AL.sbuf("f0_oneK", [128, 1], BF16),
             s_c=AL.sbuf("f0_sc", [128, 8 * 512], F32),
             sbf=AL.sbuf("f0_sbf", [128, 2 * 512], BF16),
             ssq=AL.sbuf("f0_ssq", [128, 2 * 512], BF16),
             stat=AL.sbuf("f0_stat", [1, 2560], F32),
             x1stg=AL.sbuf("f0_x1stg", [128, 2 * 512], F32))
    P = S.P
    views = dict(py=[P[:, 0:512], P[:, 512:1024]],
                 pmean=P[0:1, 2048:2560], pmsq=P[0:1, 2560:3072],
                 prb=P[:, 3072:3584], pmr=P[:, 3584:4096],
                 s_c=T["s_c"], sbf=T["sbf"], ssq=T["ssq"], stat=T["stat"],
                 x1stg=T["x1stg"], oneK=T["oneK"], ones1=T["ones1"])
    op("sync", lambda: nc.sync.dma_start(
        out=x_sb.rearrange("p (k t) -> p k t", k=8),
        in_=io["xfm"].rearrange("(k p) t -> p k t", p=128)))
    inc_last("sync", "ld", 16)
    op("sync", lambda: nc.sync.dma_start(out=fb_sb, in_=io["fb"]))
    inc_last("sync", "ld", 16)
    op("gpsimd", lambda: nc.gpsimd.dma_start(
        out=xb_sb.rearrange("p (k t) -> p k t", k=8),
        in_=io["xfm"].rearrange("(k p) t -> p k t", p=128)))
    inc_last("gpsimd", "ld", 16)
    op("gpsimd", lambda: nc.gpsimd.dma_start(
        out=A_sb.rearrange("p (k c) -> p k c", k=8),
        in_=io["Am"].rearrange("(k p) c -> p k c", p=128)))
    inc_last("gpsimd", "ld", 16)
    op("vector", lambda: nc.vector.memset(T["ones1"], 1.0))
    op("vector", lambda: nc.vector.memset(T["oneK"], 1.0 / 1024.0))
    inc_last("vector", "cv", 1)
    lds, cvs = C["ld"].v, C["cv"].v

    def y_emit(c, o, pyv):
        if c == 0 and o == 0:
            wait("tensor", "ld", lds)
            wait("tensor", "cv", cvs)
        for k in range(8):
            def yf(c=c, o=o, k=k, pyv=pyv):
                return nc.tensor.matmul(
                    pyv, A_sb[:, 1024 * k + 128 * o:1024 * k + 128 * (o + 1)],
                    xb_sb[:, 2048 * k + 512 * c:2048 * k + 512 * (c + 1)],
                    start=(k == 0), stop=(k == 7))
            op("tensor", yf)

    _emit_resid_ln(S, nc, P, views, y_emit,
                   lambda c, o: x_sb[:, 2048 * o + 512 * c:2048 * o + 512 * (c + 1)],
                   fb_sb, io["x2"])
    op("sync", lambda v=C["x1s"].v: nc.sync.wait_ge(S.sems["x1s"], v))


FFN0_SEMS = ["ld", "cv", "y", "stt", "sq", "sm", "nr", "ex", "bc", "ap", "x1s"]


def build_ffn0():
    nc = bass.Bass("TRN2", target_bir_lowering=False)
    io = {}
    io["xfm"] = nc.dram_tensor("xfm", [D, TPC], F32, kind="ExternalInput").ap()
    io["Am"] = nc.dram_tensor("Am", [D, D], F32, kind="ExternalInput").ap()
    io["fb"] = nc.dram_tensor("fb", [128, 8], F32, kind="ExternalInput").ap()
    io["x2"] = nc.dram_tensor("x2", [D, TPC], F32, kind="ExternalOutput").ap()
    with contextlib.ExitStack() as es:
        AL = Alloc(nc, es)
        S = Sched(nc, es, FFN0_SEMS)
        S.P = es.enter_context(nc.psum_tensor("P", [128, 4096], F32)).ap()
        block = es.enter_context(nc.Block())
        emit_ffn0(S, AL, io)
        S.emit(block)
    return nc


def run_ffn0(x_shards, A, fbias, res):
    nc = build_ffn0()
    fb = np.ascontiguousarray(fbias.astype(np.float32).reshape(8, 128).T)
    in_maps = [dict(xfm=x_shards[c], Am=A, fb=fb) for c in range(NCORES)]
    out = run_bass_kernel_spmd(nc, in_maps, core_ids=list(range(NCORES)))
    res.append(out)
    return [r["x2"] for r in out.results]


def _emit_quant(S, nc, out_ap, in_ap, qs, W):
    op = S.op
    nb = W // 16
    t8, t4, t2 = qs["t8"], qs["t4"], qs["t2"]
    bm, sc, rc, scb = qs["bm"], qs["sc"], qs["rc"], qs["scb"]
    u, tb = qs["u"], qs["tb"]
    A, B4, B2 = qs["A"], qs["B4"], qs["B2"]
    w1 = t8.bitcast(BF16)
    w2 = tb.bitcast(BF16)
    M1, MH = MAGIC1, MAGICH
    I32 = mybir.dt.int32
    tbi = tb.bitcast(I32)
    ini = in_ap.bitcast(I32)
    op("vector", lambda: nc.vector.tensor_scalar(
        tbi[:, 0:W], ini, 0x7FFFFFFF, None, ALU.bitwise_and, ALU.bypass))
    op("vector", lambda: nc.vector.tensor_tensor(
        t8[:, 0:W // 2], tb[:, 0:W:2], tb[:, 1:W:2], ALU.max))
    op("vector", lambda: nc.vector.tensor_tensor(
        t4[:, 0:W // 4], t8[:, 0:W // 2:2], t8[:, 1:W // 2:2], ALU.max))
    op("vector", lambda: nc.vector.tensor_tensor(
        t2[:, 0:W // 8], t4[:, 0:W // 4:2], t4[:, 1:W // 4:2], ALU.max))

    def selfsem():
        v = S.inc_last("vector", "qz", 1)
        S.wait("vector", "qz", v)
    selfsem()
    op("vector", lambda: nc.vector.tensor_tensor(
        bm[:, 0:nb], t2[:, 0:W // 8:2], t2[:, 1:W // 8:2], ALU.max))
    selfsem()
    op("vector", lambda: nc.vector.tensor_scalar(
        sc[:, 0:nb], bm[:, 0:nb], 1.0 / 6.0, 1e-12, ALU.mult, ALU.max))
    selfsem()

    def _rc():
        with nc.allow_low_precision(reason="quant recip"):
            return nc.vector.reciprocal(rc[:, 0:nb], sc[:, 0:nb])
    op("vector", _rc)
    op("vector", lambda: nc.vector.tensor_copy(scb[:, 0:nb], sc[:, 0:nb]))
    selfsem()
    rcb = rc[:, 0:nb].unsqueeze(2).broadcast_to([128, nb, 16])
    scbb = scb[:, 0:nb].unsqueeze(2).broadcast_to([128, nb, 16])
    in3 = in_ap.rearrange("p (a b) -> p a b", b=16)
    u3 = u[:, 0:W].rearrange("p (a b) -> p a b", b=16)
    op("vector", lambda: nc.vector.tensor_tensor(u3, in3, rcb, ALU.mult))
    uf = u[:, 0:W]
    tbf = tb[:, 0:W]
    op("vector", lambda: nc.vector.tensor_scalar(
        tbf, uf, MH, MH - 2.0, ALU.add, ALU.max))
    op("vector", lambda: nc.vector.tensor_scalar(
        A[:, 0:W], tbf, MH, 2.0, ALU.subtract, ALU.min))
    op("vector", lambda: nc.vector.tensor_scalar(
        tbf, uf, M1, M1 - 4.0, ALU.add, ALU.max))
    op("vector", lambda: nc.vector.tensor_scalar(
        B4[:, 0:W], tbf, M1, 4.0, ALU.subtract, ALU.min))
    op("vector", lambda: nc.vector.tensor_scalar(
        tbf, tbf, M1 - 2.0, None, ALU.max, ALU.bypass))
    op("vector", lambda: nc.vector.tensor_scalar(
        B2[:, 0:W], tbf, M1, 2.0, ALU.subtract, ALU.min))
    op("vector", lambda: nc.vector.tensor_scalar(
        w1[:, 0:W], uf, 5.0, 2.0, ALU.is_gt, ALU.mult))
    op("vector", lambda: nc.vector.tensor_scalar(
        w2[:, 0:W], uf, -5.0, 2.0, ALU.is_lt, ALU.mult))
    op("vector", lambda: nc.vector.tensor_tensor(
        A[:, 0:W], A[:, 0:W], B4[:, 0:W], ALU.add))
    op("vector", lambda: nc.vector.tensor_tensor(
        B2[:, 0:W], B2[:, 0:W], w2[:, 0:W], ALU.add))
    op("vector", lambda: nc.vector.tensor_tensor(
        A[:, 0:W], A[:, 0:W], w1[:, 0:W], ALU.add))
    op("vector", lambda: nc.vector.tensor_tensor(
        A[:, 0:W], A[:, 0:W], B2[:, 0:W], ALU.subtract))
    A3 = A[:, 0:W].rearrange("p (a b) -> p a b", b=16)
    o3 = out_ap.rearrange("p (a b) -> p a b", b=16)
    op("vector", lambda: nc.vector.tensor_tensor(o3, A3, scbb, ALU.mult))


def _mk_qscratch_al(AL, Wmax=1024):
    return dict(
        t8=AL.sbuf("q_t8", [128, Wmax // 2], F32),
        t4=AL.sbuf("q_t4", [128, Wmax // 4], F32),
        t2=AL.sbuf("q_t2", [128, Wmax // 8], F32),
        bm=AL.sbuf("q_bm", [128, Wmax // 16], F32),
        sc=AL.sbuf("q_sc", [128, Wmax // 16], F32),
        rc=AL.sbuf("q_rc", [128, Wmax // 16], F32),
        scb=AL.sbuf("q_scb", [128, Wmax // 16], BF16),
        u=AL.sbuf("q_u", [128, Wmax], F32),
        tb=AL.sbuf("q_tb", [128, Wmax], F32),
        A=AL.sbuf("q_A", [128, Wmax], BF16),
        B4=AL.sbuf("q_B4", [128, Wmax], BF16),
        B2=AL.sbuf("q_B2", [128, Wmax], BF16),
    )


def _mk_qscratch(nc, es, Wmax=1024):
    E = es.enter_context
    return dict(
        t8=E(nc.sbuf_tensor("q_t8", [128, Wmax // 2], F32)),
        t4=E(nc.sbuf_tensor("q_t4", [128, Wmax // 4], F32)),
        t2=E(nc.sbuf_tensor("q_t2", [128, Wmax // 8], F32)),
        bm=E(nc.sbuf_tensor("q_bm", [128, Wmax // 16], F32)),
        sc=E(nc.sbuf_tensor("q_sc", [128, Wmax // 16], F32)),
        rc=E(nc.sbuf_tensor("q_rc", [128, Wmax // 16], F32)),
        scb=E(nc.sbuf_tensor("q_scb", [128, Wmax // 16], BF16)),
        u=E(nc.sbuf_tensor("q_u", [128, Wmax], F32)),
        tb=E(nc.sbuf_tensor("q_tb", [128, Wmax], F32)),
        A=E(nc.sbuf_tensor("q_A", [128, Wmax], BF16)),
        B4=E(nc.sbuf_tensor("q_B4", [128, Wmax], BF16)),
        B2=E(nc.sbuf_tensor("q_B2", [128, Wmax], BF16)),
    )


FFN1_SEMS = ["ld", "cv", "wq", "wo", "xq", "xb", "gu", "hq", "hb",
             "y", "stt", "sq", "sm", "nr", "ex", "bc", "ap",
             "x1s", "xs", "ws", "ev", "qz"]


def emit_ffn1(S, AL, io, debug=False):
    II = 2816
    nc = S.nc
    xfm, wg, wu, wd = io["xfm"], io["wg"], io["wu"], io["wd"]
    ident = io["ident"]
    x4 = io["x4"]
    wgq, wuq, wdq = io["wgq"], io["wuq"], io["wdq"]
    xqd, hqd = io["xqd"], io["hqd"]
    if debug:
        dbg_g, dbg_h = io["dbg_g"], io["dbg_h"]
    if True:
        E = None
        hqfm = AL.sbuf("hqfm", [128, 22 * 1024], BF16)
        wdT = AL.sbuf("wdT", [128, 22 * 1024], BF16)
        xqfm = AL.sbuf("xqfm", [128, 8 * 1024], BF16)
        wgt = AL.sbuf("wgt", [128, 2 * 8 * 512], BF16)
        wut = AL.sbuf("wut", [128, 2 * 8 * 512], BF16)
        xstg = AL.sbuf("xstg", [128, 2 * 128], F32)
        xtm = AL.sbuf("xtm", [128, 1024], F32)
        xqtm = AL.sbuf("xqtm", [128, 1024], BF16)
        sg = AL.sbuf("sg", [128, 512], F32)
        hf32 = AL.sbuf("hf32", [128, 512], F32)
        hqt = AL.sbuf("hqt", [128, 2 * 512], BF16)
        fb_sb = AL.sbuf("fb_sb", [128, 8], F32)
        ones1 = AL.sbuf("ones1", [1, 128], BF16)
        oneK = AL.sbuf("oneK", [128, 1], BF16)
        s_cT = AL.sbuf("s_cT", [128, 8 * 512], F32)
        sbf = AL.sbuf("sbf", [128, 2 * 512], BF16)
        ssq = AL.sbuf("ssq", [128, 2 * 512], BF16)
        stat = AL.sbuf("stat", [1, 2560], F32)
        x1stg = AL.sbuf("x1stg", [128, 2 * 512], F32)
        xrs = AL.sbuf("xrs", [128, 2 * 512], F32)
        ident_sb = AL.sbuf("ident_sb", [128, 128], F32)
        qs = _mk_qscratch_al(AL)
        P = S.P
        wstg = [hqfm[:, 0:5632].bitcast(F32),
                hqfm[:, 5632:11264].bitcast(F32)]
        wqs = [hqfm[:, 11264:14080], hqfm[:, 14080:16896]]
        op, wait, inc_last, C = S.op, S.wait, S.inc_last, S.C

        op("sync", lambda: nc.sync.dma_start(out=ident_sb[:], in_=ident[:]))
        inc_last("sync", "ld", 16)
        op("vector", lambda: nc.vector.memset(fb_sb[:], 0.0))
        op("vector", lambda: nc.vector.memset(ones1[:], 1.0))
        op("vector", lambda: nc.vector.memset(oneK[:], 1.0 / 1024.0))
        inc_last("vector", "nr", 1)

        wjobs = []
        for mi, (wsrc, wdst, nch, wid) in enumerate(
                [(wg, wgq, 22, D), (wu, wuq, 22, D), (wd, wdq, 8, II)]):
            for ch in range(nch):
                wjobs.append((wsrc, wdst, ch, wid))
        for ji, (wsrc, wdst, ch, wid) in enumerate(wjobs):
            sl = ji % 2
            if ji >= 2:
                op("sync", lambda v=C["wq"].v - 1: nc.sync.wait_ge(S.sems["wq"], v))

            def wldf(wsrc=wsrc, ch=ch, wid=wid, sl=sl):
                return nc.sync.dma_start(
                    out=wstg[sl][:, 0:wid], in_=wsrc[128 * ch:128 * (ch + 1), 0:wid])
            op("sync", wldf)
            inc_last("sync", "ws", 16)
            wait("vector", "ws", C["ws"].v)
            if ji >= 2:
                wait("vector", "wo", C["wo"].v - 16)
            for off in range(0, wid, 1024):
                w_ = min(1024, wid - off)
                _emit_quant(S, nc, wqs[sl][:, off:off + w_],
                            wstg[sl][:, off:off + w_], qs, w_)
            inc_last("vector", "wq", 1)
            op("sync", lambda v=C["wq"].v: nc.sync.wait_ge(S.sems["wq"], v))

            def wstf(wdst=wdst, ch=ch, wid=wid, sl=sl):
                return nc.sync.dma_start(
                    out=wdst[128 * ch:128 * (ch + 1), 0:wid], in_=wqs[sl][:, 0:wid])
            op("sync", wstf)
            inc_last("sync", "wo", 16)
        wqdone = C["wo"].v

        op("sync", lambda: nc.sync.wait_ge(S.sems["wo"], wqdone))
        for ic in range(22):
            def wdtf(ic=ic):
                return nc.sync.dma_start_transpose(
                    wdT[:, 1024 * ic:1024 * (ic + 1)], wdq[:, 128 * ic:128 * (ic + 1)])
            op("sync", wdtf)
            inc_last("sync", "ld", 16)
        lds = C["ld"].v

        views = dict(py=[P[:, 0:512], P[:, 512:1024]],
                     pmean=P[0:1, 2048:2560], pmsq=P[0:1, 2560:3072],
                     prb=P[:, 3072:3584], pmr=P[:, 3584:4096],
                     s_c=s_cT, sbf=sbf, ssq=ssq, stat=stat, x1stg=x1stg,
                     oneK=oneK, ones1=ones1)
        ln_state = {}
        for hf in range(2):
            toff = 1024 * hf
            for tt in range(8):
                ps = P[:, 1024 * (tt % 2):1024 * (tt % 2) + 1024]
                for o in range(8):
                    xsl = (8 * tt + o) % 2
                    if 8 * tt + o >= 2:
                        op("sync", lambda v=C["xq"].v: nc.sync.wait_ge(S.sems["xq"], v))

                    def xldf(o=o, tt=tt, xsl=xsl, toff=toff):
                        return nc.sync.dma_start(
                            out=xstg[:, 128 * xsl:128 * (xsl + 1)],
                            in_=xfm[128 * o:128 * (o + 1),
                                    toff + 128 * tt:toff + 128 * (tt + 1)])
                    op("sync", xldf)
                    inc_last("sync", "xs", 16)
                    wait("tensor", "xs", C["xs"].v)
                    if tt >= 2 and o == 0:
                        wait("tensor", "xq", C["xq"].v)

                    def xtf(o=o, ps=ps, xsl=xsl):
                        return nc.tensor.transpose(
                            ps[:, 128 * o:128 * (o + 1)],
                            xstg[:, 128 * xsl:128 * (xsl + 1)], ident_sb[:])
                    op("tensor", xtf)
                    inc_last("tensor", "xq", 1)
                wait("vector", "xq", C["xq"].v)
                op("vector", lambda ps=ps: nc.vector.tensor_copy(xtm[:], ps))
                inc_last("vector", "xq", 1)
                _emit_quant(S, nc, xqtm[:], xtm[:], qs, 1024)
                inc_last("vector", "xb", 1)
                op("sync", lambda v=C["xb"].v: nc.sync.wait_ge(S.sems["xb"], v))

                def xqstf(tt=tt, toff=toff):
                    return nc.sync.dma_start(
                        out=xqd[toff + 128 * tt:toff + 128 * (tt + 1), :],
                        in_=xqtm[:])
                op("sync", xqstf)
                inc_last("sync", "xq", 16)
                wait("vector", "xq", C["xq"].v)
            op("sync", lambda v=C["xq"].v: nc.sync.wait_ge(S.sems["xq"], v))
            for hc in range(8):
                def xqtf(hc=hc, toff=toff):
                    return nc.sync.dma_start_transpose(
                        xqfm[:, 1024 * hc:1024 * (hc + 1)],
                        xqd[toff:toff + 1024, 128 * hc:128 * (hc + 1)])
                op("sync", xqtf)
                inc_last("sync", "xb", 16)
            xqfmdone = C["xb"].v

            nib = 6
            for ib in range(nib):
                wib = 512 if ib < 5 else 256
                isl = ib % 2
                if ib < 2:
                    op("sync", lambda: nc.sync.wait_ge(S.sems["wo"], 44 * 16))
                else:
                    op("sync", lambda v=C["gu"].v: nc.sync.wait_ge(S.sems["gu"], v))
                for hc in range(8):
                    def wgtf(ib=ib, hc=hc, isl=isl, wib=wib):
                        return nc.sync.dma_start_transpose(
                            wgt[:, 4096 * isl + 512 * hc:4096 * isl + 512 * hc + wib],
                            wgq[512 * ib:512 * ib + wib, 128 * hc:128 * (hc + 1)])
                    op("sync", wgtf)
                    inc_last("sync", "ld", 16)

                    def wutf(ib=ib, hc=hc, isl=isl, wib=wib):
                        return nc.sync.dma_start_transpose(
                            wut[:, 4096 * isl + 512 * hc:4096 * isl + 512 * hc + wib],
                            wuq[512 * ib:512 * ib + wib, 128 * hc:128 * (hc + 1)])
                    op("sync", wutf)
                    inc_last("sync", "ld", 16)
                wld = C["ld"].v
                for tt in range(8):
                    gsl = tt % 2
                    pg = P[:, 512 * gsl:512 * (gsl + 1)]
                    pu = P[:, 1024 + 512 * gsl:1024 + 512 * (gsl + 1)]
                    if tt == 0:
                        wait("tensor", "ld", wld)
                        wait("tensor", "xb", xqfmdone)
                    if 8 * ib + tt >= 2:
                        wait("tensor", "hq", C["hq"].v - 1)
                    for hc in range(8):
                        def gmf(hc=hc, tt=tt, isl=isl, wib=wib, pg=pg):
                            return nc.tensor.matmul(
                                pg[:, 0:wib],
                                xqfm[:, 1024 * hc + 128 * tt:1024 * hc + 128 * (tt + 1)],
                                wgt[:, 4096 * isl + 512 * hc:4096 * isl + 512 * hc + wib],
                                start=(hc == 0), stop=(hc == 7))
                        op("tensor", gmf)
                    for hc in range(8):
                        def umf(hc=hc, tt=tt, isl=isl, wib=wib, pu=pu):
                            return nc.tensor.matmul(
                                pu[:, 0:wib],
                                xqfm[:, 1024 * hc + 128 * tt:1024 * hc + 128 * (tt + 1)],
                                wut[:, 4096 * isl + 512 * hc:4096 * isl + 512 * hc + wib],
                                start=(hc == 0), stop=(hc == 7))
                        op("tensor", umf)
                    inc_last("tensor", "gu", 1)
                    wait("scalar", "gu", C["gu"].v)

                    def silf(pg=pg, wib=wib):
                        return nc.scalar.activation(sg[:, 0:wib], pg[:, 0:wib],
                                                    AF.Silu)
                    op("scalar", silf)
                    inc_last("scalar", "gu", 1)
                    wait("vector", "gu", C["gu"].v)

                    def hmf(pu=pu, wib=wib):
                        return nc.vector.tensor_tensor(
                            hf32[:, 0:wib], sg[:, 0:wib], pu[:, 0:wib], ALU.mult)
                    op("vector", hmf)
                    inc_last("vector", "gu", 1)
                    if debug and hf == 1 and ib == 0 and tt == 0:
                        op("vector", lambda pg=pg: nc.vector.tensor_copy(
                            s_cT[:, 0:512], pg[:, 0:512]))
                        op("vector", lambda: nc.vector.tensor_copy(
                            s_cT[:, 512:1024], hf32[:, 0:512]))
                        inc_last("vector", "ev", 1)
                        op("sync", lambda: nc.sync.wait_ge(S.sems["ev"], 1))
                        op("sync", lambda: nc.sync.dma_start(
                            out=dbg_g, in_=s_cT[:, 0:512]))
                        inc_last("sync", "ld", 16)
                        op("sync", lambda: nc.sync.dma_start(
                            out=dbg_h, in_=s_cT[:, 512:1024]))
                        inc_last("sync", "ld", 16)
                    hsl = (8 * ib + tt) % 2
                    if 8 * ib + tt >= 2:
                        wait("vector", "hb", C["hb"].v - 16)
                    _emit_quant(S, nc, hqt[:, 512 * hsl:512 * hsl + wib],
                                hf32[:, 0:wib], qs, wib)
                    inc_last("vector", "hq", 1)
                    op("sync", lambda v=C["hq"].v: nc.sync.wait_ge(S.sems["hq"], v))

                    def hstf(ib=ib, tt=tt, hsl=hsl, wib=wib):
                        return nc.sync.dma_start(
                            out=hqd[128 * tt:128 * (tt + 1), 512 * ib:512 * ib + wib],
                            in_=hqt[:, 512 * hsl:512 * hsl + wib])
                    op("sync", hstf)
                    inc_last("sync", "hb", 16)
            op("sync", lambda v=C["hb"].v: nc.sync.wait_ge(S.sems["hb"], v))
            if hf == 1:
                pass
            for ic in range(22):
                def hqtf(ic=ic):
                    return nc.sync.dma_start_transpose(
                        hqfm[:, 1024 * ic:1024 * (ic + 1)],
                        hqd[:, 128 * ic:128 * (ic + 1)])
                op("sync", hqtf)
                inc_last("sync", "hb", 16)
            hqdone = C["hb"].v

            xst_hist = ln_state.setdefault("xst_hist", [])

            def xap(c, o):
                return xrs[:, 512 * ((8 * c + o) % 2):512 * ((8 * c + o) % 2 + 1)]

            def xpf_emit(c, o):
                ko_ = len(xst_hist)
                if ko_ >= 2:
                    st = ln_state["stt_hist"]
                    if ko_ - 2 < len(st):
                        op("sync", lambda v=st[ko_ - 2]:
                           nc.sync.wait_ge(S.sems["stt"], v))

                def xpf(o=o, c=c):
                    return nc.sync.dma_start(
                        out=xrs[:, 512 * ((8 * c + o) % 2):512 * ((8 * c + o) % 2 + 1)],
                        in_=xfm[128 * o:128 * (o + 1), 512 * c:512 * (c + 1)])
                op("sync", xpf)
                xst_hist.append(inc_last("sync", "xs", 16))

            def y_emit(c, o, pyv, hqdone=hqdone):
                if o == 0 and c % 2 == 0:
                    wait("tensor", "hb", hqdone)
                cc_ = c % 2
                for ic in range(22):
                    def yf(c=c, o=o, ic=ic, pyv=pyv, cc_=cc_):
                        return nc.tensor.matmul(
                            pyv, wdT[:, 1024 * ic + 128 * o:1024 * ic + 128 * (o + 1)],
                            hqfm[:, 1024 * ic + 512 * cc_:1024 * ic + 512 * (cc_ + 1)],
                            start=(ic == 0), stop=(ic == 21))
                    op("tensor", yf)
                xpf_emit(c, o)
                wait("vector", "xs", xst_hist[-1])

            _emit_resid_ln(S, nc, P, views, y_emit, xap, fb_sb, x4,
                           c_list=[2 * hf, 2 * hf + 1], state=ln_state)
            wait("sync", "sm", C["sm"].v)
        op("sync", lambda v=C["x1s"].v: nc.sync.wait_ge(S.sems["x1s"], v))




def build_ffn1(debug=False):
    II = 2816
    nc = bass.Bass("TRN2", target_bir_lowering=False)
    io = {}
    io["xfm"] = nc.dram_tensor("xfm", [D, TPC], F32, kind="ExternalInput").ap()
    io["wg"] = nc.dram_tensor("wg", [II, D], F32, kind="ExternalInput").ap()
    io["wu"] = nc.dram_tensor("wu", [II, D], F32, kind="ExternalInput").ap()
    io["wd"] = nc.dram_tensor("wd", [D, II], F32, kind="ExternalInput").ap()
    io["ident"] = nc.dram_tensor("ident", [128, 128], F32, kind="ExternalInput").ap()
    io["x4"] = nc.dram_tensor("x4", [D, TPC], F32, kind="ExternalOutput").ap()
    kind2 = "ExternalOutput" if debug else "Internal"
    io["wgq"] = nc.dram_tensor("wgq", [II, D], BF16, kind=kind2).ap()
    io["wuq"] = nc.dram_tensor("wuq", [II, D], BF16, kind="Internal").ap()
    io["wdq"] = nc.dram_tensor("wdq", [D, II], BF16, kind="Internal").ap()
    io["xqd"] = nc.dram_tensor("xqd", [TPC, D], BF16, kind=kind2).ap()
    io["hqd"] = nc.dram_tensor("hqd", [1024, II], BF16, kind=kind2).ap()
    if debug:
        io["dbg_g"] = nc.dram_tensor("dbg_g", [128, 512], F32, kind="ExternalOutput").ap()
        io["dbg_h"] = nc.dram_tensor("dbg_h", [128, 512], F32, kind="ExternalOutput").ap()
    with contextlib.ExitStack() as es:
        AL = Alloc(nc, es)
        S = Sched(nc, es, FFN1_SEMS)
        S.P = es.enter_context(nc.psum_tensor("P", [128, 4096], F32)).ap()
        block = es.enter_context(nc.Block())
        emit_ffn1(S, AL, io, debug)
        S.emit(block)
    return nc


def run_ffn1(x_shards, wg, wu, wd, res, debug=False):
    nc = build_ffn1(debug)
    ident = np.eye(128, dtype=np.float32)
    in_maps = [dict(xfm=x_shards[c], wg=wg, wu=wu, wd=wd, ident=ident)
               for c in range(NCORES)]
    out = run_bass_kernel_spmd(nc, in_maps, core_ids=list(range(NCORES)))
    res.append(out)
    if debug:
        return [dict(r) for r in out.results]
    return [r["x4"] for r in out.results]



def _gather_seq(shards, b):
    out = np.zeros((S, shards[0].shape[1]), dtype=bfloat16)
    for core in range(4 * b, 4 * b + 4):
        out[_local_positions(core)] = np.asarray(shards[core])
    return out


def kernel_unfused(L0_attn_x, L0_attn_Wq, L0_attn_bq, L0_attn_Wk, L0_attn_bk,
           L0_attn_Wv, L0_attn_bv, L0_attn_Wo, L0_attn_bo,
           L0_ffn_w_freq_real, L0_ffn_w_freq_imag, L0_ffn_bias,
           L1_attn_Wq, L1_attn_Wk, L1_attn_Wv, L1_attn_Wo,
           L1_ffn_w_gate, L1_ffn_w_up, L1_ffn_w_down):
    res = []
    x_shards = [_shard_x_fm(np.asarray(L0_attn_x), c) for c in range(NCORES)]

    def qperm_w(w):
        return np.ascontiguousarray(
            np.asarray(w).reshape(-1, 16, 64)[:, QPERM].reshape(w.shape))

    q, k, v = run_qkv(0, x_shards, dict(
        wq=qperm_w(L0_attn_Wq), wk=np.asarray(L0_attn_Wk),
        wv=np.asarray(L0_attn_Wv),
        bq=np.asarray(L0_attn_bq).reshape(16, 64)[QPERM].reshape(1024),
        bk=np.asarray(L0_attn_bk), bv=np.asarray(L0_attn_bv)), res)
    k_b = [_gather_seq(k, b) for b in range(B)]
    v_b = [_gather_seq(v, b) for b in range(B)]
    x1r = run_attn(0, q, k_b, v_b, x_shards, np.asarray(L0_attn_Wo),
                   np.asarray(L0_attn_bo), res)
    x1 = [np.asarray(r["x1"]) for r in x1r]
    A = run_amat(np.asarray(L0_ffn_w_freq_real),
                 np.asarray(L0_ffn_w_freq_imag), res)
    x2 = run_ffn0(x1, A, np.asarray(L0_ffn_bias), res)
    x2 = [np.asarray(t) for t in x2]
    q, k, v = run_qkv(1, x2, dict(
        wq=qperm_w(L1_attn_Wq), wk=np.asarray(L1_attn_Wk),
        wv=np.asarray(L1_attn_Wv)), res)
    k_b = [_gather_seq(k, b) for b in range(B)]
    v_b = [_gather_seq(v, b) for b in range(B)]
    zero_bo = np.zeros(1024, np.float32)
    x3r = run_attn(1, q, k_b, v_b, x2, np.asarray(L1_attn_Wo), zero_bo, res)
    x3 = [np.asarray(r["x1"]) for r in x3r]
    x4 = run_ffn1(x3, np.asarray(L1_ffn_w_gate), np.asarray(L1_ffn_w_up),
                  np.asarray(L1_ffn_w_down), res)
    out = np.zeros((B, S, D), np.float32)
    for core in range(NCORES):
        out[core // 4][_local_positions(core)] = np.asarray(x4[core]).T
    return out



FUSED_SEMS = sorted(set(ATTN_SEMS + FFN0_SEMS + FFN1_SEMS +
                        ["mm", "rp", "st", "q", "cc", "ph", "cid"]))


class BodySched:

    def __init__(self, parent):
        self.nc = parent.nc
        self.sems = parent.sems
        self.C = parent.C
        self.P = parent.P
        self.ops = {k: [] for k in parent.ops}
    op = Sched.op
    wait = Sched.wait
    inc_last = Sched.inc_last


def _barrier(S):
    nc = S.nc
    for eng in ["sync", "gpsimd", "tensor", "scalar", "vector"]:
        S.op(eng, lambda eng=eng: getattr(nc, eng).nop())
        S.inc_last(eng, "ph", 1)
    v = S.C["ph"].v
    for eng in ["sync", "gpsimd", "tensor", "scalar", "vector"]:
        S.wait(eng, "ph", v)


def build_fused(debug=False):
    II = 2816
    nc = bass.Bass("TRN2", target_bir_lowering=False)
    io = {}

    def inp(n, sh, dt=F32):
        io[n] = nc.dram_tensor(n, sh, dt, kind="ExternalInput").ap()

    def intl(n, sh, dt):
        kind = "ExternalOutput" if (debug and n in
                                    ("x1", "x2", "x3", "qtm0", "qtm1")) else "Internal"
        io[n] = nc.dram_tensor(n, sh, dt, kind=kind).ap()

    inp("xfm", [D, TPC])
    inp("cid", [1, 1], mybir.dt.int32)
    inp("cosT", [TPC, 32]); inp("sinT", [TPC, 32])
    inp("wq0", [D, 1024]); inp("wk0", [D, 256]); inp("wv0", [D, 256])
    inp("bqkv0", [1, 1536])
    inp("wo0", [1024, D]); inp("bo0", [128, 8])
    inp("dcolT", [128, 16]); inp("trowT", [128, 512])
    inp("wfr", [D, D]); inp("wfi", [D, D]); inp("Cm", [D, D]); inp("Sm", [D, D])
    inp("Csl", [D, 128]); inp("Ssl", [D, 128]); inp("fb0", [128, 8])
    inp("wq1", [D, 1024]); inp("wk1", [D, 256]); inp("wv1", [D, 256])
    inp("wo1", [1024, D]); inp("bz", [128, 8])
    inp("albT", [128, 64 * 16])
    inp("wg", [II, D]); inp("wu", [II, D]); inp("wd", [D, II])
    inp("ident", [128, 128])
    io["out"] = nc.dram_tensor("out", [D, TPC], F32, kind="ExternalOutput").ap()
    for L in (0, 1):
        intl(f"qtm{L}", [TPC, 1024], BF16)
        intl(f"kvc{L}", [TPC, 512], BF16)
        intl(f"kvg{L}", [S_, 512], BF16)
        intl(f"dscr{L}", [64 * 512], F32)
        intl(f"rscr{L}", [64 * 512], BF16)
    intl("x1", [D, TPC], F32)
    intl("x2", [D, TPC], F32)
    intl("x3", [D, TPC], F32)
    intl("Arc", [128, D], F32)
    intl("Ag", [D, D], F32)
    intl("wgq", [II, D], BF16)
    intl("wuq", [II, D], BF16)
    intl("wdq", [D, II], BF16)
    intl("xqd", [TPC, D], BF16)
    intl("hqd", [1024, II], BF16)

    with contextlib.ExitStack() as es:
        arena = es.enter_context(nc.sbuf_tensor("arena", [128, 53000], F32)).ap()
        cid_sb = es.enter_context(nc.sbuf_tensor("cid_sb", [1, 1], mybir.dt.int32)).ap()
        AL = Alloc(nc, es, arena=arena)
        S = Sched(nc, es, FUSED_SEMS)
        S.P = es.enter_context(nc.psum_tensor("P", [128, 4096], F32)).ap()
        C = S.C
        op, inc_last = S.op, S.inc_last

        op("sync", lambda: nc.sync.dma_start(out=cid_sb, in_=io["cid"]))
        inc_last("sync", "cid", 16)

        emit_qkv(S, AL, dict(xfm=io["xfm"], wq=io["wq0"], wk=io["wk0"],
                             wv=io["wv0"], bqkv=io["bqkv0"], cosT=io["cosT"],
                             sinT=io["sinT"], qtm=io["qtm0"],
                             ktm=io["kvc0"][:, 0:256], vtm=io["kvc0"][:, 256:512]),
                 True)
        op("gpsimd", lambda v=C["st"].v: nc.gpsimd.wait_ge(S.sems["st"], v))
        op("gpsimd", lambda: nc.gpsimd.collective_compute(
            "AllGather", ALU.bypass, replica_groups=[[0, 1, 2, 3], [4, 5, 6, 7]],
            ins=[io["kvc0"].opt()], outs=[io["kvg0"].opt()]))
        inc_last("gpsimd", "cc", 1)
        _barrier(S)

        AL.reset()
        emit_amat(S, AL, dict(wfr=io["wfr"], wfi=io["wfi"], Cm=io["Cm"],
                              Sm=io["Sm"], Csl=io["Csl"], Ssl=io["Ssl"],
                              Ar=io["Arc"]))
        op("gpsimd", lambda: nc.gpsimd.collective_compute(
            "AllGather", ALU.bypass, replica_groups=[[0, 1, 2, 3, 4, 5, 6, 7]],
            ins=[io["Arc"].opt()], outs=[io["Ag"].opt()]))
        inc_last("gpsimd", "cc", 1)
        _barrier(S)

        def attn_phase(layer, qtm, kvg, xin, wo, bo, masks, xout, dscr, rscr):
            AL.reset()
            T = attn_tensors(AL, layer)
            iox = dict(qtm=qtm, kall=kvg[:, 0:256], vall=kvg[:, 256:512],
                       xfm=xin, wo=wo, bo=bo, x1=xout, dscr=dscr, rscr=rscr,
                       **masks)
            op("sync", lambda v=C["cc"].v: nc.sync.wait_ge(S.sems["cc"], v))
            emit_attn_pre(S, T, iox, layer, rank_major=True)
            if layer == 1:
                emit_attn_core(S, S, T, iox, 1, 0, emit_sync=True)
            else:
                marks = {k: len(S.ops[k]) for k in S.ops}
                snap = {k: C[k].v for k in C}
                bodies = []
                finals = []
                for ccv in range(4):
                    for k in C:
                        C[k].v = snap[k]
                    Sb = BodySched(S)
                    emit_attn_core(S, Sb, T, iox, 0, ccv,
                                   emit_sync=(ccv == 0))
                    bodies.append(Sb)
                    finals.append({k: C[k].v for k in C})
                assert all(f == finals[0] for f in finals), \
                    f"switch bodies diverge: {[{k: v for k, v in f.items() if f != finals[0]} for f in finals]}"
                S.switch_at = (marks, bodies)
            emit_attn_post(S, T, iox, layer)

        attn_phase(0, io["qtm0"], io["kvg0"], io["xfm"], io["wo0"], io["bo0"],
                   dict(dcolT=io["dcolT"], trowT=io["trowT"]), io["x1"],
                   io["dscr0"], io["rscr0"])
        _barrier(S)

        AL.reset()
        emit_ffn0(S, AL, dict(xfm=io["x1"], Am=io["Ag"], fb=io["fb0"],
                              x2=io["x2"]))
        _barrier(S)

        AL.reset()
        emit_qkv(S, AL, dict(xfm=io["x2"], wq=io["wq1"], wk=io["wk1"],
                             wv=io["wv1"], cosT=io["cosT"], sinT=io["sinT"],
                             qtm=io["qtm1"], ktm=io["kvc1"][:, 0:256],
                             vtm=io["kvc1"][:, 256:512]), False)
        op("gpsimd", lambda v=C["st"].v: nc.gpsimd.wait_ge(S.sems["st"], v))
        op("gpsimd", lambda: nc.gpsimd.collective_compute(
            "AllGather", ALU.bypass, replica_groups=[[0, 1, 2, 3], [4, 5, 6, 7]],
            ins=[io["kvc1"].opt()], outs=[io["kvg1"].opt()]))
        inc_last("gpsimd", "cc", 1)
        _barrier(S)

        attn_phase(1, io["qtm1"], io["kvg1"], io["x2"], io["wo1"], io["bz"],
                   dict(albT=io["albT"]), io["x3"], io["dscr1"], io["rscr1"])
        _barrier(S)

        AL.reset()
        emit_ffn1(S, AL, dict(xfm=io["x3"], wg=io["wg"], wu=io["wu"],
                              wd=io["wd"], ident=io["ident"], x4=io["out"],
                              wgq=io["wgq"], wuq=io["wuq"], wdq=io["wdq"],
                              xqd=io["xqd"], hqd=io["hqd"]), False)

        block = es.enter_context(nc.Block())
        marks, bodies = S.switch_at
        order = ["sync", "gpsimd", "tensor", "scalar", "vector"]

        @block.sync
        def _(sync):
            for f in S.ops["sync"]:
                f()

        @block.gpsimd
        def _(gpsimd):
            for f in S.ops["gpsimd"]:
                f()

        for eng in ["tensor", "scalar", "vector"]:
            for f in S.ops[eng][:marks[eng]]:
                f()
        for e in [nc.tensor, nc.scalar, nc.vector]:
            e.wait_ge(S.sems["cid"], 16)
        v = nc.values_load(cid_sb, engines=[mybir.EngineType.PE,
                                            mybir.EngineType.Activation,
                                            mybir.EngineType.DVE],
                           skip_runtime_bounds_check=True)
        for ccv in nc.Switch(engines=[nc.tensor, nc.scalar, nc.vector],
                             index=[v, v, v], n=4):
            for eng in ["tensor", "scalar", "vector"]:
                for f in bodies[ccv].ops[eng]:
                    f()
        for eng in ["tensor", "scalar", "vector"]:
            for f in S.ops[eng][marks[eng]:]:
                f()
        print("fused arena hwm:", AL.hwm, "f32/partition")
    return nc


_FUSED_CACHE = {}


def kernel(L0_attn_x, L0_attn_Wq, L0_attn_bq, L0_attn_Wk, L0_attn_bk,
           L0_attn_Wv, L0_attn_bv, L0_attn_Wo, L0_attn_bo,
           L0_ffn_w_freq_real, L0_ffn_w_freq_imag, L0_ffn_bias,
           L1_attn_Wq, L1_attn_Wk, L1_attn_Wv, L1_attn_Wo,
           L1_ffn_w_gate, L1_ffn_w_up, L1_ffn_w_down):
    debug = bool(_FUSED_CACHE.get("debug"))
    if "nc" not in _FUSED_CACHE:
        _FUSED_CACHE["nc"] = build_fused(debug)
    nc = _FUSED_CACHE["nc"]

    def qperm_w(w):
        return np.ascontiguousarray(
            np.asarray(w, np.float32).reshape(-1, 16, 64)[:, QPERM].reshape(w.shape))

    def colmajor8(b):
        return np.ascontiguousarray(np.asarray(b, np.float32).reshape(8, 128).T)

    Cm, Sm = _dft_tables()
    x = np.asarray(L0_attn_x)
    trow = np.tile(np.arange(512, dtype=np.float32)[None, :], (128, 1))
    alb = _host_alb()
    ident = np.eye(128, dtype=np.float32)
    bq0 = np.asarray(L0_attn_bq).reshape(16, 64)[QPERM].reshape(1024)
    bqkv0 = np.concatenate([bq0, np.asarray(L0_attn_bk),
                            np.asarray(L0_attn_bv)])[None, :].astype(np.float32)
    in_maps = []
    for core in range(NCORES):
        cos, sin = _rope_tables(core)
        sl = slice(128 * core, 128 * (core + 1))
        m = dict(
            xfm=_shard_x_fm(x, core),
            cid=np.array([[core % 4]], np.int32),
            cosT=cos, sinT=sin,
            wq0=qperm_w(L0_attn_Wq), wk0=np.asarray(L0_attn_Wk),
            wv0=np.asarray(L0_attn_Wv), bqkv0=bqkv0,
            wo0=np.asarray(L0_attn_Wo), bo0=colmajor8(L0_attn_bo),
            dcolT=_host_dcol(core % 4), trowT=trow,
            wfr=np.asarray(L0_ffn_w_freq_real), wfi=np.asarray(L0_ffn_w_freq_imag),
            Cm=Cm, Sm=Sm,
            Csl=np.ascontiguousarray(Cm[:, sl] / D),
            Ssl=np.ascontiguousarray(Sm[:, sl] / D),
            fb0=colmajor8(L0_ffn_bias),
            wq1=qperm_w(L1_attn_Wq), wk1=np.asarray(L1_attn_Wk),
            wv1=np.asarray(L1_attn_Wv), wo1=np.asarray(L1_attn_Wo),
            bz=np.zeros((128, 8), np.float32),
            albT=alb,
            wg=np.asarray(L1_ffn_w_gate), wu=np.asarray(L1_ffn_w_up),
            wd=np.asarray(L1_ffn_w_down), ident=ident,
        )
        in_maps.append(m)
    res = run_bass_kernel_spmd(nc, in_maps, core_ids=list(range(NCORES)))
    out = np.zeros((B, S_, D), np.float32)
    for core in range(NCORES):
        out[core // 4][_local_positions(core)] = np.asarray(res.results[core]["out"]).T
    if debug:
        _FUSED_CACHE["res"] = res
    return out


# version 04
# speedup vs baseline: 1.4130x; 1.4130x over previous
import contextlib
import numpy as np
import ml_dtypes
import concourse.bass as bass
import concourse.mybir as mybir
from concourse.bass_utils import run_bass_kernel_spmd

F32 = mybir.dt.float32
BF16 = mybir.dt.bfloat16
AF = mybir.ActivationFunctionType
ALU = mybir.AluOpType
bfloat16 = ml_dtypes.bfloat16

NCORES = 8
B = 2
S = 8192
D = 1024
NQ = 16
NKV = 4
HD = 64
TPC = 2048
NT = TPC // 128
ROPE_BASE = 10000.0
MAGIC1 = float(3 * 2 ** 22)
MAGICH = float(3 * 2 ** 21)
LN_EPS = 1e-5



def _strips(core):
    cc = core % 4
    return (cc, 7 - cc)


def _local_positions(core):
    s0, s1 = _strips(core)
    return np.concatenate([np.arange(s0 * 1024, (s0 + 1) * 1024),
                           np.arange(s1 * 1024, (s1 + 1) * 1024)])


def _rope_tables(core):
    inv_freq = (1.0 / ROPE_BASE ** (np.arange(0, HD, 2, dtype=np.float64) / HD))
    pos = _local_positions(core).astype(np.float64)
    ang = pos[:, None] * inv_freq[None, :]
    return np.cos(ang).astype(np.float32), np.sin(ang).astype(np.float32)


def _shard_x_fm(x, core):
    b = core // 4
    rows = x[b][_local_positions(core)]
    return np.ascontiguousarray(rows.T)


class Ctr:
    def __init__(self):
        self.v = 0

    def inc(self, n=1):
        self.v += n
        return self.v


def _bcast(ap, reps):
    p, w = ap.shape
    return ap.unsqueeze(1).broadcast_to([p, reps, w])


DTSIZE = {F32: 4, BF16: 2, mybir.dt.int32: 4}


class Alloc:

    def __init__(self, nc, es, arena=None):
        self.nc, self.es, self.arena = nc, es, arena
        self.off = 0
        self.hwm = 0

    def reset(self):
        self.phase_hwms = getattr(self, "phase_hwms", [])
        self.phase_hwms.append(self.off)
        self.off = 0

    def sbuf(self, name, shape, dtype):
        if self.arena is None:
            return self.es.enter_context(self.nc.sbuf_tensor(name, shape, dtype)).ap()
        p, w = shape
        nel = (w * DTSIZE[dtype] + 3) // 4
        assert self.off + nel <= self.arena.shape[1], \
            f"arena overflow at {name}: {self.off + nel}"
        view = self.arena[:, self.off:self.off + nel]
        self.off += nel
        self.hwm = max(self.hwm, self.off)
        if dtype != F32:
            view = view.bitcast(dtype)[:, 0:w]
        if p != 128:
            view = view[0:p, :]
        return view



def emit_qkv(S, AL, io, has_bias):
    nc = S.nc
    op, wait, inc_last, C = S.op, S.wait, S.inc_last, S.C
    xb_sb = AL.sbuf("q_xb", [128, 8 * TPC], BF16)
    wq_sb = AL.sbuf("q_wq", [128, 8 * 1024], BF16)
    wk_sb = AL.sbuf("q_wk", [128, 8 * 256], BF16)
    wv_sb = AL.sbuf("q_wv", [128, 8 * 256], BF16)
    bias_sb = AL.sbuf("q_bias", [1, 1536], BF16)
    ones1 = AL.sbuf("q_ones1", [1, 128], BF16)
    cos_sb = AL.sbuf("q_cos", [128, NT * 32], F32)
    sin_sb = AL.sbuf("q_sin", [128, NT * 32], F32)
    qo_sb = AL.sbuf("q_qo", [128, 2 * 1024], BF16)
    ko_sb = AL.sbuf("q_ko", [128, 2 * 256], BF16)
    vo_sb = AL.sbuf("q_vo", [128, 2 * 256], BF16)
    rsc = AL.sbuf("q_rsc", [128, 512], F32)
    P = S.P
    pq = [P[:, 512 * i:512 * (i + 1)] for i in range(4)]
    pk = [P[:, 2048 + 512 * i:2048 + 512 * i + 256] for i in range(2)]
    pv = [P[:, 3072 + 512 * i:3072 + 512 * i + 256] for i in range(2)]

    op("sync", lambda: nc.sync.dma_start(
        out=cos_sb.rearrange("p (t i) -> p t i", i=32),
        in_=io["cosT"].rearrange("(t p) i -> p t i", p=128)))
    inc_last("sync", "ld", 16)
    op("sync", lambda: nc.sync.dma_start(
        out=sin_sb.rearrange("p (t i) -> p t i", i=32),
        in_=io["sinT"].rearrange("(t p) i -> p t i", p=128)))
    inc_last("sync", "ld", 16)
    op("gpsimd", lambda: nc.gpsimd.dma_start(
        out=xb_sb.rearrange("p (k t) -> p k t", k=8),
        in_=io["xfm"].rearrange("(k p) t -> p k t", p=128)))
    inc_last("gpsimd", "ld", 16)
    op("gpsimd", lambda: nc.gpsimd.dma_start(
        out=wq_sb.rearrange("p (k c) -> p k c", k=8),
        in_=io["wq"].rearrange("(k p) c -> p k c", p=128)))
    inc_last("gpsimd", "ld", 16)
    op("gpsimd", lambda: nc.gpsimd.dma_start(
        out=wk_sb.rearrange("p (k c) -> p k c", k=8),
        in_=io["wk"].rearrange("(k p) c -> p k c", p=128)))
    inc_last("gpsimd", "ld", 16)
    op("gpsimd", lambda: nc.gpsimd.dma_start(
        out=wv_sb.rearrange("p (k c) -> p k c", k=8),
        in_=io["wv"].rearrange("(k p) c -> p k c", p=128)))
    inc_last("gpsimd", "ld", 16)
    if has_bias:
        op("gpsimd", lambda: nc.gpsimd.dma_start(out=bias_sb, in_=io["bqkv"]))
        inc_last("gpsimd", "ld", 16)
    lds = C["ld"].v
    op("vector", lambda: nc.vector.memset(ones1, 1.0))
    inc_last("vector", "cv", 1)
    cvs = C["cv"].v

    rp_hist = []
    st_hist = []
    for t in range(NT):
        sl = t % 2
        if t == 0:
            wait("tensor", "ld", lds)
            wait("tensor", "cv", cvs)
        if t >= 2:
            wait("tensor", "rp", rp_hist[t - 2])
        lhs = lambda k, t=t: xb_sb[:, 2048 * k + 128 * t:2048 * k + 128 * (t + 1)]
        for b in range(2):
            for k in range(8):
                def qf(b=b, k=k, t=t, sl=sl):
                    return nc.tensor.matmul(
                        pq[2 * sl + b], lhs(k, t),
                        wq_sb[:, 1024 * k + 512 * b:1024 * k + 512 * (b + 1)],
                        start=(k == 0), stop=(k == 7 and not has_bias))
                op("tensor", qf)
            if has_bias:
                def qbf(b=b, sl=sl):
                    return nc.tensor.matmul(
                        pq[2 * sl + b], ones1, io_bias_slice(bias_sb, 512 * b, 512),
                        start=False, stop=True)
                op("tensor", qbf)
        for k in range(8):
            def kf(k=k, t=t, sl=sl):
                return nc.tensor.matmul(pk[sl], lhs(k, t),
                                        wk_sb[:, 256 * k:256 * (k + 1)],
                                        start=(k == 0),
                                        stop=(k == 7 and not has_bias))
            op("tensor", kf)
        if has_bias:
            op("tensor", lambda sl=sl: nc.tensor.matmul(
                pk[sl], ones1, io_bias_slice(bias_sb, 1024, 256),
                start=False, stop=True))
        for k in range(8):
            def vf(k=k, t=t, sl=sl):
                return nc.tensor.matmul(pv[sl], lhs(k, t),
                                        wv_sb[:, 256 * k:256 * (k + 1)],
                                        start=(k == 0),
                                        stop=(k == 7 and not has_bias))
            op("tensor", vf)
        if has_bias:
            op("tensor", lambda sl=sl: nc.tensor.matmul(
                pv[sl], ones1, io_bias_slice(bias_sb, 1280, 256),
                start=False, stop=True))
        inc_last("tensor", "mm", 1)
        mmv = C["mm"].v

        wait("vector", "mm", mmv)
        if t >= 2:
            wait("vector", "st", st_hist[t - 2])
        cosv = _bcast(cos_sb[:, 32 * t:32 * (t + 1)], 8)
        sinv = _bcast(sin_sb[:, 32 * t:32 * (t + 1)], 8)
        cosv4 = _bcast(cos_sb[:, 32 * t:32 * (t + 1)], 4)
        sinv4 = _bcast(sin_sb[:, 32 * t:32 * (t + 1)], 4)
        for b in range(2):
            Pq = pq[2 * sl + b].rearrange("p (h d) -> p h d", d=64)
            Plo, Phi = Pq[:, :, 0:32], Pq[:, :, 32:64]
            qov = qo_sb[:, 1024 * sl + 512 * b:1024 * sl + 512 * (b + 1)] \
                .rearrange("p (h d) -> p h d", d=64)
            Av = rsc[:, 0:256].rearrange("p (h d) -> p h d", d=32)
            Bv = rsc[:, 256:512].rearrange("p (h d) -> p h d", d=32)
            op("vector", lambda Phi=Phi, Av=Av, sinv=sinv:
               nc.vector.tensor_tensor(Av, Phi, sinv, ALU.mult))
            op("vector", lambda Plo=Plo, Bv=Bv, cosv=cosv:
               nc.vector.tensor_tensor(Bv, Plo, cosv, ALU.mult))
            op("vector", lambda qov=qov, Bv=Bv, Av=Av:
               nc.vector.tensor_tensor(qov[:, :, 0:32], Bv, Av, ALU.subtract))
            op("vector", lambda Plo=Plo, Av=Av, sinv=sinv:
               nc.vector.tensor_tensor(Av, Plo, sinv, ALU.mult))
            op("vector", lambda Phi=Phi, Bv=Bv, cosv=cosv:
               nc.vector.tensor_tensor(Bv, Phi, cosv, ALU.mult))
            op("vector", lambda qov=qov, Bv=Bv, Av=Av:
               nc.vector.tensor_tensor(qov[:, :, 32:64], Bv, Av, ALU.add))
        Pk = pk[sl].rearrange("p (h d) -> p h d", d=64)
        Plo, Phi = Pk[:, :, 0:32], Pk[:, :, 32:64]
        kov = ko_sb[:, 256 * sl:256 * (sl + 1)].rearrange("p (h d) -> p h d", d=64)
        Av = rsc[:, 0:128].rearrange("p (h d) -> p h d", d=32)
        Bv = rsc[:, 256:384].rearrange("p (h d) -> p h d", d=32)
        op("vector", lambda Phi=Phi, Av=Av, sinv4=sinv4:
           nc.vector.tensor_tensor(Av, Phi, sinv4, ALU.mult))
        op("vector", lambda Plo=Plo, Bv=Bv, cosv4=cosv4:
           nc.vector.tensor_tensor(Bv, Plo, cosv4, ALU.mult))
        op("vector", lambda kov=kov, Bv=Bv, Av=Av:
           nc.vector.tensor_tensor(kov[:, :, 0:32], Bv, Av, ALU.subtract))
        op("vector", lambda Plo=Plo, Av=Av, sinv4=sinv4:
           nc.vector.tensor_tensor(Av, Plo, sinv4, ALU.mult))
        op("vector", lambda Phi=Phi, Bv=Bv, cosv4=cosv4:
           nc.vector.tensor_tensor(Bv, Phi, cosv4, ALU.mult))
        op("vector", lambda kov=kov, Bv=Bv, Av=Av:
           nc.vector.tensor_tensor(kov[:, :, 32:64], Bv, Av, ALU.add))
        op("vector", lambda sl=sl: nc.vector.tensor_copy(
            vo_sb[:, 256 * sl:256 * (sl + 1)], pv[sl]))
        rp_hist.append(inc_last("vector", "rp", 1))

        op("sync", lambda v=rp_hist[-1]: nc.sync.wait_ge(S.sems["rp"], v))
        op("sync", lambda t=t, sl=sl: nc.sync.dma_start(
            out=io["qtm"][128 * t:128 * (t + 1), :],
            in_=qo_sb[:, 1024 * sl:1024 * (sl + 1)]))
        inc_last("sync", "st", 16)
        op("sync", lambda t=t, sl=sl: nc.sync.dma_start(
            out=io["ktm"][128 * t:128 * (t + 1), :],
            in_=ko_sb[:, 256 * sl:256 * (sl + 1)]))
        inc_last("sync", "st", 16)
        op("sync", lambda t=t, sl=sl: nc.sync.dma_start(
            out=io["vtm"][128 * t:128 * (t + 1), :],
            in_=vo_sb[:, 256 * sl:256 * (sl + 1)]))
        st_hist.append(inc_last("sync", "st", 16))
    op("sync", lambda v=C["st"].v: nc.sync.wait_ge(S.sems["st"], v))
    return C["st"].v


def io_bias_slice(bias_sb, off, n):
    return bias_sb[0:1, off:off + n]


def build_qkv(has_bias):
    nc = bass.Bass("TRN2", target_bir_lowering=False)
    io = {}
    io["xfm"] = nc.dram_tensor("xfm", [D, TPC], F32, kind="ExternalInput").ap()
    io["wq"] = nc.dram_tensor("wq", [D, NQ * HD], F32, kind="ExternalInput").ap()
    io["wk"] = nc.dram_tensor("wk", [D, NKV * HD], F32, kind="ExternalInput").ap()
    io["wv"] = nc.dram_tensor("wv", [D, NKV * HD], F32, kind="ExternalInput").ap()
    if has_bias:
        io["bqkv"] = nc.dram_tensor("bqkv", [1, 1536], F32, kind="ExternalInput").ap()
    io["cosT"] = nc.dram_tensor("cosT", [TPC, 32], F32, kind="ExternalInput").ap()
    io["sinT"] = nc.dram_tensor("sinT", [TPC, 32], F32, kind="ExternalInput").ap()
    io["qtm"] = nc.dram_tensor("qtm", [TPC, NQ * HD], BF16, kind="ExternalOutput").ap()
    io["ktm"] = nc.dram_tensor("ktm", [TPC, NKV * HD], BF16, kind="ExternalOutput").ap()
    io["vtm"] = nc.dram_tensor("vtm", [TPC, NKV * HD], BF16, kind="ExternalOutput").ap()
    with contextlib.ExitStack() as es:
        AL = Alloc(nc, es)
        S = Sched(nc, es, ["ld", "cv", "mm", "rp", "st"])
        S.P = es.enter_context(nc.psum_tensor("P", [128, 4096], F32)).ap()
        block = es.enter_context(nc.Block())
        emit_qkv(S, AL, io, has_bias)
        S.emit(block)
    return nc


def run_qkv(layer, x_shards, weights, res):
    has_bias = (layer == 0)
    nc = build_qkv(has_bias)
    in_maps = []
    for core in range(NCORES):
        cos, sin = _rope_tables(core)
        m = dict(xfm=x_shards[core], wq=weights["wq"], wk=weights["wk"],
                 wv=weights["wv"], cosT=cos, sinT=sin)
        if has_bias:
            m["bqkv"] = np.concatenate([weights["bq"], weights["bk"],
                                        weights["bv"]])[None, :].astype(np.float32)
        in_maps.append(m)
    out = run_bass_kernel_spmd(nc, in_maps, core_ids=list(range(NCORES)))
    res.extend([out])
    return [r["qtm"] for r in out.results], [r["ktm"] for r in out.results], \
           [r["vtm"] for r in out.results]





def _alibi_slopes16():
    start = 2.0 ** (-2.0 ** (-(np.log2(16) - 3)))
    return np.array([start * start ** i for i in range(16)], dtype=np.float64)


def _l1_window_tiles():
    sl = _alibi_slopes16()
    w = np.ceil((115.0 / sl) / 128.0).astype(int)
    return np.minimum(w, 64)


def _pair_heads(p):
    base = 8 if p >= 4 else 0
    a = base + (p % 4)
    return a, a + 4


QPERM = []
for _p in range(8):
    _a, _b = _pair_heads(_p)
    QPERM += [_a, _b]


def attn_tensors(AL, layer):
    causal = (layer == 0)
    T = {}
    T["K_sb"] = [AL.sbuf(f"a_K{i}", [128, S_], BF16) for i in range(2)]
    T["V_sb"] = AL.sbuf("a_V", [128, 64 * 4 * 65], BF16)
    T["Q_sb"] = AL.sbuf("a_Q", [128, 2 * TPC], BF16)
    T["eX"] = AL.sbuf("a_eX", [128, 2 * 1024], BF16)
    T["Ofm"] = AL.sbuf("a_Ofm", [128, 8 * TPC], BF16)
    T["dsc64"] = AL.sbuf("a_dsc64", [128, 4096], BF16)
    T["denp"] = AL.sbuf("a_denp", [128, 2 * 256], BF16)
    T["rp"] = AL.sbuf("a_rp", [128, 256], BF16)
    T["rrow"] = AL.sbuf("a_rrow", [1, 16 * 512], BF16)
    T["wo_sb"] = AL.sbuf("a_wo", [128, 8 * 1024], BF16)
    T["bo_sb"] = AL.sbuf("a_bo", [128, 8], F32)
    T["ones1"] = AL.sbuf("a_ones1", [1, 128], BF16)
    T["oneK"] = AL.sbuf("a_oneK", [128, 1], BF16)
    T["msk"] = AL.sbuf("a_msk", [128, 512], BF16)
    if causal:
        T["dcol"] = AL.sbuf("a_dcol", [128, 16], F32)
        T["trow"] = AL.sbuf("a_trow", [128, 512], F32)
    else:
        T["alb"] = AL.sbuf("a_alb", [128, 64 * 16], F32)
    T["rb_sb"] = AL.sbuf("a_rb", [128, 512], BF16)
    T["On_sb"] = AL.sbuf("a_On", [128, 8 * 512], BF16)
    T["xstg"] = AL.sbuf("a_xstg", [128, 2 * 512], F32)
    T["s_c"] = AL.sbuf("a_sc", [128, 8 * 512], F32)
    T["sbf"] = AL.sbuf("a_sbf", [128, 2 * 512], BF16)
    T["ssq"] = AL.sbuf("a_ssq", [128, 2 * 512], BF16)
    T["stat"] = AL.sbuf("a_stat", [1, 2560], F32)
    T["x1stg"] = AL.sbuf("a_x1stg", [128, 2 * 512], F32)
    return T


S_ = S


def attn_psum(S):
    P = S.P
    return dict(Sp=[P[:, 0:1024], P[:, 1024:2048]],
                Oeo=[[P[0:65, 2048:2560], P[0:65, 2560:3072]],
                     [P[0:65, 3072:3584], P[0:65, 3584:4096]]],
                pbcF=P[:, 0:512], py=[P[:, 1024:1536], P[:, 1536:2048]],
                pmean=P[0:1, 2048:2560], pmsq=P[0:1, 2560:3072],
                prb=P[:, 3072:3584], pmr=P[:, 3584:4096])


def emit_attn_pre(S, T, io, layer, rank_major=False):
    nc = S.nc
    causal = (layer == 0)
    op, wait, inc_last, C = S.op, S.wait, S.inc_last, S.C
    if not rank_major:
        srcrows = [(1024 * s, 1024) for s in range(8)]
    else:
        srcrows = []
        for s in range(8):
            r = min(s, 7 - s)
            part = 0 if s < 4 else 1
            srcrows.append((2048 * r + 1024 * part, 1024))
    for t in range(2):
        for s in range(8):
            sr = srcrows[s][0]
            def ktf(t=t, s=s, sr=sr):
                return nc.sync.dma_start_transpose(
                    T["K_sb"][t][:, 1024 * s:1024 * (s + 1)],
                    io["kall"][sr:sr + 1024, 128 * t:128 * (t + 1)])
            op("sync", ktf)
            inc_last("sync", "ld", 16)
    op("vector", lambda: nc.vector.memset(T["V_sb"], 1.0))
    op("vector", lambda: nc.vector.memset(T["ones1"], 1.0))
    op("vector", lambda: nc.vector.memset(T["oneK"], 1.0 / 1024.0))
    inc_last("vector", "cv", 1)
    op("sync", lambda v=C["cv"].v: nc.sync.wait_ge(S.sems["cv"], v))
    for jv in range(4):
        for s in range(8):
            sr = srcrows[s][0]
            vdst = T["V_sb"].rearrange("p (c j d) -> p c j d", c=64, j=4)[
                :, 8 * s:8 * (s + 1), jv, 0:64]
            vsrc = io["vall"][sr:sr + 1024, :].rearrange(
                "(c p) jd -> p c jd", p=128)[:, :, 64 * jv:64 * (jv + 1)]
            op("sync", lambda vdst=vdst, vsrc=vsrc:
               nc.sync.dma_start(out=vdst, in_=vsrc))
            inc_last("sync", "ld", 16)
    if causal:
        op("sync", lambda: nc.sync.dma_start(out=T["dcol"], in_=io["dcolT"]))
        inc_last("sync", "ld", 16)
        op("sync", lambda: nc.sync.dma_start(out=T["trow"], in_=io["trowT"]))
        inc_last("sync", "ld", 16)
    else:
        op("sync", lambda: nc.sync.dma_start(out=T["alb"], in_=io["albT"]))
        inc_last("sync", "ld", 16)
    op("sync", lambda: nc.sync.dma_start(out=T["bo_sb"], in_=io["bo"]))
    inc_last("sync", "ld", 16)
    op("gpsimd", lambda: nc.gpsimd.dma_start(
        out=T["wo_sb"].rearrange("p (k c) -> p k c", k=8),
        in_=io["wo"].rearrange("(k p) c -> p k c", p=128)))
    inc_last("gpsimd", "ld", 16)
    return C["ld"].v


def emit_attn_core(S, Sb, T, io, layer, cc, emit_sync=True):
    nc = S.nc
    causal = (layer == 0)
    strips = (cc, 7 - cc)
    wtiles = _l1_window_tiles() if not causal else None
    PV = attn_psum(S)
    Sp, Oeo = PV["Sp"], PV["Oeo"]
    C = S.C
    lds = C["ld"].v

    def jrange(p, hh, c_):
        if causal:
            return list(range(8 * strips[hh] + 4 * (c_ + 1)))
        a, b = _pair_heads(p)
        return list(range(64 - max(wtiles[a], wtiles[b]), 64))

    def active(p, j, ab):
        if causal:
            return True
        return j >= 64 - wtiles[_pair_heads(p)[ab]]

    def q_load(p):
        S.op("sync", lambda p=p, sl=p % 2: nc.sync.dma_start_transpose(
            T["Q_sb"][:, TPC * sl:TPC * (sl + 1)], io["qtm"][:, 128 * p:128 * (p + 1)]))
        S.inc_last("sync", "q", 16)

    qb0 = C["q"].v
    if emit_sync:
        q_load(0)
        q_load(1)
    else:
        C["q"].inc(32)

    ex_hist = []
    mk_hist = []
    pv_hist = []
    pend_pv = []
    ucount = 0
    for p in range(8):
        hA, hB = _pair_heads(p)
        kv = [hA // 4, hB // 4]
        kt = T["K_sb"][0] if p < 4 else T["K_sb"][1]
        ktb = [64 * (kv[0] % 2), 64 * (kv[1] % 2)]
        qsl = p % 2
        for hh in range(2):
            for c_ in (0, 1):
                js = jrange(p, hh, c_)
                tqo = TPC * qsl + 1024 * hh + 512 * c_
                for ji, j in enumerate(js):
                    sl = ucount % 2
                    acts = [ab for ab in range(2) if active(p, j, ab)]
                    if ucount == 0:
                        Sb.wait("tensor", "ld", lds)
                    Sb.wait("tensor", "q", qb0 + 16 * (p + 1))
                    if ucount >= 2:
                        Sb.wait("tensor", "ex", ex_hist[ucount - 2])
                    for ab in acts:
                        def mmf(ab=ab, j=j, sl=sl, tqo=tqo, kt=kt, kb=ktb[ab]):
                            return nc.tensor.matmul(
                                Sp[sl][:, 512 * ab:512 * (ab + 1)],
                                kt[kb:kb + 64, 128 * j:128 * (j + 1)],
                                T["Q_sb"][64 * ab:64 * ab + 64, tqo:tqo + 512],
                                start=True, stop=True,
                                tile_position=(64 * ab, 0))
                        Sb.op("tensor", mmf)
                    Sb.inc_last("tensor", "qk", 1)
                    if pend_pv:
                        pend_pv.pop(0)()

                    Sb.wait("scalar", "qk", C["qk"].v)
                    if ucount >= 2:
                        Sb.wait("scalar", "pv", pv_hist[ucount - 2])
                    if causal:
                        def ef(sl=sl):
                            return nc.scalar.activation(
                                T["eX"][:, 1024 * sl:1024 * (sl + 1)], Sp[sl],
                                AF.Exp, scale=0.125)
                        Sb.op("scalar", ef)
                    else:
                        for ab in acts:
                            h = _pair_heads(p)[ab]
                            def ef(sl=sl, ab=ab, j=j, h=h):
                                return nc.scalar.activation(
                                    T["eX"][:, 1024 * sl + 512 * ab:
                                            1024 * sl + 512 * (ab + 1)],
                                    Sp[sl][:, 512 * ab:512 * (ab + 1)],
                                    AF.Exp, scale=0.125,
                                    bias=T["alb"][:, 16 * j + h:16 * j + h + 1])
                            Sb.op("scalar", ef)
                    Sb.inc_last("scalar", "ex", 1)
                    ex_hist.append(C["ex"].v)

                    if causal:
                        mi = None
                        m0 = 8 * strips[hh] + 4 * c_
                        if m0 <= j < m0 + 4:
                            mi = hh * 8 + c_ * 4 + (j - m0)
                        if mi is not None:
                            Sb.wait("vector", "ex", C["ex"].v)
                            def mkf(mi=mi):
                                return nc.vector.tensor_scalar(
                                    T["msk"], T["trow"], T["dcol"][:, mi:mi + 1], 0.0,
                                    ALU.subtract, ALU.is_ge)
                            Sb.op("vector", mkf)
                            for ab in range(2):
                                def mtf(sl=sl, ab=ab):
                                    v = T["eX"][:, 1024 * sl + 512 * ab:
                                                1024 * sl + 512 * (ab + 1)]
                                    return nc.vector.tensor_tensor(v, v, T["msk"], ALU.mult)
                                Sb.op("vector", mtf)
                            Sb.inc_last("vector", "mk", 1)
                    mk_hist.append(C["mk"].v)

                    def emit_pv(u=ucount, sl=sl, acts=tuple(acts), j=j,
                                p=p, hh=hh, c_=c_, js=tuple(js)):
                        Sb.wait("tensor", "ex", ex_hist[u])
                        Sb.wait("tensor", "mk", mk_hist[u])
                        for ab in acts:
                            h = _pair_heads(p)[ab]
                            if causal:
                                st_ = (j == 0)
                                sp_ = (j == js[-1])
                            else:
                                st_ = (j == 64 - wtiles[h])
                                sp_ = (j == 63)
                            for eo in range(2):
                                def pvf(ab=ab, sl=sl, j=j, eo=eo,
                                        kvi=kv[ab], st_=st_, sp_=sp_):
                                    vcol = (4 * j + kvi) * 65
                                    return nc.tensor.matmul(
                                        Oeo[ab][eo],
                                        T["V_sb"][64 * eo:64 * (eo + 1),
                                                  vcol:vcol + 65],
                                        T["eX"][64 * eo:64 * (eo + 1),
                                                1024 * sl + 512 * ab:
                                                1024 * sl + 512 * (ab + 1)],
                                        start=st_, stop=sp_,
                                        tile_position=(64 * eo, 0))
                                Sb.op("tensor", pvf)
                        Sb.inc_last("tensor", "pv", 1)
                        pv_hist.append(C["pv"].v)
                    pend_pv.append(emit_pv)
                    ucount += 1

                while pend_pv:
                    pend_pv.pop(0)()
                Sb.wait("vector", "pv", C["pv"].v)
                for ab in range(2):
                    h = _pair_heads(p)[ab]
                    gc = 2 * hh + c_
                    def ecf(ab=ab):
                        return nc.vector.tensor_copy(
                            T["rb_sb"][0:64, :], Oeo[ab][0][0:64, :])
                    Sb.op("vector", ecf)
                    def evf(h=h, ab=ab, gc=gc):
                        return nc.vector.tensor_tensor(
                            T["Ofm"][(h % 2) * 64:(h % 2) * 64 + 64,
                                     TPC * (h // 2) + 512 * gc:
                                     TPC * (h // 2) + 512 * (gc + 1)],
                            T["rb_sb"][0:64, :], Oeo[ab][1][0:64, :], ALU.add)
                    Sb.op("vector", evf)
                    for eo in range(2):
                        def dnf(ab=ab, c_=c_, eo=eo):
                            return nc.vector.tensor_copy(
                                T["dsc64"][64:65, 2048 * eo + 512 * (2 * ab + c_):
                                           2048 * eo + 512 * (2 * ab + c_ + 1)],
                                Oeo[ab][eo][64:65, :])
                        Sb.op("vector", dnf)
                Sb.inc_last("vector", "oev", 1)
                Sb.wait("tensor", "oev", C["oev"].v)
                if c_ == 1:
                    phi = p * 2 + hh
                    if emit_sync:
                        S.op("sync", lambda v=C["oev"].v: nc.sync.wait_ge(S.sems["oev"], v))
                        for eo in range(2):
                            def ddf(phi=phi, eo=eo):
                                return nc.sync.dma_start(
                                    out=io["dscr"].rearrange("(e a w) -> e a w", e=2, w=2048)[eo, phi:phi + 1, :],
                                    in_=T["dsc64"][64:65, 2048 * eo:2048 * (eo + 1)])
                            S.op("sync", ddf)
                            S.inc_last("sync", "dd", 16)
                    else:
                        C["dd"].inc(32)
                    Sb.wait("vector", "dd", C["dd"].v)
        if p + 2 < 8:
            if emit_sync:
                S.op("sync", lambda v=C["qk"].v: nc.sync.wait_ge(S.sems["qk"], v))
                q_load(p + 2)
            else:
                C["q"].inc(16)


def emit_attn_post(S, T, io, layer):
    nc = S.nc
    op, wait, inc_last, C = S.op, S.wait, S.inc_last, S.C
    PV = attn_psum(S)
    pbcF, py = PV["pbcF"], PV["py"]
    pmean, pmsq, prb, pmr = PV["pmean"], PV["pmsq"], PV["prb"], PV["pmr"]

    def unit_of(h, gc):
        if h < 4:
            p_, ab = h, 0
        elif h < 8:
            p_, ab = h - 4, 1
        elif h < 12:
            p_, ab = h - 4, 0
        else:
            p_, ab = h - 8, 1
        hh, c_ = gc // 2, gc % 2
        return ((p_ * 2 + hh) * 2 + ab) * 2 + c_

    op("sync", lambda v=C["dd"].v: nc.sync.wait_ge(S.sems["dd"], v))
    op("sync", lambda: nc.sync.dma_start(
        out=T["denp"].rearrange("p (e w) -> p e w", e=2),
        in_=io["dscr"].rearrange("(e p w) -> p e w", p=128, w=256)))
    inc_last("sync", "db", 16)
    wait("vector", "db", C["db"].v)
    op("vector", lambda: nc.vector.tensor_tensor(
        T["denp"][:, 0:256], T["denp"][:, 0:256], T["denp"][:, 256:512], ALU.add))

    def _recipf():
        with nc.allow_low_precision(reason="bf16 softmax normalizer"):
            return nc.vector.reciprocal(T["rp"], T["denp"][:, 0:256])
    op("vector", _recipf)
    inc_last("vector", "db", 1)
    op("sync", lambda v=C["db"].v: nc.sync.wait_ge(S.sems["db"], v))
    op("sync", lambda: nc.sync.dma_start(
        out=io["rscr"].rearrange("(p w) -> p w", w=256), in_=T["rp"]))
    inc_last("sync", "db", 16)

    stt_hist = []
    sm_hist = []
    x1s_hist = []
    xsb0 = C["xs"].v
    for c in range(4):
        op("sync", lambda v=C["db"].v: nc.sync.wait_ge(S.sems["db"], v))
        if c > 0:
            op("sync", lambda v=C["bc"].v: nc.sync.wait_ge(S.sems["bc"], v))
        hh_, c__ = c // 2, c % 2
        for quart in range(4):
            ab_ = 1 if quart in (1, 3) else 0
            p0 = [0, 0, 4, 4][quart]
            u0 = ((p0 * 2 + hh_) * 2 + ab_) * 2 + c__

            def rldf(quart=quart, u0=u0):
                return nc.sync.dma_start(
                    out=T["rrow"][0:1, 2048 * quart:2048 * (quart + 1)]
                    .rearrange("a (h w) -> a h w", w=512),
                    in_=io["rscr"].rearrange("(u w) -> u w", w=512)[u0:u0 + 25:8, :]
                    .unsqueeze(0))
            op("sync", rldf)
            inc_last("sync", "x1s", 16)
        def xpf_emit(o, c):
            ko_ = 8 * c + o
            if ko_ >= 2:
                op("sync", lambda v=stt_hist[ko_ - 2]:
                   nc.sync.wait_ge(S.sems["stt"], v))

            def xpf(o=o, c=c):
                return nc.sync.dma_start(
                    out=T["xstg"][:, 512 * (o % 2):512 * (o % 2 + 1)],
                    in_=io["xfm"][128 * o:128 * (o + 1), 512 * c:512 * (c + 1)])
            op("sync", xpf)
            inc_last("sync", "xs", 16)
        xpf_emit(0, c)
        xpf_emit(1, c)
        for k in range(8):
            if k == 0:
                wait("tensor", "x1s", C["x1s"].v)
                if c == 0:
                    wait("tensor", "oev", C["oev"].v)
            for sub in range(2):
                h = 2 * k + sub
                quart = h // 4
                hl = quart * 4 + (h % 4)

                def bcf(sub=sub, hl=hl):
                    return nc.tensor.matmul(
                        pbcF[64 * sub:64 * (sub + 1), :], T["ones1"][0:1, 0:64],
                        T["rrow"][0:1, 512 * hl:512 * (hl + 1)],
                        start=True, stop=True,
                        tile_position=(0, 64 * sub))
                op("tensor", bcf)
            inc_last("tensor", "bc", 1)
            wait("vector", "bc", C["bc"].v)
            op("vector", lambda: nc.vector.tensor_copy(T["rb_sb"], pbcF))

            def nrf(k=k, c=c):
                return nc.vector.tensor_tensor(
                    T["On_sb"][:, 512 * k:512 * (k + 1)],
                    T["Ofm"][:, TPC * k + 512 * c:TPC * k + 512 * (c + 1)],
                    T["rb_sb"], ALU.mult)
            op("vector", nrf)
            inc_last("vector", "nr", 1)
            wait("tensor", "nr", C["nr"].v)
        for o in range(8):
            ysl = o % 2
            ko = 8 * c + o
            if ko >= 2:
                wait("tensor", "stt", stt_hist[ko - 2])
            for k in range(8):
                def yf(o=o, k=k, ysl=ysl):
                    return nc.tensor.matmul(
                        py[ysl],
                        T["wo_sb"][:, 1024 * k + 128 * o:1024 * k + 128 * (o + 1)],
                        T["On_sb"][:, 512 * k:512 * (k + 1)],
                        start=(k == 0), stop=(k == 7))
                op("tensor", yf)
            inc_last("tensor", "y", 1)
            wait("vector", "y", C["y"].v)
            wait("vector", "xs", xsb0 + 16 * (8 * c + o + 1))

            def sf(o=o, ysl=ysl):
                return nc.vector.scalar_tensor_tensor(
                    T["s_c"][:, 512 * o:512 * (o + 1)], py[ysl],
                    T["bo_sb"][:, o:o + 1], T["xstg"][:, 512 * (o % 2):512 * (o % 2 + 1)],
                    ALU.add, ALU.add)
            op("vector", sf)
            stt_hist.append(inc_last("vector", "stt", 1))
            if o + 2 < 8:
                xpf_emit(o + 2, c)
            if ko >= 2:
                wait("vector", "sm", sm_hist[ko - 2])

            def sbff(o=o):
                return nc.vector.tensor_copy(
                    T["sbf"][:, 512 * (o % 2):512 * (o % 2 + 1)],
                    T["s_c"][:, 512 * o:512 * (o + 1)])
            op("vector", sbff)

            def ssqf(o=o):
                sl_ = slice(512 * (o % 2), 512 * (o % 2 + 1))
                return nc.vector.tensor_tensor(
                    T["ssq"][:, sl_], T["sbf"][:, sl_], T["sbf"][:, sl_], ALU.mult)
            op("vector", ssqf)
            inc_last("vector", "sq", 1)
            wait("tensor", "sq", C["sq"].v)

            def smf(o=o):
                return nc.tensor.matmul(
                    pmean, T["oneK"], T["sbf"][:, 512 * (o % 2):512 * (o % 2 + 1)],
                    start=(o == 0), stop=(o == 7))
            op("tensor", smf)

            def sqmf(o=o):
                return nc.tensor.matmul(
                    pmsq, T["oneK"], T["ssq"][:, 512 * (o % 2):512 * (o % 2 + 1)],
                    start=(o == 0), stop=(o == 7))
            op("tensor", sqmf)
            sm_hist.append(inc_last("tensor", "sm", 1))
        _emit_ln_tail(S, T, io, prb, pmr, pmean, pmsq, c, stt_hist, x1s_hist,
                      io["x1"])
        wait("tensor", "ap", C["ap"].v)
    op("sync", lambda v=C["x1s"].v: nc.sync.wait_ge(S.sems["x1s"], v))


def _emit_ln_tail(S, T, io, prb, pmr, pmean, pmsq, c, stt_hist, x1s_hist, xout):
    nc = S.nc
    op, wait, inc_last, C = S.op, S.wait, S.inc_last, S.C
    stat = T["stat"]
    wait("vector", "sm", C["sm"].v)
    op("vector", lambda: nc.vector.tensor_copy(stat[0:1, 0:512], pmean))
    op("vector", lambda: nc.vector.tensor_copy(stat[0:1, 512:1024], pmsq))
    op("vector", lambda: nc.vector.tensor_tensor(
        stat[0:1, 1024:1536], stat[0:1, 0:512], stat[0:1, 0:512], ALU.mult))
    op("vector", lambda: nc.vector.tensor_tensor(
        stat[0:1, 1536:2048], stat[0:1, 512:1024], stat[0:1, 1024:1536],
        ALU.subtract))
    op("vector", lambda: nc.vector.tensor_scalar(
        stat[0:1, 1536:2048], stat[0:1, 1536:2048], LN_EPS, None,
        ALU.add, ALU.bypass))
    inc_last("vector", "nr", 1)
    wait("scalar", "nr", C["nr"].v)
    op("scalar", lambda: nc.scalar.activation(
        stat[0:1, 1024:1536], stat[0:1, 1536:2048], AF.Ln))
    op("scalar", lambda: nc.scalar.activation(
        stat[0:1, 512:1024], stat[0:1, 1024:1536], AF.Exp, scale=-0.5))
    inc_last("scalar", "ex", 1)
    wait("vector", "ex", C["ex"].v)
    op("vector", lambda: nc.vector.scalar_tensor_tensor(
        stat[0:1, 1536:2048], stat[0:1, 0:512], -1.0,
        stat[0:1, 512:1024], ALU.mult, ALU.mult))
    rstd_bf = stat[0:1, 2048:2304].bitcast(BF16)
    mr_bf = stat[0:1, 2304:2560].bitcast(BF16)
    op("vector", lambda rb=rstd_bf: nc.vector.tensor_copy(rb, stat[0:1, 512:1024]))
    op("vector", lambda mb=mr_bf: nc.vector.tensor_copy(mb, stat[0:1, 1536:2048]))
    inc_last("vector", "nr", 1)
    wait("tensor", "nr", C["nr"].v)
    op("tensor", lambda rb=rstd_bf: nc.tensor.matmul(
        prb, T["ones1"], rb, start=True, stop=True))
    op("tensor", lambda mb=mr_bf: nc.tensor.matmul(
        pmr, T["ones1"], mb, start=True, stop=True))
    inc_last("tensor", "bc", 1)
    wait("vector", "bc", C["bc"].v)
    for o in range(8):
        ko = len(x1s_hist)
        osl = ko % 2
        if ko >= 2:
            wait("vector", "x1s", x1s_hist[ko - 2])

        def a1f(o=o, osl=osl):
            return nc.vector.tensor_tensor(
                T["x1stg"][:, 512 * osl:512 * (osl + 1)],
                T["s_c"][:, 512 * o:512 * (o + 1)], prb, ALU.mult)
        op("vector", a1f)

        def a2f(osl=osl):
            return nc.vector.tensor_tensor(
                T["x1stg"][:, 512 * osl:512 * (osl + 1)],
                T["x1stg"][:, 512 * osl:512 * (osl + 1)], pmr, ALU.add)
        op("vector", a2f)
        inc_last("vector", "ap", 1)
        op("sync", lambda v=C["ap"].v: nc.sync.wait_ge(S.sems["ap"], v))

        def x1f(o=o, c=c, osl=osl):
            return nc.sync.dma_start(
                out=xout[128 * o:128 * (o + 1), 512 * c:512 * (c + 1)],
                in_=T["x1stg"][:, 512 * osl:512 * (osl + 1)])
        op("sync", x1f)
        x1s_hist.append(inc_last("sync", "x1s", 16))


ATTN_SEMS = ["ld", "cv", "q", "qk", "ex", "mk", "pv", "oev", "dd", "db",
             "bc", "nr", "y", "stt", "sq", "sm", "ap", "x1s", "xs"]


def build_attn(layer, cc):
    causal = (layer == 0)
    nc = bass.Bass("TRN2", target_bir_lowering=False)
    io = {}
    io["qtm"] = nc.dram_tensor("qtm", [TPC, 1024], BF16, kind="ExternalInput").ap()
    io["kall"] = nc.dram_tensor("kall", [S_, 256], BF16, kind="ExternalInput").ap()
    io["vall"] = nc.dram_tensor("vall", [S_, 256], BF16, kind="ExternalInput").ap()
    io["xfm"] = nc.dram_tensor("xfm", [D, TPC], F32, kind="ExternalInput").ap()
    io["wo"] = nc.dram_tensor("wo", [1024, D], F32, kind="ExternalInput").ap()
    io["bo"] = nc.dram_tensor("bo", [128, 8], F32, kind="ExternalInput").ap()
    if causal:
        io["dcolT"] = nc.dram_tensor("dcolT", [128, 16], F32, kind="ExternalInput").ap()
        io["trowT"] = nc.dram_tensor("trowT", [128, 512], F32, kind="ExternalInput").ap()
    else:
        io["albT"] = nc.dram_tensor("albT", [128, 64 * 16], F32, kind="ExternalInput").ap()
    io["x1"] = nc.dram_tensor("x1", [D, TPC], F32, kind="ExternalOutput").ap()
    io["dscr"] = nc.dram_tensor("dscr", [2 * 64 * 512], BF16, kind="Internal").ap()
    io["rscr"] = nc.dram_tensor("rscr", [64 * 512], BF16, kind="Internal").ap()
    with contextlib.ExitStack() as es:
        AL = Alloc(nc, es)
        S = Sched(nc, es, ATTN_SEMS)
        S.P = es.enter_context(nc.psum_tensor("P", [128, 4096], F32)).ap()
        block = es.enter_context(nc.Block())
        T = attn_tensors(AL, layer)
        emit_attn_pre(S, T, io, layer)
        emit_attn_core(S, S, T, io, layer, cc, emit_sync=True)
        emit_attn_post(S, T, io, layer)
        S.emit(block)
    return nc


def _host_dcol(cc):
    out = np.zeros((128, 16), np.float32)
    p_ = np.arange(128, dtype=np.float32)
    for hh, st in enumerate((cc, 7 - cc)):
        for c_ in (0, 1):
            q0 = 1024 * st + 512 * c_
            for m in range(4):
                j = 8 * st + 4 * c_ + m
                out[:, hh * 8 + c_ * 4 + m] = 128 * j + p_ - q0
    return out


def _host_alb():
    sl = _alibi_slopes16()
    out = np.zeros((128, 64 * 16), np.float32)
    p_ = np.arange(128, dtype=np.float64)
    for j in range(64):
        for h in range(16):
            out[:, 16 * j + h] = (sl[h] * (128 * j + p_ - (S - 1))).astype(np.float32)
    return out


def run_attn(layer, q_shards, k_batches, v_batches, x_shards, wo, bo, res):
    outs = [None] * NCORES
    trow = np.tile(np.arange(512, dtype=np.float32)[None, :], (128, 1))
    for cc in range(4):
        nc = build_attn(layer, cc)
        cores = [cc, cc + 4]
        in_maps = []
        for core in cores:
            b = core // 4
            m = dict(qtm=q_shards[core], kall=k_batches[b], vall=v_batches[b],
                     xfm=x_shards[core], wo=wo,
                     bo=np.ascontiguousarray(bo.astype(np.float32).reshape(8, 128).T))
            if layer == 0:
                m["dcolT"] = _host_dcol(cc)
                m["trowT"] = trow
            else:
                m["albT"] = _host_alb()
            in_maps.append(m)
        out = run_bass_kernel_spmd(nc, in_maps, core_ids=cores)
        res.append(out)
        for i, core in enumerate(cores):
            outs[core] = out.results[i]
    return outs



def emit_amat(S, AL, io):
    nc = S.nc
    op, wait, inc_last, C = S.op, S.wait, S.inc_last, S.C
    wr_sb = AL.sbuf("m_wr", [128, 8 * 1024], F32)
    wi_sb = AL.sbuf("m_wi", [128, 8 * 1024], F32)
    win_sb = AL.sbuf("m_win", [128, 8 * 1024], F32)
    Cm_sb = AL.sbuf("m_Cm", [128, 8 * 1024], F32)
    Sm_sb = AL.sbuf("m_Sm", [128, 8 * 1024], F32)
    Cs_sb = AL.sbuf("m_Cs", [128, 8 * 128], F32)
    Ss_sb = AL.sbuf("m_Ss", [128, 8 * 128], F32)
    PT_sb = AL.sbuf("m_PT", [128, 1024], F32)
    QT_sb = AL.sbuf("m_QT", [128, 1024], F32)
    Ar_sb = AL.sbuf("m_Ar", [128, 1024], F32)
    P = S.P
    for name, dst, w in [("wfr", wr_sb, 1024), ("wfi", wi_sb, 1024),
                         ("Cm", Cm_sb, 1024), ("Sm", Sm_sb, 1024),
                         ("Csl", Cs_sb, 128), ("Ssl", Ss_sb, 128)]:
        op("sync", lambda name=name, dst=dst: nc.sync.dma_start(
            out=dst.rearrange("p (k c) -> p k c", k=8),
            in_=io[name].rearrange("(k p) c -> p k c", p=128)))
        inc_last("sync", "ld", 16)
    lds = C["ld"].v
    wait("vector", "ld", lds)
    op("vector", lambda: nc.vector.tensor_scalar(
        win_sb, wi_sb, -1.0, None, ALU.mult, ALU.bypass))
    inc_last("vector", "cv", 1)
    cvs = C["cv"].v
    wait("tensor", "ld", lds)
    wait("tensor", "cv", cvs)
    ev_hist = []
    for kc in range(8):
        sl = kc % 2
        if kc >= 2:
            wait("tensor", "ev", ev_hist[kc - 2])
        for jc in range(8):
            for w_sb, c_sb in [(wr_sb, Cs_sb), (wi_sb, Ss_sb)]:
                def m1(kc=kc, jc=jc, w_sb=w_sb, c_sb=c_sb, sl=sl):
                    return nc.tensor.matmul(
                        P[:, 512 * sl:512 * sl + 128],
                        w_sb[:, 1024 * jc + 128 * kc:1024 * jc + 128 * (kc + 1)],
                        c_sb[:, 128 * jc:128 * (jc + 1)],
                        start=(jc == 0 and w_sb is wr_sb),
                        stop=(jc == 7 and w_sb is wi_sb))
                op("tensor", m1)
        for jc in range(8):
            for w_sb, c_sb in [(wr_sb, Ss_sb), (win_sb, Cs_sb)]:
                def m2(kc=kc, jc=jc, w_sb=w_sb, c_sb=c_sb, sl=sl):
                    return nc.tensor.matmul(
                        P[:, 1024 + 512 * sl:1024 + 512 * sl + 128],
                        w_sb[:, 1024 * jc + 128 * kc:1024 * jc + 128 * (kc + 1)],
                        c_sb[:, 128 * jc:128 * (jc + 1)],
                        start=(jc == 0 and w_sb is wr_sb),
                        stop=(jc == 7 and w_sb is win_sb))
                op("tensor", m2)
        inc_last("tensor", "mm", 1)
        wait("vector", "mm", C["mm"].v)
        op("vector", lambda kc=kc, sl=sl: nc.vector.tensor_copy(
            PT_sb[:, 128 * kc:128 * (kc + 1)],
            P[:, 512 * sl:512 * sl + 128]))
        op("vector", lambda kc=kc, sl=sl: nc.vector.tensor_copy(
            QT_sb[:, 128 * kc:128 * (kc + 1)],
            P[:, 1024 + 512 * sl:1024 + 512 * sl + 128]))
        ev_hist.append(inc_last("vector", "ev", 1))
    wait("tensor", "ev", C["ev"].v)
    for ob in range(2):
        for kc in range(8):
            for t_sb, c_sb in [(PT_sb, Cm_sb), (QT_sb, Sm_sb)]:
                def m3(ob=ob, kc=kc, t_sb=t_sb, c_sb=c_sb):
                    return nc.tensor.matmul(
                        P[:, 512 * ob:512 * (ob + 1)],
                        t_sb[:, 128 * kc:128 * (kc + 1)],
                        c_sb[:, 1024 * kc + 512 * ob:1024 * kc + 512 * (ob + 1)],
                        start=(kc == 0 and t_sb is PT_sb),
                        stop=(kc == 7 and t_sb is QT_sb))
                op("tensor", m3)
        inc_last("tensor", "mm", 1)
        wait("vector", "mm", C["mm"].v)
        op("vector", lambda ob=ob: nc.vector.tensor_copy(
            Ar_sb[:, 512 * ob:512 * (ob + 1)], P[:, 512 * ob:512 * (ob + 1)]))
        inc_last("vector", "ev", 1)
    op("gpsimd", lambda v=C["ev"].v: nc.gpsimd.wait_ge(S.sems["ev"], v))
    op("gpsimd", lambda: nc.gpsimd.dma_start(out=io["Ar"], in_=Ar_sb))
    inc_last("gpsimd", "st", 16)
    op("gpsimd", lambda v=C["st"].v: nc.gpsimd.wait_ge(S.sems["st"], v))


def build_amat():
    nc = bass.Bass("TRN2", target_bir_lowering=False)
    io = {}
    for n, sh in [("wfr", [D, D]), ("wfi", [D, D]), ("Cm", [D, D]),
                  ("Sm", [D, D]), ("Csl", [D, 128]), ("Ssl", [D, 128])]:
        io[n] = nc.dram_tensor(n, sh, F32, kind="ExternalInput").ap()
    io["Ar"] = nc.dram_tensor("Ar", [128, D], F32, kind="ExternalOutput").ap()
    with contextlib.ExitStack() as es:
        AL = Alloc(nc, es)
        S = Sched(nc, es, ["ld", "cv", "mm", "ev", "st"])
        S.P = es.enter_context(nc.psum_tensor("P", [128, 2048], F32)).ap()
        block = es.enter_context(nc.Block())
        emit_amat(S, AL, io)
        S.emit(block)
    return nc


def _dft_tables():
    kk = np.arange(D, dtype=np.float64)
    ang = 2.0 * np.pi * np.outer(kk, kk) / D
    ang = np.mod(np.outer(kk, kk), D) * (2.0 * np.pi / D)
    return np.cos(ang).astype(np.float32), np.sin(ang).astype(np.float32)


def run_amat(wfr, wfi, res):
    nc = build_amat()
    Cm, Sm = _dft_tables()
    in_maps = []
    for core in range(NCORES):
        sl = slice(128 * core, 128 * (core + 1))
        in_maps.append(dict(
            wfr=wfr, wfi=wfi, Cm=Cm, Sm=Sm,
            Csl=np.ascontiguousarray(Cm[:, sl] / D),
            Ssl=np.ascontiguousarray(Sm[:, sl] / D)))
    out = run_bass_kernel_spmd(nc, in_maps, core_ids=list(range(NCORES)))
    res.append(out)
    return np.concatenate([r["Ar"] for r in out.results], 0)


class Sched:

    def __init__(self, nc, es, semnames):
        self.nc = nc
        self.sems = {n: es.enter_context(nc.semaphore("s_" + n)) for n in semnames}
        self.C = {n: Ctr() for n in semnames}
        self.ops = {k: [] for k in ["sync", "gpsimd", "tensor", "vector", "scalar"]}

    def op(self, eng, f):
        self.ops[eng].append(f)

    def wait(self, eng, sem, val):
        if val > 0:
            nc = self.nc
            self.op(eng, lambda s=self.sems[sem], v=val:
                    getattr(nc, eng).wait_ge(s, v))

    def inc_last(self, eng, sem, n=1):
        prev = self.ops[eng][-1]

        def f(prev=prev, s=self.sems[sem], n=n):
            inst = prev()
            inst.then_inc(s, n)
            return inst
        self.ops[eng][-1] = f
        return self.C[sem].inc(n)

    def emit(self, block):
        ops = self.ops

        @block.sync
        def _(sync):
            for f in ops["sync"]:
                f()

        @block.gpsimd
        def _(gpsimd):
            for f in ops["gpsimd"]:
                f()

        @block.tensor
        def _(tensor):
            for f in ops["tensor"]:
                f()

        @block.scalar
        def _(scalar):
            for f in ops["scalar"]:
                f()

        @block.vector
        def _(vector):
            for f in ops["vector"]:
                f()


def _emit_resid_ln(S, nc, P, views, y_emit, xap, bias_sb, xout, c_list=(0, 1, 2, 3),
                   state=None, xpf_hook=None):
    py, pmean, pmsq, prb, pmr = views["py"], views["pmean"], views["pmsq"], \
        views["prb"], views["pmr"]
    s_cT, sbf, ssq, stat, x1stg = views["s_c"], views["sbf"], views["ssq"], \
        views["stat"], views["x1stg"]
    oneK, ones1 = views["oneK"], views["ones1"]
    op, wait, inc_last, C = S.op, S.wait, S.inc_last, S.C
    if state is None:
        state = {}
    stt_hist = state.setdefault("stt_hist", [])
    sm_hist = state.setdefault("sm_hist", [])
    x1s_hist = state.setdefault("x1s_hist", [])
    kos = state.setdefault("kos", Ctr())
    for c in c_list:
        for o in range(8):
            ko = kos.v
            kos.inc(1)
            ysl = ko % 2
            if ko >= 2:
                wait("tensor", "stt", stt_hist[ko - 2])
            y_emit(c, o, py[ysl])
            inc_last("tensor", "y", 1)
            wait("vector", "y", C["y"].v)

            def sf(o=o, c=c, ysl=ysl):
                return nc.vector.scalar_tensor_tensor(
                    s_cT[:, 512 * o:512 * (o + 1)], py[ysl][:, :],
                    bias_sb[:, o:o + 1], xap(c, o),
                    ALU.add, ALU.add)
            op("vector", sf)
            stt_hist.append(inc_last("vector", "stt", 1))
            if xpf_hook is not None and o + 2 < 8:
                xpf_hook(o + 2, c)
            if ko >= 2:
                wait("vector", "sm", sm_hist[ko - 2])

            def sbff(o=o, ysl=ysl):
                return nc.vector.tensor_copy(
                    sbf[:, 512 * ysl:512 * (ysl + 1)],
                    s_cT[:, 512 * o:512 * (o + 1)])
            op("vector", sbff)

            def ssqf(ysl=ysl):
                sl_ = slice(512 * ysl, 512 * (ysl + 1))
                return nc.vector.tensor_tensor(
                    ssq[:, sl_], sbf[:, sl_], sbf[:, sl_], ALU.mult)
            op("vector", ssqf)
            inc_last("vector", "sq", 1)
            wait("tensor", "sq", C["sq"].v)

            def smf(o=o, ysl=ysl):
                return nc.tensor.matmul(
                    pmean, oneK[:], sbf[:, 512 * ysl:512 * (ysl + 1)],
                    start=(o == 0), stop=(o == 7))
            op("tensor", smf)

            def sqmf(o=o, ysl=ysl):
                return nc.tensor.matmul(
                    pmsq, oneK[:], ssq[:, 512 * ysl:512 * (ysl + 1)],
                    start=(o == 0), stop=(o == 7))
            op("tensor", sqmf)
            sm_hist.append(inc_last("tensor", "sm", 1))
        wait("vector", "sm", C["sm"].v)
        op("vector", lambda: nc.vector.tensor_copy(stat[0:1, 0:512], pmean))
        op("vector", lambda: nc.vector.tensor_copy(stat[0:1, 512:1024], pmsq))
        op("vector", lambda: nc.vector.tensor_tensor(
            stat[0:1, 1024:1536], stat[0:1, 0:512], stat[0:1, 0:512], ALU.mult))
        op("vector", lambda: nc.vector.tensor_tensor(
            stat[0:1, 1536:2048], stat[0:1, 512:1024], stat[0:1, 1024:1536],
            ALU.subtract))
        op("vector", lambda: nc.vector.tensor_scalar(
            stat[0:1, 1536:2048], stat[0:1, 1536:2048], LN_EPS, None,
            ALU.add, ALU.bypass))
        inc_last("vector", "nr", 1)
        wait("scalar", "nr", C["nr"].v)
        op("scalar", lambda: nc.scalar.activation(
            stat[0:1, 1024:1536], stat[0:1, 1536:2048], AF.Ln))
        op("scalar", lambda: nc.scalar.activation(
            stat[0:1, 512:1024], stat[0:1, 1024:1536], AF.Exp, scale=-0.5))
        inc_last("scalar", "ex", 1)
        wait("vector", "ex", C["ex"].v)
        op("vector", lambda: nc.vector.scalar_tensor_tensor(
            stat[0:1, 1536:2048], stat[0:1, 0:512], -1.0,
            stat[0:1, 512:1024], ALU.mult, ALU.mult))
        rstd_bf = stat[0:1, 2048:2304].bitcast(BF16)
        mr_bf = stat[0:1, 2304:2560].bitcast(BF16)
        op("vector", lambda rb=rstd_bf: nc.vector.tensor_copy(
            rb, stat[0:1, 512:1024]))
        op("vector", lambda mb=mr_bf: nc.vector.tensor_copy(
            mb, stat[0:1, 1536:2048]))
        inc_last("vector", "nr", 1)
        wait("tensor", "nr", C["nr"].v)
        op("tensor", lambda rb=rstd_bf: nc.tensor.matmul(
            prb, ones1[:], rb, start=True, stop=True))
        op("tensor", lambda mb=mr_bf: nc.tensor.matmul(
            pmr, ones1[:], mb, start=True, stop=True))
        inc_last("tensor", "bc", 1)
        wait("vector", "bc", C["bc"].v)
        for o in range(8):
            ko = len(x1s_hist)
            osl = ko % 2
            if ko >= 2:
                wait("vector", "x1s", x1s_hist[ko - 2])

            def a1f(o=o, osl=osl):
                return nc.vector.tensor_tensor(
                    x1stg[:, 512 * osl:512 * (osl + 1)],
                    s_cT[:, 512 * o:512 * (o + 1)], prb[:, :], ALU.mult)
            op("vector", a1f)

            def a2f(osl=osl):
                return nc.vector.tensor_tensor(
                    x1stg[:, 512 * osl:512 * (osl + 1)],
                    x1stg[:, 512 * osl:512 * (osl + 1)], pmr[:, :], ALU.add)
            op("vector", a2f)
            inc_last("vector", "ap", 1)
            op("sync", lambda v=C["ap"].v: nc.sync.wait_ge(S.sems["ap"], v))

            def x1f(o=o, c=c, osl=osl):
                return nc.sync.dma_start(
                    out=xout[128 * o:128 * (o + 1), 512 * c:512 * (c + 1)],
                    in_=x1stg[:, 512 * osl:512 * (osl + 1)])
            op("sync", x1f)
            x1s_hist.append(inc_last("sync", "x1s", 16))
        wait("tensor", "ap", C["ap"].v)


def emit_ffn0(S, AL, io):
    nc = S.nc
    op, wait, inc_last, C = S.op, S.wait, S.inc_last, S.C
    x_sb = AL.sbuf("f0_x", [128, 8 * 2048], F32)
    xb_sb = AL.sbuf("f0_xb", [128, 8 * 2048], BF16)
    A_sb = AL.sbuf("f0_A", [128, 8 * 1024], BF16)
    fb_sb = AL.sbuf("f0_fb", [128, 8], F32)
    T = dict(ones1=AL.sbuf("f0_ones1", [1, 128], BF16),
             oneK=AL.sbuf("f0_oneK", [128, 1], BF16),
             s_c=AL.sbuf("f0_sc", [128, 8 * 512], F32),
             sbf=AL.sbuf("f0_sbf", [128, 2 * 512], BF16),
             ssq=AL.sbuf("f0_ssq", [128, 2 * 512], BF16),
             stat=AL.sbuf("f0_stat", [1, 2560], F32),
             x1stg=AL.sbuf("f0_x1stg", [128, 2 * 512], F32))
    P = S.P
    views = dict(py=[P[:, 0:512], P[:, 512:1024]],
                 pmean=P[0:1, 2048:2560], pmsq=P[0:1, 2560:3072],
                 prb=P[:, 3072:3584], pmr=P[:, 3584:4096],
                 s_c=T["s_c"], sbf=T["sbf"], ssq=T["ssq"], stat=T["stat"],
                 x1stg=T["x1stg"], oneK=T["oneK"], ones1=T["ones1"])
    op("sync", lambda: nc.sync.dma_start(
        out=x_sb.rearrange("p (k t) -> p k t", k=8),
        in_=io["xfm"].rearrange("(k p) t -> p k t", p=128)))
    inc_last("sync", "ld", 16)
    op("sync", lambda: nc.sync.dma_start(out=fb_sb, in_=io["fb"]))
    inc_last("sync", "ld", 16)
    op("gpsimd", lambda: nc.gpsimd.dma_start(
        out=xb_sb.rearrange("p (k t) -> p k t", k=8),
        in_=io["xfm"].rearrange("(k p) t -> p k t", p=128)))
    inc_last("gpsimd", "ld", 16)
    op("gpsimd", lambda: nc.gpsimd.dma_start(
        out=A_sb.rearrange("p (k c) -> p k c", k=8),
        in_=io["Am"].rearrange("(k p) c -> p k c", p=128)))
    inc_last("gpsimd", "ld", 16)
    op("vector", lambda: nc.vector.memset(T["ones1"], 1.0))
    op("vector", lambda: nc.vector.memset(T["oneK"], 1.0 / 1024.0))
    inc_last("vector", "cv", 1)
    lds, cvs = C["ld"].v, C["cv"].v

    def y_emit(c, o, pyv):
        if c == 0 and o == 0:
            wait("tensor", "ld", lds)
            wait("tensor", "cv", cvs)
        for k in range(8):
            def yf(c=c, o=o, k=k, pyv=pyv):
                return nc.tensor.matmul(
                    pyv, A_sb[:, 1024 * k + 128 * o:1024 * k + 128 * (o + 1)],
                    xb_sb[:, 2048 * k + 512 * c:2048 * k + 512 * (c + 1)],
                    start=(k == 0), stop=(k == 7))
            op("tensor", yf)

    _emit_resid_ln(S, nc, P, views, y_emit,
                   lambda c, o: x_sb[:, 2048 * o + 512 * c:2048 * o + 512 * (c + 1)],
                   fb_sb, io["x2"])
    op("sync", lambda v=C["x1s"].v: nc.sync.wait_ge(S.sems["x1s"], v))


FFN0_SEMS = ["ld", "cv", "y", "stt", "sq", "sm", "nr", "ex", "bc", "ap", "x1s"]


def build_ffn0():
    nc = bass.Bass("TRN2", target_bir_lowering=False)
    io = {}
    io["xfm"] = nc.dram_tensor("xfm", [D, TPC], F32, kind="ExternalInput").ap()
    io["Am"] = nc.dram_tensor("Am", [D, D], F32, kind="ExternalInput").ap()
    io["fb"] = nc.dram_tensor("fb", [128, 8], F32, kind="ExternalInput").ap()
    io["x2"] = nc.dram_tensor("x2", [D, TPC], F32, kind="ExternalOutput").ap()
    with contextlib.ExitStack() as es:
        AL = Alloc(nc, es)
        S = Sched(nc, es, FFN0_SEMS)
        S.P = es.enter_context(nc.psum_tensor("P", [128, 4096], F32)).ap()
        block = es.enter_context(nc.Block())
        emit_ffn0(S, AL, io)
        S.emit(block)
    return nc


def run_ffn0(x_shards, A, fbias, res):
    nc = build_ffn0()
    fb = np.ascontiguousarray(fbias.astype(np.float32).reshape(8, 128).T)
    in_maps = [dict(xfm=x_shards[c], Am=A, fb=fb) for c in range(NCORES)]
    out = run_bass_kernel_spmd(nc, in_maps, core_ids=list(range(NCORES)))
    res.append(out)
    return [r["x2"] for r in out.results]


def _emit_quant(S, nc, out_ap, in_ap, qs, W):
    op = S.op
    nb = W // 16
    t8, t4, t2 = qs["t8"], qs["t4"], qs["t2"]
    bm, sc, rc, scb = qs["bm"], qs["sc"], qs["rc"], qs["scb"]
    u, tb = qs["u"], qs["tb"]
    A, B4, B2 = qs["A"], qs["B4"], qs["B2"]
    w1 = t8.bitcast(BF16)
    w2 = tb.bitcast(BF16)
    M1, MH = MAGIC1, MAGICH
    I32 = mybir.dt.int32
    tbi = tb.bitcast(I32)
    ini = in_ap.bitcast(I32)
    op("vector", lambda: nc.vector.tensor_scalar(
        tbi[:, 0:W], ini, 0x7FFFFFFF, None, ALU.bitwise_and, ALU.bypass))
    op("vector", lambda: nc.vector.tensor_tensor(
        t8[:, 0:W // 2], tb[:, 0:W:2], tb[:, 1:W:2], ALU.max))
    op("vector", lambda: nc.vector.tensor_tensor(
        t4[:, 0:W // 4], t8[:, 0:W // 2:2], t8[:, 1:W // 2:2], ALU.max))
    op("vector", lambda: nc.vector.tensor_tensor(
        t2[:, 0:W // 8], t4[:, 0:W // 4:2], t4[:, 1:W // 4:2], ALU.max))

    def selfsem():
        v = S.inc_last("vector", "qz", 1)
        S.wait("vector", "qz", v)
    selfsem()
    op("vector", lambda: nc.vector.tensor_tensor(
        bm[:, 0:nb], t2[:, 0:W // 8:2], t2[:, 1:W // 8:2], ALU.max))
    selfsem()
    op("vector", lambda: nc.vector.tensor_scalar(
        sc[:, 0:nb], bm[:, 0:nb], 1.0 / 6.0, 1e-12, ALU.mult, ALU.max))
    selfsem()

    def _rc():
        with nc.allow_low_precision(reason="quant recip"):
            return nc.vector.reciprocal(rc[:, 0:nb], sc[:, 0:nb])
    op("vector", _rc)
    op("vector", lambda: nc.vector.tensor_copy(scb[:, 0:nb], sc[:, 0:nb]))
    selfsem()
    rcb = rc[:, 0:nb].unsqueeze(2).broadcast_to([128, nb, 16])
    scbb = scb[:, 0:nb].unsqueeze(2).broadcast_to([128, nb, 16])
    in3 = in_ap.rearrange("p (a b) -> p a b", b=16)
    u3 = u[:, 0:W].rearrange("p (a b) -> p a b", b=16)
    op("vector", lambda: nc.vector.tensor_tensor(u3, in3, rcb, ALU.mult))
    uf = u[:, 0:W]
    tbf = tb[:, 0:W]
    op("vector", lambda: nc.vector.tensor_scalar(
        tbf, uf, MH, MH - 2.0, ALU.add, ALU.max))
    op("vector", lambda: nc.vector.tensor_scalar(
        A[:, 0:W], tbf, MH, 2.0, ALU.subtract, ALU.min))
    op("vector", lambda: nc.vector.tensor_scalar(
        tbf, uf, M1, M1 - 4.0, ALU.add, ALU.max))
    op("vector", lambda: nc.vector.tensor_scalar(
        B4[:, 0:W], tbf, M1, 4.0, ALU.subtract, ALU.min))
    op("vector", lambda: nc.vector.tensor_scalar(
        tbf, tbf, M1 - 2.0, None, ALU.max, ALU.bypass))
    op("vector", lambda: nc.vector.tensor_scalar(
        B2[:, 0:W], tbf, M1, 2.0, ALU.subtract, ALU.min))
    op("vector", lambda: nc.vector.tensor_scalar(
        w1[:, 0:W], uf, 5.0, 2.0, ALU.is_gt, ALU.mult))
    op("vector", lambda: nc.vector.tensor_scalar(
        w2[:, 0:W], uf, -5.0, 2.0, ALU.is_lt, ALU.mult))
    op("vector", lambda: nc.vector.tensor_tensor(
        A[:, 0:W], A[:, 0:W], B4[:, 0:W], ALU.add))
    op("vector", lambda: nc.vector.tensor_tensor(
        B2[:, 0:W], B2[:, 0:W], w2[:, 0:W], ALU.add))
    op("vector", lambda: nc.vector.tensor_tensor(
        A[:, 0:W], A[:, 0:W], w1[:, 0:W], ALU.add))
    op("vector", lambda: nc.vector.tensor_tensor(
        A[:, 0:W], A[:, 0:W], B2[:, 0:W], ALU.subtract))
    A3 = A[:, 0:W].rearrange("p (a b) -> p a b", b=16)
    o3 = out_ap.rearrange("p (a b) -> p a b", b=16)
    op("vector", lambda: nc.vector.tensor_tensor(o3, A3, scbb, ALU.mult))


def _mk_qscratch_al(AL, Wmax=1024):
    return dict(
        t8=AL.sbuf("q_t8", [128, Wmax // 2], F32),
        t4=AL.sbuf("q_t4", [128, Wmax // 4], F32),
        t2=AL.sbuf("q_t2", [128, Wmax // 8], F32),
        bm=AL.sbuf("q_bm", [128, Wmax // 16], F32),
        sc=AL.sbuf("q_sc", [128, Wmax // 16], F32),
        rc=AL.sbuf("q_rc", [128, Wmax // 16], F32),
        scb=AL.sbuf("q_scb", [128, Wmax // 16], BF16),
        u=AL.sbuf("q_u", [128, Wmax], F32),
        tb=AL.sbuf("q_tb", [128, Wmax], F32),
        A=AL.sbuf("q_A", [128, Wmax], BF16),
        B4=AL.sbuf("q_B4", [128, Wmax], BF16),
        B2=AL.sbuf("q_B2", [128, Wmax], BF16),
    )


def _mk_qscratch(nc, es, Wmax=1024):
    E = es.enter_context
    return dict(
        t8=E(nc.sbuf_tensor("q_t8", [128, Wmax // 2], F32)),
        t4=E(nc.sbuf_tensor("q_t4", [128, Wmax // 4], F32)),
        t2=E(nc.sbuf_tensor("q_t2", [128, Wmax // 8], F32)),
        bm=E(nc.sbuf_tensor("q_bm", [128, Wmax // 16], F32)),
        sc=E(nc.sbuf_tensor("q_sc", [128, Wmax // 16], F32)),
        rc=E(nc.sbuf_tensor("q_rc", [128, Wmax // 16], F32)),
        scb=E(nc.sbuf_tensor("q_scb", [128, Wmax // 16], BF16)),
        u=E(nc.sbuf_tensor("q_u", [128, Wmax], F32)),
        tb=E(nc.sbuf_tensor("q_tb", [128, Wmax], F32)),
        A=E(nc.sbuf_tensor("q_A", [128, Wmax], BF16)),
        B4=E(nc.sbuf_tensor("q_B4", [128, Wmax], BF16)),
        B2=E(nc.sbuf_tensor("q_B2", [128, Wmax], BF16)),
    )


FFN1_SEMS = ["ld", "cv", "wq", "wo", "xq", "xb", "gu", "hq", "hb",
             "y", "stt", "sq", "sm", "nr", "ex", "bc", "ap",
             "x1s", "xs", "ws", "ev", "qz"]


def emit_ffn1(S, AL, io, debug=False):
    II = 2816
    nc = S.nc
    xfm, wg, wu, wd = io["xfm"], io["wg"], io["wu"], io["wd"]
    ident = io["ident"]
    x4 = io["x4"]
    wgq, wuq, wdq = io["wgq"], io["wuq"], io["wdq"]
    xqd, hqd = io["xqd"], io["hqd"]
    if debug:
        dbg_g, dbg_h = io["dbg_g"], io["dbg_h"]
    if True:
        E = None
        hqfm = AL.sbuf("hqfm", [128, 22 * 1024], BF16)
        wdT = AL.sbuf("wdT", [128, 22 * 1024], BF16)
        xqfm = AL.sbuf("xqfm", [128, 8 * 1024], BF16)
        wgt = AL.sbuf("wgt", [128, 2 * 8 * 512], BF16)
        wut = AL.sbuf("wut", [128, 2 * 8 * 512], BF16)
        xstg = AL.sbuf("xstg", [128, 2 * 128], F32)
        xtm = AL.sbuf("xtm", [128, 1024], F32)
        xqtm = AL.sbuf("xqtm", [128, 1024], BF16)
        sg = AL.sbuf("sg", [128, 512], F32)
        hf32 = AL.sbuf("hf32", [128, 512], F32)
        hqt = AL.sbuf("hqt", [128, 2 * 512], BF16)
        fb_sb = AL.sbuf("fb_sb", [128, 8], F32)
        ones1 = AL.sbuf("ones1", [1, 128], BF16)
        oneK = AL.sbuf("oneK", [128, 1], BF16)
        s_cT = AL.sbuf("s_cT", [128, 8 * 512], F32)
        sbf = AL.sbuf("sbf", [128, 2 * 512], BF16)
        ssq = AL.sbuf("ssq", [128, 2 * 512], BF16)
        stat = AL.sbuf("stat", [1, 2560], F32)
        x1stg = AL.sbuf("x1stg", [128, 2 * 512], F32)
        xrs = AL.sbuf("xrs", [128, 2 * 512], F32)
        ident_sb = AL.sbuf("ident_sb", [128, 128], F32)
        qs = _mk_qscratch_al(AL)
        P = S.P
        wstg = [hqfm[:, 0:5632].bitcast(F32),
                hqfm[:, 5632:11264].bitcast(F32)]
        wqs = [hqfm[:, 11264:14080], hqfm[:, 14080:16896]]
        op, wait, inc_last, C = S.op, S.wait, S.inc_last, S.C

        op("sync", lambda: nc.sync.dma_start(out=ident_sb[:], in_=ident[:]))
        inc_last("sync", "ld", 16)
        op("vector", lambda: nc.vector.memset(fb_sb[:], 0.0))
        op("vector", lambda: nc.vector.memset(ones1[:], 1.0))
        op("vector", lambda: nc.vector.memset(oneK[:], 1.0 / 1024.0))
        inc_last("vector", "nr", 1)

        wjobs = []
        for mi, (wsrc, wdst, nch, wid) in enumerate(
                [(wg, wgq, 22, D), (wu, wuq, 22, D), (wd, wdq, 8, II)]):
            for ch in range(nch):
                wjobs.append((wsrc, wdst, ch, wid))
        for ji, (wsrc, wdst, ch, wid) in enumerate(wjobs):
            sl = ji % 2
            if ji >= 2:
                op("sync", lambda v=C["wq"].v - 1: nc.sync.wait_ge(S.sems["wq"], v))

            def wldf(wsrc=wsrc, ch=ch, wid=wid, sl=sl):
                return nc.sync.dma_start(
                    out=wstg[sl][:, 0:wid], in_=wsrc[128 * ch:128 * (ch + 1), 0:wid])
            op("sync", wldf)
            inc_last("sync", "ws", 16)
            wait("vector", "ws", C["ws"].v)
            if ji >= 2:
                wait("vector", "wo", C["wo"].v - 16)
            for off in range(0, wid, 1024):
                w_ = min(1024, wid - off)
                _emit_quant(S, nc, wqs[sl][:, off:off + w_],
                            wstg[sl][:, off:off + w_], qs, w_)
            inc_last("vector", "wq", 1)
            op("sync", lambda v=C["wq"].v: nc.sync.wait_ge(S.sems["wq"], v))

            def wstf(wdst=wdst, ch=ch, wid=wid, sl=sl):
                return nc.sync.dma_start(
                    out=wdst[128 * ch:128 * (ch + 1), 0:wid], in_=wqs[sl][:, 0:wid])
            op("sync", wstf)
            inc_last("sync", "wo", 16)
        wqdone = C["wo"].v

        op("sync", lambda: nc.sync.wait_ge(S.sems["wo"], wqdone))
        for ic in range(22):
            def wdtf(ic=ic):
                return nc.sync.dma_start_transpose(
                    wdT[:, 1024 * ic:1024 * (ic + 1)], wdq[:, 128 * ic:128 * (ic + 1)])
            op("sync", wdtf)
            inc_last("sync", "ld", 16)
        lds = C["ld"].v

        views = dict(py=[P[:, 0:512], P[:, 512:1024]],
                     pmean=P[0:1, 2048:2560], pmsq=P[0:1, 2560:3072],
                     prb=P[:, 3072:3584], pmr=P[:, 3584:4096],
                     s_c=s_cT, sbf=sbf, ssq=ssq, stat=stat, x1stg=x1stg,
                     oneK=oneK, ones1=ones1)
        ln_state = {}
        for hf in range(2):
            toff = 1024 * hf
            for tt in range(8):
                ps = P[:, 1024 * (tt % 2):1024 * (tt % 2) + 1024]
                for o in range(8):
                    xsl = (8 * tt + o) % 2
                    if 8 * tt + o >= 2:
                        op("sync", lambda v=C["xq"].v: nc.sync.wait_ge(S.sems["xq"], v))

                    def xldf(o=o, tt=tt, xsl=xsl, toff=toff):
                        return nc.sync.dma_start(
                            out=xstg[:, 128 * xsl:128 * (xsl + 1)],
                            in_=xfm[128 * o:128 * (o + 1),
                                    toff + 128 * tt:toff + 128 * (tt + 1)])
                    op("sync", xldf)
                    inc_last("sync", "xs", 16)
                    wait("tensor", "xs", C["xs"].v)
                    if tt >= 2 and o == 0:
                        wait("tensor", "xq", C["xq"].v)

                    def xtf(o=o, ps=ps, xsl=xsl):
                        return nc.tensor.transpose(
                            ps[:, 128 * o:128 * (o + 1)],
                            xstg[:, 128 * xsl:128 * (xsl + 1)], ident_sb[:])
                    op("tensor", xtf)
                    inc_last("tensor", "xq", 1)
                wait("vector", "xq", C["xq"].v)
                op("vector", lambda ps=ps: nc.vector.tensor_copy(xtm[:], ps))
                inc_last("vector", "xq", 1)
                _emit_quant(S, nc, xqtm[:], xtm[:], qs, 1024)
                inc_last("vector", "xb", 1)
                op("sync", lambda v=C["xb"].v: nc.sync.wait_ge(S.sems["xb"], v))

                def xqstf(tt=tt, toff=toff):
                    return nc.sync.dma_start(
                        out=xqd[toff + 128 * tt:toff + 128 * (tt + 1), :],
                        in_=xqtm[:])
                op("sync", xqstf)
                inc_last("sync", "xq", 16)
                wait("vector", "xq", C["xq"].v)
            op("sync", lambda v=C["xq"].v: nc.sync.wait_ge(S.sems["xq"], v))
            for hc in range(8):
                def xqtf(hc=hc, toff=toff):
                    return nc.sync.dma_start_transpose(
                        xqfm[:, 1024 * hc:1024 * (hc + 1)],
                        xqd[toff:toff + 1024, 128 * hc:128 * (hc + 1)])
                op("sync", xqtf)
                inc_last("sync", "xb", 16)
            xqfmdone = C["xb"].v

            nib = 6
            for ib in range(nib):
                wib = 512 if ib < 5 else 256
                isl = ib % 2
                if ib < 2:
                    op("sync", lambda: nc.sync.wait_ge(S.sems["wo"], 44 * 16))
                else:
                    op("sync", lambda v=C["gu"].v: nc.sync.wait_ge(S.sems["gu"], v))
                for hc in range(8):
                    def wgtf(ib=ib, hc=hc, isl=isl, wib=wib):
                        return nc.sync.dma_start_transpose(
                            wgt[:, 4096 * isl + 512 * hc:4096 * isl + 512 * hc + wib],
                            wgq[512 * ib:512 * ib + wib, 128 * hc:128 * (hc + 1)])
                    op("sync", wgtf)
                    inc_last("sync", "ld", 16)

                    def wutf(ib=ib, hc=hc, isl=isl, wib=wib):
                        return nc.sync.dma_start_transpose(
                            wut[:, 4096 * isl + 512 * hc:4096 * isl + 512 * hc + wib],
                            wuq[512 * ib:512 * ib + wib, 128 * hc:128 * (hc + 1)])
                    op("sync", wutf)
                    inc_last("sync", "ld", 16)
                wld = C["ld"].v
                for tt in range(8):
                    gsl = tt % 2
                    pg = P[:, 512 * gsl:512 * (gsl + 1)]
                    pu = P[:, 1024 + 512 * gsl:1024 + 512 * (gsl + 1)]
                    if tt == 0:
                        wait("tensor", "ld", wld)
                        wait("tensor", "xb", xqfmdone)
                    if 8 * ib + tt >= 2:
                        wait("tensor", "hq", C["hq"].v - 1)
                    for hc in range(8):
                        def gmf(hc=hc, tt=tt, isl=isl, wib=wib, pg=pg):
                            return nc.tensor.matmul(
                                pg[:, 0:wib],
                                xqfm[:, 1024 * hc + 128 * tt:1024 * hc + 128 * (tt + 1)],
                                wgt[:, 4096 * isl + 512 * hc:4096 * isl + 512 * hc + wib],
                                start=(hc == 0), stop=(hc == 7))
                        op("tensor", gmf)
                    for hc in range(8):
                        def umf(hc=hc, tt=tt, isl=isl, wib=wib, pu=pu):
                            return nc.tensor.matmul(
                                pu[:, 0:wib],
                                xqfm[:, 1024 * hc + 128 * tt:1024 * hc + 128 * (tt + 1)],
                                wut[:, 4096 * isl + 512 * hc:4096 * isl + 512 * hc + wib],
                                start=(hc == 0), stop=(hc == 7))
                        op("tensor", umf)
                    inc_last("tensor", "gu", 1)
                    wait("scalar", "gu", C["gu"].v)

                    def silf(pg=pg, wib=wib):
                        return nc.scalar.activation(sg[:, 0:wib], pg[:, 0:wib],
                                                    AF.Silu)
                    op("scalar", silf)
                    inc_last("scalar", "gu", 1)
                    wait("vector", "gu", C["gu"].v)

                    def hmf(pu=pu, wib=wib):
                        return nc.vector.tensor_tensor(
                            hf32[:, 0:wib], sg[:, 0:wib], pu[:, 0:wib], ALU.mult)
                    op("vector", hmf)
                    inc_last("vector", "gu", 1)
                    if debug and hf == 1 and ib == 0 and tt == 0:
                        op("vector", lambda pg=pg: nc.vector.tensor_copy(
                            s_cT[:, 0:512], pg[:, 0:512]))
                        op("vector", lambda: nc.vector.tensor_copy(
                            s_cT[:, 512:1024], hf32[:, 0:512]))
                        inc_last("vector", "ev", 1)
                        op("sync", lambda: nc.sync.wait_ge(S.sems["ev"], 1))
                        op("sync", lambda: nc.sync.dma_start(
                            out=dbg_g, in_=s_cT[:, 0:512]))
                        inc_last("sync", "ld", 16)
                        op("sync", lambda: nc.sync.dma_start(
                            out=dbg_h, in_=s_cT[:, 512:1024]))
                        inc_last("sync", "ld", 16)
                    hsl = (8 * ib + tt) % 2
                    if 8 * ib + tt >= 2:
                        wait("vector", "hb", C["hb"].v - 16)
                    _emit_quant(S, nc, hqt[:, 512 * hsl:512 * hsl + wib],
                                hf32[:, 0:wib], qs, wib)
                    inc_last("vector", "hq", 1)
                    op("sync", lambda v=C["hq"].v: nc.sync.wait_ge(S.sems["hq"], v))

                    def hstf(ib=ib, tt=tt, hsl=hsl, wib=wib):
                        return nc.sync.dma_start(
                            out=hqd[128 * tt:128 * (tt + 1), 512 * ib:512 * ib + wib],
                            in_=hqt[:, 512 * hsl:512 * hsl + wib])
                    op("sync", hstf)
                    inc_last("sync", "hb", 16)
            op("sync", lambda v=C["hb"].v: nc.sync.wait_ge(S.sems["hb"], v))
            if hf == 1:
                pass
            for ic in range(22):
                def hqtf(ic=ic):
                    return nc.sync.dma_start_transpose(
                        hqfm[:, 1024 * ic:1024 * (ic + 1)],
                        hqd[:, 128 * ic:128 * (ic + 1)])
                op("sync", hqtf)
                inc_last("sync", "hb", 16)
            hqdone = C["hb"].v

            xst_hist = ln_state.setdefault("xst_hist", [])

            def xap(c, o):
                return xrs[:, 512 * ((8 * c + o) % 2):512 * ((8 * c + o) % 2 + 1)]

            def xpf_emit(c, o):
                ko_ = len(xst_hist)
                if ko_ >= 2:
                    st = ln_state["stt_hist"]
                    if ko_ - 2 < len(st):
                        op("sync", lambda v=st[ko_ - 2]:
                           nc.sync.wait_ge(S.sems["stt"], v))

                def xpf(o=o, c=c):
                    return nc.sync.dma_start(
                        out=xrs[:, 512 * ((8 * c + o) % 2):512 * ((8 * c + o) % 2 + 1)],
                        in_=xfm[128 * o:128 * (o + 1), 512 * c:512 * (c + 1)])
                op("sync", xpf)
                xst_hist.append(inc_last("sync", "xs", 16))

            def y_emit(c, o, pyv, hqdone=hqdone):
                if o == 0 and c % 2 == 0:
                    wait("tensor", "hb", hqdone)
                cc_ = c % 2
                for ic in range(22):
                    def yf(c=c, o=o, ic=ic, pyv=pyv, cc_=cc_):
                        return nc.tensor.matmul(
                            pyv, wdT[:, 1024 * ic + 128 * o:1024 * ic + 128 * (o + 1)],
                            hqfm[:, 1024 * ic + 512 * cc_:1024 * ic + 512 * (cc_ + 1)],
                            start=(ic == 0), stop=(ic == 21))
                    op("tensor", yf)
                xpf_emit(c, o)
                wait("vector", "xs", xst_hist[-1])

            _emit_resid_ln(S, nc, P, views, y_emit, xap, fb_sb, x4,
                           c_list=[2 * hf, 2 * hf + 1], state=ln_state)
            wait("sync", "sm", C["sm"].v)
        op("sync", lambda v=C["x1s"].v: nc.sync.wait_ge(S.sems["x1s"], v))




def build_ffn1(debug=False):
    II = 2816
    nc = bass.Bass("TRN2", target_bir_lowering=False)
    io = {}
    io["xfm"] = nc.dram_tensor("xfm", [D, TPC], F32, kind="ExternalInput").ap()
    io["wg"] = nc.dram_tensor("wg", [II, D], F32, kind="ExternalInput").ap()
    io["wu"] = nc.dram_tensor("wu", [II, D], F32, kind="ExternalInput").ap()
    io["wd"] = nc.dram_tensor("wd", [D, II], F32, kind="ExternalInput").ap()
    io["ident"] = nc.dram_tensor("ident", [128, 128], F32, kind="ExternalInput").ap()
    io["x4"] = nc.dram_tensor("x4", [D, TPC], F32, kind="ExternalOutput").ap()
    kind2 = "ExternalOutput" if debug else "Internal"
    io["wgq"] = nc.dram_tensor("wgq", [II, D], BF16, kind=kind2).ap()
    io["wuq"] = nc.dram_tensor("wuq", [II, D], BF16, kind="Internal").ap()
    io["wdq"] = nc.dram_tensor("wdq", [D, II], BF16, kind="Internal").ap()
    io["xqd"] = nc.dram_tensor("xqd", [TPC, D], BF16, kind=kind2).ap()
    io["hqd"] = nc.dram_tensor("hqd", [1024, II], BF16, kind=kind2).ap()
    if debug:
        io["dbg_g"] = nc.dram_tensor("dbg_g", [128, 512], F32, kind="ExternalOutput").ap()
        io["dbg_h"] = nc.dram_tensor("dbg_h", [128, 512], F32, kind="ExternalOutput").ap()
    with contextlib.ExitStack() as es:
        AL = Alloc(nc, es)
        S = Sched(nc, es, FFN1_SEMS)
        S.P = es.enter_context(nc.psum_tensor("P", [128, 4096], F32)).ap()
        block = es.enter_context(nc.Block())
        emit_ffn1(S, AL, io, debug)
        S.emit(block)
    return nc


def run_ffn1(x_shards, wg, wu, wd, res, debug=False):
    nc = build_ffn1(debug)
    ident = np.eye(128, dtype=np.float32)
    in_maps = [dict(xfm=x_shards[c], wg=wg, wu=wu, wd=wd, ident=ident)
               for c in range(NCORES)]
    out = run_bass_kernel_spmd(nc, in_maps, core_ids=list(range(NCORES)))
    res.append(out)
    if debug:
        return [dict(r) for r in out.results]
    return [r["x4"] for r in out.results]



def _gather_seq(shards, b):
    out = np.zeros((S, shards[0].shape[1]), dtype=bfloat16)
    for core in range(4 * b, 4 * b + 4):
        out[_local_positions(core)] = np.asarray(shards[core])
    return out


def kernel_unfused(L0_attn_x, L0_attn_Wq, L0_attn_bq, L0_attn_Wk, L0_attn_bk,
           L0_attn_Wv, L0_attn_bv, L0_attn_Wo, L0_attn_bo,
           L0_ffn_w_freq_real, L0_ffn_w_freq_imag, L0_ffn_bias,
           L1_attn_Wq, L1_attn_Wk, L1_attn_Wv, L1_attn_Wo,
           L1_ffn_w_gate, L1_ffn_w_up, L1_ffn_w_down):
    res = []
    x_shards = [_shard_x_fm(np.asarray(L0_attn_x), c) for c in range(NCORES)]

    def qperm_w(w):
        return np.ascontiguousarray(
            np.asarray(w).reshape(-1, 16, 64)[:, QPERM].reshape(w.shape))

    q, k, v = run_qkv(0, x_shards, dict(
        wq=qperm_w(L0_attn_Wq), wk=np.asarray(L0_attn_Wk),
        wv=np.asarray(L0_attn_Wv),
        bq=np.asarray(L0_attn_bq).reshape(16, 64)[QPERM].reshape(1024),
        bk=np.asarray(L0_attn_bk), bv=np.asarray(L0_attn_bv)), res)
    k_b = [_gather_seq(k, b) for b in range(B)]
    v_b = [_gather_seq(v, b) for b in range(B)]
    x1r = run_attn(0, q, k_b, v_b, x_shards, np.asarray(L0_attn_Wo),
                   np.asarray(L0_attn_bo), res)
    x1 = [np.asarray(r["x1"]) for r in x1r]
    A = run_amat(np.asarray(L0_ffn_w_freq_real),
                 np.asarray(L0_ffn_w_freq_imag), res)
    x2 = run_ffn0(x1, A, np.asarray(L0_ffn_bias), res)
    x2 = [np.asarray(t) for t in x2]
    q, k, v = run_qkv(1, x2, dict(
        wq=qperm_w(L1_attn_Wq), wk=np.asarray(L1_attn_Wk),
        wv=np.asarray(L1_attn_Wv)), res)
    k_b = [_gather_seq(k, b) for b in range(B)]
    v_b = [_gather_seq(v, b) for b in range(B)]
    zero_bo = np.zeros(1024, np.float32)
    x3r = run_attn(1, q, k_b, v_b, x2, np.asarray(L1_attn_Wo), zero_bo, res)
    x3 = [np.asarray(r["x1"]) for r in x3r]
    x4 = run_ffn1(x3, np.asarray(L1_ffn_w_gate), np.asarray(L1_ffn_w_up),
                  np.asarray(L1_ffn_w_down), res)
    out = np.zeros((B, S, D), np.float32)
    for core in range(NCORES):
        out[core // 4][_local_positions(core)] = np.asarray(x4[core]).T
    return out



FUSED_SEMS = sorted(set(ATTN_SEMS + FFN0_SEMS + FFN1_SEMS +
                        ["mm", "rp", "st", "q", "cc", "ph", "cid"]))


class BodySched:

    def __init__(self, parent):
        self.nc = parent.nc
        self.sems = parent.sems
        self.C = parent.C
        self.P = parent.P
        self.ops = {k: [] for k in parent.ops}
    op = Sched.op
    wait = Sched.wait
    inc_last = Sched.inc_last


def _barrier(S):
    nc = S.nc
    for eng in ["sync", "gpsimd", "tensor", "scalar", "vector"]:
        S.op(eng, lambda eng=eng: getattr(nc, eng).nop())
        S.inc_last(eng, "ph", 1)
    v = S.C["ph"].v
    for eng in ["sync", "gpsimd", "tensor", "scalar", "vector"]:
        S.wait(eng, "ph", v)


def build_fused(debug=False):
    II = 2816
    nc = bass.Bass("TRN2", target_bir_lowering=False)
    io = {}

    def inp(n, sh, dt=F32):
        io[n] = nc.dram_tensor(n, sh, dt, kind="ExternalInput").ap()

    def intl(n, sh, dt):
        kind = "ExternalOutput" if (debug and n in
                                    ("x1", "x2", "x3", "qtm0", "qtm1")) else "Internal"
        io[n] = nc.dram_tensor(n, sh, dt, kind=kind).ap()

    inp("xfm", [D, TPC])
    inp("cid", [1, 1], mybir.dt.int32)
    inp("cosT", [TPC, 32]); inp("sinT", [TPC, 32])
    inp("wq0", [D, 1024]); inp("wk0", [D, 256]); inp("wv0", [D, 256])
    inp("bqkv0", [1, 1536])
    inp("wo0", [1024, D]); inp("bo0", [128, 8])
    inp("dcolT", [128, 16]); inp("trowT", [128, 512])
    inp("wfr", [D, D]); inp("wfi", [D, D]); inp("Cm", [D, D]); inp("Sm", [D, D])
    inp("Csl", [D, 128]); inp("Ssl", [D, 128]); inp("fb0", [128, 8])
    inp("wq1", [D, 1024]); inp("wk1", [D, 256]); inp("wv1", [D, 256])
    inp("wo1", [1024, D]); inp("bz", [128, 8])
    inp("albT", [128, 64 * 16])
    inp("wg", [II, D]); inp("wu", [II, D]); inp("wd", [D, II])
    inp("ident", [128, 128])
    io["out"] = nc.dram_tensor("out", [D, TPC], F32, kind="ExternalOutput").ap()
    for L in (0, 1):
        intl(f"qtm{L}", [TPC, 1024], BF16)
        intl(f"kvc{L}", [TPC, 512], BF16)
        intl(f"kvg{L}", [S_, 512], BF16)
        intl(f"dscr{L}", [2 * 64 * 512], BF16)
        intl(f"rscr{L}", [64 * 512], BF16)
    intl("x1", [D, TPC], F32)
    intl("x2", [D, TPC], F32)
    intl("x3", [D, TPC], F32)
    intl("Arc", [128, D], F32)
    intl("Ag", [D, D], F32)
    intl("wgq", [II, D], BF16)
    intl("wuq", [II, D], BF16)
    intl("wdq", [D, II], BF16)
    intl("xqd", [TPC, D], BF16)
    intl("hqd", [1024, II], BF16)

    with contextlib.ExitStack() as es:
        arena = es.enter_context(nc.sbuf_tensor("arena", [128, 53000], F32)).ap()
        cid_sb = es.enter_context(nc.sbuf_tensor("cid_sb", [1, 1], mybir.dt.int32)).ap()
        AL = Alloc(nc, es, arena=arena)
        S = Sched(nc, es, FUSED_SEMS)
        S.P = es.enter_context(nc.psum_tensor("P", [128, 4096], F32)).ap()
        C = S.C
        op, inc_last = S.op, S.inc_last

        op("sync", lambda: nc.sync.dma_start(out=cid_sb, in_=io["cid"]))
        inc_last("sync", "cid", 16)

        emit_qkv(S, AL, dict(xfm=io["xfm"], wq=io["wq0"], wk=io["wk0"],
                             wv=io["wv0"], bqkv=io["bqkv0"], cosT=io["cosT"],
                             sinT=io["sinT"], qtm=io["qtm0"],
                             ktm=io["kvc0"][:, 0:256], vtm=io["kvc0"][:, 256:512]),
                 True)
        op("gpsimd", lambda v=C["st"].v: nc.gpsimd.wait_ge(S.sems["st"], v))
        op("gpsimd", lambda: nc.gpsimd.collective_compute(
            "AllGather", ALU.bypass, replica_groups=[[0, 1, 2, 3], [4, 5, 6, 7]],
            ins=[io["kvc0"].opt()], outs=[io["kvg0"].opt()]))
        inc_last("gpsimd", "cc", 1)
        _barrier(S)

        AL.reset()
        emit_amat(S, AL, dict(wfr=io["wfr"], wfi=io["wfi"], Cm=io["Cm"],
                              Sm=io["Sm"], Csl=io["Csl"], Ssl=io["Ssl"],
                              Ar=io["Arc"]))
        op("gpsimd", lambda: nc.gpsimd.collective_compute(
            "AllGather", ALU.bypass, replica_groups=[[0, 1, 2, 3, 4, 5, 6, 7]],
            ins=[io["Arc"].opt()], outs=[io["Ag"].opt()]))
        inc_last("gpsimd", "cc", 1)
        _barrier(S)

        def attn_phase(layer, qtm, kvg, xin, wo, bo, masks, xout, dscr, rscr):
            AL.reset()
            T = attn_tensors(AL, layer)
            iox = dict(qtm=qtm, kall=kvg[:, 0:256], vall=kvg[:, 256:512],
                       xfm=xin, wo=wo, bo=bo, x1=xout, dscr=dscr, rscr=rscr,
                       **masks)
            op("sync", lambda v=C["cc"].v: nc.sync.wait_ge(S.sems["cc"], v))
            emit_attn_pre(S, T, iox, layer, rank_major=True)
            if layer == 1:
                emit_attn_core(S, S, T, iox, 1, 0, emit_sync=True)
            else:
                marks = {k: len(S.ops[k]) for k in S.ops}
                snap = {k: C[k].v for k in C}
                bodies = []
                finals = []
                for ccv in range(4):
                    for k in C:
                        C[k].v = snap[k]
                    Sb = BodySched(S)
                    emit_attn_core(S, Sb, T, iox, 0, ccv,
                                   emit_sync=(ccv == 0))
                    bodies.append(Sb)
                    finals.append({k: C[k].v for k in C})
                assert all(f == finals[0] for f in finals), \
                    f"switch bodies diverge: {[{k: v for k, v in f.items() if f != finals[0]} for f in finals]}"
                S.switch_at = (marks, bodies)
            emit_attn_post(S, T, iox, layer)

        attn_phase(0, io["qtm0"], io["kvg0"], io["xfm"], io["wo0"], io["bo0"],
                   dict(dcolT=io["dcolT"], trowT=io["trowT"]), io["x1"],
                   io["dscr0"], io["rscr0"])
        _barrier(S)

        AL.reset()
        emit_ffn0(S, AL, dict(xfm=io["x1"], Am=io["Ag"], fb=io["fb0"],
                              x2=io["x2"]))
        _barrier(S)

        AL.reset()
        emit_qkv(S, AL, dict(xfm=io["x2"], wq=io["wq1"], wk=io["wk1"],
                             wv=io["wv1"], cosT=io["cosT"], sinT=io["sinT"],
                             qtm=io["qtm1"], ktm=io["kvc1"][:, 0:256],
                             vtm=io["kvc1"][:, 256:512]), False)
        op("gpsimd", lambda v=C["st"].v: nc.gpsimd.wait_ge(S.sems["st"], v))
        op("gpsimd", lambda: nc.gpsimd.collective_compute(
            "AllGather", ALU.bypass, replica_groups=[[0, 1, 2, 3], [4, 5, 6, 7]],
            ins=[io["kvc1"].opt()], outs=[io["kvg1"].opt()]))
        inc_last("gpsimd", "cc", 1)
        _barrier(S)

        attn_phase(1, io["qtm1"], io["kvg1"], io["x2"], io["wo1"], io["bz"],
                   dict(albT=io["albT"]), io["x3"], io["dscr1"], io["rscr1"])
        _barrier(S)

        AL.reset()
        emit_ffn1(S, AL, dict(xfm=io["x3"], wg=io["wg"], wu=io["wu"],
                              wd=io["wd"], ident=io["ident"], x4=io["out"],
                              wgq=io["wgq"], wuq=io["wuq"], wdq=io["wdq"],
                              xqd=io["xqd"], hqd=io["hqd"]), False)

        block = es.enter_context(nc.Block())
        marks, bodies = S.switch_at
        order = ["sync", "gpsimd", "tensor", "scalar", "vector"]

        @block.sync
        def _(sync):
            for f in S.ops["sync"]:
                f()

        @block.gpsimd
        def _(gpsimd):
            for f in S.ops["gpsimd"]:
                f()

        for eng in ["tensor", "scalar", "vector"]:
            for f in S.ops[eng][:marks[eng]]:
                f()
        for e in [nc.tensor, nc.scalar, nc.vector]:
            e.wait_ge(S.sems["cid"], 16)
        v = nc.values_load(cid_sb, engines=[mybir.EngineType.PE,
                                            mybir.EngineType.Activation,
                                            mybir.EngineType.DVE],
                           skip_runtime_bounds_check=True)
        for ccv in nc.Switch(engines=[nc.tensor, nc.scalar, nc.vector],
                             index=[v, v, v], n=4):
            for eng in ["tensor", "scalar", "vector"]:
                for f in bodies[ccv].ops[eng]:
                    f()
        for eng in ["tensor", "scalar", "vector"]:
            for f in S.ops[eng][marks[eng]:]:
                f()
        print("fused arena hwm:", AL.hwm, "f32/partition")
    return nc


_FUSED_CACHE = {}


def kernel(L0_attn_x, L0_attn_Wq, L0_attn_bq, L0_attn_Wk, L0_attn_bk,
           L0_attn_Wv, L0_attn_bv, L0_attn_Wo, L0_attn_bo,
           L0_ffn_w_freq_real, L0_ffn_w_freq_imag, L0_ffn_bias,
           L1_attn_Wq, L1_attn_Wk, L1_attn_Wv, L1_attn_Wo,
           L1_ffn_w_gate, L1_ffn_w_up, L1_ffn_w_down):
    debug = bool(_FUSED_CACHE.get("debug"))
    if "nc" not in _FUSED_CACHE:
        _FUSED_CACHE["nc"] = build_fused(debug)
    nc = _FUSED_CACHE["nc"]

    def qperm_w(w):
        return np.ascontiguousarray(
            np.asarray(w, np.float32).reshape(-1, 16, 64)[:, QPERM].reshape(w.shape))

    def colmajor8(b):
        return np.ascontiguousarray(np.asarray(b, np.float32).reshape(8, 128).T)

    Cm, Sm = _dft_tables()
    x = np.asarray(L0_attn_x)
    trow = np.tile(np.arange(512, dtype=np.float32)[None, :], (128, 1))
    alb = _host_alb()
    ident = np.eye(128, dtype=np.float32)
    bq0 = np.asarray(L0_attn_bq).reshape(16, 64)[QPERM].reshape(1024)
    bqkv0 = np.concatenate([bq0, np.asarray(L0_attn_bk),
                            np.asarray(L0_attn_bv)])[None, :].astype(np.float32)
    in_maps = []
    for core in range(NCORES):
        cos, sin = _rope_tables(core)
        sl = slice(128 * core, 128 * (core + 1))
        m = dict(
            xfm=_shard_x_fm(x, core),
            cid=np.array([[core % 4]], np.int32),
            cosT=cos, sinT=sin,
            wq0=qperm_w(L0_attn_Wq), wk0=np.asarray(L0_attn_Wk),
            wv0=np.asarray(L0_attn_Wv), bqkv0=bqkv0,
            wo0=np.asarray(L0_attn_Wo), bo0=colmajor8(L0_attn_bo),
            dcolT=_host_dcol(core % 4), trowT=trow,
            wfr=np.asarray(L0_ffn_w_freq_real), wfi=np.asarray(L0_ffn_w_freq_imag),
            Cm=Cm, Sm=Sm,
            Csl=np.ascontiguousarray(Cm[:, sl] / D),
            Ssl=np.ascontiguousarray(Sm[:, sl] / D),
            fb0=colmajor8(L0_ffn_bias),
            wq1=qperm_w(L1_attn_Wq), wk1=np.asarray(L1_attn_Wk),
            wv1=np.asarray(L1_attn_Wv), wo1=np.asarray(L1_attn_Wo),
            bz=np.zeros((128, 8), np.float32),
            albT=alb,
            wg=np.asarray(L1_ffn_w_gate), wu=np.asarray(L1_ffn_w_up),
            wd=np.asarray(L1_ffn_w_down), ident=ident,
        )
        in_maps.append(m)
    res = run_bass_kernel_spmd(nc, in_maps, core_ids=list(range(NCORES)))
    out = np.zeros((B, S_, D), np.float32)
    for core in range(NCORES):
        out[core // 4][_local_positions(core)] = np.asarray(res.results[core]["out"]).T
    if debug:
        _FUSED_CACHE["res"] = res
    return out


# version 09
# speedup vs baseline: 1.4580x; 1.0319x over previous
import contextlib
import numpy as np
import ml_dtypes
import concourse.bass as bass
import concourse.mybir as mybir
from concourse.bass_utils import run_bass_kernel_spmd

F32 = mybir.dt.float32
BF16 = mybir.dt.bfloat16
AF = mybir.ActivationFunctionType
ALU = mybir.AluOpType
bfloat16 = ml_dtypes.bfloat16

NCORES = 8
B = 2
S = 8192
D = 1024
NQ = 16
NKV = 4
HD = 64
TPC = 2048
NT = TPC // 128
ROPE_BASE = 10000.0
MAGIC1 = float(3 * 2 ** 22)
MAGICH = float(3 * 2 ** 21)
LN_EPS = 1e-5



def _strips(core):
    cc = core % 4
    return (cc, 7 - cc)


def _local_positions(core):
    s0, s1 = _strips(core)
    return np.concatenate([np.arange(s0 * 1024, (s0 + 1) * 1024),
                           np.arange(s1 * 1024, (s1 + 1) * 1024)])


def _rope_tables(core):
    inv_freq = (1.0 / ROPE_BASE ** (np.arange(0, HD, 2, dtype=np.float64) / HD))
    pos = _local_positions(core).astype(np.float64)
    ang = pos[:, None] * inv_freq[None, :]
    return np.cos(ang).astype(np.float32), np.sin(ang).astype(np.float32)


def _shard_x_fm(x, core):
    b = core // 4
    rows = x[b][_local_positions(core)]
    return np.ascontiguousarray(rows.T)


class Ctr:
    def __init__(self):
        self.v = 0

    def inc(self, n=1):
        self.v += n
        return self.v


def _bcast(ap, reps):
    p, w = ap.shape
    return ap.unsqueeze(1).broadcast_to([p, reps, w])


DTSIZE = {F32: 4, BF16: 2, mybir.dt.int32: 4}


class Alloc:

    def __init__(self, nc, es, arena=None):
        self.nc, self.es, self.arena = nc, es, arena
        self.off = 0
        self.hwm = 0

    def reset(self):
        self.phase_hwms = getattr(self, "phase_hwms", [])
        self.phase_hwms.append(self.off)
        self.off = 0

    def at(self, off, name, shape, dtype):
        p, w = shape
        nel = (w * DTSIZE[dtype] + 3) // 4
        assert off + nel <= self.arena.shape[1]
        view = self.arena[:, off:off + nel]
        if dtype != F32:
            view = view.bitcast(dtype)[:, 0:w]
        if p != 128:
            view = view[0:p, :]
        return view

    def sbuf(self, name, shape, dtype):
        if self.arena is None:
            return self.es.enter_context(self.nc.sbuf_tensor(name, shape, dtype)).ap()
        p, w = shape
        nel = (w * DTSIZE[dtype] + 3) // 4
        lim = getattr(self, "limit", None) or self.arena.shape[1]
        assert self.off + nel <= lim, \
            f"arena overflow at {name}: {self.off + nel} > {lim}"
        view = self.arena[:, self.off:self.off + nel]
        self.off += nel
        self.hwm = max(self.hwm, self.off)
        if dtype != F32:
            view = view.bitcast(dtype)[:, 0:w]
        if p != 128:
            view = view[0:p, :]
        return view



def emit_qkv(S, AL, io, has_bias):
    nc = S.nc
    op, wait, inc_last, C = S.op, S.wait, S.inc_last, S.C
    xb_sb = AL.sbuf("q_xb", [128, 8 * TPC], BF16)
    wq_sb = AL.sbuf("q_wq", [128, 8 * 1024], BF16)
    wk_sb = AL.sbuf("q_wk", [128, 8 * 256], BF16)
    wv_sb = AL.sbuf("q_wv", [128, 8 * 256], BF16)
    bias_sb = AL.sbuf("q_bias", [1, 1536], BF16)
    ones1 = AL.sbuf("q_ones1", [1, 128], BF16)
    cos_sb = AL.sbuf("q_cos", [128, NT * 32], F32)
    sin_sb = AL.sbuf("q_sin", [128, NT * 32], F32)
    qo_sb = AL.sbuf("q_qo", [128, 2 * 1024], BF16)
    ko_sb = AL.sbuf("q_ko", [128, 2 * 256], BF16)
    vo_sb = AL.sbuf("q_vo", [128, 2 * 256], BF16)
    rsc = AL.sbuf("q_rsc", [128, 512], F32)
    P = S.P
    pq = [P[:, 512 * i:512 * (i + 1)] for i in range(4)]
    pk = [P[:, 2048 + 512 * i:2048 + 512 * i + 256] for i in range(2)]
    pv = [P[:, 3072 + 512 * i:3072 + 512 * i + 256] for i in range(2)]

    op("sync", lambda: nc.sync.dma_start(
        out=cos_sb.rearrange("p (t i) -> p t i", i=32),
        in_=io["cosT"].rearrange("(t p) i -> p t i", p=128)))
    inc_last("sync", "ld", 16)
    op("sync", lambda: nc.sync.dma_start(
        out=sin_sb.rearrange("p (t i) -> p t i", i=32),
        in_=io["sinT"].rearrange("(t p) i -> p t i", p=128)))
    inc_last("sync", "ld", 16)
    op("gpsimd", lambda: nc.gpsimd.dma_start(
        out=xb_sb.rearrange("p (k t) -> p k t", k=8),
        in_=io["xfm"].rearrange("(k p) t -> p k t", p=128)))
    inc_last("gpsimd", "ld", 16)
    op("gpsimd", lambda: nc.gpsimd.dma_start(
        out=wq_sb.rearrange("p (k c) -> p k c", k=8),
        in_=io["wq"].rearrange("(k p) c -> p k c", p=128)))
    inc_last("gpsimd", "ld", 16)
    op("gpsimd", lambda: nc.gpsimd.dma_start(
        out=wk_sb.rearrange("p (k c) -> p k c", k=8),
        in_=io["wk"].rearrange("(k p) c -> p k c", p=128)))
    inc_last("gpsimd", "ld", 16)
    op("gpsimd", lambda: nc.gpsimd.dma_start(
        out=wv_sb.rearrange("p (k c) -> p k c", k=8),
        in_=io["wv"].rearrange("(k p) c -> p k c", p=128)))
    inc_last("gpsimd", "ld", 16)
    if has_bias:
        op("gpsimd", lambda: nc.gpsimd.dma_start(out=bias_sb, in_=io["bqkv"]))
        inc_last("gpsimd", "ld", 16)
    lds = C["ld"].v
    op("vector", lambda: nc.vector.memset(ones1, 1.0))
    inc_last("vector", "cv", 1)
    cvs = C["cv"].v

    rp_hist = []
    st_hist = []
    for t in range(NT):
        sl = t % 2
        if t == 0:
            wait("tensor", "ld", lds)
            wait("tensor", "cv", cvs)
        if t >= 2:
            wait("tensor", "rp", rp_hist[t - 2])
        lhs = lambda k, t=t: xb_sb[:, 2048 * k + 128 * t:2048 * k + 128 * (t + 1)]
        for b in range(2):
            for k in range(8):
                def qf(b=b, k=k, t=t, sl=sl):
                    return nc.tensor.matmul(
                        pq[2 * sl + b], lhs(k, t),
                        wq_sb[:, 1024 * k + 512 * b:1024 * k + 512 * (b + 1)],
                        start=(k == 0), stop=(k == 7 and not has_bias))
                op("tensor", qf)
            if has_bias:
                def qbf(b=b, sl=sl):
                    return nc.tensor.matmul(
                        pq[2 * sl + b], ones1, io_bias_slice(bias_sb, 512 * b, 512),
                        start=False, stop=True)
                op("tensor", qbf)
        for k in range(8):
            def kf(k=k, t=t, sl=sl):
                return nc.tensor.matmul(pk[sl], lhs(k, t),
                                        wk_sb[:, 256 * k:256 * (k + 1)],
                                        start=(k == 0),
                                        stop=(k == 7 and not has_bias))
            op("tensor", kf)
        if has_bias:
            op("tensor", lambda sl=sl: nc.tensor.matmul(
                pk[sl], ones1, io_bias_slice(bias_sb, 1024, 256),
                start=False, stop=True))
        for k in range(8):
            def vf(k=k, t=t, sl=sl):
                return nc.tensor.matmul(pv[sl], lhs(k, t),
                                        wv_sb[:, 256 * k:256 * (k + 1)],
                                        start=(k == 0),
                                        stop=(k == 7 and not has_bias))
            op("tensor", vf)
        if has_bias:
            op("tensor", lambda sl=sl: nc.tensor.matmul(
                pv[sl], ones1, io_bias_slice(bias_sb, 1280, 256),
                start=False, stop=True))
        inc_last("tensor", "mm", 1)
        mmv = C["mm"].v

        wait("vector", "mm", mmv)
        if t >= 2:
            wait("vector", "st", st_hist[t - 2])
        cosv = _bcast(cos_sb[:, 32 * t:32 * (t + 1)], 8)
        sinv = _bcast(sin_sb[:, 32 * t:32 * (t + 1)], 8)
        cosv4 = _bcast(cos_sb[:, 32 * t:32 * (t + 1)], 4)
        sinv4 = _bcast(sin_sb[:, 32 * t:32 * (t + 1)], 4)
        for b in range(2):
            Pq = pq[2 * sl + b].rearrange("p (h d) -> p h d", d=64)
            Plo, Phi = Pq[:, :, 0:32], Pq[:, :, 32:64]
            qov = qo_sb[:, 1024 * sl + 512 * b:1024 * sl + 512 * (b + 1)] \
                .rearrange("p (h d) -> p h d", d=64)
            Av = rsc[:, 0:256].rearrange("p (h d) -> p h d", d=32)
            Bv = rsc[:, 256:512].rearrange("p (h d) -> p h d", d=32)
            op("vector", lambda Phi=Phi, Av=Av, sinv=sinv:
               nc.vector.tensor_tensor(Av, Phi, sinv, ALU.mult))
            op("vector", lambda Plo=Plo, Bv=Bv, cosv=cosv:
               nc.vector.tensor_tensor(Bv, Plo, cosv, ALU.mult))
            op("vector", lambda qov=qov, Bv=Bv, Av=Av:
               nc.vector.tensor_tensor(qov[:, :, 0:32], Bv, Av, ALU.subtract))
            op("vector", lambda Plo=Plo, Av=Av, sinv=sinv:
               nc.vector.tensor_tensor(Av, Plo, sinv, ALU.mult))
            op("vector", lambda Phi=Phi, Bv=Bv, cosv=cosv:
               nc.vector.tensor_tensor(Bv, Phi, cosv, ALU.mult))
            op("vector", lambda qov=qov, Bv=Bv, Av=Av:
               nc.vector.tensor_tensor(qov[:, :, 32:64], Bv, Av, ALU.add))
        Pk = pk[sl].rearrange("p (h d) -> p h d", d=64)
        Plo, Phi = Pk[:, :, 0:32], Pk[:, :, 32:64]
        kov = ko_sb[:, 256 * sl:256 * (sl + 1)].rearrange("p (h d) -> p h d", d=64)
        Av = rsc[:, 0:128].rearrange("p (h d) -> p h d", d=32)
        Bv = rsc[:, 256:384].rearrange("p (h d) -> p h d", d=32)
        op("vector", lambda Phi=Phi, Av=Av, sinv4=sinv4:
           nc.vector.tensor_tensor(Av, Phi, sinv4, ALU.mult))
        op("vector", lambda Plo=Plo, Bv=Bv, cosv4=cosv4:
           nc.vector.tensor_tensor(Bv, Plo, cosv4, ALU.mult))
        op("vector", lambda kov=kov, Bv=Bv, Av=Av:
           nc.vector.tensor_tensor(kov[:, :, 0:32], Bv, Av, ALU.subtract))
        op("vector", lambda Plo=Plo, Av=Av, sinv4=sinv4:
           nc.vector.tensor_tensor(Av, Plo, sinv4, ALU.mult))
        op("vector", lambda Phi=Phi, Bv=Bv, cosv4=cosv4:
           nc.vector.tensor_tensor(Bv, Phi, cosv4, ALU.mult))
        op("vector", lambda kov=kov, Bv=Bv, Av=Av:
           nc.vector.tensor_tensor(kov[:, :, 32:64], Bv, Av, ALU.add))
        op("vector", lambda sl=sl: nc.vector.tensor_copy(
            vo_sb[:, 256 * sl:256 * (sl + 1)], pv[sl]))
        rp_hist.append(inc_last("vector", "rp", 1))

        op("sync", lambda v=rp_hist[-1]: nc.sync.wait_ge(S.sems["rp"], v))
        op("sync", lambda t=t, sl=sl: nc.sync.dma_start(
            out=io["qtm"][128 * t:128 * (t + 1), :],
            in_=qo_sb[:, 1024 * sl:1024 * (sl + 1)]))
        inc_last("sync", "st", 16)
        op("sync", lambda t=t, sl=sl: nc.sync.dma_start(
            out=io["ktm"][128 * t:128 * (t + 1), :],
            in_=ko_sb[:, 256 * sl:256 * (sl + 1)]))
        inc_last("sync", "st", 16)
        op("sync", lambda t=t, sl=sl: nc.sync.dma_start(
            out=io["vtm"][128 * t:128 * (t + 1), :],
            in_=vo_sb[:, 256 * sl:256 * (sl + 1)]))
        st_hist.append(inc_last("sync", "st", 16))
    op("sync", lambda v=C["st"].v: nc.sync.wait_ge(S.sems["st"], v))
    return C["st"].v


def io_bias_slice(bias_sb, off, n):
    return bias_sb[0:1, off:off + n]


def build_qkv(has_bias):
    nc = bass.Bass("TRN2", target_bir_lowering=False)
    io = {}
    io["xfm"] = nc.dram_tensor("xfm", [D, TPC], F32, kind="ExternalInput").ap()
    io["wq"] = nc.dram_tensor("wq", [D, NQ * HD], F32, kind="ExternalInput").ap()
    io["wk"] = nc.dram_tensor("wk", [D, NKV * HD], F32, kind="ExternalInput").ap()
    io["wv"] = nc.dram_tensor("wv", [D, NKV * HD], F32, kind="ExternalInput").ap()
    if has_bias:
        io["bqkv"] = nc.dram_tensor("bqkv", [1, 1536], F32, kind="ExternalInput").ap()
    io["cosT"] = nc.dram_tensor("cosT", [TPC, 32], F32, kind="ExternalInput").ap()
    io["sinT"] = nc.dram_tensor("sinT", [TPC, 32], F32, kind="ExternalInput").ap()
    io["qtm"] = nc.dram_tensor("qtm", [TPC, NQ * HD], BF16, kind="ExternalOutput").ap()
    io["ktm"] = nc.dram_tensor("ktm", [TPC, NKV * HD], BF16, kind="ExternalOutput").ap()
    io["vtm"] = nc.dram_tensor("vtm", [TPC, NKV * HD], BF16, kind="ExternalOutput").ap()
    with contextlib.ExitStack() as es:
        AL = Alloc(nc, es)
        S = Sched(nc, es, ["ld", "cv", "mm", "rp", "st"])
        S.P = es.enter_context(nc.psum_tensor("P", [128, 4096], F32)).ap()
        block = es.enter_context(nc.Block())
        emit_qkv(S, AL, io, has_bias)
        S.emit(block)
    return nc


def run_qkv(layer, x_shards, weights, res):
    has_bias = (layer == 0)
    nc = build_qkv(has_bias)
    in_maps = []
    for core in range(NCORES):
        cos, sin = _rope_tables(core)
        m = dict(xfm=x_shards[core], wq=weights["wq"], wk=weights["wk"],
                 wv=weights["wv"], cosT=cos, sinT=sin)
        if has_bias:
            m["bqkv"] = np.concatenate([weights["bq"], weights["bk"],
                                        weights["bv"]])[None, :].astype(np.float32)
        in_maps.append(m)
    out = run_bass_kernel_spmd(nc, in_maps, core_ids=list(range(NCORES)))
    res.extend([out])
    return [r["qtm"] for r in out.results], [r["ktm"] for r in out.results], \
           [r["vtm"] for r in out.results]





def _alibi_slopes16():
    start = 2.0 ** (-2.0 ** (-(np.log2(16) - 3)))
    return np.array([start * start ** i for i in range(16)], dtype=np.float64)


def _l1_window_tiles():
    sl = _alibi_slopes16()
    w = np.ceil((115.0 / sl) / 128.0).astype(int)
    return np.minimum(w, 64)


def _pair_heads(p):
    base = 8 if p >= 4 else 0
    a = base + (p % 4)
    return a, a + 4


QPERM = []
for _p in range(8):
    _a, _b = _pair_heads(_p)
    QPERM += [_a, _b]


def attn_tensors(AL, layer):
    causal = (layer == 0)
    T = {}
    T["K_sb"] = [AL.sbuf(f"a_K{i}", [128, S_], BF16) for i in range(2)]
    T["V_sb"] = AL.sbuf("a_V", [128, 64 * 4 * 65], BF16)
    T["Q_sb"] = AL.sbuf("a_Q", [128, 2 * TPC], BF16)
    T["eX"] = AL.sbuf("a_eX", [128, 2 * 1024], BF16)
    T["Ofm"] = AL.sbuf("a_Ofm", [128, 8 * TPC], BF16)
    T["dsc64"] = AL.sbuf("a_dsc64", [128, 4096], BF16)
    T["denp"] = AL.sbuf("a_denp", [128, 2 * 256], BF16)
    T["rp"] = AL.sbuf("a_rp", [128, 256], BF16)
    T["rrow"] = AL.sbuf("a_rrow", [1, 8 * 512], BF16)
    T["wo_sb"] = AL.sbuf("a_wo", [128, 8 * 1024], BF16)
    T["bo_sb"] = AL.sbuf("a_bo", [128, 8], F32)
    T["ones1"] = AL.sbuf("a_ones1", [1, 128], BF16)
    T["oneK"] = AL.sbuf("a_oneK", [128, 1], BF16)
    T["msk"] = AL.sbuf("a_msk", [128, 512], BF16)
    if causal:
        T["dcol"] = AL.sbuf("a_dcol", [128, 16], F32)
        T["trow"] = AL.sbuf("a_trow", [128, 512], F32)
    else:
        T["alb"] = AL.sbuf("a_alb", [128, 64 * 16], F32)
    T["rb_sb"] = AL.sbuf("a_rb", [128, 512], BF16)
    T["On_sb"] = AL.sbuf("a_On", [128, 8 * 512], BF16)
    T["xstg"] = AL.sbuf("a_xstg", [128, 2 * 512], F32)
    T["s_c"] = AL.sbuf("a_sc", [128, 8 * 512], F32)
    T["sbf"] = AL.sbuf("a_sbf", [128, 2 * 512], BF16)
    T["ssq"] = AL.sbuf("a_ssq", [128, 2 * 512], BF16)
    T["stat"] = AL.sbuf("a_stat", [1, 2560], F32)
    T["x1stg"] = AL.sbuf("a_x1stg", [128, 2 * 512], F32)
    return T


S_ = S


def attn_psum(S):
    P = S.P
    return dict(Sp=[P[:, 0:1024], P[:, 1024:2048]],
                Oeo=[[P[0:65, 2048:2560], P[0:65, 2560:3072]],
                     [P[0:65, 3072:3584], P[0:65, 3584:4096]]],
                pbcF=P[:, 0:512], py=[P[:, 1024:1536], P[:, 1536:2048]],
                pmean=P[0:1, 2048:2560], pmsq=P[0:1, 2560:3072],
                prb=P[:, 3072:3584], pmr=P[:, 3584:4096])


def emit_attn_pre(S, T, io, layer, rank_major=False):
    nc = S.nc
    causal = (layer == 0)
    op, wait, inc_last, C = S.op, S.wait, S.inc_last, S.C
    if not rank_major:
        srcrows = [(1024 * s, 1024) for s in range(8)]
    else:
        srcrows = []
        for s in range(8):
            r = min(s, 7 - s)
            part = 0 if s < 4 else 1
            srcrows.append((2048 * r + 1024 * part, 1024))
    for t in range(2):
        for s in range(8):
            sr = srcrows[s][0]
            def ktf(t=t, s=s, sr=sr):
                return nc.sync.dma_start_transpose(
                    T["K_sb"][t][:, 1024 * s:1024 * (s + 1)],
                    io["kall"][sr:sr + 1024, 128 * t:128 * (t + 1)])
            op("sync", ktf)
            inc_last("sync", "ld", 16)
    op("vector", lambda: nc.vector.memset(T["V_sb"], 1.0))
    op("vector", lambda: nc.vector.memset(T["ones1"], 1.0))
    op("vector", lambda: nc.vector.memset(T["oneK"], 1.0 / 1024.0))
    inc_last("vector", "cv", 1)
    op("sync", lambda v=C["cv"].v: nc.sync.wait_ge(S.sems["cv"], v))
    for jv in range(4):
        for s in range(8):
            sr = srcrows[s][0]
            vdst = T["V_sb"].rearrange("p (c j d) -> p c j d", c=64, j=4)[
                :, 8 * s:8 * (s + 1), jv, 0:64]
            vsrc = io["vall"][sr:sr + 1024, :].rearrange(
                "(c p) jd -> p c jd", p=128)[:, :, 64 * jv:64 * (jv + 1)]
            op("sync", lambda vdst=vdst, vsrc=vsrc:
               nc.sync.dma_start(out=vdst, in_=vsrc))
            inc_last("sync", "ld", 16)
    if causal:
        op("sync", lambda: nc.sync.dma_start(out=T["dcol"], in_=io["dcolT"]))
        inc_last("sync", "ld", 16)
        op("sync", lambda: nc.sync.dma_start(out=T["trow"], in_=io["trowT"]))
        inc_last("sync", "ld", 16)
    else:
        op("sync", lambda: nc.sync.dma_start(out=T["alb"], in_=io["albT"]))
        inc_last("sync", "ld", 16)
    op("sync", lambda: nc.sync.dma_start(out=T["bo_sb"], in_=io["bo"]))
    inc_last("sync", "ld", 16)
    op("gpsimd", lambda: nc.gpsimd.dma_start(
        out=T["wo_sb"].rearrange("p (k c) -> p k c", k=8),
        in_=io["wo"].rearrange("(k p) c -> p k c", p=128)))
    inc_last("gpsimd", "ld", 16)
    return C["ld"].v


def emit_attn_core(S, Sb, T, io, layer, cc, emit_sync=True, filler=None):
    nc = S.nc
    causal = (layer == 0)
    strips = (cc, 7 - cc)
    wtiles = _l1_window_tiles() if not causal else None
    PV = attn_psum(S)
    Sp, Oeo = PV["Sp"], PV["Oeo"]
    C = S.C
    lds = C["ld"].v

    def jrange(p, hh, c_):
        if causal:
            return list(range(8 * strips[hh] + 4 * (c_ + 1)))
        a, b = _pair_heads(p)
        return list(range(64 - max(wtiles[a], wtiles[b]), 64))

    def active(p, j, ab):
        if causal:
            return True
        return j >= 64 - wtiles[_pair_heads(p)[ab]]

    def q_load(p):
        S.op("sync", lambda p=p, sl=p % 2: nc.sync.dma_start_transpose(
            T["Q_sb"][:, TPC * sl:TPC * (sl + 1)], io["qtm"][:, 128 * p:128 * (p + 1)]))
        S.inc_last("sync", "q", 16)

    qb0 = C["q"].v
    if emit_sync:
        q_load(0)
        q_load(1)
    else:
        C["q"].inc(32)

    ex_hist = []
    mk_hist = []
    pv_hist = []
    pend_pv = []
    ucount = 0
    for p in range(8):
        hA, hB = _pair_heads(p)
        kv = [hA // 4, hB // 4]
        kt = T["K_sb"][0] if p < 4 else T["K_sb"][1]
        ktb = [64 * (kv[0] % 2), 64 * (kv[1] % 2)]
        qsl = p % 2
        for hh in range(2):
            for c_ in (0, 1):
                js = jrange(p, hh, c_)
                tqo = TPC * qsl + 1024 * hh + 512 * c_
                for ji, j in enumerate(js):
                    sl = ucount % 2
                    acts = [ab for ab in range(2) if active(p, j, ab)]
                    if ucount == 0:
                        Sb.wait("tensor", "ld", lds)
                    Sb.wait("tensor", "q", qb0 + 16 * (p + 1))
                    if ucount >= 2:
                        Sb.wait("tensor", "ex", ex_hist[ucount - 2])
                    for ab in acts:
                        def mmf(ab=ab, j=j, sl=sl, tqo=tqo, kt=kt, kb=ktb[ab]):
                            return nc.tensor.matmul(
                                Sp[sl][:, 512 * ab:512 * (ab + 1)],
                                kt[kb:kb + 64, 128 * j:128 * (j + 1)],
                                T["Q_sb"][64 * ab:64 * ab + 64, tqo:tqo + 512],
                                start=True, stop=True,
                                tile_position=(64 * ab, 0))
                        Sb.op("tensor", mmf)
                    Sb.inc_last("tensor", "qk", 1)
                    if pend_pv:
                        pend_pv.pop(0)()

                    Sb.wait("scalar", "qk", C["qk"].v)
                    if ucount >= 2:
                        Sb.wait("scalar", "pv", pv_hist[ucount - 2])
                    if causal:
                        def ef(sl=sl):
                            return nc.scalar.activation(
                                T["eX"][:, 1024 * sl:1024 * (sl + 1)], Sp[sl],
                                AF.Exp, scale=0.125)
                        Sb.op("scalar", ef)
                    else:
                        for ab in acts:
                            h = _pair_heads(p)[ab]
                            def ef(sl=sl, ab=ab, j=j, h=h):
                                return nc.scalar.activation(
                                    T["eX"][:, 1024 * sl + 512 * ab:
                                            1024 * sl + 512 * (ab + 1)],
                                    Sp[sl][:, 512 * ab:512 * (ab + 1)],
                                    AF.Exp, scale=0.125,
                                    bias=T["alb"][:, 16 * j + h:16 * j + h + 1])
                            Sb.op("scalar", ef)
                    Sb.inc_last("scalar", "ex", 1)
                    ex_hist.append(C["ex"].v)

                    if causal:
                        mi = None
                        m0 = 8 * strips[hh] + 4 * c_
                        if m0 <= j < m0 + 4:
                            mi = hh * 8 + c_ * 4 + (j - m0)
                        if mi is not None:
                            Sb.wait("vector", "ex", C["ex"].v)
                            def mkf(mi=mi):
                                return nc.vector.tensor_scalar(
                                    T["msk"], T["trow"], T["dcol"][:, mi:mi + 1], 0.0,
                                    ALU.subtract, ALU.is_ge)
                            Sb.op("vector", mkf)
                            for ab in range(2):
                                def mtf(sl=sl, ab=ab):
                                    v = T["eX"][:, 1024 * sl + 512 * ab:
                                                1024 * sl + 512 * (ab + 1)]
                                    return nc.vector.tensor_tensor(v, v, T["msk"], ALU.mult)
                                Sb.op("vector", mtf)
                            Sb.inc_last("vector", "mk", 1)
                    mk_hist.append(C["mk"].v)

                    def emit_pv(u=ucount, sl=sl, acts=tuple(acts), j=j,
                                p=p, hh=hh, c_=c_, js=tuple(js)):
                        Sb.wait("tensor", "ex", ex_hist[u])
                        Sb.wait("tensor", "mk", mk_hist[u])
                        for ab in acts:
                            h = _pair_heads(p)[ab]
                            if causal:
                                st_ = (j == 0)
                                sp_ = (j == js[-1])
                            else:
                                st_ = (j == 64 - wtiles[h])
                                sp_ = (j == 63)
                            for eo in range(2):
                                def pvf(ab=ab, sl=sl, j=j, eo=eo,
                                        kvi=kv[ab], st_=st_, sp_=sp_):
                                    vcol = (4 * j + kvi) * 65
                                    return nc.tensor.matmul(
                                        Oeo[ab][eo],
                                        T["V_sb"][64 * eo:64 * (eo + 1),
                                                  vcol:vcol + 65],
                                        T["eX"][64 * eo:64 * (eo + 1),
                                                1024 * sl + 512 * ab:
                                                1024 * sl + 512 * (ab + 1)],
                                        start=st_, stop=sp_,
                                        tile_position=(64 * eo, 0))
                                Sb.op("tensor", pvf)
                        Sb.inc_last("tensor", "pv", 1)
                        pv_hist.append(C["pv"].v)
                    pend_pv.append(emit_pv)
                    ucount += 1

                while pend_pv:
                    pend_pv.pop(0)()
                Sb.wait("vector", "pv", C["pv"].v)
                for ab in range(2):
                    h = _pair_heads(p)[ab]
                    gc = 2 * hh + c_
                    def ecf(ab=ab):
                        return nc.vector.tensor_copy(
                            T["rb_sb"][0:64, :], Oeo[ab][0][0:64, :])
                    Sb.op("vector", ecf)
                    def evf(h=h, ab=ab, gc=gc):
                        return nc.vector.tensor_tensor(
                            T["Ofm"][(h % 2) * 64:(h % 2) * 64 + 64,
                                     TPC * (h // 2) + 512 * gc:
                                     TPC * (h // 2) + 512 * (gc + 1)],
                            T["rb_sb"][0:64, :], Oeo[ab][1][0:64, :], ALU.add)
                    Sb.op("vector", evf)
                    for eo in range(2):
                        def dnf(ab=ab, c_=c_, eo=eo):
                            return nc.vector.tensor_copy(
                                T["dsc64"][64:65, 2048 * eo + 512 * (2 * ab + c_):
                                           2048 * eo + 512 * (2 * ab + c_ + 1)],
                                Oeo[ab][eo][64:65, :])
                        Sb.op("vector", dnf)
                Sb.inc_last("vector", "oev", 1)
                Sb.wait("tensor", "oev", C["oev"].v)

                if c_ == 1:
                    phi = p * 2 + hh
                    if emit_sync:
                        S.op("sync", lambda v=C["oev"].v: nc.sync.wait_ge(S.sems["oev"], v))
                        for eo in range(2):
                            def ddf(phi=phi, eo=eo):
                                return nc.sync.dma_start(
                                    out=io["dscr"].rearrange("(e a w) -> e a w", e=2, w=2048)[eo, phi:phi + 1, :],
                                    in_=T["dsc64"][64:65, 2048 * eo:2048 * (eo + 1)])
                            S.op("sync", ddf)
                            S.inc_last("sync", "dd", 16)
                    else:
                        C["dd"].inc(32)
                    Sb.wait("vector", "dd", C["dd"].v)
                if filler is not None:
                    for _ in range(3):
                        filler(Sb, emit_sync)
        if p + 2 < 8:
            if emit_sync:
                S.op("sync", lambda v=C["qk"].v: nc.sync.wait_ge(S.sems["qk"], v))
                q_load(p + 2)
            else:
                C["q"].inc(16)


def emit_attn_post(S, T, io, layer):
    nc = S.nc
    op, wait, inc_last, C = S.op, S.wait, S.inc_last, S.C
    PV = attn_psum(S)
    pbcF, py = PV["pbcF"], PV["py"]
    pmean, pmsq, prb, pmr = PV["pmean"], PV["pmsq"], PV["prb"], PV["pmr"]

    def unit_of(h, gc):
        if h < 4:
            p_, ab = h, 0
        elif h < 8:
            p_, ab = h - 4, 1
        elif h < 12:
            p_, ab = h - 4, 0
        else:
            p_, ab = h - 8, 1
        hh, c_ = gc // 2, gc % 2
        return ((p_ * 2 + hh) * 2 + ab) * 2 + c_

    op("sync", lambda v=C["dd"].v: nc.sync.wait_ge(S.sems["dd"], v))
    op("sync", lambda: nc.sync.dma_start(
        out=T["denp"].rearrange("p (e w) -> p e w", e=2),
        in_=io["dscr"].rearrange("(e p w) -> p e w", p=128, w=256)))
    inc_last("sync", "db", 16)
    wait("vector", "db", C["db"].v)
    op("vector", lambda: nc.vector.tensor_tensor(
        T["denp"][:, 0:256], T["denp"][:, 0:256], T["denp"][:, 256:512], ALU.add))

    def _recipf():
        with nc.allow_low_precision(reason="bf16 softmax normalizer"):
            return nc.vector.reciprocal(T["rp"], T["denp"][:, 0:256])
    op("vector", _recipf)
    inc_last("vector", "db", 1)
    op("sync", lambda v=C["db"].v: nc.sync.wait_ge(S.sems["db"], v))
    op("sync", lambda: nc.sync.dma_start(
        out=io["rscr"].rearrange("(p w) -> p w", w=256), in_=T["rp"]))
    inc_last("sync", "db", 16)

    stt_hist = []
    sm_hist = []
    x1s_hist = []
    xsb0 = C["xs"].v
    for c in range(4):
        op("sync", lambda v=C["db"].v: nc.sync.wait_ge(S.sems["db"], v))
        if c > 0:
            op("sync", lambda v=C["bc"].v: nc.sync.wait_ge(S.sems["bc"], v))
        hh_, c__ = c // 2, c % 2

        def rload(quart):
            ab_ = 1 if quart in (1, 3) else 0
            p0 = [0, 0, 4, 4][quart]
            u0 = ((p0 * 2 + hh_) * 2 + ab_) * 2 + c__

            def rldf(quart=quart, u0=u0):
                return nc.sync.dma_start(
                    out=T["rrow"][0:1, 2048 * (quart % 2):2048 * (quart % 2 + 1)]
                    .rearrange("a (h w) -> a h w", w=512),
                    in_=io["rscr"].rearrange("(u w) -> u w", w=512)[u0:u0 + 25:8, :]
                    .unsqueeze(0))
            op("sync", rldf)
            inc_last("sync", "x1s", 16)
        rload(0)
        rload(1)
        def xpf_emit(o, c):
            ko_ = 8 * c + o
            if ko_ >= 2:
                op("sync", lambda v=stt_hist[ko_ - 2]:
                   nc.sync.wait_ge(S.sems["stt"], v))

            def xpf(o=o, c=c):
                return nc.sync.dma_start(
                    out=T["xstg"][:, 512 * (o % 2):512 * (o % 2 + 1)],
                    in_=io["xfm"][128 * o:128 * (o + 1), 512 * c:512 * (c + 1)])
            op("sync", xpf)
            inc_last("sync", "xs", 16)
        xpf_emit(0, c)
        xpf_emit(1, c)
        for k in range(8):
            if k == 0:
                wait("tensor", "x1s", C["x1s"].v)
                if c == 0:
                    wait("tensor", "oev", C["oev"].v)
            if k == 4:
                op("sync", lambda v=C["bc"].v: nc.sync.wait_ge(S.sems["bc"], v))
                rload(2)
                rload(3)
                wait("tensor", "x1s", C["x1s"].v)
            for sub in range(2):
                h = 2 * k + sub
                quart = h // 4
                hl = (quart % 2) * 4 + (h % 4)

                def bcf(sub=sub, hl=hl):
                    return nc.tensor.matmul(
                        pbcF[64 * sub:64 * (sub + 1), :], T["ones1"][0:1, 0:64],
                        T["rrow"][0:1, 512 * hl:512 * (hl + 1)],
                        start=True, stop=True,
                        tile_position=(0, 64 * sub))
                op("tensor", bcf)
            inc_last("tensor", "bc", 1)
            wait("vector", "bc", C["bc"].v)
            op("vector", lambda: nc.vector.tensor_copy(T["rb_sb"], pbcF))

            def nrf(k=k, c=c):
                return nc.vector.tensor_tensor(
                    T["On_sb"][:, 512 * k:512 * (k + 1)],
                    T["Ofm"][:, TPC * k + 512 * c:TPC * k + 512 * (c + 1)],
                    T["rb_sb"], ALU.mult)
            op("vector", nrf)
            inc_last("vector", "nr", 1)
            wait("tensor", "nr", C["nr"].v)
        for o in range(8):
            ysl = o % 2
            ko = 8 * c + o
            if ko >= 2:
                wait("tensor", "stt", stt_hist[ko - 2])
            for k in range(8):
                def yf(o=o, k=k, ysl=ysl):
                    return nc.tensor.matmul(
                        py[ysl],
                        T["wo_sb"][:, 1024 * k + 128 * o:1024 * k + 128 * (o + 1)],
                        T["On_sb"][:, 512 * k:512 * (k + 1)],
                        start=(k == 0), stop=(k == 7))
                op("tensor", yf)
            inc_last("tensor", "y", 1)
            wait("vector", "y", C["y"].v)
            wait("vector", "xs", xsb0 + 16 * (8 * c + o + 1))

            def sf(o=o, ysl=ysl):
                return nc.vector.scalar_tensor_tensor(
                    T["s_c"][:, 512 * o:512 * (o + 1)], py[ysl],
                    T["bo_sb"][:, o:o + 1], T["xstg"][:, 512 * (o % 2):512 * (o % 2 + 1)],
                    ALU.add, ALU.add)
            op("vector", sf)
            stt_hist.append(inc_last("vector", "stt", 1))
            if o + 2 < 8:
                xpf_emit(o + 2, c)
            if ko >= 2:
                wait("vector", "sm", sm_hist[ko - 2])

            def sbff(o=o):
                return nc.vector.tensor_copy(
                    T["sbf"][:, 512 * (o % 2):512 * (o % 2 + 1)],
                    T["s_c"][:, 512 * o:512 * (o + 1)])
            op("vector", sbff)

            def ssqf(o=o):
                sl_ = slice(512 * (o % 2), 512 * (o % 2 + 1))
                return nc.vector.tensor_tensor(
                    T["ssq"][:, sl_], T["sbf"][:, sl_], T["sbf"][:, sl_], ALU.mult)
            op("vector", ssqf)
            inc_last("vector", "sq", 1)
            wait("tensor", "sq", C["sq"].v)

            def smf(o=o):
                return nc.tensor.matmul(
                    pmean, T["oneK"], T["sbf"][:, 512 * (o % 2):512 * (o % 2 + 1)],
                    start=(o == 0), stop=(o == 7))
            op("tensor", smf)

            def sqmf(o=o):
                return nc.tensor.matmul(
                    pmsq, T["oneK"], T["ssq"][:, 512 * (o % 2):512 * (o % 2 + 1)],
                    start=(o == 0), stop=(o == 7))
            op("tensor", sqmf)
            sm_hist.append(inc_last("tensor", "sm", 1))
        _emit_ln_tail(S, T, io, prb, pmr, pmean, pmsq, c, stt_hist, x1s_hist,
                      io["x1"])
        wait("tensor", "ap", C["ap"].v)
    op("sync", lambda v=C["x1s"].v: nc.sync.wait_ge(S.sems["x1s"], v))


def _emit_ln_tail(S, T, io, prb, pmr, pmean, pmsq, c, stt_hist, x1s_hist, xout):
    nc = S.nc
    op, wait, inc_last, C = S.op, S.wait, S.inc_last, S.C
    stat = T["stat"]
    wait("vector", "sm", C["sm"].v)
    op("vector", lambda: nc.vector.tensor_copy(stat[0:1, 0:512], pmean))
    op("vector", lambda: nc.vector.tensor_copy(stat[0:1, 512:1024], pmsq))
    op("vector", lambda: nc.vector.tensor_tensor(
        stat[0:1, 1024:1536], stat[0:1, 0:512], stat[0:1, 0:512], ALU.mult))
    op("vector", lambda: nc.vector.tensor_tensor(
        stat[0:1, 1536:2048], stat[0:1, 512:1024], stat[0:1, 1024:1536],
        ALU.subtract))
    op("vector", lambda: nc.vector.tensor_scalar(
        stat[0:1, 1536:2048], stat[0:1, 1536:2048], LN_EPS, None,
        ALU.add, ALU.bypass))
    inc_last("vector", "nr", 1)
    wait("scalar", "nr", C["nr"].v)
    op("scalar", lambda: nc.scalar.activation(
        stat[0:1, 1024:1536], stat[0:1, 1536:2048], AF.Ln))
    op("scalar", lambda: nc.scalar.activation(
        stat[0:1, 512:1024], stat[0:1, 1024:1536], AF.Exp, scale=-0.5))
    inc_last("scalar", "ex", 1)
    wait("vector", "ex", C["ex"].v)
    op("vector", lambda: nc.vector.scalar_tensor_tensor(
        stat[0:1, 1536:2048], stat[0:1, 0:512], -1.0,
        stat[0:1, 512:1024], ALU.mult, ALU.mult))
    rstd_bf = stat[0:1, 2048:2304].bitcast(BF16)
    mr_bf = stat[0:1, 2304:2560].bitcast(BF16)
    op("vector", lambda rb=rstd_bf: nc.vector.tensor_copy(rb, stat[0:1, 512:1024]))
    op("vector", lambda mb=mr_bf: nc.vector.tensor_copy(mb, stat[0:1, 1536:2048]))
    inc_last("vector", "nr", 1)
    wait("tensor", "nr", C["nr"].v)
    op("tensor", lambda rb=rstd_bf: nc.tensor.matmul(
        prb, T["ones1"], rb, start=True, stop=True))
    op("tensor", lambda mb=mr_bf: nc.tensor.matmul(
        pmr, T["ones1"], mb, start=True, stop=True))
    inc_last("tensor", "bc", 1)
    wait("vector", "bc", C["bc"].v)
    for o in range(8):
        ko = len(x1s_hist)
        osl = ko % 2
        if ko >= 2:
            wait("vector", "x1s", x1s_hist[ko - 2])

        def a1f(o=o, osl=osl):
            return nc.vector.tensor_tensor(
                T["x1stg"][:, 512 * osl:512 * (osl + 1)],
                T["s_c"][:, 512 * o:512 * (o + 1)], prb, ALU.mult)
        op("vector", a1f)

        def a2f(osl=osl):
            return nc.vector.tensor_tensor(
                T["x1stg"][:, 512 * osl:512 * (osl + 1)],
                T["x1stg"][:, 512 * osl:512 * (osl + 1)], pmr, ALU.add)
        op("vector", a2f)
        inc_last("vector", "ap", 1)
        op("sync", lambda v=C["ap"].v: nc.sync.wait_ge(S.sems["ap"], v))

        def x1f(o=o, c=c, osl=osl):
            return nc.sync.dma_start(
                out=xout[128 * o:128 * (o + 1), 512 * c:512 * (c + 1)],
                in_=T["x1stg"][:, 512 * osl:512 * (osl + 1)])
        op("sync", x1f)
        x1s_hist.append(inc_last("sync", "x1s", 16))


ATTN_SEMS = ["ld", "cv", "q", "qk", "ex", "mk", "pv", "oev", "dd", "db",
             "bc", "nr", "y", "stt", "sq", "sm", "ap", "x1s", "xs"]


def build_attn(layer, cc):
    causal = (layer == 0)
    nc = bass.Bass("TRN2", target_bir_lowering=False)
    io = {}
    io["qtm"] = nc.dram_tensor("qtm", [TPC, 1024], BF16, kind="ExternalInput").ap()
    io["kall"] = nc.dram_tensor("kall", [S_, 256], BF16, kind="ExternalInput").ap()
    io["vall"] = nc.dram_tensor("vall", [S_, 256], BF16, kind="ExternalInput").ap()
    io["xfm"] = nc.dram_tensor("xfm", [D, TPC], F32, kind="ExternalInput").ap()
    io["wo"] = nc.dram_tensor("wo", [1024, D], F32, kind="ExternalInput").ap()
    io["bo"] = nc.dram_tensor("bo", [128, 8], F32, kind="ExternalInput").ap()
    if causal:
        io["dcolT"] = nc.dram_tensor("dcolT", [128, 16], F32, kind="ExternalInput").ap()
        io["trowT"] = nc.dram_tensor("trowT", [128, 512], F32, kind="ExternalInput").ap()
    else:
        io["albT"] = nc.dram_tensor("albT", [128, 64 * 16], F32, kind="ExternalInput").ap()
    io["x1"] = nc.dram_tensor("x1", [D, TPC], F32, kind="ExternalOutput").ap()
    io["dscr"] = nc.dram_tensor("dscr", [2 * 64 * 512], BF16, kind="Internal").ap()
    io["rscr"] = nc.dram_tensor("rscr", [64 * 512], BF16, kind="Internal").ap()
    with contextlib.ExitStack() as es:
        AL = Alloc(nc, es)
        S = Sched(nc, es, ATTN_SEMS)
        S.P = es.enter_context(nc.psum_tensor("P", [128, 4096], F32)).ap()
        block = es.enter_context(nc.Block())
        T = attn_tensors(AL, layer)
        emit_attn_pre(S, T, io, layer)
        emit_attn_core(S, S, T, io, layer, cc, emit_sync=True)
        emit_attn_post(S, T, io, layer)
        S.emit(block)
    return nc


def _host_dcol(cc):
    out = np.zeros((128, 16), np.float32)
    p_ = np.arange(128, dtype=np.float32)
    for hh, st in enumerate((cc, 7 - cc)):
        for c_ in (0, 1):
            q0 = 1024 * st + 512 * c_
            for m in range(4):
                j = 8 * st + 4 * c_ + m
                out[:, hh * 8 + c_ * 4 + m] = 128 * j + p_ - q0
    return out


def _host_alb():
    sl = _alibi_slopes16()
    out = np.zeros((128, 64 * 16), np.float32)
    p_ = np.arange(128, dtype=np.float64)
    for j in range(64):
        for h in range(16):
            out[:, 16 * j + h] = (sl[h] * (128 * j + p_ - (S - 1))).astype(np.float32)
    return out


def run_attn(layer, q_shards, k_batches, v_batches, x_shards, wo, bo, res):
    outs = [None] * NCORES
    trow = np.tile(np.arange(512, dtype=np.float32)[None, :], (128, 1))
    for cc in range(4):
        nc = build_attn(layer, cc)
        cores = [cc, cc + 4]
        in_maps = []
        for core in cores:
            b = core // 4
            m = dict(qtm=q_shards[core], kall=k_batches[b], vall=v_batches[b],
                     xfm=x_shards[core], wo=wo,
                     bo=np.ascontiguousarray(bo.astype(np.float32).reshape(8, 128).T))
            if layer == 0:
                m["dcolT"] = _host_dcol(cc)
                m["trowT"] = trow
            else:
                m["albT"] = _host_alb()
            in_maps.append(m)
        out = run_bass_kernel_spmd(nc, in_maps, core_ids=cores)
        res.append(out)
        for i, core in enumerate(cores):
            outs[core] = out.results[i]
    return outs



def emit_amat(S, AL, io):
    nc = S.nc
    op, wait, inc_last, C = S.op, S.wait, S.inc_last, S.C
    wr_sb = AL.sbuf("m_wr", [128, 8 * 1024], F32)
    wi_sb = AL.sbuf("m_wi", [128, 8 * 1024], F32)
    win_sb = AL.sbuf("m_win", [128, 8 * 1024], F32)
    Cm_sb = AL.sbuf("m_Cm", [128, 8 * 1024], F32)
    Sm_sb = AL.sbuf("m_Sm", [128, 8 * 1024], F32)
    Cs_sb = AL.sbuf("m_Cs", [128, 8 * 128], F32)
    Ss_sb = AL.sbuf("m_Ss", [128, 8 * 128], F32)
    PT_sb = AL.sbuf("m_PT", [128, 1024], F32)
    QT_sb = AL.sbuf("m_QT", [128, 1024], F32)
    Ar_sb = AL.sbuf("m_Ar", [128, 1024], F32)
    P = S.P
    for name, dst, w in [("wfr", wr_sb, 1024), ("wfi", wi_sb, 1024),
                         ("Cm", Cm_sb, 1024), ("Sm", Sm_sb, 1024),
                         ("Csl", Cs_sb, 128), ("Ssl", Ss_sb, 128)]:
        op("sync", lambda name=name, dst=dst: nc.sync.dma_start(
            out=dst.rearrange("p (k c) -> p k c", k=8),
            in_=io[name].rearrange("(k p) c -> p k c", p=128)))
        inc_last("sync", "ld", 16)
    lds = C["ld"].v
    wait("vector", "ld", lds)
    op("vector", lambda: nc.vector.tensor_scalar(
        win_sb, wi_sb, -1.0, None, ALU.mult, ALU.bypass))
    inc_last("vector", "cv", 1)
    cvs = C["cv"].v
    wait("tensor", "ld", lds)
    wait("tensor", "cv", cvs)
    ev_hist = []
    for kc in range(8):
        sl = kc % 2
        if kc >= 2:
            wait("tensor", "ev", ev_hist[kc - 2])
        for jc in range(8):
            for w_sb, c_sb in [(wr_sb, Cs_sb), (wi_sb, Ss_sb)]:
                def m1(kc=kc, jc=jc, w_sb=w_sb, c_sb=c_sb, sl=sl):
                    return nc.tensor.matmul(
                        P[:, 512 * sl:512 * sl + 128],
                        w_sb[:, 1024 * jc + 128 * kc:1024 * jc + 128 * (kc + 1)],
                        c_sb[:, 128 * jc:128 * (jc + 1)],
                        start=(jc == 0 and w_sb is wr_sb),
                        stop=(jc == 7 and w_sb is wi_sb))
                op("tensor", m1)
        for jc in range(8):
            for w_sb, c_sb in [(wr_sb, Ss_sb), (win_sb, Cs_sb)]:
                def m2(kc=kc, jc=jc, w_sb=w_sb, c_sb=c_sb, sl=sl):
                    return nc.tensor.matmul(
                        P[:, 1024 + 512 * sl:1024 + 512 * sl + 128],
                        w_sb[:, 1024 * jc + 128 * kc:1024 * jc + 128 * (kc + 1)],
                        c_sb[:, 128 * jc:128 * (jc + 1)],
                        start=(jc == 0 and w_sb is wr_sb),
                        stop=(jc == 7 and w_sb is win_sb))
                op("tensor", m2)
        inc_last("tensor", "mm", 1)
        wait("vector", "mm", C["mm"].v)
        op("vector", lambda kc=kc, sl=sl: nc.vector.tensor_copy(
            PT_sb[:, 128 * kc:128 * (kc + 1)],
            P[:, 512 * sl:512 * sl + 128]))
        op("vector", lambda kc=kc, sl=sl: nc.vector.tensor_copy(
            QT_sb[:, 128 * kc:128 * (kc + 1)],
            P[:, 1024 + 512 * sl:1024 + 512 * sl + 128]))
        ev_hist.append(inc_last("vector", "ev", 1))
    wait("tensor", "ev", C["ev"].v)
    for ob in range(2):
        for kc in range(8):
            for t_sb, c_sb in [(PT_sb, Cm_sb), (QT_sb, Sm_sb)]:
                def m3(ob=ob, kc=kc, t_sb=t_sb, c_sb=c_sb):
                    return nc.tensor.matmul(
                        P[:, 512 * ob:512 * (ob + 1)],
                        t_sb[:, 128 * kc:128 * (kc + 1)],
                        c_sb[:, 1024 * kc + 512 * ob:1024 * kc + 512 * (ob + 1)],
                        start=(kc == 0 and t_sb is PT_sb),
                        stop=(kc == 7 and t_sb is QT_sb))
                op("tensor", m3)
        inc_last("tensor", "mm", 1)
        wait("vector", "mm", C["mm"].v)
        op("vector", lambda ob=ob: nc.vector.tensor_copy(
            Ar_sb[:, 512 * ob:512 * (ob + 1)], P[:, 512 * ob:512 * (ob + 1)]))
        inc_last("vector", "ev", 1)
    op("gpsimd", lambda v=C["ev"].v: nc.gpsimd.wait_ge(S.sems["ev"], v))
    op("gpsimd", lambda: nc.gpsimd.dma_start(out=io["Ar"], in_=Ar_sb))
    inc_last("gpsimd", "st", 16)
    op("gpsimd", lambda v=C["st"].v: nc.gpsimd.wait_ge(S.sems["st"], v))


def build_amat():
    nc = bass.Bass("TRN2", target_bir_lowering=False)
    io = {}
    for n, sh in [("wfr", [D, D]), ("wfi", [D, D]), ("Cm", [D, D]),
                  ("Sm", [D, D]), ("Csl", [D, 128]), ("Ssl", [D, 128])]:
        io[n] = nc.dram_tensor(n, sh, F32, kind="ExternalInput").ap()
    io["Ar"] = nc.dram_tensor("Ar", [128, D], F32, kind="ExternalOutput").ap()
    with contextlib.ExitStack() as es:
        AL = Alloc(nc, es)
        S = Sched(nc, es, ["ld", "cv", "mm", "ev", "st"])
        S.P = es.enter_context(nc.psum_tensor("P", [128, 2048], F32)).ap()
        block = es.enter_context(nc.Block())
        emit_amat(S, AL, io)
        S.emit(block)
    return nc


def _dft_tables():
    kk = np.arange(D, dtype=np.float64)
    ang = 2.0 * np.pi * np.outer(kk, kk) / D
    ang = np.mod(np.outer(kk, kk), D) * (2.0 * np.pi / D)
    return np.cos(ang).astype(np.float32), np.sin(ang).astype(np.float32)


def run_amat(wfr, wfi, res):
    nc = build_amat()
    Cm, Sm = _dft_tables()
    in_maps = []
    for core in range(NCORES):
        sl = slice(128 * core, 128 * (core + 1))
        in_maps.append(dict(
            wfr=wfr, wfi=wfi, Cm=Cm, Sm=Sm,
            Csl=np.ascontiguousarray(Cm[:, sl] / D),
            Ssl=np.ascontiguousarray(Sm[:, sl] / D)))
    out = run_bass_kernel_spmd(nc, in_maps, core_ids=list(range(NCORES)))
    res.append(out)
    return np.concatenate([r["Ar"] for r in out.results], 0)


class Sched:

    def __init__(self, nc, es, semnames):
        self.nc = nc
        self.sems = {n: es.enter_context(nc.semaphore("s_" + n)) for n in semnames}
        self.C = {n: Ctr() for n in semnames}
        self.ops = {k: [] for k in ["sync", "gpsimd", "tensor", "vector", "scalar"]}

    def op(self, eng, f):
        self.ops[eng].append(f)

    def wait(self, eng, sem, val):
        if val > 0:
            nc = self.nc
            self.op(eng, lambda s=self.sems[sem], v=val:
                    getattr(nc, eng).wait_ge(s, v))

    def inc_last(self, eng, sem, n=1):
        prev = self.ops[eng][-1]

        def f(prev=prev, s=self.sems[sem], n=n):
            inst = prev()
            inst.then_inc(s, n)
            return inst
        self.ops[eng][-1] = f
        return self.C[sem].inc(n)

    def emit(self, block):
        ops = self.ops

        @block.sync
        def _(sync):
            for f in ops["sync"]:
                f()

        @block.gpsimd
        def _(gpsimd):
            for f in ops["gpsimd"]:
                f()

        @block.tensor
        def _(tensor):
            for f in ops["tensor"]:
                f()

        @block.scalar
        def _(scalar):
            for f in ops["scalar"]:
                f()

        @block.vector
        def _(vector):
            for f in ops["vector"]:
                f()


def _emit_resid_ln(S, nc, P, views, y_emit, xap, bias_sb, xout, c_list=(0, 1, 2, 3),
                   state=None, xpf_hook=None):
    py, pmean, pmsq, prb, pmr = views["py"], views["pmean"], views["pmsq"], \
        views["prb"], views["pmr"]
    s_cT, sbf, ssq, stat, x1stg = views["s_c"], views["sbf"], views["ssq"], \
        views["stat"], views["x1stg"]
    oneK, ones1 = views["oneK"], views["ones1"]
    op, wait, inc_last, C = S.op, S.wait, S.inc_last, S.C
    if state is None:
        state = {}
    stt_hist = state.setdefault("stt_hist", [])
    sm_hist = state.setdefault("sm_hist", [])
    x1s_hist = state.setdefault("x1s_hist", [])
    kos = state.setdefault("kos", Ctr())
    for c in c_list:
        for o in range(8):
            ko = kos.v
            kos.inc(1)
            ysl = ko % 2
            if ko >= 2:
                wait("tensor", "stt", stt_hist[ko - 2])
            y_emit(c, o, py[ysl])
            inc_last("tensor", "y", 1)
            wait("vector", "y", C["y"].v)

            def sf(o=o, c=c, ysl=ysl):
                return nc.vector.scalar_tensor_tensor(
                    s_cT[:, 512 * o:512 * (o + 1)], py[ysl][:, :],
                    bias_sb[:, o:o + 1], xap(c, o),
                    ALU.add, ALU.add)
            op("vector", sf)
            stt_hist.append(inc_last("vector", "stt", 1))
            if xpf_hook is not None and o + 2 < 8:
                xpf_hook(o + 2, c)
            if ko >= 2:
                wait("vector", "sm", sm_hist[ko - 2])

            def sbff(o=o, ysl=ysl):
                return nc.vector.tensor_copy(
                    sbf[:, 512 * ysl:512 * (ysl + 1)],
                    s_cT[:, 512 * o:512 * (o + 1)])
            op("vector", sbff)

            def ssqf(ysl=ysl):
                sl_ = slice(512 * ysl, 512 * (ysl + 1))
                return nc.vector.tensor_tensor(
                    ssq[:, sl_], sbf[:, sl_], sbf[:, sl_], ALU.mult)
            op("vector", ssqf)
            inc_last("vector", "sq", 1)
            wait("tensor", "sq", C["sq"].v)

            def smf(o=o, ysl=ysl):
                return nc.tensor.matmul(
                    pmean, oneK[:], sbf[:, 512 * ysl:512 * (ysl + 1)],
                    start=(o == 0), stop=(o == 7))
            op("tensor", smf)

            def sqmf(o=o, ysl=ysl):
                return nc.tensor.matmul(
                    pmsq, oneK[:], ssq[:, 512 * ysl:512 * (ysl + 1)],
                    start=(o == 0), stop=(o == 7))
            op("tensor", sqmf)
            sm_hist.append(inc_last("tensor", "sm", 1))
        wait("vector", "sm", C["sm"].v)
        op("vector", lambda: nc.vector.tensor_copy(stat[0:1, 0:512], pmean))
        op("vector", lambda: nc.vector.tensor_copy(stat[0:1, 512:1024], pmsq))
        op("vector", lambda: nc.vector.tensor_tensor(
            stat[0:1, 1024:1536], stat[0:1, 0:512], stat[0:1, 0:512], ALU.mult))
        op("vector", lambda: nc.vector.tensor_tensor(
            stat[0:1, 1536:2048], stat[0:1, 512:1024], stat[0:1, 1024:1536],
            ALU.subtract))
        op("vector", lambda: nc.vector.tensor_scalar(
            stat[0:1, 1536:2048], stat[0:1, 1536:2048], LN_EPS, None,
            ALU.add, ALU.bypass))
        inc_last("vector", "nr", 1)
        wait("scalar", "nr", C["nr"].v)
        op("scalar", lambda: nc.scalar.activation(
            stat[0:1, 1024:1536], stat[0:1, 1536:2048], AF.Ln))
        op("scalar", lambda: nc.scalar.activation(
            stat[0:1, 512:1024], stat[0:1, 1024:1536], AF.Exp, scale=-0.5))
        inc_last("scalar", "ex", 1)
        wait("vector", "ex", C["ex"].v)
        op("vector", lambda: nc.vector.scalar_tensor_tensor(
            stat[0:1, 1536:2048], stat[0:1, 0:512], -1.0,
            stat[0:1, 512:1024], ALU.mult, ALU.mult))
        rstd_bf = stat[0:1, 2048:2304].bitcast(BF16)
        mr_bf = stat[0:1, 2304:2560].bitcast(BF16)
        op("vector", lambda rb=rstd_bf: nc.vector.tensor_copy(
            rb, stat[0:1, 512:1024]))
        op("vector", lambda mb=mr_bf: nc.vector.tensor_copy(
            mb, stat[0:1, 1536:2048]))
        inc_last("vector", "nr", 1)
        wait("tensor", "nr", C["nr"].v)
        op("tensor", lambda rb=rstd_bf: nc.tensor.matmul(
            prb, ones1[:], rb, start=True, stop=True))
        op("tensor", lambda mb=mr_bf: nc.tensor.matmul(
            pmr, ones1[:], mb, start=True, stop=True))
        inc_last("tensor", "bc", 1)
        wait("vector", "bc", C["bc"].v)
        for o in range(8):
            ko = len(x1s_hist)
            osl = ko % 2
            if ko >= 2:
                wait("vector", "x1s", x1s_hist[ko - 2])

            def a1f(o=o, osl=osl):
                return nc.vector.tensor_tensor(
                    x1stg[:, 512 * osl:512 * (osl + 1)],
                    s_cT[:, 512 * o:512 * (o + 1)], prb[:, :], ALU.mult)
            op("vector", a1f)

            def a2f(osl=osl):
                return nc.vector.tensor_tensor(
                    x1stg[:, 512 * osl:512 * (osl + 1)],
                    x1stg[:, 512 * osl:512 * (osl + 1)], pmr[:, :], ALU.add)
            op("vector", a2f)
            inc_last("vector", "ap", 1)
            op("sync", lambda v=C["ap"].v: nc.sync.wait_ge(S.sems["ap"], v))

            def x1f(o=o, c=c, osl=osl):
                return nc.sync.dma_start(
                    out=xout[128 * o:128 * (o + 1), 512 * c:512 * (c + 1)],
                    in_=x1stg[:, 512 * osl:512 * (osl + 1)])
            op("sync", x1f)
            x1s_hist.append(inc_last("sync", "x1s", 16))
        wait("tensor", "ap", C["ap"].v)


def emit_ffn0(S, AL, io):
    nc = S.nc
    op, wait, inc_last, C = S.op, S.wait, S.inc_last, S.C
    x_sb = AL.sbuf("f0_x", [128, 8 * 2048], F32)
    xb_sb = AL.sbuf("f0_xb", [128, 8 * 2048], BF16)
    A_sb = AL.sbuf("f0_A", [128, 8 * 1024], BF16)
    fb_sb = AL.sbuf("f0_fb", [128, 8], F32)
    T = dict(ones1=AL.sbuf("f0_ones1", [1, 128], BF16),
             oneK=AL.sbuf("f0_oneK", [128, 1], BF16),
             s_c=AL.sbuf("f0_sc", [128, 8 * 512], F32),
             sbf=AL.sbuf("f0_sbf", [128, 2 * 512], BF16),
             ssq=AL.sbuf("f0_ssq", [128, 2 * 512], BF16),
             stat=AL.sbuf("f0_stat", [1, 2560], F32),
             x1stg=AL.sbuf("f0_x1stg", [128, 2 * 512], F32))
    P = S.P
    views = dict(py=[P[:, 0:512], P[:, 512:1024]],
                 pmean=P[0:1, 2048:2560], pmsq=P[0:1, 2560:3072],
                 prb=P[:, 3072:3584], pmr=P[:, 3584:4096],
                 s_c=T["s_c"], sbf=T["sbf"], ssq=T["ssq"], stat=T["stat"],
                 x1stg=T["x1stg"], oneK=T["oneK"], ones1=T["ones1"])
    op("sync", lambda: nc.sync.dma_start(
        out=x_sb.rearrange("p (k t) -> p k t", k=8),
        in_=io["xfm"].rearrange("(k p) t -> p k t", p=128)))
    inc_last("sync", "ld", 16)
    op("sync", lambda: nc.sync.dma_start(out=fb_sb, in_=io["fb"]))
    inc_last("sync", "ld", 16)
    op("gpsimd", lambda: nc.gpsimd.dma_start(
        out=xb_sb.rearrange("p (k t) -> p k t", k=8),
        in_=io["xfm"].rearrange("(k p) t -> p k t", p=128)))
    inc_last("gpsimd", "ld", 16)
    op("gpsimd", lambda: nc.gpsimd.dma_start(
        out=A_sb.rearrange("p (k c) -> p k c", k=8),
        in_=io["Am"].rearrange("(k p) c -> p k c", p=128)))
    inc_last("gpsimd", "ld", 16)
    op("vector", lambda: nc.vector.memset(T["ones1"], 1.0))
    op("vector", lambda: nc.vector.memset(T["oneK"], 1.0 / 1024.0))
    inc_last("vector", "cv", 1)
    lds, cvs = C["ld"].v, C["cv"].v

    def y_emit(c, o, pyv):
        if c == 0 and o == 0:
            wait("tensor", "ld", lds)
            wait("tensor", "cv", cvs)
        for k in range(8):
            def yf(c=c, o=o, k=k, pyv=pyv):
                return nc.tensor.matmul(
                    pyv, A_sb[:, 1024 * k + 128 * o:1024 * k + 128 * (o + 1)],
                    xb_sb[:, 2048 * k + 512 * c:2048 * k + 512 * (c + 1)],
                    start=(k == 0), stop=(k == 7))
            op("tensor", yf)

    _emit_resid_ln(S, nc, P, views, y_emit,
                   lambda c, o: x_sb[:, 2048 * o + 512 * c:2048 * o + 512 * (c + 1)],
                   fb_sb, io["x2"])
    op("sync", lambda v=C["x1s"].v: nc.sync.wait_ge(S.sems["x1s"], v))


FFN0_SEMS = ["ld", "cv", "y", "stt", "sq", "sm", "nr", "ex", "bc", "ap", "x1s"]


def build_ffn0():
    nc = bass.Bass("TRN2", target_bir_lowering=False)
    io = {}
    io["xfm"] = nc.dram_tensor("xfm", [D, TPC], F32, kind="ExternalInput").ap()
    io["Am"] = nc.dram_tensor("Am", [D, D], F32, kind="ExternalInput").ap()
    io["fb"] = nc.dram_tensor("fb", [128, 8], F32, kind="ExternalInput").ap()
    io["x2"] = nc.dram_tensor("x2", [D, TPC], F32, kind="ExternalOutput").ap()
    with contextlib.ExitStack() as es:
        AL = Alloc(nc, es)
        S = Sched(nc, es, FFN0_SEMS)
        S.P = es.enter_context(nc.psum_tensor("P", [128, 4096], F32)).ap()
        block = es.enter_context(nc.Block())
        emit_ffn0(S, AL, io)
        S.emit(block)
    return nc


def run_ffn0(x_shards, A, fbias, res):
    nc = build_ffn0()
    fb = np.ascontiguousarray(fbias.astype(np.float32).reshape(8, 128).T)
    in_maps = [dict(xfm=x_shards[c], Am=A, fb=fb) for c in range(NCORES)]
    out = run_bass_kernel_spmd(nc, in_maps, core_ids=list(range(NCORES)))
    res.append(out)
    return [r["x2"] for r in out.results]


def _emit_quant(S, nc, out_ap, in_ap, qs, W):
    op = S.op
    nb = W // 16
    t8, t4, t2 = qs["t8"], qs["t4"], qs["t2"]
    bm, sc, rc, scb = qs["bm"], qs["sc"], qs["rc"], qs["scb"]
    u, tb = qs["u"], qs["tb"]
    A, B4, B2 = qs["A"], qs["B4"], qs["B2"]
    w1 = t8.bitcast(BF16)
    w2 = tb.bitcast(BF16)
    M1, MH = MAGIC1, MAGICH
    I32 = mybir.dt.int32
    tbi = tb.bitcast(I32)
    ini = in_ap.bitcast(I32)
    def selfsem():
        v = S.inc_last("vector", "qz", 1)
        S.wait("vector", "qz", v)

    op("vector", lambda: nc.vector.tensor_scalar(
        tbi[:, 0:W], ini, 0x7FFFFFFF, None, ALU.bitwise_and, ALU.bypass))
    selfsem()
    op("vector", lambda: nc.vector.tensor_tensor(
        t8[:, 0:W // 2], tb[:, 0:W:2], tb[:, 1:W:2], ALU.max))
    selfsem()
    op("vector", lambda: nc.vector.tensor_tensor(
        t4[:, 0:W // 4], t8[:, 0:W // 2:2], t8[:, 1:W // 2:2], ALU.max))
    selfsem()
    op("vector", lambda: nc.vector.tensor_tensor(
        t2[:, 0:W // 8], t4[:, 0:W // 4:2], t4[:, 1:W // 4:2], ALU.max))
    selfsem()
    op("vector", lambda: nc.vector.tensor_tensor(
        bm[:, 0:nb], t2[:, 0:W // 8:2], t2[:, 1:W // 8:2], ALU.max))
    selfsem()
    op("vector", lambda: nc.vector.tensor_scalar(
        sc[:, 0:nb], bm[:, 0:nb], 1.0 / 6.0, 1e-12, ALU.mult, ALU.max))
    selfsem()

    def _rc():
        with nc.allow_low_precision(reason="quant recip"):
            return nc.vector.reciprocal(rc[:, 0:nb], sc[:, 0:nb])
    op("vector", _rc)
    op("vector", lambda: nc.vector.tensor_copy(scb[:, 0:nb], sc[:, 0:nb]))
    selfsem()
    rcb = rc[:, 0:nb].unsqueeze(2).broadcast_to([128, nb, 16])
    scbb = scb[:, 0:nb].unsqueeze(2).broadcast_to([128, nb, 16])
    in3 = in_ap.rearrange("p (a b) -> p a b", b=16)
    u3 = u[:, 0:W].rearrange("p (a b) -> p a b", b=16)
    op("vector", lambda: nc.vector.tensor_tensor(u3, in3, rcb, ALU.mult))
    selfsem()
    uf = u[:, 0:W]
    tbf = tb[:, 0:W]
    op("vector", lambda: nc.vector.tensor_scalar(
        tbf, uf, MH, MH - 2.0, ALU.add, ALU.max))
    selfsem()
    op("vector", lambda: nc.vector.tensor_scalar(
        A[:, 0:W], tbf, MH, 2.0, ALU.subtract, ALU.min))
    op("vector", lambda: nc.vector.tensor_scalar(
        tbf, uf, M1, M1 - 4.0, ALU.add, ALU.max))
    selfsem()
    op("vector", lambda: nc.vector.tensor_scalar(
        B4[:, 0:W], tbf, M1, 4.0, ALU.subtract, ALU.min))
    op("vector", lambda: nc.vector.tensor_scalar(
        tbf, tbf, M1 - 2.0, None, ALU.max, ALU.bypass))
    selfsem()
    op("vector", lambda: nc.vector.tensor_scalar(
        B2[:, 0:W], tbf, M1, 2.0, ALU.subtract, ALU.min))
    op("vector", lambda: nc.vector.tensor_scalar(
        w1[:, 0:W], uf, 5.0, 2.0, ALU.is_gt, ALU.mult))
    op("vector", lambda: nc.vector.tensor_scalar(
        w2[:, 0:W], uf, -5.0, 2.0, ALU.is_lt, ALU.mult))
    selfsem()
    op("vector", lambda: nc.vector.tensor_tensor(
        A[:, 0:W], A[:, 0:W], B4[:, 0:W], ALU.add))
    op("vector", lambda: nc.vector.tensor_tensor(
        B2[:, 0:W], B2[:, 0:W], w2[:, 0:W], ALU.add))
    selfsem()
    op("vector", lambda: nc.vector.tensor_tensor(
        A[:, 0:W], A[:, 0:W], w1[:, 0:W], ALU.add))
    selfsem()
    op("vector", lambda: nc.vector.tensor_tensor(
        A[:, 0:W], A[:, 0:W], B2[:, 0:W], ALU.subtract))
    selfsem()
    A3 = A[:, 0:W].rearrange("p (a b) -> p a b", b=16)
    o3 = out_ap.rearrange("p (a b) -> p a b", b=16)
    op("vector", lambda: nc.vector.tensor_tensor(o3, A3, scbb, ALU.mult))


def _mk_qscratch_al(AL, Wmax=1024):
    return dict(
        t8=AL.sbuf("q_t8", [128, Wmax // 2], F32),
        t4=AL.sbuf("q_t4", [128, Wmax // 4], F32),
        t2=AL.sbuf("q_t2", [128, Wmax // 8], F32),
        bm=AL.sbuf("q_bm", [128, Wmax // 16], F32),
        sc=AL.sbuf("q_sc", [128, Wmax // 16], F32),
        rc=AL.sbuf("q_rc", [128, Wmax // 16], F32),
        scb=AL.sbuf("q_scb", [128, Wmax // 16], BF16),
        u=AL.sbuf("q_u", [128, Wmax], F32),
        tb=AL.sbuf("q_tb", [128, Wmax], F32),
        A=AL.sbuf("q_A", [128, Wmax], BF16),
        B4=AL.sbuf("q_B4", [128, Wmax], BF16),
        B2=AL.sbuf("q_B2", [128, Wmax], BF16),
    )


def _mk_qscratch(nc, es, Wmax=1024):
    E = es.enter_context
    return dict(
        t8=E(nc.sbuf_tensor("q_t8", [128, Wmax // 2], F32)),
        t4=E(nc.sbuf_tensor("q_t4", [128, Wmax // 4], F32)),
        t2=E(nc.sbuf_tensor("q_t2", [128, Wmax // 8], F32)),
        bm=E(nc.sbuf_tensor("q_bm", [128, Wmax // 16], F32)),
        sc=E(nc.sbuf_tensor("q_sc", [128, Wmax // 16], F32)),
        rc=E(nc.sbuf_tensor("q_rc", [128, Wmax // 16], F32)),
        scb=E(nc.sbuf_tensor("q_scb", [128, Wmax // 16], BF16)),
        u=E(nc.sbuf_tensor("q_u", [128, Wmax], F32)),
        tb=E(nc.sbuf_tensor("q_tb", [128, Wmax], F32)),
        A=E(nc.sbuf_tensor("q_A", [128, Wmax], BF16)),
        B4=E(nc.sbuf_tensor("q_B4", [128, Wmax], BF16)),
        B2=E(nc.sbuf_tensor("q_B2", [128, Wmax], BF16)),
    )


FFN1_SEMS = ["ld", "cv", "wq", "wo", "xq", "xb", "gu", "hq", "hb",
             "y", "stt", "sq", "sm", "nr", "ex", "bc", "ap",
             "x1s", "xs", "ws", "ev", "qz"]


def emit_ffn1(S, AL, io, debug=False, wq_pre=None):
    II = 2816
    nc = S.nc
    xfm, wg, wu, wd = io["xfm"], io["wg"], io["wu"], io["wd"]
    ident = io["ident"]
    x4 = io["x4"]
    wgq, wuq, wdq = io["wgq"], io["wuq"], io["wdq"]
    xqd, hqd = io["xqd"], io["hqd"]
    if debug:
        dbg_g, dbg_h = io["dbg_g"], io["dbg_h"]
    if True:
        E = None
        hqfm = AL.sbuf("hqfm", [128, 22 * 1024], BF16)
        wdT = AL.sbuf("wdT", [128, 22 * 1024], BF16)
        xqfm = AL.sbuf("xqfm", [128, 8 * 1024], BF16)
        wgt = AL.sbuf("wgt", [128, 2 * 8 * 512], BF16)
        wut = AL.sbuf("wut", [128, 2 * 8 * 512], BF16)
        xstg = AL.sbuf("xstg", [128, 2 * 128], F32)
        xtm = AL.sbuf("xtm", [128, 1024], F32)
        xqtm = AL.sbuf("xqtm", [128, 1024], BF16)
        sg = AL.sbuf("sg", [128, 512], F32)
        hf32 = AL.sbuf("hf32", [128, 512], F32)
        hqt = AL.sbuf("hqt", [128, 2 * 512], BF16)
        fb_sb = AL.sbuf("fb_sb", [128, 8], F32)
        ones1 = AL.sbuf("ones1", [1, 128], BF16)
        oneK = AL.sbuf("oneK", [128, 1], BF16)
        s_cT = AL.sbuf("s_cT", [128, 8 * 512], F32)
        sbf = AL.sbuf("sbf", [128, 2 * 512], BF16)
        ssq = AL.sbuf("ssq", [128, 2 * 512], BF16)
        stat = AL.sbuf("stat", [1, 2560], F32)
        x1stg = AL.sbuf("x1stg", [128, 2 * 512], F32)
        xrs = AL.sbuf("xrs", [128, 2 * 512], F32)
        ident_sb = AL.sbuf("ident_sb", [128, 128], F32)
        qs = _mk_qscratch_al(AL)
        P = S.P
        wstg = [hqfm[:, 0:5632].bitcast(F32),
                hqfm[:, 5632:11264].bitcast(F32)]
        wqs = [hqfm[:, 11264:14080], hqfm[:, 14080:16896]]
        op, wait, inc_last, C = S.op, S.wait, S.inc_last, S.C

        op("sync", lambda: nc.sync.dma_start(out=ident_sb[:], in_=ident[:]))
        inc_last("sync", "ld", 16)
        op("vector", lambda: nc.vector.memset(fb_sb[:], 0.0))
        op("vector", lambda: nc.vector.memset(ones1[:], 1.0))
        op("vector", lambda: nc.vector.memset(oneK[:], 1.0 / 1024.0))
        inc_last("vector", "nr", 1)

        if wq_pre is None:
            wjobs = []
            for mi, (wsrc, wdst, nch, wid) in enumerate(
                    [(wg, wgq, 22, D), (wu, wuq, 22, D), (wd, wdq, 8, II)]):
                for ch in range(nch):
                    wjobs.append((wsrc, wdst, ch, wid))
            for ji, (wsrc, wdst, ch, wid) in enumerate(wjobs):
                sl = ji % 2
                if ji >= 2:
                    op("sync", lambda v=C["wq"].v - 1: nc.sync.wait_ge(S.sems["wq"], v))

                def wldf(wsrc=wsrc, ch=ch, wid=wid, sl=sl):
                    return nc.sync.dma_start(
                        out=wstg[sl][:, 0:wid], in_=wsrc[128 * ch:128 * (ch + 1), 0:wid])
                op("sync", wldf)
                inc_last("sync", "ws", 16)
                wait("vector", "ws", C["ws"].v)
                if ji >= 2:
                    wait("vector", "wo", C["wo"].v - 16)
                for off in range(0, wid, 1024):
                    w_ = min(1024, wid - off)
                    _emit_quant(S, nc, wqs[sl][:, off:off + w_],
                                wstg[sl][:, off:off + w_], qs, w_)
                inc_last("vector", "wq", 1)
                op("sync", lambda v=C["wq"].v: nc.sync.wait_ge(S.sems["wq"], v))

                def wstf(wdst=wdst, ch=ch, wid=wid, sl=sl):
                    return nc.sync.dma_start(
                        out=wdst[128 * ch:128 * (ch + 1), 0:wid], in_=wqs[sl][:, 0:wid])
                op("sync", wstf)
                inc_last("sync", "wo", 16)
            wqdone = C["wo"].v
        else:
            wqdone = wq_pre

        op("sync", lambda: nc.sync.wait_ge(S.sems["wo"], wqdone))
        for ic in range(22):
            def wdtf(ic=ic):
                return nc.sync.dma_start_transpose(
                    wdT[:, 1024 * ic:1024 * (ic + 1)], wdq[:, 128 * ic:128 * (ic + 1)])
            op("sync", wdtf)
            inc_last("sync", "ld", 16)
        lds = C["ld"].v

        views = dict(py=[P[:, 0:512], P[:, 512:1024]],
                     pmean=P[0:1, 2048:2560], pmsq=P[0:1, 2560:3072],
                     prb=P[:, 3072:3584], pmr=P[:, 3584:4096],
                     s_c=s_cT, sbf=sbf, ssq=ssq, stat=stat, x1stg=x1stg,
                     oneK=oneK, ones1=ones1)
        ln_state = {}
        for hf in range(2):
            toff = 1024 * hf
            for tt in range(8):
                ps = P[:, 1024 * (tt % 2):1024 * (tt % 2) + 1024]
                for o in range(8):
                    xsl = (8 * tt + o) % 2
                    if 8 * tt + o >= 2:
                        op("sync", lambda v=C["xq"].v: nc.sync.wait_ge(S.sems["xq"], v))

                    def xldf(o=o, tt=tt, xsl=xsl, toff=toff):
                        return nc.sync.dma_start(
                            out=xstg[:, 128 * xsl:128 * (xsl + 1)],
                            in_=xfm[128 * o:128 * (o + 1),
                                    toff + 128 * tt:toff + 128 * (tt + 1)])
                    op("sync", xldf)
                    inc_last("sync", "xs", 16)
                    wait("tensor", "xs", C["xs"].v)
                    if tt >= 2 and o == 0:
                        wait("tensor", "xq", C["xq"].v)

                    def xtf(o=o, ps=ps, xsl=xsl):
                        return nc.tensor.transpose(
                            ps[:, 128 * o:128 * (o + 1)],
                            xstg[:, 128 * xsl:128 * (xsl + 1)], ident_sb[:])
                    op("tensor", xtf)
                    inc_last("tensor", "xq", 1)
                wait("vector", "xq", C["xq"].v)
                op("vector", lambda ps=ps: nc.vector.tensor_copy(xtm[:], ps))
                inc_last("vector", "xq", 1)
                _emit_quant(S, nc, xqtm[:], xtm[:], qs, 1024)
                inc_last("vector", "xb", 1)
                op("sync", lambda v=C["xb"].v: nc.sync.wait_ge(S.sems["xb"], v))

                def xqstf(tt=tt, toff=toff):
                    return nc.sync.dma_start(
                        out=xqd[toff + 128 * tt:toff + 128 * (tt + 1), :],
                        in_=xqtm[:])
                op("sync", xqstf)
                inc_last("sync", "xq", 16)
                wait("vector", "xq", C["xq"].v)
            op("sync", lambda v=C["xq"].v: nc.sync.wait_ge(S.sems["xq"], v))
            for hc in range(8):
                def xqtf(hc=hc, toff=toff):
                    return nc.sync.dma_start_transpose(
                        xqfm[:, 1024 * hc:1024 * (hc + 1)],
                        xqd[toff:toff + 1024, 128 * hc:128 * (hc + 1)])
                op("sync", xqtf)
                inc_last("sync", "xb", 16)
            xqfmdone = C["xb"].v

            nib = 6
            for ib in range(nib):
                wib = 512 if ib < 5 else 256
                isl = ib % 2
                if ib < 2:
                    op("sync", lambda v=wqdone: nc.sync.wait_ge(S.sems["wo"], v))
                else:
                    op("sync", lambda v=C["gu"].v: nc.sync.wait_ge(S.sems["gu"], v))
                for hc in range(8):
                    def wgtf(ib=ib, hc=hc, isl=isl, wib=wib):
                        return nc.sync.dma_start_transpose(
                            wgt[:, 4096 * isl + 512 * hc:4096 * isl + 512 * hc + wib],
                            wgq[512 * ib:512 * ib + wib, 128 * hc:128 * (hc + 1)])
                    op("sync", wgtf)
                    inc_last("sync", "ld", 16)

                    def wutf(ib=ib, hc=hc, isl=isl, wib=wib):
                        return nc.sync.dma_start_transpose(
                            wut[:, 4096 * isl + 512 * hc:4096 * isl + 512 * hc + wib],
                            wuq[512 * ib:512 * ib + wib, 128 * hc:128 * (hc + 1)])
                    op("sync", wutf)
                    inc_last("sync", "ld", 16)
                wld = C["ld"].v
                for tt in range(8):
                    gsl = tt % 2
                    pg = P[:, 512 * gsl:512 * (gsl + 1)]
                    pu = P[:, 1024 + 512 * gsl:1024 + 512 * (gsl + 1)]
                    if tt == 0:
                        wait("tensor", "ld", wld)
                        wait("tensor", "xb", xqfmdone)
                    if 8 * ib + tt >= 2:
                        wait("tensor", "hq", C["hq"].v - 1)
                    for hc in range(8):
                        def gmf(hc=hc, tt=tt, isl=isl, wib=wib, pg=pg):
                            return nc.tensor.matmul(
                                pg[:, 0:wib],
                                xqfm[:, 1024 * hc + 128 * tt:1024 * hc + 128 * (tt + 1)],
                                wgt[:, 4096 * isl + 512 * hc:4096 * isl + 512 * hc + wib],
                                start=(hc == 0), stop=(hc == 7))
                        op("tensor", gmf)
                    for hc in range(8):
                        def umf(hc=hc, tt=tt, isl=isl, wib=wib, pu=pu):
                            return nc.tensor.matmul(
                                pu[:, 0:wib],
                                xqfm[:, 1024 * hc + 128 * tt:1024 * hc + 128 * (tt + 1)],
                                wut[:, 4096 * isl + 512 * hc:4096 * isl + 512 * hc + wib],
                                start=(hc == 0), stop=(hc == 7))
                        op("tensor", umf)
                    inc_last("tensor", "gu", 1)
                    wait("scalar", "gu", C["gu"].v)

                    def silf(pg=pg, wib=wib):
                        return nc.scalar.activation(sg[:, 0:wib], pg[:, 0:wib],
                                                    AF.Silu)
                    op("scalar", silf)
                    inc_last("scalar", "gu", 1)
                    wait("vector", "gu", C["gu"].v)

                    def hmf(pu=pu, wib=wib):
                        return nc.vector.tensor_tensor(
                            hf32[:, 0:wib], sg[:, 0:wib], pu[:, 0:wib], ALU.mult)
                    op("vector", hmf)
                    inc_last("vector", "gu", 1)
                    if debug and hf == 1 and ib == 0 and tt == 0:
                        op("vector", lambda pg=pg: nc.vector.tensor_copy(
                            s_cT[:, 0:512], pg[:, 0:512]))
                        op("vector", lambda: nc.vector.tensor_copy(
                            s_cT[:, 512:1024], hf32[:, 0:512]))
                        inc_last("vector", "ev", 1)
                        op("sync", lambda: nc.sync.wait_ge(S.sems["ev"], 1))
                        op("sync", lambda: nc.sync.dma_start(
                            out=dbg_g, in_=s_cT[:, 0:512]))
                        inc_last("sync", "ld", 16)
                        op("sync", lambda: nc.sync.dma_start(
                            out=dbg_h, in_=s_cT[:, 512:1024]))
                        inc_last("sync", "ld", 16)
                    hsl = (8 * ib + tt) % 2
                    if 8 * ib + tt >= 2:
                        wait("vector", "hb", C["hb"].v - 16)
                    _emit_quant(S, nc, hqt[:, 512 * hsl:512 * hsl + wib],
                                hf32[:, 0:wib], qs, wib)
                    inc_last("vector", "hq", 1)
                    op("sync", lambda v=C["hq"].v: nc.sync.wait_ge(S.sems["hq"], v))

                    def hstf(ib=ib, tt=tt, hsl=hsl, wib=wib):
                        return nc.sync.dma_start(
                            out=hqd[128 * tt:128 * (tt + 1), 512 * ib:512 * ib + wib],
                            in_=hqt[:, 512 * hsl:512 * hsl + wib])
                    op("sync", hstf)
                    inc_last("sync", "hb", 16)
            op("sync", lambda v=C["hb"].v: nc.sync.wait_ge(S.sems["hb"], v))
            if hf == 1:
                pass
            for ic in range(22):
                def hqtf(ic=ic):
                    return nc.sync.dma_start_transpose(
                        hqfm[:, 1024 * ic:1024 * (ic + 1)],
                        hqd[:, 128 * ic:128 * (ic + 1)])
                op("sync", hqtf)
                inc_last("sync", "hb", 16)
            hqdone = C["hb"].v

            xst_hist = ln_state.setdefault("xst_hist", [])

            def xap(c, o):
                return xrs[:, 512 * ((8 * c + o) % 2):512 * ((8 * c + o) % 2 + 1)]

            def xpf_emit(c, o):
                ko_ = len(xst_hist)
                if ko_ >= 2:
                    st = ln_state["stt_hist"]
                    if ko_ - 2 < len(st):
                        op("sync", lambda v=st[ko_ - 2]:
                           nc.sync.wait_ge(S.sems["stt"], v))

                def xpf(o=o, c=c):
                    return nc.sync.dma_start(
                        out=xrs[:, 512 * ((8 * c + o) % 2):512 * ((8 * c + o) % 2 + 1)],
                        in_=xfm[128 * o:128 * (o + 1), 512 * c:512 * (c + 1)])
                op("sync", xpf)
                xst_hist.append(inc_last("sync", "xs", 16))

            def y_emit(c, o, pyv, hqdone=hqdone):
                if o == 0 and c % 2 == 0:
                    wait("tensor", "hb", hqdone)
                cc_ = c % 2
                for ic in range(22):
                    def yf(c=c, o=o, ic=ic, pyv=pyv, cc_=cc_):
                        return nc.tensor.matmul(
                            pyv, wdT[:, 1024 * ic + 128 * o:1024 * ic + 128 * (o + 1)],
                            hqfm[:, 1024 * ic + 512 * cc_:1024 * ic + 512 * (cc_ + 1)],
                            start=(ic == 0), stop=(ic == 21))
                    op("tensor", yf)
                xpf_emit(c, o)
                wait("vector", "xs", xst_hist[-1])

            _emit_resid_ln(S, nc, P, views, y_emit, xap, fb_sb, x4,
                           c_list=[2 * hf, 2 * hf + 1], state=ln_state)
            wait("sync", "sm", C["sm"].v)
        op("sync", lambda v=C["x1s"].v: nc.sync.wait_ge(S.sems["x1s"], v))




class WQuant:

    JW = 384

    def __init__(self, S, AL, base, io):
        self.S = S
        nc = S.nc
        II = 2816
        self.jobs = []
        for src_, dst, nch, wid in [(io["wg"], io["wgq"], 22, 1024),
                                    (io["wu"], io["wuq"], 22, 1024),
                                    (io["wd"], io["wdq"], 8, II)]:
            for ch in range(nch):
                for co in range(0, wid, self.JW):
                    w = min(self.JW, wid - co)
                    self.jobs.append((src_, dst, ch, co, w))
        o = base
        def at(name, shape, dtype):
            nonlocal o
            p, w = shape
            nel = (w * DTSIZE[dtype] + 3) // 4
            v = AL.at(o, name, shape, dtype)
            o += nel
            return v
        self.wstg = [at(f"wq_stg{i}", [128, self.JW], F32) for i in range(2)]
        self.wqs = [at(f"wq_out{i}", [128, self.JW], BF16) for i in range(2)]
        W = self.JW
        self.qs = dict(
            t8=at("wq_t8", [128, W // 2], F32),
            t4=at("wq_t4", [128, W // 4], F32),
            t2=at("wq_t2", [128, W // 8], F32),
            bm=at("wq_bm", [128, W // 16], F32),
            sc=at("wq_sc", [128, W // 16], F32),
            rc=at("wq_rc", [128, W // 16], F32),
            scb=at("wq_scb", [128, W // 16], BF16),
            u=at("wq_u", [128, W], F32),
            tb=at("wq_tb", [128, W], F32),
            A=at("wq_A", [128, W], BF16),
            B4=at("wq_B4", [128, W], BF16),
            B2=at("wq_B2", [128, W], BF16),
        )
        self.top = o
        self.ws_base = S.C["ws"].v
        self.k = 0
        self.calls = 0
        self.wq_hist = []
        self.wo_hist = []

    def snapshot(self):
        return (self.k, self.calls, list(self.wq_hist), list(self.wo_hist))

    def restore(self, st):
        self.k, self.calls, wq, wo = st
        self.wq_hist = list(wq)
        self.wo_hist = list(wo)

    def _indma(self, S, j, emit_sync):
        if j >= len(self.jobs):
            return
        nc = S.nc
        if not emit_sync:
            S.C["ws"].inc(16)
            return
        if j >= 2:
            S.op("sync", lambda v=self.wq_hist[j - 2]:
                 nc.sync.wait_ge(S.sems["wq"], v))
        src_, dst, ch, co, w = self.jobs[j]
        S.op("sync", lambda src_=src_, ch=ch, co=co, w=w, sl=j % 2:
             nc.sync.dma_start(out=self.wstg[sl][:, 0:w],
                               in_=src_[128 * ch:128 * (ch + 1), co:co + w]))
        S.inc_last("sync", "ws", 16)

    def start(self, S, emit_sync):
        self._indma(S, 0, emit_sync)
        self._indma(S, 1, emit_sync)

    def step(self, Sb, emit_sync):
        if self.k >= len(self.jobs):
            return
        S, nc = self.S, self.S.nc
        j = self.k
        sl = j % 2
        src_, dst, ch, co, w = self.jobs[j]
        Sb.wait("vector", "ws", 16 * (j + 1) + self.ws_base)
        if j >= 2:
            Sb.wait("vector", "wo", self.wo_hist[j - 2])
        _emit_quant(Sb, nc, self.wqs[sl][:, 0:w], self.wstg[sl][:, 0:w],
                    self.qs, w)
        self.wq_hist.append(Sb.inc_last("vector", "wq", 1))
        if emit_sync:
            S.op("sync", lambda v=self.wq_hist[-1]:
                 nc.sync.wait_ge(S.sems["wq"], v))
            S.op("sync", lambda dst=dst, ch=ch, co=co, w=w, sl=sl:
                 nc.sync.dma_start(out=dst[128 * ch:128 * (ch + 1), co:co + w],
                                   in_=self.wqs[sl][:, 0:w]))
            self.wo_hist.append(S.inc_last("sync", "wo", 16))
        else:
            self.wo_hist.append(S.C["wo"].inc(16))
        self._indma(S, j + 2, emit_sync)
        self.k = j + 1


def build_ffn1(debug=False):
    II = 2816
    nc = bass.Bass("TRN2", target_bir_lowering=False)
    io = {}
    io["xfm"] = nc.dram_tensor("xfm", [D, TPC], F32, kind="ExternalInput").ap()
    io["wg"] = nc.dram_tensor("wg", [II, D], F32, kind="ExternalInput").ap()
    io["wu"] = nc.dram_tensor("wu", [II, D], F32, kind="ExternalInput").ap()
    io["wd"] = nc.dram_tensor("wd", [D, II], F32, kind="ExternalInput").ap()
    io["ident"] = nc.dram_tensor("ident", [128, 128], F32, kind="ExternalInput").ap()
    io["x4"] = nc.dram_tensor("x4", [D, TPC], F32, kind="ExternalOutput").ap()
    kind2 = "ExternalOutput" if debug else "Internal"
    io["wgq"] = nc.dram_tensor("wgq", [II, D], BF16, kind=kind2).ap()
    io["wuq"] = nc.dram_tensor("wuq", [II, D], BF16, kind="Internal").ap()
    io["wdq"] = nc.dram_tensor("wdq", [D, II], BF16, kind="Internal").ap()
    io["xqd"] = nc.dram_tensor("xqd", [TPC, D], BF16, kind=kind2).ap()
    io["hqd"] = nc.dram_tensor("hqd", [1024, II], BF16, kind=kind2).ap()
    if debug:
        io["dbg_g"] = nc.dram_tensor("dbg_g", [128, 512], F32, kind="ExternalOutput").ap()
        io["dbg_h"] = nc.dram_tensor("dbg_h", [128, 512], F32, kind="ExternalOutput").ap()
    with contextlib.ExitStack() as es:
        AL = Alloc(nc, es)
        S = Sched(nc, es, FFN1_SEMS)
        S.P = es.enter_context(nc.psum_tensor("P", [128, 4096], F32)).ap()
        block = es.enter_context(nc.Block())
        emit_ffn1(S, AL, io, debug)
        S.emit(block)
    return nc


def run_ffn1(x_shards, wg, wu, wd, res, debug=False):
    nc = build_ffn1(debug)
    ident = np.eye(128, dtype=np.float32)
    in_maps = [dict(xfm=x_shards[c], wg=wg, wu=wu, wd=wd, ident=ident)
               for c in range(NCORES)]
    out = run_bass_kernel_spmd(nc, in_maps, core_ids=list(range(NCORES)))
    res.append(out)
    if debug:
        return [dict(r) for r in out.results]
    return [r["x4"] for r in out.results]



def _gather_seq(shards, b):
    out = np.zeros((S, shards[0].shape[1]), dtype=bfloat16)
    for core in range(4 * b, 4 * b + 4):
        out[_local_positions(core)] = np.asarray(shards[core])
    return out


def kernel_unfused(L0_attn_x, L0_attn_Wq, L0_attn_bq, L0_attn_Wk, L0_attn_bk,
           L0_attn_Wv, L0_attn_bv, L0_attn_Wo, L0_attn_bo,
           L0_ffn_w_freq_real, L0_ffn_w_freq_imag, L0_ffn_bias,
           L1_attn_Wq, L1_attn_Wk, L1_attn_Wv, L1_attn_Wo,
           L1_ffn_w_gate, L1_ffn_w_up, L1_ffn_w_down):
    res = []
    x_shards = [_shard_x_fm(np.asarray(L0_attn_x), c) for c in range(NCORES)]

    def qperm_w(w):
        return np.ascontiguousarray(
            np.asarray(w).reshape(-1, 16, 64)[:, QPERM].reshape(w.shape))

    q, k, v = run_qkv(0, x_shards, dict(
        wq=qperm_w(L0_attn_Wq), wk=np.asarray(L0_attn_Wk),
        wv=np.asarray(L0_attn_Wv),
        bq=np.asarray(L0_attn_bq).reshape(16, 64)[QPERM].reshape(1024),
        bk=np.asarray(L0_attn_bk), bv=np.asarray(L0_attn_bv)), res)
    k_b = [_gather_seq(k, b) for b in range(B)]
    v_b = [_gather_seq(v, b) for b in range(B)]
    x1r = run_attn(0, q, k_b, v_b, x_shards, np.asarray(L0_attn_Wo),
                   np.asarray(L0_attn_bo), res)
    x1 = [np.asarray(r["x1"]) for r in x1r]
    A = run_amat(np.asarray(L0_ffn_w_freq_real),
                 np.asarray(L0_ffn_w_freq_imag), res)
    x2 = run_ffn0(x1, A, np.asarray(L0_ffn_bias), res)
    x2 = [np.asarray(t) for t in x2]
    q, k, v = run_qkv(1, x2, dict(
        wq=qperm_w(L1_attn_Wq), wk=np.asarray(L1_attn_Wk),
        wv=np.asarray(L1_attn_Wv)), res)
    k_b = [_gather_seq(k, b) for b in range(B)]
    v_b = [_gather_seq(v, b) for b in range(B)]
    zero_bo = np.zeros(1024, np.float32)
    x3r = run_attn(1, q, k_b, v_b, x2, np.asarray(L1_attn_Wo), zero_bo, res)
    x3 = [np.asarray(r["x1"]) for r in x3r]
    x4 = run_ffn1(x3, np.asarray(L1_ffn_w_gate), np.asarray(L1_ffn_w_up),
                  np.asarray(L1_ffn_w_down), res)
    out = np.zeros((B, S, D), np.float32)
    for core in range(NCORES):
        out[core // 4][_local_positions(core)] = np.asarray(x4[core]).T
    return out



FUSED_SEMS = sorted(set(ATTN_SEMS + FFN0_SEMS + FFN1_SEMS +
                        ["mm", "rp", "st", "q", "cc", "ph", "cid"]))


class BodySched:

    def __init__(self, parent):
        self.nc = parent.nc
        self.sems = parent.sems
        self.C = parent.C
        self.P = parent.P
        self.ops = {k: [] for k in parent.ops}
    op = Sched.op
    wait = Sched.wait
    inc_last = Sched.inc_last


def _barrier(S):
    nc = S.nc
    for eng in ["sync", "gpsimd", "tensor", "scalar", "vector"]:
        S.op(eng, lambda eng=eng: getattr(nc, eng).nop())
        S.inc_last(eng, "ph", 1)
    v = S.C["ph"].v
    for eng in ["sync", "gpsimd", "tensor", "scalar", "vector"]:
        S.wait(eng, "ph", v)


def build_fused(debug=False):
    II = 2816
    nc = bass.Bass("TRN2", target_bir_lowering=False)
    io = {}

    def inp(n, sh, dt=F32):
        io[n] = nc.dram_tensor(n, sh, dt, kind="ExternalInput").ap()

    def intl(n, sh, dt):
        kind = "ExternalOutput" if (debug and n in
                                    ("x1", "x2", "x3", "qtm0", "qtm1")) else "Internal"
        io[n] = nc.dram_tensor(n, sh, dt, kind=kind).ap()

    inp("xfm", [D, TPC])
    inp("cid", [1, 1], mybir.dt.int32)
    inp("cosT", [TPC, 32]); inp("sinT", [TPC, 32])
    inp("wq0", [D, 1024]); inp("wk0", [D, 256]); inp("wv0", [D, 256])
    inp("bqkv0", [1, 1536])
    inp("wo0", [1024, D]); inp("bo0", [128, 8])
    inp("dcolT", [128, 16]); inp("trowT", [128, 512])
    inp("wfr", [D, D]); inp("wfi", [D, D]); inp("Cm", [D, D]); inp("Sm", [D, D])
    inp("Csl", [D, 128]); inp("Ssl", [D, 128]); inp("fb0", [128, 8])
    inp("wq1", [D, 1024]); inp("wk1", [D, 256]); inp("wv1", [D, 256])
    inp("wo1", [1024, D]); inp("bz", [128, 8])
    inp("albT", [128, 64 * 16])
    inp("wg", [II, D]); inp("wu", [II, D]); inp("wd", [D, II])
    inp("ident", [128, 128])
    io["out"] = nc.dram_tensor("out", [D, TPC], F32, kind="ExternalOutput").ap()
    for L in (0, 1):
        intl(f"qtm{L}", [TPC, 1024], BF16)
        intl(f"kvc{L}", [TPC, 512], BF16)
        intl(f"kvg{L}", [S_, 512], BF16)
        intl(f"dscr{L}", [2 * 64 * 512], BF16)
        intl(f"rscr{L}", [64 * 512], BF16)
    intl("x1", [D, TPC], F32)
    intl("x2", [D, TPC], F32)
    intl("x3", [D, TPC], F32)
    intl("Arc", [128, D], F32)
    intl("Ag", [D, D], F32)
    intl("wgq", [II, D], BF16)
    intl("wuq", [II, D], BF16)
    intl("wdq", [D, II], BF16)
    intl("xqd", [TPC, D], BF16)
    intl("hqd", [1024, II], BF16)

    with contextlib.ExitStack() as es:
        arena = es.enter_context(nc.sbuf_tensor("arena", [128, 53000], F32)).ap()
        cid_sb = es.enter_context(nc.sbuf_tensor("cid_sb", [1, 1], mybir.dt.int32)).ap()
        AL = Alloc(nc, es, arena=arena)
        S = Sched(nc, es, FUSED_SEMS)
        S.P = es.enter_context(nc.psum_tensor("P", [128, 4096], F32)).ap()
        C = S.C
        op, inc_last = S.op, S.inc_last

        op("sync", lambda: nc.sync.dma_start(out=cid_sb, in_=io["cid"]))
        inc_last("sync", "cid", 16)

        RBASE = 50080
        wq = WQuant(S, AL, RBASE, io)
        assert wq.top <= 53000, wq.top
        AL.limit = RBASE
        wq.start(S, True)

        emit_qkv(S, AL, dict(xfm=io["xfm"], wq=io["wq0"], wk=io["wk0"],
                             wv=io["wv0"], bqkv=io["bqkv0"], cosT=io["cosT"],
                             sinT=io["sinT"], qtm=io["qtm0"],
                             ktm=io["kvc0"][:, 0:256], vtm=io["kvc0"][:, 256:512]),
                 True)
        op("gpsimd", lambda v=C["st"].v: nc.gpsimd.wait_ge(S.sems["st"], v))
        op("gpsimd", lambda: nc.gpsimd.collective_compute(
            "AllGather", ALU.bypass, replica_groups=[[0, 1, 2, 3], [4, 5, 6, 7]],
            ins=[io["kvc0"].opt()], outs=[io["kvg0"].opt()]))
        inc_last("gpsimd", "cc", 1)
        _barrier(S)

        AL.reset()
        emit_amat(S, AL, dict(wfr=io["wfr"], wfi=io["wfi"], Cm=io["Cm"],
                              Sm=io["Sm"], Csl=io["Csl"], Ssl=io["Ssl"],
                              Ar=io["Arc"]))
        op("gpsimd", lambda: nc.gpsimd.collective_compute(
            "AllGather", ALU.bypass, replica_groups=[[0, 1, 2, 3, 4, 5, 6, 7]],
            ins=[io["Arc"].opt()], outs=[io["Ag"].opt()]))
        inc_last("gpsimd", "cc", 1)
        _barrier(S)

        def attn_phase(layer, qtm, kvg, xin, wo, bo, masks, xout, dscr, rscr):
            AL.reset()
            T = attn_tensors(AL, layer)
            iox = dict(qtm=qtm, kall=kvg[:, 0:256], vall=kvg[:, 256:512],
                       xfm=xin, wo=wo, bo=bo, x1=xout, dscr=dscr, rscr=rscr,
                       **masks)
            op("sync", lambda v=C["cc"].v: nc.sync.wait_ge(S.sems["cc"], v))
            emit_attn_pre(S, T, iox, layer, rank_major=True)
            if layer == 1:
                emit_attn_core(S, S, T, iox, 1, 0, emit_sync=True,
                               filler=lambda Sb_, es_: wq.step(Sb_, es_))
            else:
                marks = {k: len(S.ops[k]) for k in S.ops}
                snap = {k: C[k].v for k in C}
                bodies = []
                finals = []
                wqsnap = wq.snapshot()
                for ccv in range(4):
                    for k in C:
                        C[k].v = snap[k]
                    wq.restore(wqsnap)
                    Sb = BodySched(S)
                    emit_attn_core(S, Sb, T, iox, 0, ccv,
                                   emit_sync=(ccv == 0),
                                   filler=lambda Sb_, es_: wq.step(Sb_, es_))
                    bodies.append(Sb)
                    finals.append({k: C[k].v for k in C})
                assert all(f == finals[0] for f in finals), \
                    f"switch bodies diverge: {[{k: v for k, v in f.items() if f != finals[0]} for f in finals]}"
                S.switch_at = (marks, bodies)
            emit_attn_post(S, T, iox, layer)

        attn_phase(0, io["qtm0"], io["kvg0"], io["xfm"], io["wo0"], io["bo0"],
                   dict(dcolT=io["dcolT"], trowT=io["trowT"]), io["x1"],
                   io["dscr0"], io["rscr0"])
        print("wq jobs after attn0:", wq.k, "/", len(wq.jobs))
        _barrier(S)

        AL.reset()
        emit_ffn0(S, AL, dict(xfm=io["x1"], Am=io["Ag"], fb=io["fb0"],
                              x2=io["x2"]))
        _barrier(S)

        AL.reset()
        emit_qkv(S, AL, dict(xfm=io["x2"], wq=io["wq1"], wk=io["wk1"],
                             wv=io["wv1"], cosT=io["cosT"], sinT=io["sinT"],
                             qtm=io["qtm1"], ktm=io["kvc1"][:, 0:256],
                             vtm=io["kvc1"][:, 256:512]), False)
        op("gpsimd", lambda v=C["st"].v: nc.gpsimd.wait_ge(S.sems["st"], v))
        op("gpsimd", lambda: nc.gpsimd.collective_compute(
            "AllGather", ALU.bypass, replica_groups=[[0, 1, 2, 3], [4, 5, 6, 7]],
            ins=[io["kvc1"].opt()], outs=[io["kvg1"].opt()]))
        inc_last("gpsimd", "cc", 1)
        _barrier(S)

        attn_phase(1, io["qtm1"], io["kvg1"], io["x2"], io["wo1"], io["bz"],
                   dict(albT=io["albT"]), io["x3"], io["dscr1"], io["rscr1"])
        print("wq jobs after attn1:", wq.k, "/", len(wq.jobs))
        while wq.k < len(wq.jobs):
            wq.step(S, True)
        _barrier(S)
        AL.limit = None

        AL.reset()
        emit_ffn1(S, AL, dict(xfm=io["x3"], wg=io["wg"], wu=io["wu"],
                              wd=io["wd"], ident=io["ident"], x4=io["out"],
                              wgq=io["wgq"], wuq=io["wuq"], wdq=io["wdq"],
                              xqd=io["xqd"], hqd=io["hqd"]), False,
                   wq_pre=S.C["wo"].v)

        block = es.enter_context(nc.Block())
        marks, bodies = S.switch_at
        order = ["sync", "gpsimd", "tensor", "scalar", "vector"]

        @block.sync
        def _(sync):
            for f in S.ops["sync"]:
                f()

        @block.gpsimd
        def _(gpsimd):
            for f in S.ops["gpsimd"]:
                f()

        for eng in ["tensor", "scalar", "vector"]:
            for f in S.ops[eng][:marks[eng]]:
                f()
        for e in [nc.tensor, nc.scalar, nc.vector]:
            e.wait_ge(S.sems["cid"], 16)
        v = nc.values_load(cid_sb, engines=[mybir.EngineType.PE,
                                            mybir.EngineType.Activation,
                                            mybir.EngineType.DVE],
                           skip_runtime_bounds_check=True)
        for ccv in nc.Switch(engines=[nc.tensor, nc.scalar, nc.vector],
                             index=[v, v, v], n=4):
            for eng in ["tensor", "scalar", "vector"]:
                for f in bodies[ccv].ops[eng]:
                    f()
        for eng in ["tensor", "scalar", "vector"]:
            for f in S.ops[eng][marks[eng]:]:
                f()
        print("fused arena hwm:", AL.hwm, "f32/partition")
    return nc


_FUSED_CACHE = {}


def kernel(L0_attn_x, L0_attn_Wq, L0_attn_bq, L0_attn_Wk, L0_attn_bk,
           L0_attn_Wv, L0_attn_bv, L0_attn_Wo, L0_attn_bo,
           L0_ffn_w_freq_real, L0_ffn_w_freq_imag, L0_ffn_bias,
           L1_attn_Wq, L1_attn_Wk, L1_attn_Wv, L1_attn_Wo,
           L1_ffn_w_gate, L1_ffn_w_up, L1_ffn_w_down):
    debug = bool(_FUSED_CACHE.get("debug"))
    if "nc" not in _FUSED_CACHE:
        _FUSED_CACHE["nc"] = build_fused(debug)
    nc = _FUSED_CACHE["nc"]

    def qperm_w(w):
        return np.ascontiguousarray(
            np.asarray(w, np.float32).reshape(-1, 16, 64)[:, QPERM].reshape(w.shape))

    def colmajor8(b):
        return np.ascontiguousarray(np.asarray(b, np.float32).reshape(8, 128).T)

    Cm, Sm = _dft_tables()
    x = np.asarray(L0_attn_x)
    trow = np.tile(np.arange(512, dtype=np.float32)[None, :], (128, 1))
    alb = _host_alb()
    ident = np.eye(128, dtype=np.float32)
    bq0 = np.asarray(L0_attn_bq).reshape(16, 64)[QPERM].reshape(1024)
    bqkv0 = np.concatenate([bq0, np.asarray(L0_attn_bk),
                            np.asarray(L0_attn_bv)])[None, :].astype(np.float32)
    in_maps = []
    for core in range(NCORES):
        cos, sin = _rope_tables(core)
        sl = slice(128 * core, 128 * (core + 1))
        m = dict(
            xfm=_shard_x_fm(x, core),
            cid=np.array([[core % 4]], np.int32),
            cosT=cos, sinT=sin,
            wq0=qperm_w(L0_attn_Wq), wk0=np.asarray(L0_attn_Wk),
            wv0=np.asarray(L0_attn_Wv), bqkv0=bqkv0,
            wo0=np.asarray(L0_attn_Wo), bo0=colmajor8(L0_attn_bo),
            dcolT=_host_dcol(core % 4), trowT=trow,
            wfr=np.asarray(L0_ffn_w_freq_real), wfi=np.asarray(L0_ffn_w_freq_imag),
            Cm=Cm, Sm=Sm,
            Csl=np.ascontiguousarray(Cm[:, sl] / D),
            Ssl=np.ascontiguousarray(Sm[:, sl] / D),
            fb0=colmajor8(L0_ffn_bias),
            wq1=qperm_w(L1_attn_Wq), wk1=np.asarray(L1_attn_Wk),
            wv1=np.asarray(L1_attn_Wv), wo1=np.asarray(L1_attn_Wo),
            bz=np.zeros((128, 8), np.float32),
            albT=alb,
            wg=np.asarray(L1_ffn_w_gate), wu=np.asarray(L1_ffn_w_up),
            wd=np.asarray(L1_ffn_w_down), ident=ident,
        )
        in_maps.append(m)
    res = run_bass_kernel_spmd(nc, in_maps, core_ids=list(range(NCORES)))
    out = np.zeros((B, S_, D), np.float32)
    for core in range(NCORES):
        out[core // 4][_local_positions(core)] = np.asarray(res.results[core]["out"]).T
    if debug:
        _FUSED_CACHE["res"] = res
    return out


# version 10 (final)
# speedup vs baseline: 1.5565x; 1.0675x over previous
import contextlib
import numpy as np
import ml_dtypes
import concourse.bass as bass
import concourse.mybir as mybir
from concourse.bass_utils import run_bass_kernel_spmd

F32 = mybir.dt.float32
BF16 = mybir.dt.bfloat16
AF = mybir.ActivationFunctionType
ALU = mybir.AluOpType
bfloat16 = ml_dtypes.bfloat16

NCORES = 8
B = 2
S = 8192
D = 1024
NQ = 16
NKV = 4
HD = 64
TPC = 2048
NT = TPC // 128
ROPE_BASE = 10000.0
MAGIC1 = float(3 * 2 ** 22)
MAGICH = float(3 * 2 ** 21)
LN_EPS = 1e-5



def _strips(core):
    cc = core % 4
    return (cc, 7 - cc)


def _local_positions(core):
    s0, s1 = _strips(core)
    return np.concatenate([np.arange(s0 * 1024, (s0 + 1) * 1024),
                           np.arange(s1 * 1024, (s1 + 1) * 1024)])


def _rope_tables(core):
    inv_freq = (1.0 / ROPE_BASE ** (np.arange(0, HD, 2, dtype=np.float64) / HD))
    pos = _local_positions(core).astype(np.float64)
    ang = pos[:, None] * inv_freq[None, :]
    return np.cos(ang).astype(np.float32), np.sin(ang).astype(np.float32)


def _shard_x_fm(x, core):
    b = core // 4
    rows = x[b][_local_positions(core)]
    return np.ascontiguousarray(rows.T)


class Ctr:
    def __init__(self):
        self.v = 0

    def inc(self, n=1):
        self.v += n
        return self.v


def _bcast(ap, reps):
    p, w = ap.shape
    return ap.unsqueeze(1).broadcast_to([p, reps, w])


DTSIZE = {F32: 4, BF16: 2, mybir.dt.int32: 4}


class Alloc:

    def __init__(self, nc, es, arena=None):
        self.nc, self.es, self.arena = nc, es, arena
        self.off = 0
        self.hwm = 0

    def reset(self):
        self.phase_hwms = getattr(self, "phase_hwms", [])
        self.phase_hwms.append(self.off)
        self.off = 0

    def at(self, off, name, shape, dtype):
        p, w = shape
        nel = (w * DTSIZE[dtype] + 3) // 4
        assert off + nel <= self.arena.shape[1]
        view = self.arena[:, off:off + nel]
        if dtype != F32:
            view = view.bitcast(dtype)[:, 0:w]
        if p != 128:
            view = view[0:p, :]
        return view

    def sbuf(self, name, shape, dtype):
        if self.arena is None:
            return self.es.enter_context(self.nc.sbuf_tensor(name, shape, dtype)).ap()
        p, w = shape
        nel = (w * DTSIZE[dtype] + 3) // 4
        lim = getattr(self, "limit", None) or self.arena.shape[1]
        assert self.off + nel <= lim, \
            f"arena overflow at {name}: {self.off + nel} > {lim}"
        view = self.arena[:, self.off:self.off + nel]
        self.off += nel
        self.hwm = max(self.hwm, self.off)
        if dtype != F32:
            view = view.bitcast(dtype)[:, 0:w]
        if p != 128:
            view = view[0:p, :]
        return view



def emit_qkv(S, AL, io, has_bias):
    nc = S.nc
    op, wait, inc_last, C = S.op, S.wait, S.inc_last, S.C
    xb_sb = AL.sbuf("q_xb", [128, 8 * TPC], BF16)
    wq_sb = AL.sbuf("q_wq", [128, 8 * 1024], BF16)
    wk_sb = AL.sbuf("q_wk", [128, 8 * 256], BF16)
    wv_sb = AL.sbuf("q_wv", [128, 8 * 256], BF16)
    bias_sb = AL.sbuf("q_bias", [1, 1536], BF16)
    ones1 = AL.sbuf("q_ones1", [1, 128], BF16)
    cos_sb = AL.sbuf("q_cos", [128, NT * 32], F32)
    sin_sb = AL.sbuf("q_sin", [128, NT * 32], F32)
    qo_sb = AL.sbuf("q_qo", [128, 2 * 1024], BF16)
    ko_sb = AL.sbuf("q_ko", [128, 2 * 256], BF16)
    vo_sb = AL.sbuf("q_vo", [128, 2 * 256], BF16)
    rsc = AL.sbuf("q_rsc", [128, 512], F32)
    P = S.P
    pq = [P[:, 512 * i:512 * (i + 1)] for i in range(4)]
    pk = [P[:, 2048 + 512 * i:2048 + 512 * i + 256] for i in range(2)]
    pv = [P[:, 3072 + 512 * i:3072 + 512 * i + 256] for i in range(2)]

    op("sync", lambda: nc.sync.dma_start(
        out=cos_sb.rearrange("p (t i) -> p t i", i=32),
        in_=io["cosT"].rearrange("(t p) i -> p t i", p=128)))
    inc_last("sync", "ld", 16)
    op("sync", lambda: nc.sync.dma_start(
        out=sin_sb.rearrange("p (t i) -> p t i", i=32),
        in_=io["sinT"].rearrange("(t p) i -> p t i", p=128)))
    inc_last("sync", "ld", 16)
    op("gpsimd", lambda: nc.gpsimd.dma_start(
        out=xb_sb.rearrange("p (k t) -> p k t", k=8),
        in_=io["xfm"].rearrange("(k p) t -> p k t", p=128)))
    inc_last("gpsimd", "ld", 16)
    op("gpsimd", lambda: nc.gpsimd.dma_start(
        out=wq_sb.rearrange("p (k c) -> p k c", k=8),
        in_=io["wq"].rearrange("(k p) c -> p k c", p=128)))
    inc_last("gpsimd", "ld", 16)
    op("gpsimd", lambda: nc.gpsimd.dma_start(
        out=wk_sb.rearrange("p (k c) -> p k c", k=8),
        in_=io["wk"].rearrange("(k p) c -> p k c", p=128)))
    inc_last("gpsimd", "ld", 16)
    op("gpsimd", lambda: nc.gpsimd.dma_start(
        out=wv_sb.rearrange("p (k c) -> p k c", k=8),
        in_=io["wv"].rearrange("(k p) c -> p k c", p=128)))
    inc_last("gpsimd", "ld", 16)
    if has_bias:
        op("gpsimd", lambda: nc.gpsimd.dma_start(out=bias_sb, in_=io["bqkv"]))
        inc_last("gpsimd", "ld", 16)
    lds = C["ld"].v
    op("vector", lambda: nc.vector.memset(ones1, 1.0))
    inc_last("vector", "cv", 1)
    cvs = C["cv"].v

    rp_hist = []
    st_hist = []
    for t in range(NT):
        sl = t % 2
        if t == 0:
            wait("tensor", "ld", lds)
            wait("tensor", "cv", cvs)
        if t >= 2:
            wait("tensor", "rp", rp_hist[t - 2])
        lhs = lambda k, t=t: xb_sb[:, 2048 * k + 128 * t:2048 * k + 128 * (t + 1)]
        for b in range(2):
            for k in range(8):
                def qf(b=b, k=k, t=t, sl=sl):
                    return nc.tensor.matmul(
                        pq[2 * sl + b], lhs(k, t),
                        wq_sb[:, 1024 * k + 512 * b:1024 * k + 512 * (b + 1)],
                        start=(k == 0), stop=(k == 7 and not has_bias))
                op("tensor", qf)
            if has_bias:
                def qbf(b=b, sl=sl):
                    return nc.tensor.matmul(
                        pq[2 * sl + b], ones1, io_bias_slice(bias_sb, 512 * b, 512),
                        start=False, stop=True)
                op("tensor", qbf)
        for k in range(8):
            def kf(k=k, t=t, sl=sl):
                return nc.tensor.matmul(pk[sl], lhs(k, t),
                                        wk_sb[:, 256 * k:256 * (k + 1)],
                                        start=(k == 0),
                                        stop=(k == 7 and not has_bias))
            op("tensor", kf)
        if has_bias:
            op("tensor", lambda sl=sl: nc.tensor.matmul(
                pk[sl], ones1, io_bias_slice(bias_sb, 1024, 256),
                start=False, stop=True))
        for k in range(8):
            def vf(k=k, t=t, sl=sl):
                return nc.tensor.matmul(pv[sl], lhs(k, t),
                                        wv_sb[:, 256 * k:256 * (k + 1)],
                                        start=(k == 0),
                                        stop=(k == 7 and not has_bias))
            op("tensor", vf)
        if has_bias:
            op("tensor", lambda sl=sl: nc.tensor.matmul(
                pv[sl], ones1, io_bias_slice(bias_sb, 1280, 256),
                start=False, stop=True))
        inc_last("tensor", "mm", 1)
        mmv = C["mm"].v

        wait("vector", "mm", mmv)
        if t >= 2:
            wait("vector", "st", st_hist[t - 2])
        cosv = _bcast(cos_sb[:, 32 * t:32 * (t + 1)], 8)
        sinv = _bcast(sin_sb[:, 32 * t:32 * (t + 1)], 8)
        cosv4 = _bcast(cos_sb[:, 32 * t:32 * (t + 1)], 4)
        sinv4 = _bcast(sin_sb[:, 32 * t:32 * (t + 1)], 4)
        for b in range(2):
            Pq = pq[2 * sl + b].rearrange("p (h d) -> p h d", d=64)
            Plo, Phi = Pq[:, :, 0:32], Pq[:, :, 32:64]
            qov = qo_sb[:, 1024 * sl + 512 * b:1024 * sl + 512 * (b + 1)] \
                .rearrange("p (h d) -> p h d", d=64)
            Av = rsc[:, 0:256].rearrange("p (h d) -> p h d", d=32)
            Bv = rsc[:, 256:512].rearrange("p (h d) -> p h d", d=32)
            op("vector", lambda Phi=Phi, Av=Av, sinv=sinv:
               nc.vector.tensor_tensor(Av, Phi, sinv, ALU.mult))
            op("vector", lambda Plo=Plo, Bv=Bv, cosv=cosv:
               nc.vector.tensor_tensor(Bv, Plo, cosv, ALU.mult))
            op("vector", lambda qov=qov, Bv=Bv, Av=Av:
               nc.vector.tensor_tensor(qov[:, :, 0:32], Bv, Av, ALU.subtract))
            op("vector", lambda Plo=Plo, Av=Av, sinv=sinv:
               nc.vector.tensor_tensor(Av, Plo, sinv, ALU.mult))
            op("vector", lambda Phi=Phi, Bv=Bv, cosv=cosv:
               nc.vector.tensor_tensor(Bv, Phi, cosv, ALU.mult))
            op("vector", lambda qov=qov, Bv=Bv, Av=Av:
               nc.vector.tensor_tensor(qov[:, :, 32:64], Bv, Av, ALU.add))
        Pk = pk[sl].rearrange("p (h d) -> p h d", d=64)
        Plo, Phi = Pk[:, :, 0:32], Pk[:, :, 32:64]
        kov = ko_sb[:, 256 * sl:256 * (sl + 1)].rearrange("p (h d) -> p h d", d=64)
        Av = rsc[:, 0:128].rearrange("p (h d) -> p h d", d=32)
        Bv = rsc[:, 256:384].rearrange("p (h d) -> p h d", d=32)
        op("vector", lambda Phi=Phi, Av=Av, sinv4=sinv4:
           nc.vector.tensor_tensor(Av, Phi, sinv4, ALU.mult))
        op("vector", lambda Plo=Plo, Bv=Bv, cosv4=cosv4:
           nc.vector.tensor_tensor(Bv, Plo, cosv4, ALU.mult))
        op("vector", lambda kov=kov, Bv=Bv, Av=Av:
           nc.vector.tensor_tensor(kov[:, :, 0:32], Bv, Av, ALU.subtract))
        op("vector", lambda Plo=Plo, Av=Av, sinv4=sinv4:
           nc.vector.tensor_tensor(Av, Plo, sinv4, ALU.mult))
        op("vector", lambda Phi=Phi, Bv=Bv, cosv4=cosv4:
           nc.vector.tensor_tensor(Bv, Phi, cosv4, ALU.mult))
        op("vector", lambda kov=kov, Bv=Bv, Av=Av:
           nc.vector.tensor_tensor(kov[:, :, 32:64], Bv, Av, ALU.add))
        op("vector", lambda sl=sl: nc.vector.tensor_copy(
            vo_sb[:, 256 * sl:256 * (sl + 1)], pv[sl]))
        rp_hist.append(inc_last("vector", "rp", 1))

        op("sync", lambda v=rp_hist[-1]: nc.sync.wait_ge(S.sems["rp"], v))
        op("sync", lambda t=t, sl=sl: nc.sync.dma_start(
            out=io["qtm"][128 * t:128 * (t + 1), :],
            in_=qo_sb[:, 1024 * sl:1024 * (sl + 1)]))
        inc_last("sync", "st", 16)
        op("sync", lambda t=t, sl=sl: nc.sync.dma_start(
            out=io["ktm"][128 * t:128 * (t + 1), :],
            in_=ko_sb[:, 256 * sl:256 * (sl + 1)]))
        inc_last("sync", "st", 16)
        op("sync", lambda t=t, sl=sl: nc.sync.dma_start(
            out=io["vtm"][128 * t:128 * (t + 1), :],
            in_=vo_sb[:, 256 * sl:256 * (sl + 1)]))
        st_hist.append(inc_last("sync", "st", 16))
    op("sync", lambda v=C["st"].v: nc.sync.wait_ge(S.sems["st"], v))
    return C["st"].v


def io_bias_slice(bias_sb, off, n):
    return bias_sb[0:1, off:off + n]


def build_qkv(has_bias):
    nc = bass.Bass("TRN2", target_bir_lowering=False)
    io = {}
    io["xfm"] = nc.dram_tensor("xfm", [D, TPC], F32, kind="ExternalInput").ap()
    io["wq"] = nc.dram_tensor("wq", [D, NQ * HD], F32, kind="ExternalInput").ap()
    io["wk"] = nc.dram_tensor("wk", [D, NKV * HD], F32, kind="ExternalInput").ap()
    io["wv"] = nc.dram_tensor("wv", [D, NKV * HD], F32, kind="ExternalInput").ap()
    if has_bias:
        io["bqkv"] = nc.dram_tensor("bqkv", [1, 1536], F32, kind="ExternalInput").ap()
    io["cosT"] = nc.dram_tensor("cosT", [TPC, 32], F32, kind="ExternalInput").ap()
    io["sinT"] = nc.dram_tensor("sinT", [TPC, 32], F32, kind="ExternalInput").ap()
    io["qtm"] = nc.dram_tensor("qtm", [TPC, NQ * HD], BF16, kind="ExternalOutput").ap()
    io["ktm"] = nc.dram_tensor("ktm", [TPC, NKV * HD], BF16, kind="ExternalOutput").ap()
    io["vtm"] = nc.dram_tensor("vtm", [TPC, NKV * HD], BF16, kind="ExternalOutput").ap()
    with contextlib.ExitStack() as es:
        AL = Alloc(nc, es)
        S = Sched(nc, es, ["ld", "cv", "mm", "rp", "st"])
        S.P = es.enter_context(nc.psum_tensor("P", [128, 4096], F32)).ap()
        block = es.enter_context(nc.Block())
        emit_qkv(S, AL, io, has_bias)
        S.emit(block)
    return nc


def run_qkv(layer, x_shards, weights, res):
    has_bias = (layer == 0)
    nc = build_qkv(has_bias)
    in_maps = []
    for core in range(NCORES):
        cos, sin = _rope_tables(core)
        m = dict(xfm=x_shards[core], wq=weights["wq"], wk=weights["wk"],
                 wv=weights["wv"], cosT=cos, sinT=sin)
        if has_bias:
            m["bqkv"] = np.concatenate([weights["bq"], weights["bk"],
                                        weights["bv"]])[None, :].astype(np.float32)
        in_maps.append(m)
    out = run_bass_kernel_spmd(nc, in_maps, core_ids=list(range(NCORES)))
    res.extend([out])
    return [r["qtm"] for r in out.results], [r["ktm"] for r in out.results], \
           [r["vtm"] for r in out.results]





def _alibi_slopes16():
    start = 2.0 ** (-2.0 ** (-(np.log2(16) - 3)))
    return np.array([start * start ** i for i in range(16)], dtype=np.float64)


def _l1_window_tiles():
    sl = _alibi_slopes16()
    w = np.ceil((115.0 / sl) / 128.0).astype(int)
    return np.minimum(w, 64)


def _pair_heads(p):
    base = 8 if p >= 4 else 0
    a = base + (p % 4)
    return a, a + 4


QPERM = []
for _p in range(8):
    _a, _b = _pair_heads(_p)
    QPERM += [_a, _b]


def attn_tensors(AL, layer):
    causal = (layer == 0)
    T = {}
    T["K_sb"] = [AL.sbuf(f"a_K{i}", [128, S_], BF16) for i in range(2)]
    T["V_sb"] = AL.sbuf("a_V", [128, 64 * 4 * 65], BF16)
    T["Q_sb"] = AL.sbuf("a_Q", [128, 2 * TPC], BF16)
    T["eX"] = AL.sbuf("a_eX", [128, 2 * 1024], BF16)
    T["Ofm"] = AL.sbuf("a_Ofm", [128, 8 * TPC], BF16)
    T["dsc64"] = AL.sbuf("a_dsc64", [128, 4096], BF16)
    T["denp"] = AL.sbuf("a_denp", [128, 2 * 256], BF16)
    T["rp"] = AL.sbuf("a_rp", [128, 256], BF16)
    T["rrow"] = AL.sbuf("a_rrow", [1, 8 * 512], BF16)
    T["wo_sb"] = AL.sbuf("a_wo", [128, 8 * 1024], BF16)
    T["bo_sb"] = AL.sbuf("a_bo", [128, 8], F32)
    T["ones1"] = AL.sbuf("a_ones1", [1, 128], BF16)
    T["oneK"] = AL.sbuf("a_oneK", [128, 1], BF16)
    T["msk"] = AL.sbuf("a_msk", [128, 512], BF16)
    if causal:
        T["dcol"] = AL.sbuf("a_dcol", [128, 16], F32)
        T["trow"] = AL.sbuf("a_trow", [128, 512], F32)
    else:
        T["alb"] = AL.sbuf("a_alb", [128, 64 * 16], F32)
    T["rb_sb"] = AL.sbuf("a_rb", [128, 512], BF16)
    T["On_sb"] = AL.sbuf("a_On", [128, 8 * 512], BF16)
    T["xstg"] = AL.sbuf("a_xstg", [128, 2 * 512], F32)
    T["s_c"] = AL.sbuf("a_sc", [128, 8 * 512], F32)
    T["sbf"] = AL.sbuf("a_sbf", [128, 2 * 512], BF16)
    T["ssq"] = AL.sbuf("a_ssq", [128, 2 * 512], BF16)
    T["stat"] = AL.sbuf("a_stat", [1, 2560], F32)
    T["x1stg"] = AL.sbuf("a_x1stg", [128, 2 * 512], F32)
    if causal:
        T["msk4"] = T["s_c"].bitcast(BF16)[:, 0:2048]
    return T


S_ = S


def attn_psum(S):
    P = S.P
    return dict(Sp=[P[:, 0:1024], P[:, 1024:2048]],
                Oeo=[[P[0:65, 2048:2560], P[0:65, 2560:3072]],
                     [P[0:65, 3072:3584], P[0:65, 3584:4096]]],
                pbcF=P[:, 0:512], py=[P[:, 1024:1536], P[:, 1536:2048]],
                pmean=P[0:1, 2048:2560], pmsq=P[0:1, 2560:3072],
                prb=P[:, 3072:3584], pmr=P[:, 3584:4096])


def emit_attn_pre(S, T, io, layer, rank_major=False):
    nc = S.nc
    causal = (layer == 0)
    op, wait, inc_last, C = S.op, S.wait, S.inc_last, S.C
    if not rank_major:
        srcrows = [(1024 * s, 1024) for s in range(8)]
    else:
        srcrows = []
        for s in range(8):
            r = min(s, 7 - s)
            part = 0 if s < 4 else 1
            srcrows.append((2048 * r + 1024 * part, 1024))
    for t in range(2):
        for s in range(8):
            sr = srcrows[s][0]
            def ktf(t=t, s=s, sr=sr):
                return nc.sync.dma_start_transpose(
                    T["K_sb"][t][:, 1024 * s:1024 * (s + 1)],
                    io["kall"][sr:sr + 1024, 128 * t:128 * (t + 1)])
            op("sync", ktf)
            inc_last("sync", "ld", 16)
    op("vector", lambda: nc.vector.memset(T["V_sb"], 1.0))
    op("vector", lambda: nc.vector.memset(T["ones1"], 1.0))
    op("vector", lambda: nc.vector.memset(T["oneK"], 1.0 / 1024.0))
    inc_last("vector", "cv", 1)
    op("sync", lambda v=C["cv"].v: nc.sync.wait_ge(S.sems["cv"], v))
    for jv in range(4):
        for s in range(8):
            sr = srcrows[s][0]
            vdst = T["V_sb"].rearrange("p (c j d) -> p c j d", c=64, j=4)[
                :, 8 * s:8 * (s + 1), jv, 0:64]
            vsrc = io["vall"][sr:sr + 1024, :].rearrange(
                "(c p) jd -> p c jd", p=128)[:, :, 64 * jv:64 * (jv + 1)]
            op("sync", lambda vdst=vdst, vsrc=vsrc:
               nc.sync.dma_start(out=vdst, in_=vsrc))
            inc_last("sync", "ld", 16)
    if causal:
        op("sync", lambda: nc.sync.dma_start(out=T["dcol"], in_=io["dcolT"]))
        inc_last("sync", "ld", 16)
        op("sync", lambda: nc.sync.dma_start(out=T["trow"], in_=io["trowT"]))
        inc_last("sync", "ld", 16)
        if "dmskT" in io:
            op("sync", lambda: nc.sync.dma_start(out=T["msk4"], in_=io["dmskT"]))
            inc_last("sync", "ld", 16)
    else:
        op("sync", lambda: nc.sync.dma_start(out=T["alb"], in_=io["albT"]))
        inc_last("sync", "ld", 16)
    op("sync", lambda: nc.sync.dma_start(out=T["bo_sb"], in_=io["bo"]))
    inc_last("sync", "ld", 16)
    op("gpsimd", lambda: nc.gpsimd.dma_start(
        out=T["wo_sb"].rearrange("p (k c) -> p k c", k=8),
        in_=io["wo"].rearrange("(k p) c -> p k c", p=128)))
    inc_last("gpsimd", "ld", 16)
    return C["ld"].v


def emit_attn_core(S, Sb, T, io, layer, cc, emit_sync=True, filler=None):
    nc = S.nc
    causal = (layer == 0)
    strips = (cc, 7 - cc)
    wtiles = _l1_window_tiles() if not causal else None

    def fill_positions(hh, js):
        if filler is None:
            return ()
        n = len(js)
        if causal:
            if hh != 1:
                return ()
            q = 6
        else:
            q = 5 if n >= 64 else (2 if n >= 10 else 1)
        stepw = max(2, (n - 2) // q)
        pos, x = [], 1
        for _ in range(q):
            if x >= n:
                break
            pos.append(x)
            x += stepw
        return tuple(pos)
    PV = attn_psum(S)
    Sp, Oeo = PV["Sp"], PV["Oeo"]
    C = S.C
    lds = C["ld"].v

    def jrange(p, hh, c_):
        if causal:
            return list(range(8 * strips[hh] + 4 * (c_ + 1)))
        a, b = _pair_heads(p)
        return list(range(64 - max(wtiles[a], wtiles[b]), 64))

    def active(p, j, ab):
        if causal:
            return True
        return j >= 64 - wtiles[_pair_heads(p)[ab]]

    def q_load(p):
        S.op("sync", lambda p=p, sl=p % 2: nc.sync.dma_start_transpose(
            T["Q_sb"][:, TPC * sl:TPC * (sl + 1)], io["qtm"][:, 128 * p:128 * (p + 1)]))
        S.inc_last("sync", "q", 16)

    qb0 = C["q"].v
    if emit_sync:
        q_load(0)
        q_load(1)
    else:
        C["q"].inc(32)

    ex_hist = []
    mk_hist = []
    pv_hist = []
    pend_pv = []
    ucount = 0
    for p in range(8):
        hA, hB = _pair_heads(p)
        kv = [hA // 4, hB // 4]
        kt = T["K_sb"][0] if p < 4 else T["K_sb"][1]
        ktb = [64 * (kv[0] % 2), 64 * (kv[1] % 2)]
        qsl = p % 2
        for hh in range(2):
            for c_ in (0, 1):
                js = jrange(p, hh, c_)
                tqo = TPC * qsl + 1024 * hh + 512 * c_
                fpos = fill_positions(hh, js)
                for ji, j in enumerate(js):
                    sl = ucount % 2
                    acts = [ab for ab in range(2) if active(p, j, ab)]
                    mi = None
                    if causal:
                        m0 = 8 * strips[hh] + 4 * c_
                        if m0 <= j < m0 + 4:
                            mi = j - m0
                    if ucount == 0:
                        Sb.wait("tensor", "ld", lds)
                    Sb.wait("tensor", "q", qb0 + 16 * (p + 1))
                    if ucount >= 2:
                        Sb.wait("tensor", "ex", ex_hist[ucount - 2])
                    for ab in acts:
                        def mmf(ab=ab, j=j, sl=sl, tqo=tqo, kt=kt, kb=ktb[ab]):
                            return nc.tensor.matmul(
                                Sp[sl][:, 512 * ab:512 * (ab + 1)],
                                kt[kb:kb + 64, 128 * j:128 * (j + 1)],
                                T["Q_sb"][64 * ab:64 * ab + 64, tqo:tqo + 512],
                                start=True, stop=True,
                                tile_position=(64 * ab, 0))
                        Sb.op("tensor", mmf)
                    Sb.inc_last("tensor", "qk", 1)
                    if pend_pv:
                        pend_pv.pop(0)()

                    Sb.wait("scalar", "qk", C["qk"].v)
                    if ucount >= 2:
                        Sb.wait("scalar", "pv", pv_hist[ucount - 2])
                    if causal:
                        def ef(sl=sl):
                            return nc.scalar.activation(
                                T["eX"][:, 1024 * sl:1024 * (sl + 1)], Sp[sl],
                                AF.Exp, scale=0.125)
                        Sb.op("scalar", ef)
                    else:
                        for ab in acts:
                            h = _pair_heads(p)[ab]
                            def ef(sl=sl, ab=ab, j=j, h=h):
                                return nc.scalar.activation(
                                    T["eX"][:, 1024 * sl + 512 * ab:
                                            1024 * sl + 512 * (ab + 1)],
                                    Sp[sl][:, 512 * ab:512 * (ab + 1)],
                                    AF.Exp, scale=0.125,
                                    bias=T["alb"][:, 16 * j + h:16 * j + h + 1])
                            Sb.op("scalar", ef)
                    Sb.inc_last("scalar", "ex", 1)
                    ex_hist.append(C["ex"].v)

                    if causal and mi is not None:
                        Sb.wait("vector", "ex", C["ex"].v)
                        for ab in range(2):
                            def mtf(sl=sl, ab=ab, mi=mi):
                                v = T["eX"][:, 1024 * sl + 512 * ab:
                                            1024 * sl + 512 * (ab + 1)]
                                return nc.vector.tensor_tensor(
                                    v, v, T["msk4"][:, 512 * mi:512 * (mi + 1)],
                                    ALU.mult)
                            Sb.op("vector", mtf)
                        Sb.inc_last("vector", "mk", 1)
                    mk_hist.append(C["mk"].v)

                    def emit_pv(u=ucount, sl=sl, acts=tuple(acts), j=j,
                                p=p, hh=hh, c_=c_, js=tuple(js)):
                        Sb.wait("tensor", "ex", ex_hist[u])
                        Sb.wait("tensor", "mk", mk_hist[u])
                        for ab in acts:
                            h = _pair_heads(p)[ab]
                            if causal:
                                st_ = (j == 0)
                                sp_ = (j == js[-1])
                            else:
                                st_ = (j == 64 - wtiles[h])
                                sp_ = (j == 63)
                            for eo in range(2):
                                def pvf(ab=ab, sl=sl, j=j, eo=eo,
                                        kvi=kv[ab], st_=st_, sp_=sp_):
                                    vcol = (4 * j + kvi) * 65
                                    return nc.tensor.matmul(
                                        Oeo[ab][eo],
                                        T["V_sb"][64 * eo:64 * (eo + 1),
                                                  vcol:vcol + 65],
                                        T["eX"][64 * eo:64 * (eo + 1),
                                                1024 * sl + 512 * ab:
                                                1024 * sl + 512 * (ab + 1)],
                                        start=st_, stop=sp_,
                                        tile_position=(64 * eo, 0))
                                Sb.op("tensor", pvf)
                        Sb.inc_last("tensor", "pv", 1)
                        pv_hist.append(C["pv"].v)
                    pend_pv.append(emit_pv)
                    ucount += 1
                    if ji in fpos:
                        filler(Sb, emit_sync)

                while pend_pv:
                    pend_pv.pop(0)()
                Sb.wait("vector", "pv", C["pv"].v)
                for ab in range(2):
                    h = _pair_heads(p)[ab]
                    gc = 2 * hh + c_
                    def ecf(ab=ab):
                        return nc.vector.tensor_copy(
                            T["rb_sb"][0:64, :], Oeo[ab][0][0:64, :])
                    Sb.op("vector", ecf)
                    def evf(h=h, ab=ab, gc=gc):
                        return nc.vector.tensor_tensor(
                            T["Ofm"][(h % 2) * 64:(h % 2) * 64 + 64,
                                     TPC * (h // 2) + 512 * gc:
                                     TPC * (h // 2) + 512 * (gc + 1)],
                            T["rb_sb"][0:64, :], Oeo[ab][1][0:64, :], ALU.add)
                    Sb.op("vector", evf)
                    for eo in range(2):
                        def dnf(ab=ab, c_=c_, eo=eo):
                            return nc.vector.tensor_copy(
                                T["dsc64"][64:65, 2048 * eo + 512 * (2 * ab + c_):
                                           2048 * eo + 512 * (2 * ab + c_ + 1)],
                                Oeo[ab][eo][64:65, :])
                        Sb.op("vector", dnf)
                Sb.inc_last("vector", "oev", 1)
                Sb.wait("tensor", "oev", C["oev"].v)

                if c_ == 1:
                    phi = p * 2 + hh
                    if emit_sync:
                        S.op("sync", lambda v=C["oev"].v: nc.sync.wait_ge(S.sems["oev"], v))
                        for eo in range(2):
                            def ddf(phi=phi, eo=eo):
                                return nc.sync.dma_start(
                                    out=io["dscr"].rearrange("(e a w) -> e a w", e=2, w=2048)[eo, phi:phi + 1, :],
                                    in_=T["dsc64"][64:65, 2048 * eo:2048 * (eo + 1)])
                            S.op("sync", ddf)
                            S.inc_last("sync", "dd", 16)
                    else:
                        C["dd"].inc(32)
                    Sb.wait("vector", "dd", C["dd"].v)
        if p + 2 < 8:
            if emit_sync:
                S.op("sync", lambda v=C["qk"].v: nc.sync.wait_ge(S.sems["qk"], v))
                q_load(p + 2)
            else:
                C["q"].inc(16)


def emit_attn_post(S, T, io, layer):
    nc = S.nc
    op, wait, inc_last, C = S.op, S.wait, S.inc_last, S.C
    PV = attn_psum(S)
    pbcF, py = PV["pbcF"], PV["py"]
    pmean, pmsq, prb, pmr = PV["pmean"], PV["pmsq"], PV["prb"], PV["pmr"]

    def unit_of(h, gc):
        if h < 4:
            p_, ab = h, 0
        elif h < 8:
            p_, ab = h - 4, 1
        elif h < 12:
            p_, ab = h - 4, 0
        else:
            p_, ab = h - 8, 1
        hh, c_ = gc // 2, gc % 2
        return ((p_ * 2 + hh) * 2 + ab) * 2 + c_

    op("sync", lambda v=C["dd"].v: nc.sync.wait_ge(S.sems["dd"], v))
    op("sync", lambda: nc.sync.dma_start(
        out=T["denp"].rearrange("p (e w) -> p e w", e=2),
        in_=io["dscr"].rearrange("(e p w) -> p e w", p=128, w=256)))
    inc_last("sync", "db", 16)
    wait("vector", "db", C["db"].v)
    op("vector", lambda: nc.vector.tensor_tensor(
        T["denp"][:, 0:256], T["denp"][:, 0:256], T["denp"][:, 256:512], ALU.add))

    def _recipf():
        with nc.allow_low_precision(reason="bf16 softmax normalizer"):
            return nc.vector.reciprocal(T["rp"], T["denp"][:, 0:256])
    op("vector", _recipf)
    inc_last("vector", "db", 1)
    op("sync", lambda v=C["db"].v: nc.sync.wait_ge(S.sems["db"], v))
    op("sync", lambda: nc.sync.dma_start(
        out=io["rscr"].rearrange("(p w) -> p w", w=256), in_=T["rp"]))
    inc_last("sync", "db", 16)

    stt_hist = []
    sm_hist = []
    x1s_hist = []
    xsb0 = C["xs"].v
    for c in range(4):
        op("sync", lambda v=C["db"].v: nc.sync.wait_ge(S.sems["db"], v))
        if c > 0:
            op("sync", lambda v=C["bc"].v: nc.sync.wait_ge(S.sems["bc"], v))
        hh_, c__ = c // 2, c % 2

        def rload(quart):
            ab_ = 1 if quart in (1, 3) else 0
            p0 = [0, 0, 4, 4][quart]
            u0 = ((p0 * 2 + hh_) * 2 + ab_) * 2 + c__

            def rldf(quart=quart, u0=u0):
                return nc.sync.dma_start(
                    out=T["rrow"][0:1, 2048 * (quart % 2):2048 * (quart % 2 + 1)]
                    .rearrange("a (h w) -> a h w", w=512),
                    in_=io["rscr"].rearrange("(u w) -> u w", w=512)[u0:u0 + 25:8, :]
                    .unsqueeze(0))
            op("sync", rldf)
            inc_last("sync", "x1s", 16)
        rload(0)
        rload(1)
        def xpf_emit(o, c):
            ko_ = 8 * c + o
            if ko_ >= 2:
                op("sync", lambda v=stt_hist[ko_ - 2]:
                   nc.sync.wait_ge(S.sems["stt"], v))

            def xpf(o=o, c=c):
                return nc.sync.dma_start(
                    out=T["xstg"][:, 512 * (o % 2):512 * (o % 2 + 1)],
                    in_=io["xfm"][128 * o:128 * (o + 1), 512 * c:512 * (c + 1)])
            op("sync", xpf)
            inc_last("sync", "xs", 16)
        xpf_emit(0, c)
        xpf_emit(1, c)
        for k in range(8):
            if k == 0:
                wait("tensor", "x1s", C["x1s"].v)
                if c == 0:
                    wait("tensor", "oev", C["oev"].v)
            if k == 4:
                op("sync", lambda v=C["bc"].v: nc.sync.wait_ge(S.sems["bc"], v))
                rload(2)
                rload(3)
                wait("tensor", "x1s", C["x1s"].v)
            for sub in range(2):
                h = 2 * k + sub
                quart = h // 4
                hl = (quart % 2) * 4 + (h % 4)

                def bcf(sub=sub, hl=hl):
                    return nc.tensor.matmul(
                        pbcF[64 * sub:64 * (sub + 1), :], T["ones1"][0:1, 0:64],
                        T["rrow"][0:1, 512 * hl:512 * (hl + 1)],
                        start=True, stop=True,
                        tile_position=(0, 64 * sub))
                op("tensor", bcf)
            inc_last("tensor", "bc", 1)
            wait("vector", "bc", C["bc"].v)
            op("vector", lambda: nc.vector.tensor_copy(T["rb_sb"], pbcF))

            def nrf(k=k, c=c):
                return nc.vector.tensor_tensor(
                    T["On_sb"][:, 512 * k:512 * (k + 1)],
                    T["Ofm"][:, TPC * k + 512 * c:TPC * k + 512 * (c + 1)],
                    T["rb_sb"], ALU.mult)
            op("vector", nrf)
            inc_last("vector", "nr", 1)
            wait("tensor", "nr", C["nr"].v)
        for o in range(8):
            ysl = o % 2
            ko = 8 * c + o
            if ko >= 2:
                wait("tensor", "stt", stt_hist[ko - 2])
            for k in range(8):
                def yf(o=o, k=k, ysl=ysl):
                    return nc.tensor.matmul(
                        py[ysl],
                        T["wo_sb"][:, 1024 * k + 128 * o:1024 * k + 128 * (o + 1)],
                        T["On_sb"][:, 512 * k:512 * (k + 1)],
                        start=(k == 0), stop=(k == 7))
                op("tensor", yf)
            inc_last("tensor", "y", 1)
            wait("vector", "y", C["y"].v)
            wait("vector", "xs", xsb0 + 16 * (8 * c + o + 1))

            def sf(o=o, ysl=ysl):
                return nc.vector.scalar_tensor_tensor(
                    T["s_c"][:, 512 * o:512 * (o + 1)], py[ysl],
                    T["bo_sb"][:, o:o + 1], T["xstg"][:, 512 * (o % 2):512 * (o % 2 + 1)],
                    ALU.add, ALU.add)
            op("vector", sf)
            stt_hist.append(inc_last("vector", "stt", 1))
            if o + 2 < 8:
                xpf_emit(o + 2, c)
            if ko >= 2:
                wait("vector", "sm", sm_hist[ko - 2])

            def sbff(o=o):
                return nc.vector.tensor_copy(
                    T["sbf"][:, 512 * (o % 2):512 * (o % 2 + 1)],
                    T["s_c"][:, 512 * o:512 * (o + 1)])
            op("vector", sbff)

            def ssqf(o=o):
                sl_ = slice(512 * (o % 2), 512 * (o % 2 + 1))
                return nc.vector.tensor_tensor(
                    T["ssq"][:, sl_], T["sbf"][:, sl_], T["sbf"][:, sl_], ALU.mult)
            op("vector", ssqf)
            inc_last("vector", "sq", 1)
            wait("tensor", "sq", C["sq"].v)

            def smf(o=o):
                return nc.tensor.matmul(
                    pmean, T["oneK"], T["sbf"][:, 512 * (o % 2):512 * (o % 2 + 1)],
                    start=(o == 0), stop=(o == 7))
            op("tensor", smf)

            def sqmf(o=o):
                return nc.tensor.matmul(
                    pmsq, T["oneK"], T["ssq"][:, 512 * (o % 2):512 * (o % 2 + 1)],
                    start=(o == 0), stop=(o == 7))
            op("tensor", sqmf)
            sm_hist.append(inc_last("tensor", "sm", 1))
        _emit_ln_tail(S, T, io, prb, pmr, pmean, pmsq, c, stt_hist, x1s_hist,
                      io["x1"])
        wait("tensor", "ap", C["ap"].v)
    op("sync", lambda v=C["x1s"].v: nc.sync.wait_ge(S.sems["x1s"], v))


def _emit_ln_tail(S, T, io, prb, pmr, pmean, pmsq, c, stt_hist, x1s_hist, xout):
    nc = S.nc
    op, wait, inc_last, C = S.op, S.wait, S.inc_last, S.C
    stat = T["stat"]
    wait("vector", "sm", C["sm"].v)
    op("vector", lambda: nc.vector.tensor_copy(stat[0:1, 0:512], pmean))
    op("vector", lambda: nc.vector.tensor_copy(stat[0:1, 512:1024], pmsq))
    op("vector", lambda: nc.vector.tensor_tensor(
        stat[0:1, 1024:1536], stat[0:1, 0:512], stat[0:1, 0:512], ALU.mult))
    op("vector", lambda: nc.vector.tensor_tensor(
        stat[0:1, 1536:2048], stat[0:1, 512:1024], stat[0:1, 1024:1536],
        ALU.subtract))
    op("vector", lambda: nc.vector.tensor_scalar(
        stat[0:1, 1536:2048], stat[0:1, 1536:2048], LN_EPS, None,
        ALU.add, ALU.bypass))
    inc_last("vector", "nr", 1)
    wait("scalar", "nr", C["nr"].v)
    op("scalar", lambda: nc.scalar.activation(
        stat[0:1, 1024:1536], stat[0:1, 1536:2048], AF.Ln))
    op("scalar", lambda: nc.scalar.activation(
        stat[0:1, 512:1024], stat[0:1, 1024:1536], AF.Exp, scale=-0.5))
    inc_last("scalar", "ex", 1)
    wait("vector", "ex", C["ex"].v)
    op("vector", lambda: nc.vector.scalar_tensor_tensor(
        stat[0:1, 1536:2048], stat[0:1, 0:512], -1.0,
        stat[0:1, 512:1024], ALU.mult, ALU.mult))
    rstd_bf = stat[0:1, 2048:2304].bitcast(BF16)
    mr_bf = stat[0:1, 2304:2560].bitcast(BF16)
    op("vector", lambda rb=rstd_bf: nc.vector.tensor_copy(rb, stat[0:1, 512:1024]))
    op("vector", lambda mb=mr_bf: nc.vector.tensor_copy(mb, stat[0:1, 1536:2048]))
    inc_last("vector", "nr", 1)
    wait("tensor", "nr", C["nr"].v)
    op("tensor", lambda rb=rstd_bf: nc.tensor.matmul(
        prb, T["ones1"], rb, start=True, stop=True))
    op("tensor", lambda mb=mr_bf: nc.tensor.matmul(
        pmr, T["ones1"], mb, start=True, stop=True))
    inc_last("tensor", "bc", 1)
    wait("vector", "bc", C["bc"].v)
    for o in range(8):
        ko = len(x1s_hist)
        osl = ko % 2
        if ko >= 2:
            wait("vector", "x1s", x1s_hist[ko - 2])

        def a1f(o=o, osl=osl):
            return nc.vector.tensor_tensor(
                T["x1stg"][:, 512 * osl:512 * (osl + 1)],
                T["s_c"][:, 512 * o:512 * (o + 1)], prb, ALU.mult)
        op("vector", a1f)

        def a2f(osl=osl):
            return nc.vector.tensor_tensor(
                T["x1stg"][:, 512 * osl:512 * (osl + 1)],
                T["x1stg"][:, 512 * osl:512 * (osl + 1)], pmr, ALU.add)
        op("vector", a2f)
        inc_last("vector", "ap", 1)
        op("sync", lambda v=C["ap"].v: nc.sync.wait_ge(S.sems["ap"], v))

        def x1f(o=o, c=c, osl=osl):
            return nc.sync.dma_start(
                out=xout[128 * o:128 * (o + 1), 512 * c:512 * (c + 1)],
                in_=T["x1stg"][:, 512 * osl:512 * (osl + 1)])
        op("sync", x1f)
        x1s_hist.append(inc_last("sync", "x1s", 16))


ATTN_SEMS = ["ld", "cv", "q", "qk", "ex", "mk", "pv", "oev", "dd", "db",
             "bc", "nr", "y", "stt", "sq", "sm", "ap", "x1s", "xs"]


def build_attn(layer, cc):
    causal = (layer == 0)
    nc = bass.Bass("TRN2", target_bir_lowering=False)
    io = {}
    io["qtm"] = nc.dram_tensor("qtm", [TPC, 1024], BF16, kind="ExternalInput").ap()
    io["kall"] = nc.dram_tensor("kall", [S_, 256], BF16, kind="ExternalInput").ap()
    io["vall"] = nc.dram_tensor("vall", [S_, 256], BF16, kind="ExternalInput").ap()
    io["xfm"] = nc.dram_tensor("xfm", [D, TPC], F32, kind="ExternalInput").ap()
    io["wo"] = nc.dram_tensor("wo", [1024, D], F32, kind="ExternalInput").ap()
    io["bo"] = nc.dram_tensor("bo", [128, 8], F32, kind="ExternalInput").ap()
    if causal:
        io["dcolT"] = nc.dram_tensor("dcolT", [128, 16], F32, kind="ExternalInput").ap()
        io["trowT"] = nc.dram_tensor("trowT", [128, 512], F32, kind="ExternalInput").ap()
    else:
        io["albT"] = nc.dram_tensor("albT", [128, 64 * 16], F32, kind="ExternalInput").ap()
    io["x1"] = nc.dram_tensor("x1", [D, TPC], F32, kind="ExternalOutput").ap()
    io["dscr"] = nc.dram_tensor("dscr", [2 * 64 * 512], BF16, kind="Internal").ap()
    io["rscr"] = nc.dram_tensor("rscr", [64 * 512], BF16, kind="Internal").ap()
    with contextlib.ExitStack() as es:
        AL = Alloc(nc, es)
        S = Sched(nc, es, ATTN_SEMS)
        S.P = es.enter_context(nc.psum_tensor("P", [128, 4096], F32)).ap()
        block = es.enter_context(nc.Block())
        T = attn_tensors(AL, layer)
        emit_attn_pre(S, T, io, layer)
        emit_attn_core(S, S, T, io, layer, cc, emit_sync=True)
        emit_attn_post(S, T, io, layer)
        S.emit(block)
    return nc


def _host_dcol(cc):
    out = np.zeros((128, 16), np.float32)
    p_ = np.arange(128, dtype=np.float32)
    for hh, st in enumerate((cc, 7 - cc)):
        for c_ in (0, 1):
            q0 = 1024 * st + 512 * c_
            for m in range(4):
                j = 8 * st + 4 * c_ + m
                out[:, hh * 8 + c_ * 4 + m] = 128 * j + p_ - q0
    return out


def _host_alb():
    sl = _alibi_slopes16()
    out = np.zeros((128, 64 * 16), np.float32)
    p_ = np.arange(128, dtype=np.float64)
    for j in range(64):
        for h in range(16):
            out[:, 16 * j + h] = (sl[h] * (128 * j + p_ - (S - 1))).astype(np.float32)
    return out


def run_attn(layer, q_shards, k_batches, v_batches, x_shards, wo, bo, res):
    outs = [None] * NCORES
    trow = np.tile(np.arange(512, dtype=np.float32)[None, :], (128, 1))
    for cc in range(4):
        nc = build_attn(layer, cc)
        cores = [cc, cc + 4]
        in_maps = []
        for core in cores:
            b = core // 4
            m = dict(qtm=q_shards[core], kall=k_batches[b], vall=v_batches[b],
                     xfm=x_shards[core], wo=wo,
                     bo=np.ascontiguousarray(bo.astype(np.float32).reshape(8, 128).T))
            if layer == 0:
                m["dcolT"] = _host_dcol(cc)
                m["trowT"] = trow
            else:
                m["albT"] = _host_alb()
            in_maps.append(m)
        out = run_bass_kernel_spmd(nc, in_maps, core_ids=cores)
        res.append(out)
        for i, core in enumerate(cores):
            outs[core] = out.results[i]
    return outs



def emit_amat(S, AL, io):
    nc = S.nc
    op, wait, inc_last, C = S.op, S.wait, S.inc_last, S.C
    wr_sb = AL.sbuf("m_wr", [128, 8 * 1024], F32)
    wi_sb = AL.sbuf("m_wi", [128, 8 * 1024], F32)
    win_sb = AL.sbuf("m_win", [128, 8 * 1024], F32)
    Cm_sb = AL.sbuf("m_Cm", [128, 8 * 1024], F32)
    Sm_sb = AL.sbuf("m_Sm", [128, 8 * 1024], F32)
    Cs_sb = AL.sbuf("m_Cs", [128, 8 * 128], F32)
    Ss_sb = AL.sbuf("m_Ss", [128, 8 * 128], F32)
    PT_sb = AL.sbuf("m_PT", [128, 1024], F32)
    QT_sb = AL.sbuf("m_QT", [128, 1024], F32)
    Ar_sb = AL.sbuf("m_Ar", [128, 1024], F32)
    P = S.P
    for name, dst, w in [("wfr", wr_sb, 1024), ("wfi", wi_sb, 1024),
                         ("Cm", Cm_sb, 1024), ("Sm", Sm_sb, 1024),
                         ("Csl", Cs_sb, 128), ("Ssl", Ss_sb, 128)]:
        op("sync", lambda name=name, dst=dst: nc.sync.dma_start(
            out=dst.rearrange("p (k c) -> p k c", k=8),
            in_=io[name].rearrange("(k p) c -> p k c", p=128)))
        inc_last("sync", "ld", 16)
    lds = C["ld"].v
    wait("vector", "ld", lds)
    op("vector", lambda: nc.vector.tensor_scalar(
        win_sb, wi_sb, -1.0, None, ALU.mult, ALU.bypass))
    inc_last("vector", "cv", 1)
    cvs = C["cv"].v
    wait("tensor", "ld", lds)
    wait("tensor", "cv", cvs)
    ev_hist = []
    for kc in range(8):
        sl = kc % 2
        if kc >= 2:
            wait("tensor", "ev", ev_hist[kc - 2])
        for jc in range(8):
            for w_sb, c_sb in [(wr_sb, Cs_sb), (wi_sb, Ss_sb)]:
                def m1(kc=kc, jc=jc, w_sb=w_sb, c_sb=c_sb, sl=sl):
                    return nc.tensor.matmul(
                        P[:, 512 * sl:512 * sl + 128],
                        w_sb[:, 1024 * jc + 128 * kc:1024 * jc + 128 * (kc + 1)],
                        c_sb[:, 128 * jc:128 * (jc + 1)],
                        start=(jc == 0 and w_sb is wr_sb),
                        stop=(jc == 7 and w_sb is wi_sb))
                op("tensor", m1)
        for jc in range(8):
            for w_sb, c_sb in [(wr_sb, Ss_sb), (win_sb, Cs_sb)]:
                def m2(kc=kc, jc=jc, w_sb=w_sb, c_sb=c_sb, sl=sl):
                    return nc.tensor.matmul(
                        P[:, 1024 + 512 * sl:1024 + 512 * sl + 128],
                        w_sb[:, 1024 * jc + 128 * kc:1024 * jc + 128 * (kc + 1)],
                        c_sb[:, 128 * jc:128 * (jc + 1)],
                        start=(jc == 0 and w_sb is wr_sb),
                        stop=(jc == 7 and w_sb is win_sb))
                op("tensor", m2)
        inc_last("tensor", "mm", 1)
        wait("vector", "mm", C["mm"].v)
        op("vector", lambda kc=kc, sl=sl: nc.vector.tensor_copy(
            PT_sb[:, 128 * kc:128 * (kc + 1)],
            P[:, 512 * sl:512 * sl + 128]))
        op("vector", lambda kc=kc, sl=sl: nc.vector.tensor_copy(
            QT_sb[:, 128 * kc:128 * (kc + 1)],
            P[:, 1024 + 512 * sl:1024 + 512 * sl + 128]))
        ev_hist.append(inc_last("vector", "ev", 1))
    wait("tensor", "ev", C["ev"].v)
    for ob in range(2):
        for kc in range(8):
            for t_sb, c_sb in [(PT_sb, Cm_sb), (QT_sb, Sm_sb)]:
                def m3(ob=ob, kc=kc, t_sb=t_sb, c_sb=c_sb):
                    return nc.tensor.matmul(
                        P[:, 512 * ob:512 * (ob + 1)],
                        t_sb[:, 128 * kc:128 * (kc + 1)],
                        c_sb[:, 1024 * kc + 512 * ob:1024 * kc + 512 * (ob + 1)],
                        start=(kc == 0 and t_sb is PT_sb),
                        stop=(kc == 7 and t_sb is QT_sb))
                op("tensor", m3)
        inc_last("tensor", "mm", 1)
        wait("vector", "mm", C["mm"].v)
        op("vector", lambda ob=ob: nc.vector.tensor_copy(
            Ar_sb[:, 512 * ob:512 * (ob + 1)], P[:, 512 * ob:512 * (ob + 1)]))
        inc_last("vector", "ev", 1)
    op("gpsimd", lambda v=C["ev"].v: nc.gpsimd.wait_ge(S.sems["ev"], v))
    op("gpsimd", lambda: nc.gpsimd.dma_start(out=io["Ar"], in_=Ar_sb))
    inc_last("gpsimd", "st", 16)
    op("gpsimd", lambda v=C["st"].v: nc.gpsimd.wait_ge(S.sems["st"], v))


def build_amat():
    nc = bass.Bass("TRN2", target_bir_lowering=False)
    io = {}
    for n, sh in [("wfr", [D, D]), ("wfi", [D, D]), ("Cm", [D, D]),
                  ("Sm", [D, D]), ("Csl", [D, 128]), ("Ssl", [D, 128])]:
        io[n] = nc.dram_tensor(n, sh, F32, kind="ExternalInput").ap()
    io["Ar"] = nc.dram_tensor("Ar", [128, D], F32, kind="ExternalOutput").ap()
    with contextlib.ExitStack() as es:
        AL = Alloc(nc, es)
        S = Sched(nc, es, ["ld", "cv", "mm", "ev", "st"])
        S.P = es.enter_context(nc.psum_tensor("P", [128, 2048], F32)).ap()
        block = es.enter_context(nc.Block())
        emit_amat(S, AL, io)
        S.emit(block)
    return nc


def _dft_tables():
    kk = np.arange(D, dtype=np.float64)
    ang = 2.0 * np.pi * np.outer(kk, kk) / D
    ang = np.mod(np.outer(kk, kk), D) * (2.0 * np.pi / D)
    return np.cos(ang).astype(np.float32), np.sin(ang).astype(np.float32)


def run_amat(wfr, wfi, res):
    nc = build_amat()
    Cm, Sm = _dft_tables()
    in_maps = []
    for core in range(NCORES):
        sl = slice(128 * core, 128 * (core + 1))
        in_maps.append(dict(
            wfr=wfr, wfi=wfi, Cm=Cm, Sm=Sm,
            Csl=np.ascontiguousarray(Cm[:, sl] / D),
            Ssl=np.ascontiguousarray(Sm[:, sl] / D)))
    out = run_bass_kernel_spmd(nc, in_maps, core_ids=list(range(NCORES)))
    res.append(out)
    return np.concatenate([r["Ar"] for r in out.results], 0)


class Sched:

    def __init__(self, nc, es, semnames):
        self.nc = nc
        self.sems = {n: es.enter_context(nc.semaphore("s_" + n)) for n in semnames}
        self.C = {n: Ctr() for n in semnames}
        self.ops = {k: [] for k in ["sync", "gpsimd", "tensor", "vector", "scalar"]}

    def op(self, eng, f):
        self.ops[eng].append(f)

    def wait(self, eng, sem, val):
        if val > 0:
            nc = self.nc
            self.op(eng, lambda s=self.sems[sem], v=val:
                    getattr(nc, eng).wait_ge(s, v))

    def inc_last(self, eng, sem, n=1):
        prev = self.ops[eng][-1]

        def f(prev=prev, s=self.sems[sem], n=n):
            inst = prev()
            inst.then_inc(s, n)
            return inst
        self.ops[eng][-1] = f
        return self.C[sem].inc(n)

    def emit(self, block):
        ops = self.ops

        @block.sync
        def _(sync):
            for f in ops["sync"]:
                f()

        @block.gpsimd
        def _(gpsimd):
            for f in ops["gpsimd"]:
                f()

        @block.tensor
        def _(tensor):
            for f in ops["tensor"]:
                f()

        @block.scalar
        def _(scalar):
            for f in ops["scalar"]:
                f()

        @block.vector
        def _(vector):
            for f in ops["vector"]:
                f()


def _emit_resid_ln(S, nc, P, views, y_emit, xap, bias_sb, xout, c_list=(0, 1, 2, 3),
                   state=None, xpf_hook=None):
    py, pmean, pmsq, prb, pmr = views["py"], views["pmean"], views["pmsq"], \
        views["prb"], views["pmr"]
    s_cT, sbf, ssq, stat, x1stg = views["s_c"], views["sbf"], views["ssq"], \
        views["stat"], views["x1stg"]
    oneK, ones1 = views["oneK"], views["ones1"]
    op, wait, inc_last, C = S.op, S.wait, S.inc_last, S.C
    if state is None:
        state = {}
    stt_hist = state.setdefault("stt_hist", [])
    sm_hist = state.setdefault("sm_hist", [])
    x1s_hist = state.setdefault("x1s_hist", [])
    kos = state.setdefault("kos", Ctr())
    for c in c_list:
        for o in range(8):
            ko = kos.v
            kos.inc(1)
            ysl = ko % 2
            if ko >= 2:
                wait("tensor", "stt", stt_hist[ko - 2])
            y_emit(c, o, py[ysl])
            inc_last("tensor", "y", 1)
            wait("vector", "y", C["y"].v)

            def sf(o=o, c=c, ysl=ysl):
                return nc.vector.scalar_tensor_tensor(
                    s_cT[:, 512 * o:512 * (o + 1)], py[ysl][:, :],
                    bias_sb[:, o:o + 1], xap(c, o),
                    ALU.add, ALU.add)
            op("vector", sf)
            stt_hist.append(inc_last("vector", "stt", 1))
            if xpf_hook is not None and o + 2 < 8:
                xpf_hook(o + 2, c)
            if ko >= 2:
                wait("vector", "sm", sm_hist[ko - 2])

            def sbff(o=o, ysl=ysl):
                return nc.vector.tensor_copy(
                    sbf[:, 512 * ysl:512 * (ysl + 1)],
                    s_cT[:, 512 * o:512 * (o + 1)])
            op("vector", sbff)

            def ssqf(ysl=ysl):
                sl_ = slice(512 * ysl, 512 * (ysl + 1))
                return nc.vector.tensor_tensor(
                    ssq[:, sl_], sbf[:, sl_], sbf[:, sl_], ALU.mult)
            op("vector", ssqf)
            inc_last("vector", "sq", 1)
            wait("tensor", "sq", C["sq"].v)

            def smf(o=o, ysl=ysl):
                return nc.tensor.matmul(
                    pmean, oneK[:], sbf[:, 512 * ysl:512 * (ysl + 1)],
                    start=(o == 0), stop=(o == 7))
            op("tensor", smf)

            def sqmf(o=o, ysl=ysl):
                return nc.tensor.matmul(
                    pmsq, oneK[:], ssq[:, 512 * ysl:512 * (ysl + 1)],
                    start=(o == 0), stop=(o == 7))
            op("tensor", sqmf)
            sm_hist.append(inc_last("tensor", "sm", 1))
        wait("vector", "sm", C["sm"].v)
        op("vector", lambda: nc.vector.tensor_copy(stat[0:1, 0:512], pmean))
        op("vector", lambda: nc.vector.tensor_copy(stat[0:1, 512:1024], pmsq))
        op("vector", lambda: nc.vector.tensor_tensor(
            stat[0:1, 1024:1536], stat[0:1, 0:512], stat[0:1, 0:512], ALU.mult))
        op("vector", lambda: nc.vector.tensor_tensor(
            stat[0:1, 1536:2048], stat[0:1, 512:1024], stat[0:1, 1024:1536],
            ALU.subtract))
        op("vector", lambda: nc.vector.tensor_scalar(
            stat[0:1, 1536:2048], stat[0:1, 1536:2048], LN_EPS, None,
            ALU.add, ALU.bypass))
        inc_last("vector", "nr", 1)
        wait("scalar", "nr", C["nr"].v)
        op("scalar", lambda: nc.scalar.activation(
            stat[0:1, 1024:1536], stat[0:1, 1536:2048], AF.Ln))
        op("scalar", lambda: nc.scalar.activation(
            stat[0:1, 512:1024], stat[0:1, 1024:1536], AF.Exp, scale=-0.5))
        inc_last("scalar", "ex", 1)
        wait("vector", "ex", C["ex"].v)
        op("vector", lambda: nc.vector.scalar_tensor_tensor(
            stat[0:1, 1536:2048], stat[0:1, 0:512], -1.0,
            stat[0:1, 512:1024], ALU.mult, ALU.mult))
        rstd_bf = stat[0:1, 2048:2304].bitcast(BF16)
        mr_bf = stat[0:1, 2304:2560].bitcast(BF16)
        op("vector", lambda rb=rstd_bf: nc.vector.tensor_copy(
            rb, stat[0:1, 512:1024]))
        op("vector", lambda mb=mr_bf: nc.vector.tensor_copy(
            mb, stat[0:1, 1536:2048]))
        inc_last("vector", "nr", 1)
        wait("tensor", "nr", C["nr"].v)
        op("tensor", lambda rb=rstd_bf: nc.tensor.matmul(
            prb, ones1[:], rb, start=True, stop=True))
        op("tensor", lambda mb=mr_bf: nc.tensor.matmul(
            pmr, ones1[:], mb, start=True, stop=True))
        inc_last("tensor", "bc", 1)
        wait("vector", "bc", C["bc"].v)
        for o in range(8):
            ko = len(x1s_hist)
            osl = ko % 2
            if ko >= 2:
                wait("vector", "x1s", x1s_hist[ko - 2])

            def a1f(o=o, osl=osl):
                return nc.vector.tensor_tensor(
                    x1stg[:, 512 * osl:512 * (osl + 1)],
                    s_cT[:, 512 * o:512 * (o + 1)], prb[:, :], ALU.mult)
            op("vector", a1f)

            def a2f(osl=osl):
                return nc.vector.tensor_tensor(
                    x1stg[:, 512 * osl:512 * (osl + 1)],
                    x1stg[:, 512 * osl:512 * (osl + 1)], pmr[:, :], ALU.add)
            op("vector", a2f)
            inc_last("vector", "ap", 1)
            op("sync", lambda v=C["ap"].v: nc.sync.wait_ge(S.sems["ap"], v))

            def x1f(o=o, c=c, osl=osl):
                return nc.sync.dma_start(
                    out=xout[128 * o:128 * (o + 1), 512 * c:512 * (c + 1)],
                    in_=x1stg[:, 512 * osl:512 * (osl + 1)])
            op("sync", x1f)
            x1s_hist.append(inc_last("sync", "x1s", 16))
        wait("tensor", "ap", C["ap"].v)


def emit_ffn0(S, AL, io):
    nc = S.nc
    op, wait, inc_last, C = S.op, S.wait, S.inc_last, S.C
    x_sb = AL.sbuf("f0_x", [128, 8 * 2048], F32)
    xb_sb = AL.sbuf("f0_xb", [128, 8 * 2048], BF16)
    A_sb = AL.sbuf("f0_A", [128, 8 * 1024], BF16)
    fb_sb = AL.sbuf("f0_fb", [128, 8], F32)
    T = dict(ones1=AL.sbuf("f0_ones1", [1, 128], BF16),
             oneK=AL.sbuf("f0_oneK", [128, 1], BF16),
             s_c=AL.sbuf("f0_sc", [128, 8 * 512], F32),
             sbf=AL.sbuf("f0_sbf", [128, 2 * 512], BF16),
             ssq=AL.sbuf("f0_ssq", [128, 2 * 512], BF16),
             stat=AL.sbuf("f0_stat", [1, 2560], F32),
             x1stg=AL.sbuf("f0_x1stg", [128, 2 * 512], F32))
    P = S.P
    views = dict(py=[P[:, 0:512], P[:, 512:1024]],
                 pmean=P[0:1, 2048:2560], pmsq=P[0:1, 2560:3072],
                 prb=P[:, 3072:3584], pmr=P[:, 3584:4096],
                 s_c=T["s_c"], sbf=T["sbf"], ssq=T["ssq"], stat=T["stat"],
                 x1stg=T["x1stg"], oneK=T["oneK"], ones1=T["ones1"])
    op("sync", lambda: nc.sync.dma_start(
        out=x_sb.rearrange("p (k t) -> p k t", k=8),
        in_=io["xfm"].rearrange("(k p) t -> p k t", p=128)))
    inc_last("sync", "ld", 16)
    op("sync", lambda: nc.sync.dma_start(out=fb_sb, in_=io["fb"]))
    inc_last("sync", "ld", 16)
    op("gpsimd", lambda: nc.gpsimd.dma_start(
        out=xb_sb.rearrange("p (k t) -> p k t", k=8),
        in_=io["xfm"].rearrange("(k p) t -> p k t", p=128)))
    inc_last("gpsimd", "ld", 16)
    op("gpsimd", lambda: nc.gpsimd.dma_start(
        out=A_sb.rearrange("p (k c) -> p k c", k=8),
        in_=io["Am"].rearrange("(k p) c -> p k c", p=128)))
    inc_last("gpsimd", "ld", 16)
    op("vector", lambda: nc.vector.memset(T["ones1"], 1.0))
    op("vector", lambda: nc.vector.memset(T["oneK"], 1.0 / 1024.0))
    inc_last("vector", "cv", 1)
    lds, cvs = C["ld"].v, C["cv"].v

    def y_emit(c, o, pyv):
        if c == 0 and o == 0:
            wait("tensor", "ld", lds)
            wait("tensor", "cv", cvs)
        for k in range(8):
            def yf(c=c, o=o, k=k, pyv=pyv):
                return nc.tensor.matmul(
                    pyv, A_sb[:, 1024 * k + 128 * o:1024 * k + 128 * (o + 1)],
                    xb_sb[:, 2048 * k + 512 * c:2048 * k + 512 * (c + 1)],
                    start=(k == 0), stop=(k == 7))
            op("tensor", yf)

    _emit_resid_ln(S, nc, P, views, y_emit,
                   lambda c, o: x_sb[:, 2048 * o + 512 * c:2048 * o + 512 * (c + 1)],
                   fb_sb, io["x2"])
    op("sync", lambda v=C["x1s"].v: nc.sync.wait_ge(S.sems["x1s"], v))


FFN0_SEMS = ["ld", "cv", "y", "stt", "sq", "sm", "nr", "ex", "bc", "ap", "x1s"]


def build_ffn0():
    nc = bass.Bass("TRN2", target_bir_lowering=False)
    io = {}
    io["xfm"] = nc.dram_tensor("xfm", [D, TPC], F32, kind="ExternalInput").ap()
    io["Am"] = nc.dram_tensor("Am", [D, D], F32, kind="ExternalInput").ap()
    io["fb"] = nc.dram_tensor("fb", [128, 8], F32, kind="ExternalInput").ap()
    io["x2"] = nc.dram_tensor("x2", [D, TPC], F32, kind="ExternalOutput").ap()
    with contextlib.ExitStack() as es:
        AL = Alloc(nc, es)
        S = Sched(nc, es, FFN0_SEMS)
        S.P = es.enter_context(nc.psum_tensor("P", [128, 4096], F32)).ap()
        block = es.enter_context(nc.Block())
        emit_ffn0(S, AL, io)
        S.emit(block)
    return nc


def run_ffn0(x_shards, A, fbias, res):
    nc = build_ffn0()
    fb = np.ascontiguousarray(fbias.astype(np.float32).reshape(8, 128).T)
    in_maps = [dict(xfm=x_shards[c], Am=A, fb=fb) for c in range(NCORES)]
    out = run_bass_kernel_spmd(nc, in_maps, core_ids=list(range(NCORES)))
    res.append(out)
    return [r["x2"] for r in out.results]


def _emit_quant(S, nc, out_ap, in_ap, qs, W):
    op = S.op
    nb = W // 16
    t8, t4, t2 = qs["t8"], qs["t4"], qs["t2"]
    bm, sc, rc, scb = qs["bm"], qs["sc"], qs["rc"], qs["scb"]
    u, tb = qs["u"], qs["tb"]
    A, B4, B2 = qs["A"], qs["B4"], qs["B2"]
    w1 = t8.bitcast(BF16)
    w2 = tb.bitcast(BF16)
    M1, MH = MAGIC1, MAGICH
    I32 = mybir.dt.int32
    tbi = tb.bitcast(I32)
    ini = in_ap.bitcast(I32)
    def selfsem():
        v = S.inc_last("vector", "qz", 1)
        S.wait("vector", "qz", v)

    op("vector", lambda: nc.vector.tensor_scalar(
        tbi[:, 0:W], ini, 0x7FFFFFFF, None, ALU.bitwise_and, ALU.bypass))
    selfsem()
    op("vector", lambda: nc.vector.tensor_tensor(
        t8[:, 0:W // 2], tb[:, 0:W:2], tb[:, 1:W:2], ALU.max))
    selfsem()
    op("vector", lambda: nc.vector.tensor_tensor(
        t4[:, 0:W // 4], t8[:, 0:W // 2:2], t8[:, 1:W // 2:2], ALU.max))
    selfsem()
    op("vector", lambda: nc.vector.tensor_tensor(
        t2[:, 0:W // 8], t4[:, 0:W // 4:2], t4[:, 1:W // 4:2], ALU.max))
    selfsem()
    op("vector", lambda: nc.vector.tensor_tensor(
        bm[:, 0:nb], t2[:, 0:W // 8:2], t2[:, 1:W // 8:2], ALU.max))
    selfsem()
    op("vector", lambda: nc.vector.tensor_scalar(
        sc[:, 0:nb], bm[:, 0:nb], 1.0 / 6.0, 1e-12, ALU.mult, ALU.max))
    selfsem()

    def _rc():
        with nc.allow_low_precision(reason="quant recip"):
            return nc.vector.reciprocal(rc[:, 0:nb], sc[:, 0:nb])
    op("vector", _rc)
    op("vector", lambda: nc.vector.tensor_copy(scb[:, 0:nb], sc[:, 0:nb]))
    selfsem()
    rcb = rc[:, 0:nb].unsqueeze(2).broadcast_to([128, nb, 16])
    scbb = scb[:, 0:nb].unsqueeze(2).broadcast_to([128, nb, 16])
    in3 = in_ap.rearrange("p (a b) -> p a b", b=16)
    u3 = u[:, 0:W].rearrange("p (a b) -> p a b", b=16)
    op("vector", lambda: nc.vector.tensor_tensor(u3, in3, rcb, ALU.mult))
    selfsem()
    uf = u[:, 0:W]
    tbf = tb[:, 0:W]
    op("vector", lambda: nc.vector.tensor_scalar(
        tbf, uf, MH, MH - 2.0, ALU.add, ALU.max))
    selfsem()
    op("vector", lambda: nc.vector.tensor_scalar(
        A[:, 0:W], tbf, MH, 2.0, ALU.subtract, ALU.min))
    op("vector", lambda: nc.vector.tensor_scalar(
        tbf, uf, M1, M1 - 4.0, ALU.add, ALU.max))
    selfsem()
    op("vector", lambda: nc.vector.tensor_scalar(
        B4[:, 0:W], tbf, M1, 4.0, ALU.subtract, ALU.min))
    op("vector", lambda: nc.vector.tensor_scalar(
        tbf, tbf, M1 - 2.0, None, ALU.max, ALU.bypass))
    selfsem()
    op("vector", lambda: nc.vector.tensor_scalar(
        B2[:, 0:W], tbf, M1, 2.0, ALU.subtract, ALU.min))
    op("vector", lambda: nc.vector.tensor_scalar(
        w1[:, 0:W], uf, 5.0, 2.0, ALU.is_gt, ALU.mult))
    op("vector", lambda: nc.vector.tensor_scalar(
        w2[:, 0:W], uf, -5.0, 2.0, ALU.is_lt, ALU.mult))
    selfsem()
    op("vector", lambda: nc.vector.tensor_tensor(
        A[:, 0:W], A[:, 0:W], B4[:, 0:W], ALU.add))
    op("vector", lambda: nc.vector.tensor_tensor(
        B2[:, 0:W], B2[:, 0:W], w2[:, 0:W], ALU.add))
    selfsem()
    op("vector", lambda: nc.vector.tensor_tensor(
        A[:, 0:W], A[:, 0:W], w1[:, 0:W], ALU.add))
    selfsem()
    op("vector", lambda: nc.vector.tensor_tensor(
        A[:, 0:W], A[:, 0:W], B2[:, 0:W], ALU.subtract))
    selfsem()
    A3 = A[:, 0:W].rearrange("p (a b) -> p a b", b=16)
    o3 = out_ap.rearrange("p (a b) -> p a b", b=16)
    op("vector", lambda: nc.vector.tensor_tensor(o3, A3, scbb, ALU.mult))


def _mk_qscratch_al(AL, Wmax=1024):
    return dict(
        t8=AL.sbuf("q_t8", [128, Wmax // 2], F32),
        t4=AL.sbuf("q_t4", [128, Wmax // 4], F32),
        t2=AL.sbuf("q_t2", [128, Wmax // 8], F32),
        bm=AL.sbuf("q_bm", [128, Wmax // 16], F32),
        sc=AL.sbuf("q_sc", [128, Wmax // 16], F32),
        rc=AL.sbuf("q_rc", [128, Wmax // 16], F32),
        scb=AL.sbuf("q_scb", [128, Wmax // 16], BF16),
        u=AL.sbuf("q_u", [128, Wmax], F32),
        tb=AL.sbuf("q_tb", [128, Wmax], F32),
        A=AL.sbuf("q_A", [128, Wmax], BF16),
        B4=AL.sbuf("q_B4", [128, Wmax], BF16),
        B2=AL.sbuf("q_B2", [128, Wmax], BF16),
    )


def _mk_qscratch(nc, es, Wmax=1024):
    E = es.enter_context
    return dict(
        t8=E(nc.sbuf_tensor("q_t8", [128, Wmax // 2], F32)),
        t4=E(nc.sbuf_tensor("q_t4", [128, Wmax // 4], F32)),
        t2=E(nc.sbuf_tensor("q_t2", [128, Wmax // 8], F32)),
        bm=E(nc.sbuf_tensor("q_bm", [128, Wmax // 16], F32)),
        sc=E(nc.sbuf_tensor("q_sc", [128, Wmax // 16], F32)),
        rc=E(nc.sbuf_tensor("q_rc", [128, Wmax // 16], F32)),
        scb=E(nc.sbuf_tensor("q_scb", [128, Wmax // 16], BF16)),
        u=E(nc.sbuf_tensor("q_u", [128, Wmax], F32)),
        tb=E(nc.sbuf_tensor("q_tb", [128, Wmax], F32)),
        A=E(nc.sbuf_tensor("q_A", [128, Wmax], BF16)),
        B4=E(nc.sbuf_tensor("q_B4", [128, Wmax], BF16)),
        B2=E(nc.sbuf_tensor("q_B2", [128, Wmax], BF16)),
    )


FFN1_SEMS = ["ld", "cv", "wq", "wo", "xq", "xb", "gu", "hq", "hb",
             "y", "stt", "sq", "sm", "nr", "ex", "bc", "ap",
             "x1s", "xs", "ws", "ev", "qz"]


def emit_ffn1(S, AL, io, debug=False, wq_pre=None):
    II = 2816
    nc = S.nc
    xfm, wg, wu, wd = io["xfm"], io["wg"], io["wu"], io["wd"]
    ident = io["ident"]
    x4 = io["x4"]
    wgq, wuq, wdq = io["wgq"], io["wuq"], io["wdq"]
    xqd, hqd = io["xqd"], io["hqd"]
    if debug:
        dbg_g, dbg_h = io["dbg_g"], io["dbg_h"]
    if True:
        E = None
        hqfm = AL.sbuf("hqfm", [128, 22 * 1024], BF16)
        wdT = AL.sbuf("wdT", [128, 22 * 1024], BF16)
        xqfm = AL.sbuf("xqfm", [128, 8 * 1024], BF16)
        wgt = AL.sbuf("wgt", [128, 2 * 8 * 512], BF16)
        wut = AL.sbuf("wut", [128, 2 * 8 * 512], BF16)
        xstg = AL.sbuf("xstg", [128, 2 * 128], F32)
        xtm = AL.sbuf("xtm", [128, 1024], F32)
        xqtm = AL.sbuf("xqtm", [128, 1024], BF16)
        sg = AL.sbuf("sg", [128, 512], F32)
        hf32 = AL.sbuf("hf32", [128, 512], F32)
        hqt = AL.sbuf("hqt", [128, 2 * 512], BF16)
        fb_sb = AL.sbuf("fb_sb", [128, 8], F32)
        ones1 = AL.sbuf("ones1", [1, 128], BF16)
        oneK = AL.sbuf("oneK", [128, 1], BF16)
        s_cT = AL.sbuf("s_cT", [128, 8 * 512], F32)
        sbf = AL.sbuf("sbf", [128, 2 * 512], BF16)
        ssq = AL.sbuf("ssq", [128, 2 * 512], BF16)
        stat = AL.sbuf("stat", [1, 2560], F32)
        x1stg = AL.sbuf("x1stg", [128, 2 * 512], F32)
        xrs = AL.sbuf("xrs", [128, 2 * 512], F32)
        ident_sb = AL.sbuf("ident_sb", [128, 128], F32)
        qs = _mk_qscratch_al(AL)
        P = S.P
        wstg = [hqfm[:, 0:5632].bitcast(F32),
                hqfm[:, 5632:11264].bitcast(F32)]
        wqs = [hqfm[:, 11264:14080], hqfm[:, 14080:16896]]
        op, wait, inc_last, C = S.op, S.wait, S.inc_last, S.C

        op("sync", lambda: nc.sync.dma_start(out=ident_sb[:], in_=ident[:]))
        inc_last("sync", "ld", 16)
        op("vector", lambda: nc.vector.memset(fb_sb[:], 0.0))
        op("vector", lambda: nc.vector.memset(ones1[:], 1.0))
        op("vector", lambda: nc.vector.memset(oneK[:], 1.0 / 1024.0))
        inc_last("vector", "nr", 1)

        if wq_pre is None:
            wjobs = []
            for mi, (wsrc, wdst, nch, wid) in enumerate(
                    [(wg, wgq, 22, D), (wu, wuq, 22, D), (wd, wdq, 8, II)]):
                for ch in range(nch):
                    wjobs.append((wsrc, wdst, ch, wid))
            for ji, (wsrc, wdst, ch, wid) in enumerate(wjobs):
                sl = ji % 2
                if ji >= 2:
                    op("sync", lambda v=C["wq"].v - 1: nc.sync.wait_ge(S.sems["wq"], v))

                def wldf(wsrc=wsrc, ch=ch, wid=wid, sl=sl):
                    return nc.sync.dma_start(
                        out=wstg[sl][:, 0:wid], in_=wsrc[128 * ch:128 * (ch + 1), 0:wid])
                op("sync", wldf)
                inc_last("sync", "ws", 16)
                wait("vector", "ws", C["ws"].v)
                if ji >= 2:
                    wait("vector", "wo", C["wo"].v - 16)
                for off in range(0, wid, 1024):
                    w_ = min(1024, wid - off)
                    _emit_quant(S, nc, wqs[sl][:, off:off + w_],
                                wstg[sl][:, off:off + w_], qs, w_)
                inc_last("vector", "wq", 1)
                op("sync", lambda v=C["wq"].v: nc.sync.wait_ge(S.sems["wq"], v))

                def wstf(wdst=wdst, ch=ch, wid=wid, sl=sl):
                    return nc.sync.dma_start(
                        out=wdst[128 * ch:128 * (ch + 1), 0:wid], in_=wqs[sl][:, 0:wid])
                op("sync", wstf)
                inc_last("sync", "wo", 16)
            wqdone = C["wo"].v
        else:
            wqdone = wq_pre

        op("sync", lambda: nc.sync.wait_ge(S.sems["wo"], wqdone))
        for ic in range(22):
            def wdtf(ic=ic):
                return nc.sync.dma_start_transpose(
                    wdT[:, 1024 * ic:1024 * (ic + 1)], wdq[:, 128 * ic:128 * (ic + 1)])
            op("sync", wdtf)
            inc_last("sync", "ld", 16)
        lds = C["ld"].v

        views = dict(py=[P[:, 0:512], P[:, 512:1024]],
                     pmean=P[0:1, 2048:2560], pmsq=P[0:1, 2560:3072],
                     prb=P[:, 3072:3584], pmr=P[:, 3584:4096],
                     s_c=s_cT, sbf=sbf, ssq=ssq, stat=stat, x1stg=x1stg,
                     oneK=oneK, ones1=ones1)
        ln_state = {}
        for hf in range(2):
            toff = 1024 * hf
            for tt in range(8):
                ps = P[:, 1024 * (tt % 2):1024 * (tt % 2) + 1024]
                for o in range(8):
                    xsl = (8 * tt + o) % 2
                    if 8 * tt + o >= 2:
                        op("sync", lambda v=C["xq"].v: nc.sync.wait_ge(S.sems["xq"], v))

                    def xldf(o=o, tt=tt, xsl=xsl, toff=toff):
                        return nc.sync.dma_start(
                            out=xstg[:, 128 * xsl:128 * (xsl + 1)],
                            in_=xfm[128 * o:128 * (o + 1),
                                    toff + 128 * tt:toff + 128 * (tt + 1)])
                    op("sync", xldf)
                    inc_last("sync", "xs", 16)
                    wait("tensor", "xs", C["xs"].v)
                    if tt >= 2 and o == 0:
                        wait("tensor", "xq", C["xq"].v)

                    def xtf(o=o, ps=ps, xsl=xsl):
                        return nc.tensor.transpose(
                            ps[:, 128 * o:128 * (o + 1)],
                            xstg[:, 128 * xsl:128 * (xsl + 1)], ident_sb[:])
                    op("tensor", xtf)
                    inc_last("tensor", "xq", 1)
                wait("vector", "xq", C["xq"].v)
                op("vector", lambda ps=ps: nc.vector.tensor_copy(xtm[:], ps))
                inc_last("vector", "xq", 1)
                _emit_quant(S, nc, xqtm[:], xtm[:], qs, 1024)
                inc_last("vector", "xb", 1)
                op("sync", lambda v=C["xb"].v: nc.sync.wait_ge(S.sems["xb"], v))

                def xqstf(tt=tt, toff=toff):
                    return nc.sync.dma_start(
                        out=xqd[toff + 128 * tt:toff + 128 * (tt + 1), :],
                        in_=xqtm[:])
                op("sync", xqstf)
                inc_last("sync", "xq", 16)
                wait("vector", "xq", C["xq"].v)
            op("sync", lambda v=C["xq"].v: nc.sync.wait_ge(S.sems["xq"], v))
            for hc in range(8):
                def xqtf(hc=hc, toff=toff):
                    return nc.sync.dma_start_transpose(
                        xqfm[:, 1024 * hc:1024 * (hc + 1)],
                        xqd[toff:toff + 1024, 128 * hc:128 * (hc + 1)])
                op("sync", xqtf)
                inc_last("sync", "xb", 16)
            xqfmdone = C["xb"].v

            nib = 6
            for ib in range(nib):
                wib = 512 if ib < 5 else 256
                isl = ib % 2
                if ib < 2:
                    op("sync", lambda v=wqdone: nc.sync.wait_ge(S.sems["wo"], v))
                else:
                    op("sync", lambda v=C["gu"].v: nc.sync.wait_ge(S.sems["gu"], v))
                for hc in range(8):
                    def wgtf(ib=ib, hc=hc, isl=isl, wib=wib):
                        return nc.sync.dma_start_transpose(
                            wgt[:, 4096 * isl + 512 * hc:4096 * isl + 512 * hc + wib],
                            wgq[512 * ib:512 * ib + wib, 128 * hc:128 * (hc + 1)])
                    op("sync", wgtf)
                    inc_last("sync", "ld", 16)

                    def wutf(ib=ib, hc=hc, isl=isl, wib=wib):
                        return nc.sync.dma_start_transpose(
                            wut[:, 4096 * isl + 512 * hc:4096 * isl + 512 * hc + wib],
                            wuq[512 * ib:512 * ib + wib, 128 * hc:128 * (hc + 1)])
                    op("sync", wutf)
                    inc_last("sync", "ld", 16)
                wld = C["ld"].v
                for tt in range(8):
                    gsl = tt % 2
                    pg = P[:, 512 * gsl:512 * (gsl + 1)]
                    pu = P[:, 1024 + 512 * gsl:1024 + 512 * (gsl + 1)]
                    if tt == 0:
                        wait("tensor", "ld", wld)
                        wait("tensor", "xb", xqfmdone)
                    if 8 * ib + tt >= 2:
                        wait("tensor", "hq", C["hq"].v - 1)
                    for hc in range(8):
                        def gmf(hc=hc, tt=tt, isl=isl, wib=wib, pg=pg):
                            return nc.tensor.matmul(
                                pg[:, 0:wib],
                                xqfm[:, 1024 * hc + 128 * tt:1024 * hc + 128 * (tt + 1)],
                                wgt[:, 4096 * isl + 512 * hc:4096 * isl + 512 * hc + wib],
                                start=(hc == 0), stop=(hc == 7))
                        op("tensor", gmf)
                    for hc in range(8):
                        def umf(hc=hc, tt=tt, isl=isl, wib=wib, pu=pu):
                            return nc.tensor.matmul(
                                pu[:, 0:wib],
                                xqfm[:, 1024 * hc + 128 * tt:1024 * hc + 128 * (tt + 1)],
                                wut[:, 4096 * isl + 512 * hc:4096 * isl + 512 * hc + wib],
                                start=(hc == 0), stop=(hc == 7))
                        op("tensor", umf)
                    inc_last("tensor", "gu", 1)
                    wait("scalar", "gu", C["gu"].v)

                    def silf(pg=pg, wib=wib):
                        return nc.scalar.activation(sg[:, 0:wib], pg[:, 0:wib],
                                                    AF.Silu)
                    op("scalar", silf)
                    inc_last("scalar", "gu", 1)
                    wait("vector", "gu", C["gu"].v)

                    def hmf(pu=pu, wib=wib):
                        return nc.vector.tensor_tensor(
                            hf32[:, 0:wib], sg[:, 0:wib], pu[:, 0:wib], ALU.mult)
                    op("vector", hmf)
                    inc_last("vector", "gu", 1)
                    if debug and hf == 1 and ib == 0 and tt == 0:
                        op("vector", lambda pg=pg: nc.vector.tensor_copy(
                            s_cT[:, 0:512], pg[:, 0:512]))
                        op("vector", lambda: nc.vector.tensor_copy(
                            s_cT[:, 512:1024], hf32[:, 0:512]))
                        inc_last("vector", "ev", 1)
                        op("sync", lambda: nc.sync.wait_ge(S.sems["ev"], 1))
                        op("sync", lambda: nc.sync.dma_start(
                            out=dbg_g, in_=s_cT[:, 0:512]))
                        inc_last("sync", "ld", 16)
                        op("sync", lambda: nc.sync.dma_start(
                            out=dbg_h, in_=s_cT[:, 512:1024]))
                        inc_last("sync", "ld", 16)
                    hsl = (8 * ib + tt) % 2
                    if 8 * ib + tt >= 2:
                        wait("vector", "hb", C["hb"].v - 16)
                    _emit_quant(S, nc, hqt[:, 512 * hsl:512 * hsl + wib],
                                hf32[:, 0:wib], qs, wib)
                    inc_last("vector", "hq", 1)
                    op("sync", lambda v=C["hq"].v: nc.sync.wait_ge(S.sems["hq"], v))

                    def hstf(ib=ib, tt=tt, hsl=hsl, wib=wib):
                        return nc.sync.dma_start(
                            out=hqd[128 * tt:128 * (tt + 1), 512 * ib:512 * ib + wib],
                            in_=hqt[:, 512 * hsl:512 * hsl + wib])
                    op("sync", hstf)
                    inc_last("sync", "hb", 16)
            op("sync", lambda v=C["hb"].v: nc.sync.wait_ge(S.sems["hb"], v))
            if hf == 1:
                pass
            for ic in range(22):
                def hqtf(ic=ic):
                    return nc.sync.dma_start_transpose(
                        hqfm[:, 1024 * ic:1024 * (ic + 1)],
                        hqd[:, 128 * ic:128 * (ic + 1)])
                op("sync", hqtf)
                inc_last("sync", "hb", 16)
            hqdone = C["hb"].v

            xst_hist = ln_state.setdefault("xst_hist", [])

            def xap(c, o):
                return xrs[:, 512 * ((8 * c + o) % 2):512 * ((8 * c + o) % 2 + 1)]

            def xpf_emit(c, o):
                ko_ = len(xst_hist)
                if ko_ >= 2:
                    st = ln_state["stt_hist"]
                    if ko_ - 2 < len(st):
                        op("sync", lambda v=st[ko_ - 2]:
                           nc.sync.wait_ge(S.sems["stt"], v))

                def xpf(o=o, c=c):
                    return nc.sync.dma_start(
                        out=xrs[:, 512 * ((8 * c + o) % 2):512 * ((8 * c + o) % 2 + 1)],
                        in_=xfm[128 * o:128 * (o + 1), 512 * c:512 * (c + 1)])
                op("sync", xpf)
                xst_hist.append(inc_last("sync", "xs", 16))

            def y_emit(c, o, pyv, hqdone=hqdone):
                if o == 0 and c % 2 == 0:
                    wait("tensor", "hb", hqdone)
                cc_ = c % 2
                for ic in range(22):
                    def yf(c=c, o=o, ic=ic, pyv=pyv, cc_=cc_):
                        return nc.tensor.matmul(
                            pyv, wdT[:, 1024 * ic + 128 * o:1024 * ic + 128 * (o + 1)],
                            hqfm[:, 1024 * ic + 512 * cc_:1024 * ic + 512 * (cc_ + 1)],
                            start=(ic == 0), stop=(ic == 21))
                    op("tensor", yf)
                xpf_emit(c, o)
                wait("vector", "xs", xst_hist[-1])

            _emit_resid_ln(S, nc, P, views, y_emit, xap, fb_sb, x4,
                           c_list=[2 * hf, 2 * hf + 1], state=ln_state)
            wait("sync", "sm", C["sm"].v)
        op("sync", lambda v=C["x1s"].v: nc.sync.wait_ge(S.sems["x1s"], v))




class WQuant:

    JW = 384

    def __init__(self, S, AL, base, io):
        self.S = S
        nc = S.nc
        II = 2816
        self.jobs = []
        for src_, dst, nch, wid in [(io["wg"], io["wgq"], 22, 1024),
                                    (io["wu"], io["wuq"], 22, 1024),
                                    (io["wd"], io["wdq"], 8, II)]:
            for ch in range(nch):
                for co in range(0, wid, self.JW):
                    w = min(self.JW, wid - co)
                    self.jobs.append((src_, dst, ch, co, w))
        o = base
        def at(name, shape, dtype):
            nonlocal o
            p, w = shape
            nel = (w * DTSIZE[dtype] + 3) // 4
            v = AL.at(o, name, shape, dtype)
            o += nel
            return v
        self.wstg = [at(f"wq_stg{i}", [128, self.JW], F32) for i in range(2)]
        self.wqs = [at(f"wq_out{i}", [128, self.JW], BF16) for i in range(2)]
        W = self.JW
        self.qs = dict(
            t8=at("wq_t8", [128, W // 2], F32),
            t4=at("wq_t4", [128, W // 4], F32),
            t2=at("wq_t2", [128, W // 8], F32),
            bm=at("wq_bm", [128, W // 16], F32),
            sc=at("wq_sc", [128, W // 16], F32),
            rc=at("wq_rc", [128, W // 16], F32),
            scb=at("wq_scb", [128, W // 16], BF16),
            u=at("wq_u", [128, W], F32),
            tb=at("wq_tb", [128, W], F32),
            A=at("wq_A", [128, W], BF16),
            B4=at("wq_B4", [128, W], BF16),
            B2=at("wq_B2", [128, W], BF16),
        )
        self.top = o
        self.ws_base = S.C["ws"].v
        self.k = 0
        self.calls = 0
        self.wq_hist = []
        self.wo_hist = []

    def snapshot(self):
        return (self.k, self.calls, list(self.wq_hist), list(self.wo_hist))

    def restore(self, st):
        self.k, self.calls, wq, wo = st
        self.wq_hist = list(wq)
        self.wo_hist = list(wo)

    def _indma(self, S, j, emit_sync):
        if j >= len(self.jobs):
            return
        nc = S.nc
        if not emit_sync:
            S.C["ws"].inc(16)
            return
        if j >= 2:
            S.op("sync", lambda v=self.wq_hist[j - 2]:
                 nc.sync.wait_ge(S.sems["wq"], v))
        src_, dst, ch, co, w = self.jobs[j]
        S.op("sync", lambda src_=src_, ch=ch, co=co, w=w, sl=j % 2:
             nc.sync.dma_start(out=self.wstg[sl][:, 0:w],
                               in_=src_[128 * ch:128 * (ch + 1), co:co + w]))
        S.inc_last("sync", "ws", 16)

    def start(self, S, emit_sync):
        self._indma(S, 0, emit_sync)
        self._indma(S, 1, emit_sync)

    def step(self, Sb, emit_sync):
        if self.k >= len(self.jobs):
            return
        S, nc = self.S, self.S.nc
        j = self.k
        sl = j % 2
        src_, dst, ch, co, w = self.jobs[j]
        Sb.wait("vector", "ws", 16 * (j + 1) + self.ws_base)
        if j >= 2:
            Sb.wait("vector", "wo", self.wo_hist[j - 2])
        _emit_quant(Sb, nc, self.wqs[sl][:, 0:w], self.wstg[sl][:, 0:w],
                    self.qs, w)
        self.wq_hist.append(Sb.inc_last("vector", "wq", 1))
        if emit_sync:
            S.op("sync", lambda v=self.wq_hist[-1]:
                 nc.sync.wait_ge(S.sems["wq"], v))
            S.op("sync", lambda dst=dst, ch=ch, co=co, w=w, sl=sl:
                 nc.sync.dma_start(out=dst[128 * ch:128 * (ch + 1), co:co + w],
                                   in_=self.wqs[sl][:, 0:w]))
            self.wo_hist.append(S.inc_last("sync", "wo", 16))
        else:
            self.wo_hist.append(S.C["wo"].inc(16))
        self._indma(S, j + 2, emit_sync)
        self.k = j + 1


def build_ffn1(debug=False):
    II = 2816
    nc = bass.Bass("TRN2", target_bir_lowering=False)
    io = {}
    io["xfm"] = nc.dram_tensor("xfm", [D, TPC], F32, kind="ExternalInput").ap()
    io["wg"] = nc.dram_tensor("wg", [II, D], F32, kind="ExternalInput").ap()
    io["wu"] = nc.dram_tensor("wu", [II, D], F32, kind="ExternalInput").ap()
    io["wd"] = nc.dram_tensor("wd", [D, II], F32, kind="ExternalInput").ap()
    io["ident"] = nc.dram_tensor("ident", [128, 128], F32, kind="ExternalInput").ap()
    io["x4"] = nc.dram_tensor("x4", [D, TPC], F32, kind="ExternalOutput").ap()
    kind2 = "ExternalOutput" if debug else "Internal"
    io["wgq"] = nc.dram_tensor("wgq", [II, D], BF16, kind=kind2).ap()
    io["wuq"] = nc.dram_tensor("wuq", [II, D], BF16, kind="Internal").ap()
    io["wdq"] = nc.dram_tensor("wdq", [D, II], BF16, kind="Internal").ap()
    io["xqd"] = nc.dram_tensor("xqd", [TPC, D], BF16, kind=kind2).ap()
    io["hqd"] = nc.dram_tensor("hqd", [1024, II], BF16, kind=kind2).ap()
    if debug:
        io["dbg_g"] = nc.dram_tensor("dbg_g", [128, 512], F32, kind="ExternalOutput").ap()
        io["dbg_h"] = nc.dram_tensor("dbg_h", [128, 512], F32, kind="ExternalOutput").ap()
    with contextlib.ExitStack() as es:
        AL = Alloc(nc, es)
        S = Sched(nc, es, FFN1_SEMS)
        S.P = es.enter_context(nc.psum_tensor("P", [128, 4096], F32)).ap()
        block = es.enter_context(nc.Block())
        emit_ffn1(S, AL, io, debug)
        S.emit(block)
    return nc


def run_ffn1(x_shards, wg, wu, wd, res, debug=False):
    nc = build_ffn1(debug)
    ident = np.eye(128, dtype=np.float32)
    in_maps = [dict(xfm=x_shards[c], wg=wg, wu=wu, wd=wd, ident=ident)
               for c in range(NCORES)]
    out = run_bass_kernel_spmd(nc, in_maps, core_ids=list(range(NCORES)))
    res.append(out)
    if debug:
        return [dict(r) for r in out.results]
    return [r["x4"] for r in out.results]



def _gather_seq(shards, b):
    out = np.zeros((S, shards[0].shape[1]), dtype=bfloat16)
    for core in range(4 * b, 4 * b + 4):
        out[_local_positions(core)] = np.asarray(shards[core])
    return out


def kernel_unfused(L0_attn_x, L0_attn_Wq, L0_attn_bq, L0_attn_Wk, L0_attn_bk,
           L0_attn_Wv, L0_attn_bv, L0_attn_Wo, L0_attn_bo,
           L0_ffn_w_freq_real, L0_ffn_w_freq_imag, L0_ffn_bias,
           L1_attn_Wq, L1_attn_Wk, L1_attn_Wv, L1_attn_Wo,
           L1_ffn_w_gate, L1_ffn_w_up, L1_ffn_w_down):
    res = []
    x_shards = [_shard_x_fm(np.asarray(L0_attn_x), c) for c in range(NCORES)]

    def qperm_w(w):
        return np.ascontiguousarray(
            np.asarray(w).reshape(-1, 16, 64)[:, QPERM].reshape(w.shape))

    q, k, v = run_qkv(0, x_shards, dict(
        wq=qperm_w(L0_attn_Wq), wk=np.asarray(L0_attn_Wk),
        wv=np.asarray(L0_attn_Wv),
        bq=np.asarray(L0_attn_bq).reshape(16, 64)[QPERM].reshape(1024),
        bk=np.asarray(L0_attn_bk), bv=np.asarray(L0_attn_bv)), res)
    k_b = [_gather_seq(k, b) for b in range(B)]
    v_b = [_gather_seq(v, b) for b in range(B)]
    x1r = run_attn(0, q, k_b, v_b, x_shards, np.asarray(L0_attn_Wo),
                   np.asarray(L0_attn_bo), res)
    x1 = [np.asarray(r["x1"]) for r in x1r]
    A = run_amat(np.asarray(L0_ffn_w_freq_real),
                 np.asarray(L0_ffn_w_freq_imag), res)
    x2 = run_ffn0(x1, A, np.asarray(L0_ffn_bias), res)
    x2 = [np.asarray(t) for t in x2]
    q, k, v = run_qkv(1, x2, dict(
        wq=qperm_w(L1_attn_Wq), wk=np.asarray(L1_attn_Wk),
        wv=np.asarray(L1_attn_Wv)), res)
    k_b = [_gather_seq(k, b) for b in range(B)]
    v_b = [_gather_seq(v, b) for b in range(B)]
    zero_bo = np.zeros(1024, np.float32)
    x3r = run_attn(1, q, k_b, v_b, x2, np.asarray(L1_attn_Wo), zero_bo, res)
    x3 = [np.asarray(r["x1"]) for r in x3r]
    x4 = run_ffn1(x3, np.asarray(L1_ffn_w_gate), np.asarray(L1_ffn_w_up),
                  np.asarray(L1_ffn_w_down), res)
    out = np.zeros((B, S, D), np.float32)
    for core in range(NCORES):
        out[core // 4][_local_positions(core)] = np.asarray(x4[core]).T
    return out



FUSED_SEMS = sorted(set(ATTN_SEMS + FFN0_SEMS + FFN1_SEMS +
                        ["mm", "rp", "st", "q", "cc", "ph", "cid"]))


class BodySched:

    def __init__(self, parent):
        self.nc = parent.nc
        self.sems = parent.sems
        self.C = parent.C
        self.P = parent.P
        self.ops = {k: [] for k in parent.ops}
    op = Sched.op
    wait = Sched.wait
    inc_last = Sched.inc_last


def _barrier(S):
    nc = S.nc
    for eng in ["sync", "gpsimd", "tensor", "scalar", "vector"]:
        S.op(eng, lambda eng=eng: getattr(nc, eng).nop())
        S.inc_last(eng, "ph", 1)
    v = S.C["ph"].v
    for eng in ["sync", "gpsimd", "tensor", "scalar", "vector"]:
        S.wait(eng, "ph", v)


def build_fused(debug=False):
    II = 2816
    nc = bass.Bass("TRN2", target_bir_lowering=False)
    io = {}

    def inp(n, sh, dt=F32):
        io[n] = nc.dram_tensor(n, sh, dt, kind="ExternalInput").ap()

    def intl(n, sh, dt):
        kind = "ExternalOutput" if (debug and n in
                                    ("x1", "x2", "x3", "qtm0", "qtm1")) else "Internal"
        io[n] = nc.dram_tensor(n, sh, dt, kind=kind).ap()

    inp("xfm", [D, TPC])
    inp("cid", [1, 1], mybir.dt.int32)
    inp("cosT", [TPC, 32]); inp("sinT", [TPC, 32])
    inp("wq0", [D, 1024]); inp("wk0", [D, 256]); inp("wv0", [D, 256])
    inp("bqkv0", [1, 1536])
    inp("wo0", [1024, D]); inp("bo0", [128, 8])
    inp("dcolT", [128, 16]); inp("trowT", [128, 512])
    inp("dmskT", [128, 2048], BF16)
    inp("wfr", [D, D]); inp("wfi", [D, D]); inp("Cm", [D, D]); inp("Sm", [D, D])
    inp("Csl", [D, 128]); inp("Ssl", [D, 128]); inp("fb0", [128, 8])
    inp("wq1", [D, 1024]); inp("wk1", [D, 256]); inp("wv1", [D, 256])
    inp("wo1", [1024, D]); inp("bz", [128, 8])
    inp("albT", [128, 64 * 16])
    inp("wg", [II, D]); inp("wu", [II, D]); inp("wd", [D, II])
    inp("ident", [128, 128])
    io["out"] = nc.dram_tensor("out", [D, TPC], F32, kind="ExternalOutput").ap()
    for L in (0, 1):
        intl(f"qtm{L}", [TPC, 1024], BF16)
        intl(f"kvc{L}", [TPC, 512], BF16)
        intl(f"kvg{L}", [S_, 512], BF16)
        intl(f"dscr{L}", [2 * 64 * 512], BF16)
        intl(f"rscr{L}", [64 * 512], BF16)
    intl("x1", [D, TPC], F32)
    intl("x2", [D, TPC], F32)
    intl("x3", [D, TPC], F32)
    intl("Arc", [128, D], F32)
    intl("Ag", [D, D], F32)
    intl("wgq", [II, D], BF16)
    intl("wuq", [II, D], BF16)
    intl("wdq", [D, II], BF16)
    intl("xqd", [TPC, D], BF16)
    intl("hqd", [1024, II], BF16)

    with contextlib.ExitStack() as es:
        arena = es.enter_context(nc.sbuf_tensor("arena", [128, 53000], F32)).ap()
        cid_sb = es.enter_context(nc.sbuf_tensor("cid_sb", [1, 1], mybir.dt.int32)).ap()
        AL = Alloc(nc, es, arena=arena)
        S = Sched(nc, es, FUSED_SEMS)
        S.P = es.enter_context(nc.psum_tensor("P", [128, 4096], F32)).ap()
        C = S.C
        op, inc_last = S.op, S.inc_last

        op("sync", lambda: nc.sync.dma_start(out=cid_sb, in_=io["cid"]))
        inc_last("sync", "cid", 16)

        RBASE = 50080
        wq = WQuant(S, AL, RBASE, io)
        assert wq.top <= 53000, wq.top
        AL.limit = RBASE
        wq.start(S, True)

        emit_qkv(S, AL, dict(xfm=io["xfm"], wq=io["wq0"], wk=io["wk0"],
                             wv=io["wv0"], bqkv=io["bqkv0"], cosT=io["cosT"],
                             sinT=io["sinT"], qtm=io["qtm0"],
                             ktm=io["kvc0"][:, 0:256], vtm=io["kvc0"][:, 256:512]),
                 True)
        op("gpsimd", lambda v=C["st"].v: nc.gpsimd.wait_ge(S.sems["st"], v))
        op("gpsimd", lambda: nc.gpsimd.collective_compute(
            "AllGather", ALU.bypass, replica_groups=[[0, 1, 2, 3], [4, 5, 6, 7]],
            ins=[io["kvc0"].opt()], outs=[io["kvg0"].opt()]))
        inc_last("gpsimd", "cc", 1)
        _barrier(S)

        AL.reset()
        emit_amat(S, AL, dict(wfr=io["wfr"], wfi=io["wfi"], Cm=io["Cm"],
                              Sm=io["Sm"], Csl=io["Csl"], Ssl=io["Ssl"],
                              Ar=io["Arc"]))
        op("gpsimd", lambda: nc.gpsimd.collective_compute(
            "AllGather", ALU.bypass, replica_groups=[[0, 1, 2, 3, 4, 5, 6, 7]],
            ins=[io["Arc"].opt()], outs=[io["Ag"].opt()]))
        inc_last("gpsimd", "cc", 1)
        _barrier(S)

        def attn_phase(layer, qtm, kvg, xin, wo, bo, masks, xout, dscr, rscr):
            AL.reset()
            T = attn_tensors(AL, layer)
            iox = dict(qtm=qtm, kall=kvg[:, 0:256], vall=kvg[:, 256:512],
                       xfm=xin, wo=wo, bo=bo, x1=xout, dscr=dscr, rscr=rscr,
                       **masks)
            op("sync", lambda v=C["cc"].v: nc.sync.wait_ge(S.sems["cc"], v))
            emit_attn_pre(S, T, iox, layer, rank_major=True)
            if layer == 1:
                emit_attn_core(S, S, T, iox, 1, 0, emit_sync=True,
                               filler=lambda Sb_, es_: wq.step(Sb_, es_))
            else:
                marks = {k: len(S.ops[k]) for k in S.ops}
                snap = {k: C[k].v for k in C}
                bodies = []
                finals = []
                wqsnap = wq.snapshot()
                for ccv in range(4):
                    for k in C:
                        C[k].v = snap[k]
                    wq.restore(wqsnap)
                    Sb = BodySched(S)
                    emit_attn_core(S, Sb, T, iox, 0, ccv,
                                   emit_sync=(ccv == 0),
                                   filler=lambda Sb_, es_: wq.step(Sb_, es_))
                    bodies.append(Sb)
                    finals.append({k: C[k].v for k in C})
                assert all(f == finals[0] for f in finals), \
                    f"switch bodies diverge: {[{k: v for k, v in f.items() if f != finals[0]} for f in finals]}"
                S.switch_at = (marks, bodies)
            emit_attn_post(S, T, iox, layer)

        attn_phase(0, io["qtm0"], io["kvg0"], io["xfm"], io["wo0"], io["bo0"],
                   dict(dcolT=io["dcolT"], trowT=io["trowT"],
                        dmskT=io["dmskT"]), io["x1"],
                   io["dscr0"], io["rscr0"])
        print("wq jobs after attn0:", wq.k, "/", len(wq.jobs))
        _barrier(S)

        AL.reset()
        emit_ffn0(S, AL, dict(xfm=io["x1"], Am=io["Ag"], fb=io["fb0"],
                              x2=io["x2"]))
        _barrier(S)

        AL.reset()
        emit_qkv(S, AL, dict(xfm=io["x2"], wq=io["wq1"], wk=io["wk1"],
                             wv=io["wv1"], cosT=io["cosT"], sinT=io["sinT"],
                             qtm=io["qtm1"], ktm=io["kvc1"][:, 0:256],
                             vtm=io["kvc1"][:, 256:512]), False)
        op("gpsimd", lambda v=C["st"].v: nc.gpsimd.wait_ge(S.sems["st"], v))
        op("gpsimd", lambda: nc.gpsimd.collective_compute(
            "AllGather", ALU.bypass, replica_groups=[[0, 1, 2, 3], [4, 5, 6, 7]],
            ins=[io["kvc1"].opt()], outs=[io["kvg1"].opt()]))
        inc_last("gpsimd", "cc", 1)
        _barrier(S)

        attn_phase(1, io["qtm1"], io["kvg1"], io["x2"], io["wo1"], io["bz"],
                   dict(albT=io["albT"]), io["x3"], io["dscr1"], io["rscr1"])
        print("wq jobs after attn1:", wq.k, "/", len(wq.jobs))
        while wq.k < len(wq.jobs):
            wq.step(S, True)
        _barrier(S)
        AL.limit = None

        AL.reset()
        emit_ffn1(S, AL, dict(xfm=io["x3"], wg=io["wg"], wu=io["wu"],
                              wd=io["wd"], ident=io["ident"], x4=io["out"],
                              wgq=io["wgq"], wuq=io["wuq"], wdq=io["wdq"],
                              xqd=io["xqd"], hqd=io["hqd"]), False,
                   wq_pre=S.C["wo"].v)

        block = es.enter_context(nc.Block())
        marks, bodies = S.switch_at
        order = ["sync", "gpsimd", "tensor", "scalar", "vector"]

        @block.sync
        def _(sync):
            for f in S.ops["sync"]:
                f()

        @block.gpsimd
        def _(gpsimd):
            for f in S.ops["gpsimd"]:
                f()

        for eng in ["tensor", "scalar", "vector"]:
            for f in S.ops[eng][:marks[eng]]:
                f()
        for e in [nc.tensor, nc.scalar, nc.vector]:
            e.wait_ge(S.sems["cid"], 16)
        v = nc.values_load(cid_sb, engines=[mybir.EngineType.PE,
                                            mybir.EngineType.Activation,
                                            mybir.EngineType.DVE],
                           skip_runtime_bounds_check=True)
        for ccv in nc.Switch(engines=[nc.tensor, nc.scalar, nc.vector],
                             index=[v, v, v], n=4):
            for eng in ["tensor", "scalar", "vector"]:
                for f in bodies[ccv].ops[eng]:
                    f()
        for eng in ["tensor", "scalar", "vector"]:
            for f in S.ops[eng][marks[eng]:]:
                f()
        print("fused arena hwm:", AL.hwm, "f32/partition")
    return nc


_FUSED_CACHE = {}


def kernel(L0_attn_x, L0_attn_Wq, L0_attn_bq, L0_attn_Wk, L0_attn_bk,
           L0_attn_Wv, L0_attn_bv, L0_attn_Wo, L0_attn_bo,
           L0_ffn_w_freq_real, L0_ffn_w_freq_imag, L0_ffn_bias,
           L1_attn_Wq, L1_attn_Wk, L1_attn_Wv, L1_attn_Wo,
           L1_ffn_w_gate, L1_ffn_w_up, L1_ffn_w_down):
    debug = bool(_FUSED_CACHE.get("debug"))
    if "nc" not in _FUSED_CACHE:
        _FUSED_CACHE["nc"] = build_fused(debug)
    nc = _FUSED_CACHE["nc"]

    def qperm_w(w):
        return np.ascontiguousarray(
            np.asarray(w, np.float32).reshape(-1, 16, 64)[:, QPERM].reshape(w.shape))

    def colmajor8(b):
        return np.ascontiguousarray(np.asarray(b, np.float32).reshape(8, 128).T)

    Cm, Sm = _dft_tables()
    x = np.asarray(L0_attn_x)
    trow = np.tile(np.arange(512, dtype=np.float32)[None, :], (128, 1))
    alb = _host_alb()
    ident = np.eye(128, dtype=np.float32)
    p_ = np.arange(128, dtype=np.float32)[:, None]
    q_ = np.arange(512, dtype=np.float32)[None, :]
    dmsk = np.concatenate(
        [np.where(q_ >= 128 * mm + p_, 1.0, 0.0) for mm in range(4)],
        axis=1).astype(bfloat16)
    bq0 = np.asarray(L0_attn_bq).reshape(16, 64)[QPERM].reshape(1024)
    bqkv0 = np.concatenate([bq0, np.asarray(L0_attn_bk),
                            np.asarray(L0_attn_bv)])[None, :].astype(np.float32)
    in_maps = []
    for core in range(NCORES):
        cos, sin = _rope_tables(core)
        sl = slice(128 * core, 128 * (core + 1))
        m = dict(
            xfm=_shard_x_fm(x, core),
            cid=np.array([[core % 4]], np.int32),
            cosT=cos, sinT=sin,
            wq0=qperm_w(L0_attn_Wq), wk0=np.asarray(L0_attn_Wk),
            wv0=np.asarray(L0_attn_Wv), bqkv0=bqkv0,
            wo0=np.asarray(L0_attn_Wo), bo0=colmajor8(L0_attn_bo),
            dcolT=_host_dcol(core % 4), trowT=trow,
            dmskT=dmsk,
            wfr=np.asarray(L0_ffn_w_freq_real), wfi=np.asarray(L0_ffn_w_freq_imag),
            Cm=Cm, Sm=Sm,
            Csl=np.ascontiguousarray(Cm[:, sl] / D),
            Ssl=np.ascontiguousarray(Sm[:, sl] / D),
            fb0=colmajor8(L0_ffn_bias),
            wq1=qperm_w(L1_attn_Wq), wk1=np.asarray(L1_attn_Wk),
            wv1=np.asarray(L1_attn_Wv), wo1=np.asarray(L1_attn_Wo),
            bz=np.zeros((128, 8), np.float32),
            albT=alb,
            wg=np.asarray(L1_ffn_w_gate), wu=np.asarray(L1_ffn_w_up),
            wd=np.asarray(L1_ffn_w_down), ident=ident,
        )
        in_maps.append(m)
    res = run_bass_kernel_spmd(nc, in_maps, core_ids=list(range(NCORES)))
    out = np.zeros((B, S_, D), np.float32)
    for core in range(NCORES):
        out[core // 4][_local_positions(core)] = np.asarray(res.results[core]["out"]).T
    if debug:
        _FUSED_CACHE["res"] = res
    return out
